# version 01
# baseline (speedup 1.0000x reference)
import math
from contextlib import ExitStack

import numpy as np
import ml_dtypes
import concourse.bass as bass
import concourse.mybir as mybir
from concourse.bass_utils import run_bass_kernel_spmd

F32 = mybir.dt.float32
BF16 = mybir.dt.bfloat16
I32 = mybir.dt.int32
U32 = mybir.dt.uint32
AF = mybir.ActivationFunctionType
ALU = mybir.AluOpType
ds = bass.ds
AX = mybir.AxisListType

B, S, D_MODEL = 4, 8192, 1024
NCORES = 8
T = 4096
NT = T // 128
E = 8
KSEL = 4096
C_CAP = 640
FPAD = 2816
NH, HD = 16, 64
NB, MAXD = 320, 800

bf16 = ml_dtypes.bfloat16


class Prog:

    ENGS = ("sync", "gpsimd", "vector", "scalar", "tensor")

    def __init__(self, sm):
        self.streams = {e: [] for e in self.ENGS}
        self.counts = {k: 0 for k in sm}
        self.sm = sm

    def on(self, eng, fn):
        self.streams[eng].append(fn)

    def bump(self, sem, n=1):
        self.counts[sem] += n
        return self.counts[sem]

    def val(self, sem):
        return self.counts[sem]

    def fenced(self, engname, semname, ops, pre_waits=()):
        base = self.val(semname)
        smd = self.sm

        def fn(eng, ops=tuple(ops), base=base, pre=tuple(pre_waits), semname=semname):
            for ws, wv in pre:
                eng.wait_ge(smd[ws], wv)
            if base:
                eng.wait_ge(smd[semname], base)
            for j, op in enumerate(ops):
                if j > 0:
                    eng.wait_ge(smd[semname], base + j)
                op(eng).then_inc(smd[semname], 1)
        self.on(engname, fn)
        return self.bump(semname, len(ops))

    def replay(self, blk):
        for name in self.ENGS:
            stream = self.streams[name]
            if not stream:
                continue
            dec = getattr(blk, name)

            def make(stream):
                def _body(eng):
                    for fn in stream:
                        fn(eng)
                return _body
            dec(make(stream))


def sem_dict(ctx, nc, names):
    return {n: ctx.enter_context(nc.semaphore(n)) for n in names}


def _host_consts():
    t = {}
    t["ident"] = np.eye(128, dtype=bf16)
    return t


def _bucket_table():
    nb = NB // 2
    rel = np.arange(-(S - 1), S)
    buckets = (rel > 0).astype(np.int64) * nb
    rel_abs = np.abs(rel)
    mx = nb // 2
    small = rel_abs < mx
    large = np.log(np.maximum(rel_abs, 1).astype(np.float32) / mx) / math.log(MAXD / mx) * (nb - mx)
    large = (mx + large).astype(np.int64)
    large = np.minimum(large, nb - 1)
    return buckets + np.where(small, rel_abs, large)


class LnRouterEmitter:

    def __init__(self, p, sm, nc, ctx, tag, psT=None, psL=None, xt=None, lnb=None,
                 fmb=None, sq=None):
        e = ctx.enter_context
        self.p, self.sm = p, sm
        self.xt = xt or [e(nc.sbuf_tensor(f"{tag}xt{i}", [128, 1024], F32)) for i in range(2)]
        self.lnb = lnb or [e(nc.sbuf_tensor(f"{tag}lnb{i}", [128, 1024], BF16)) for i in range(2)]
        self.fmb = fmb or [e(nc.sbuf_tensor(f"{tag}fmb{i}", [128, 1024], BF16)) for i in range(2)]
        self.sum1 = e(nc.sbuf_tensor(f"{tag}sum1", [128, 1], F32))
        self.mean = e(nc.sbuf_tensor(f"{tag}mean", [128, 1], F32))
        self.sq = sq if sq is not None else e(nc.sbuf_tensor(f"{tag}sq", [128, 1024], BF16))
        self.ssq = e(nc.sbuf_tensor(f"{tag}ssq", [128, 1], F32))
        self.var = e(nc.sbuf_tensor(f"{tag}var", [128, 1], F32))
        self.rstd = e(nc.sbuf_tensor(f"{tag}rstd", [128, 1], F32))
        self.nmr = e(nc.sbuf_tensor(f"{tag}nmr", [128, 1], F32))
        self.smt = e(nc.sbuf_tensor(f"{tag}smt", [128, 8], F32))
        self.mx1 = e(nc.sbuf_tensor(f"{tag}mx1", [128, 1], F32))
        self.se1 = e(nc.sbuf_tensor(f"{tag}se1", [128, 1], F32))
        if psT is None:
            self.psT = [e(nc.psum_tensor(f"{tag}psT{i}", [128, 128], BF16)) for i in range(2)]
            self.psL = e(nc.psum_tensor(f"{tag}psL", [128, 8], F32))
        else:
            self.psT = psT
            self.psL = psL
        self.xt_free = [0, 0]
        self.lnb_free = [0, 0]
        self.fmb_free = [0, 0]

    def tile(self, i, x_dram, ln_out, sm_out, wr_sb, brrep, eps_sb, ident,
             extra_ln_sink=None):
        p, sm = self.p, self.sm
        par = i % 2
        xti, lnbi, fmbi = self.xt[par], self.lnb[par], self.fmb[par]

        def f(eng, i=i, xti=xti, wv=self.xt_free[par]):
            if wv:
                eng.wait_ge(sm["v"], wv)
            eng.dma_start(out=xti[:], in_=x_dram[ds(i * 128, 128), :]).then_inc(sm["ld"], 16)
        p.on("sync", f)
        lv = p.bump("ld", 16)

        vn = p.fenced("vector", "v", [
            lambda eng, xti=xti: eng.reduce_sum(self.sum1[:], xti[:], axis=AX.X),
        ], pre_waits=[("ld", lv)])

        sn = p.fenced("scalar", "s", [
            lambda eng, xti=xti: eng.activation(self.sq[:], xti[:], AF.Square,
                                                bias=0.0, scale=1.0,
                                                accum_out=self.ssq[:]),
        ], pre_waits=[("ld", lv), ("v", p.val("v"))])

        vn = p.fenced("vector", "v", [
            lambda eng: eng.tensor_scalar(self.mean[:], self.sum1[:], 1.0 / 1024,
                                          None, ALU.mult),
            lambda eng: eng.tensor_mul(self.nmr[:], self.mean[:], self.mean[:]),
            lambda eng: eng.scalar_tensor_tensor(self.var[:], self.ssq[:], 1.0 / 1024,
                                                 self.nmr[:], ALU.mult, ALU.subtract),
        ], pre_waits=[("v", vn), ("s", sn)])

        sn = p.fenced("scalar", "s", [
            lambda eng: eng.activation(self.rstd[:], self.var[:], AF.Sqrt,
                                       bias=eps_sb[:], scale=1.0),
        ], pre_waits=[("v", vn)])

        vn = p.fenced("vector", "v", [
            lambda eng: eng.reciprocal(self.rstd[:], self.rstd[:]),
            lambda eng: eng.tensor_mul(self.nmr[:], self.mean[:], self.rstd[:]),
            lambda eng: eng.tensor_scalar(self.nmr[:], self.nmr[:], -1.0, None, ALU.mult),
            lambda eng, xti=xti, lnbi=lnbi: eng.tensor_scalar(
                lnbi[:], xti[:], self.rstd[:], self.nmr[:], ALU.mult, ALU.add),
        ], pre_waits=[("s", sn), ("t", self.lnb_free[par])])
        self.xt_free[par] = vn

        def f(eng, vn=vn, lnbi=lnbi, i=i):
            eng.wait_ge(sm["v"], vn)
            eng.dma_start(out=ln_out[ds(i * 128, 128), :], in_=lnbi[:]).then_inc(sm["st"], 16)
        p.on("sync", f)
        p.bump("st", 16)
        if extra_ln_sink is not None:
            extra_ln_sink(lnbi, vn)

        tbase = p.val("t")
        vbase0 = p.val("v")

        def f(eng, vn=vn, lnbi=lnbi, tbase=tbase, vbase0=vbase0, ident=ident):
            eng.wait_ge(sm["v"], vn)
            for c in range(8):
                if c >= 2:
                    eng.wait_ge(sm["v"], vbase0 + c - 1)
                eng.transpose(self.psT[c % 2][:], lnbi[:, ds(c * 128, 128)],
                              ident[:]).then_inc(sm["t"], 1)
        p.on("tensor", f)
        p.bump("t", 8)

        def f(eng, tbase=tbase, fmbi=fmbi):
            for c in range(8):
                eng.wait_ge(sm["t"], tbase + c + 1)
                eng.tensor_copy(fmbi[:, ds(c * 128, 128)], self.psT[c % 2][:]).then_inc(sm["v"], 1)
        p.on("vector", f)
        p.bump("v", 8)
        vafter = p.val("v")

        def f(eng, vafter=vafter, fmbi=fmbi, wr_sb=wr_sb):
            eng.wait_ge(sm["v"], vafter)
            for c in range(8):
                inst = eng.matmul(self.psL[:], fmbi[:, ds(c * 128, 128)],
                                  wr_sb[:, ds(c * 8, 8)], start=(c == 0), stop=(c == 7))
            inst.then_inc(sm["t"], 1)
        p.on("tensor", f)
        tn = p.bump("t")
        self.lnb_free[par] = tn
        self.fmb_free[par] = tn

        vn = p.fenced("vector", "v", [
            lambda eng: eng.tensor_add(self.smt[:], self.psL[:], brrep[:]),
            lambda eng: eng.reduce_max(self.mx1[:], self.smt[:], axis=AX.X, negate=True),
        ], pre_waits=[("t", tn)])

        def f(eng, vn=vn):
            eng.wait_ge(sm["v"], vn)
            eng.activation(self.smt[:], self.smt[:], AF.Exp, bias=self.mx1[:], scale=1.0,
                           accum_out=self.se1[:]).then_inc(sm["s"], 1)
        p.on("scalar", f)
        sn = p.bump("s")

        vn = p.fenced("vector", "v", [
            lambda eng: eng.reciprocal(self.se1[:], self.se1[:]),
            lambda eng: eng.tensor_scalar(self.smt[:], self.smt[:], self.se1[:],
                                          None, ALU.mult),
        ], pre_waits=[("s", sn)])

        def f(eng, vn=vn, i=i):
            eng.wait_ge(sm["v"], vn)
            eng.dma_start(out=sm_out[ds(i * 128, 128), :], in_=self.smt[:]).then_inc(sm["st"], 16)
        p.on("sync", f)
        p.bump("st", 16)


def build_A():
    nc = bass.Bass("TRN2", target_bir_lowering=False, debug=False)
    x = nc.dram_tensor("x", [T, D_MODEL], F32, kind="ExternalInput").ap()
    wr = nc.dram_tensor("wr", [D_MODEL, E], F32, kind="ExternalInput").ap()
    br = nc.dram_tensor("br", [1, E], F32, kind="ExternalInput").ap()
    identp = nc.dram_tensor("identp", [128, 128], BF16, kind="ExternalInput").ap()
    ln_o = nc.dram_tensor("ln_o", [T, D_MODEL], BF16, kind="ExternalOutput").ap()
    sm_o = nc.dram_tensor("sm_o", [T, E], F32, kind="ExternalOutput").ap()

    with ExitStack() as ctx:
        e = ctx.enter_context
        wr_sb = e(nc.sbuf_tensor("wr_sb", [128, 64], BF16))
        brrep = e(nc.sbuf_tensor("brrep", [128, 8], F32))
        brrow = e(nc.sbuf_tensor("brrow", [1, 8], F32))
        ones1 = e(nc.sbuf_tensor("ones1", [1, 128], F32))
        eps_sb = e(nc.sbuf_tensor("eps_sb", [128, 1], F32))
        ident = e(nc.sbuf_tensor("ident", [128, 128], BF16))
        psB = e(nc.psum_tensor("psB", [128, 8], F32))

        sm = sem_dict(ctx, nc, ["ld", "v", "t", "s", "st"])
        blk = e(nc.Block())
        p = Prog(sm)

        def f(eng):
            eng.dma_start(out=brrow[:], in_=br).then_inc(sm["ld"], 16)
            eng.dma_start(out=ident[:], in_=identp).then_inc(sm["ld"], 16)
            eng.dma_start(out=wr_sb[:].rearrange("p (c e) -> p c e", e=8),
                          in_=wr.rearrange("(c p) e -> p c e", p=128)).then_inc(sm["ld"], 16)
        p.on("gpsimd", f)
        ldn = p.bump("ld", 48)

        def f(eng):
            eng.memset(eps_sb[:], 1e-5)
            eng.memset(ones1[:], 1.0).then_inc(sm["v"], 1)
        p.on("vector", f)
        vn = p.bump("v")

        def f(eng, ldn=ldn, vn=vn):
            eng.wait_ge(sm["ld"], ldn)
            eng.wait_ge(sm["v"], vn)
            eng.matmul(psB[:], ones1[:], brrow[:], start=True, stop=True).then_inc(sm["t"], 1)
        p.on("tensor", f)
        tn = p.bump("t")

        def f(eng, tn=tn):
            eng.wait_ge(sm["t"], tn)
            eng.tensor_copy(brrep[:], psB[:]).then_inc(sm["v"], 1)
        p.on("vector", f)
        p.bump("v")

        em = LnRouterEmitter(p, sm, nc, ctx, "a")
        for i in range(NT):
            em.tile(i, x, ln_o, sm_o, wr_sb, brrep, eps_sb, ident)

        def f(eng, n=p.val("st")):
            eng.wait_ge(sm["st"], n)
        p.on("sync", f)

        p.replay(blk)
    return nc



def build_B(dbg=False):
    nc = bass.Bass("TRN2", target_bir_lowering=False, debug=False)
    x0 = nc.dram_tensor("x0", [T, D_MODEL], F32, kind="ExternalInput").ap()
    ln1 = nc.dram_tensor("ln1", [T, D_MODEL], BF16, kind="ExternalInput").ap()
    sm_resh = nc.dram_tensor("sm_resh", [128, 2048], F32, kind="ExternalInput").ap()
    smT_loc = nc.dram_tensor("smT_loc", [E, T], F32, kind="ExternalInput").ap()
    sm_loc = nc.dram_tensor("sm_loc", [T, E], F32, kind="ExternalInput").ap()
    wupP = nc.dram_tensor("wupP", [4, D_MODEL, FPAD], F32, kind="ExternalInput").ap()
    bupP = nc.dram_tensor("bupP", [128, 88], F32, kind="ExternalInput").ap()
    wdnP = nc.dram_tensor("wdnP", [E, FPAD, D_MODEL], F32, kind="ExternalInput").ap()
    bdnP = nc.dram_tensor("bdnP", [128, 64], F32, kind="ExternalInput").ap()
    wqT = nc.dram_tensor("wqT", [D_MODEL, D_MODEL], F32, kind="ExternalInput").ap()
    wkT = nc.dram_tensor("wkT", [D_MODEL, D_MODEL], F32, kind="ExternalInput").ap()
    wvT = nc.dram_tensor("wvT", [D_MODEL, D_MODEL], F32, kind="ExternalInput").ap()
    bq_r = nc.dram_tensor("bq_r", [128, 8], F32, kind="ExternalInput").ap()
    bk_r = nc.dram_tensor("bk_r", [128, 8], F32, kind="ExternalInput").ap()
    bv_r = nc.dram_tensor("bv_r", [128, 8], F32, kind="ExternalInput").ap()
    gruW = nc.dram_tensor("gruW", [8, 64], F32, kind="ExternalInput").ap()
    gruB = nc.dram_tensor("gruB", [8, 1], F32, kind="ExternalInput").ap()
    gruC = nc.dram_tensor("gruC", [1, 16], F32, kind="ExternalInput").ap()
    g48a_i = nc.dram_tensor("g48a_i", [8, 1], F32, kind="ExternalInput").ap()
    g48b_i = nc.dram_tensor("g48b_i", [8, 1], F32, kind="ExternalInput").ap()
    grpsum = nc.dram_tensor("grpsum", [128, 128], F32, kind="ExternalInput").ap()
    thrsel = nc.dram_tensor("thrsel", [128, 8], F32, kind="ExternalInput").ap()
    eoff_i = nc.dram_tensor("eoff_i", [8, 1], F32, kind="ExternalInput").ap()
    bnd_i = nc.dram_tensor("bnd_i", [8, 1], F32, kind="ExternalInput").ap()
    identp = nc.dram_tensor("identp", [128, 128], BF16, kind="ExternalInput").ap()
    idf32p = nc.dram_tensor("idf32p", [128, 128], F32, kind="ExternalInput").ap()

    x1_o = nc.dram_tensor("x1_o", [T, D_MODEL], F32, kind="ExternalOutput").ap()
    qT_o = nc.dram_tensor("qT_o", [D_MODEL, T], BF16, kind="ExternalOutput").ap()
    kT_o = nc.dram_tensor("kT_o", [D_MODEL, T], BF16, kind="ExternalOutput").ap()
    vT_o = nc.dram_tensor("vT_o", [D_MODEL, T], BF16, kind="ExternalOutput").ap()
    gate_o = nc.dram_tensor("gate_o", [NH, T], F32, kind="ExternalOutput").ap()
    thr_o = nc.dram_tensor("thr_o", [8, 1], F32, kind="ExternalOutput").ap()

    kind_dbg = "ExternalOutput" if dbg else "Internal"
    gath = nc.dram_tensor("gath", [E * C_CAP, 1040], BF16, kind=kind_dbg).ap()
    posS_o = nc.dram_tensor("posS_o", [128, 8 * NT], U32, kind=kind_dbg).ap()
    posG_o = nc.dram_tensor("posG_o", [128, 8 * NT], U32, kind=kind_dbg).ap()
    outb = nc.dram_tensor("outb", [E * C_CAP + 128, D_MODEL], F32, kind=kind_dbg).ap()

    with ExitStack() as ctx:
        e = ctx.enter_context
        lo = e(nc.sbuf_tensor("lo", [128, 1], F32))
        hi = e(nc.sbuf_tensor("hi", [128, 1], F32))
        mid = e(nc.sbuf_tensor("mid", [128, 1], F32))
        cnt = e(nc.sbuf_tensor("cnt", [128, 1], F32))
        cond = e(nc.sbuf_tensor("cond", [128, 1], F32))
        ncond = e(nc.sbuf_tensor("ncond", [128, 1], F32))
        tmp1 = e(nc.sbuf_tensor("tmp1", [128, 1], F32))
        tmp2 = e(nc.sbuf_tensor("tmp2", [128, 1], F32))
        grp_sb = e(nc.sbuf_tensor("grp_sb", [128, 128], F32))
        tsel_sb = e(nc.sbuf_tensor("tsel_sb", [128, 8], F32))
        thr8 = e(nc.sbuf_tensor("thr8", [8, 1], F32))
        eoff_sb = e(nc.sbuf_tensor("eoff_sb", [8, 1], F32))
        bnd_sb = e(nc.sbuf_tensor("bnd_sb", [8, 1], F32))
        posS = e(nc.sbuf_tensor("posS", [128, 8 * NT], U32))
        posG = e(nc.sbuf_tensor("posG", [128, 8 * NT], U32))
        ident = e(nc.sbuf_tensor("ident", [128, 128], BF16))
        idf32 = e(nc.sbuf_tensor("idf32", [128, 128], F32))
        stg = [e(nc.sbuf_tensor(f"stg{i}", [128, 1040], BF16)) for i in range(2)]

        arena = e(nc.sbuf_tensor("arena", [128, 90112], mybir.dt.uint8))
        wexp = [arena[:, ds(i * 45056, 45056)].bitcast(BF16) for i in range(2)]
        ln2fm = arena[:, 0:65536].bitcast(BF16)
        xq = arena[:, 65536:81920].bitcast(F32)
        qkvw = [arena[:, ds(81920 + i * 2048, 2048)].bitcast(BF16) for i in range(2)]
        a4 = arena[:, 0:65536].bitcast(F32)
        smT = a4[0:8, 0:4096]
        maskT = a4[0:8, 4096:8192]
        psc = a4[0:8, 8192:12288]
        pgc = a4[0:8, 12288:16384]

        Xg = e(nc.sbuf_tensor("Xg", [128, 5120], BF16))
        midT = e(nc.sbuf_tensor("midT", [128, 14080], BF16))
        vals = midT[:, 0:4096].bitcast(F32)
        cmpd = Xg[:, 0:2048]
        dnf = e(nc.sbuf_tensor("dnf", [128, 640], F32))
        outT = e(nc.sbuf_tensor("outT", [128, 5120], F32))
        gt = [e(nc.sbuf_tensor(f"gt{j}", [128, 1040], BF16)) for j in range(5)]
        gv = e(nc.sbuf_tensor("gv", [128, 8], F32))
        zrow = e(nc.sbuf_tensor("zrow", [128, 1024], F32))
        bup_sb = e(nc.sbuf_tensor("bup_sb", [128, 88], F32))
        bdn_sb = e(nc.sbuf_tensor("bdn_sb", [128, 64], F32))
        bq_sb = e(nc.sbuf_tensor("bq_sb", [128, 8], F32))
        bk_sb = e(nc.sbuf_tensor("bk_sb", [128, 8], F32))
        bv_sb = e(nc.sbuf_tensor("bv_sb", [128, 8], F32))
        bq8_sb = e(nc.sbuf_tensor("bq8_sb", [128, 8], F32))
        lnt = e(nc.sbuf_tensor("lnt", [128, 1024], BF16))
        qkvo = [e(nc.sbuf_tensor(f"qkvo{i}", [128, 512], BF16)) for i in range(2)]

        gruW_sb = e(nc.sbuf_tensor("gruW_sb", [8, 64], F32))
        gruB_sb = e(nc.sbuf_tensor("gruB_sb", [8, 1], F32))
        gruC_sb = e(nc.sbuf_tensor("gruC_sb", [1, 16], F32))
        g48a_sb = e(nc.sbuf_tensor("g48a_sb", [8, 1], F32))
        g48b_sb = e(nc.sbuf_tensor("g48b_sb", [8, 1], F32))
        wa2 = e(nc.sbuf_tensor("wa2", [128, 1], BF16))
        wb2 = e(nc.sbuf_tensor("wb2", [128, 1], BF16))
        ba_sb = e(nc.sbuf_tensor("ba_sb", [1, 1], F32))
        bb_sb = e(nc.sbuf_tensor("bb_sb", [1, 1], F32))
        sga = e(nc.sbuf_tensor("sga", [1, 512], F32))
        sgb = e(nc.sbuf_tensor("sgb", [1, 512], F32))
        grow = e(nc.sbuf_tensor("grow", [1, 512], F32))

        sum1 = e(nc.sbuf_tensor("sum1", [128, 1], F32))
        mean = e(nc.sbuf_tensor("mean", [128, 1], F32))
        sq = e(nc.sbuf_tensor("sq", [128, 1024], BF16))
        ssq = e(nc.sbuf_tensor("ssq", [128, 1], F32))
        var = e(nc.sbuf_tensor("var", [128, 1], F32))
        rstd = e(nc.sbuf_tensor("rstd", [128, 1], F32))
        nmr = e(nc.sbuf_tensor("nmr", [128, 1], F32))
        eps_sb = e(nc.sbuf_tensor("eps_sb", [128, 1], F32))

        psA = e(nc.psum_tensor("psA", [128, 4096], F32))
        psUP = [psA[:, 0:1024], psA[:, 1024:2048]]
        psT2 = [psA[:, ds(2048 + i * 512, 512)].bitcast(BF16)[:, 0:128] for i in range(2)]
        psF2 = [psA[:, 2048:2176], psA[:, 2560:2688]]
        psC = psA[:, 3072:3073]
        psT8 = psA[0:8, 3074:3075]
        psWa = psA[0:64, 3076:3077]
        psWb = psA[0:64, 3078:3079]
        psBa = psA[0:1, 3080:3081]
        psBb = psA[0:1, 3082:3083]
        psGa = psA[0:1, 3072:3584]
        psGb = psA[0:1, 3584:4096]
        psQ = [psA[:, ds(i * 512, 512)] for i in range(4)]

        sm = sem_dict(ctx, nc, ["ld", "lw", "v", "t", "s", "st", "gq", "ga"])
        blk = e(nc.Block())
        p = Prog(sm)
        REGS = {}

        def bound_reg(eng, key, value):
            if key not in REGS:
                REGS[key] = eng.to_reg(value)
            return REGS[key]

        def f(eng):
            for dst, src in [(vals, sm_resh), (smT, smT_loc), (grp_sb, grpsum),
                             (tsel_sb, thrsel), (eoff_sb, eoff_i), (bnd_sb, bnd_i),
                             (bup_sb, bupP), (bdn_sb, bdnP), (bq_sb, bq_r),
                             (bk_sb, bk_r), (bv_sb, bv_r), (gruW_sb, gruW),
                             (gruB_sb, gruB), (gruC_sb, gruC), (g48a_sb, g48a_i),
                             (g48b_sb, g48b_i), (idf32, idf32p)]:
                eng.dma_start(out=dst[:], in_=src).then_inc(sm["ld"], 16)
        p.on("sync", f)
        ld_pro = p.bump("ld", 16 * 17)

        def f(eng):
            eng.dma_start(out=ident[:], in_=identp).then_inc(sm["lw"], 16)
        p.on("gpsimd", f)
        lw_id = p.bump("lw", 16)

        vz = p.fenced("vector", "v", [
            lambda eng: eng.memset(lo[:], 0.0),
            lambda eng: eng.memset(hi[:], 1.0000001),
            lambda eng: eng.memset(mid[:], 0.50000006),
            lambda eng: eng.memset(zrow[:], 0.0),
            lambda eng: eng.memset(eps_sb[:], 1e-5),
        ])

        def f(eng, vz=vz):
            eng.wait_ge(sm["v"], vz)
            eng.dma_start(out=outb[ds(E * C_CAP, 128), :], in_=zrow[:]).then_inc(sm["st"], 16)
        p.on("sync", f)
        st_zero = p.bump("st", 16)

        NITER = 48
        for it in range(NITER):
            vn = p.fenced("vector", "v", [
                lambda eng: eng.tensor_scalar(cmpd[:], vals[:], mid[:], 0.0,
                                              ALU.is_ge, ALU.add, accum_out=cnt[:]),
            ], pre_waits=[("ld", 16 * 2)] if it == 0 else ())

            def f(eng, vn=vn, first=(it == 0)):
                eng.wait_ge(sm["v"], vn)
                if first:
                    eng.wait_ge(sm["ld"], ld_pro)
                eng.matmul(psC, grp_sb[:], cnt[:], start=True, stop=True).then_inc(sm["t"], 1)
            p.on("tensor", f)
            tn = p.bump("t")

            p.fenced("vector", "v", [
                lambda eng: eng.tensor_scalar(cond[:], psC, 4096.0, None, ALU.is_ge),
                lambda eng: eng.tensor_scalar(ncond[:], cond[:], -1.0, 1.0, ALU.mult, ALU.add),
                lambda eng: eng.tensor_sub(tmp1[:], mid[:], lo[:]),
                lambda eng: eng.tensor_mul(tmp1[:], tmp1[:], cond[:]),
                lambda eng: eng.tensor_add(lo[:], lo[:], tmp1[:]),
                lambda eng: eng.tensor_sub(tmp2[:], mid[:], hi[:]),
                lambda eng: eng.tensor_mul(tmp2[:], tmp2[:], ncond[:]),
                lambda eng: eng.tensor_add(hi[:], hi[:], tmp2[:]),
                lambda eng: eng.tensor_add(mid[:], lo[:], hi[:]),
                lambda eng: eng.tensor_scalar(mid[:], mid[:], 0.5, None, ALU.mult),
            ], pre_waits=[("t", tn)])

        vlast = p.val("v")

        def f(eng, vlast=vlast):
            eng.wait_ge(sm["v"], vlast)
            eng.matmul(psT8, tsel_sb[:], lo[:], start=True, stop=True).then_inc(sm["t"], 1)
        p.on("tensor", f)
        tn = p.bump("t")
        vn = p.fenced("vector", "v", [
            lambda eng: eng.tensor_copy(thr8[:], psT8),
        ], pre_waits=[("t", tn)])

        def f(eng, vn=vn):
            eng.wait_ge(sm["v"], vn)
            eng.dma_start(out=thr_o, in_=thr8[:]).then_inc(sm["st"], 16)
        p.on("sync", f)
        p.bump("st", 16)

        DUMP = float(E * C_CAP)
        BIGF = float(1 << 26)
        vpos_chain = p.fenced("vector", "v", [
            lambda eng: eng.tensor_scalar(maskT, smT, thr8[:], None, ALU.is_ge),
            lambda eng: eng.memset(psc, 0.0),
            lambda eng: eng.tensor_tensor_scan(smT, maskT, psc, 0.0, ALU.add, ALU.add),
            lambda eng: eng.tensor_scalar(psc, smT, eoff_sb[:], -1.0, ALU.add, ALU.add),
            lambda eng: eng.tensor_mul(pgc, psc, maskT),
            lambda eng: eng.scalar_tensor_tensor(pgc, maskT, -DUMP, pgc, ALU.mult, ALU.add),
            lambda eng: eng.tensor_scalar(pgc, pgc, DUMP, None, ALU.add),
            lambda eng: eng.tensor_scalar(smT, pgc, bnd_sb[:], None, ALU.is_ge),
            lambda eng: eng.tensor_scalar(psc, psc, 0.0, None, ALU.add),
            lambda eng: eng.tensor_scalar(maskT, maskT, -1.0, 1.0, ALU.mult, ALU.add),
            lambda eng: eng.scalar_tensor_tensor(psc, maskT, BIGF, psc, ALU.mult, ALU.add),
            lambda eng: eng.tensor_scalar(maskT, smT, 0.0, None, ALU.add),
            lambda eng: eng.tensor_scalar(smT, pgc, -1.0, DUMP, ALU.mult, ALU.add),
            lambda eng: eng.tensor_mul(smT, smT, maskT),
            lambda eng: eng.tensor_add(pgc, pgc, smT),
        ], pre_waits=[("v", vn)])

        vtr_copies = p.val("v")
        for i in range(NT):
            for which, src, dstp in ((0, psc, posS), (1, pgc, posG)):
                q = i * 2 + which
                par = q % 2

                def f(eng, i=i, src=src, par=par, q=q, vpos_chain=vpos_chain,
                      vtr_copies=vtr_copies):
                    if q == 0:
                        eng.wait_ge(sm["v"], vpos_chain)
                    if q >= 2:
                        eng.wait_ge(sm["v"], vtr_copies + q - 1)
                    eng.transpose(psF2[par][:, 0:8], src[:, ds(i * 128, 128)],
                                  idf32[0:8, 0:8]).then_inc(sm["t"], 1)
                p.on("tensor", f)
                tn = p.bump("t")

                def f(eng, tn=tn, i=i, par=par, dstp=dstp):
                    eng.wait_ge(sm["t"], tn)
                    eng.tensor_copy(dstp[:, ds(i * 8, 8)], psF2[par][:, 0:8]).then_inc(sm["v"], 1)
                p.on("vector", f)
                p.bump("v")
        vpos = p.val("v")

        if True:
            def f(eng, vpos=vpos):
                eng.wait_ge(sm["v"], vpos)
                eng.dma_start(out=posS_o, in_=posS[:]).then_inc(sm["st"], 16)
                eng.dma_start(out=posG_o, in_=posG[:]).then_inc(sm["st"], 16)
            p.on("sync", f)
            p.bump("st", 32)

        for i in range(NT):
            par = i % 2
            sti = stg[par]
            gq_before = p.val("gq")

            def f(eng, i=i, sti=sti, wv=gq_before - 16 * 8):
                if wv > 0:
                    eng.wait_ge(sm["gq"], wv)
                eng.dma_start(out=sti[:, 0:1024], in_=ln1[ds(i * 128, 128), :]).then_inc(sm["ld"], 16)
                eng.dma_start(out=sti[:].bitcast(F32)[:, 512:520],
                              in_=sm_loc[ds(i * 128, 128), :]).then_inc(sm["ld"], 16)
            p.on("sync", f)
            lv = p.bump("ld", 32)

            def f(eng, i=i, sti=sti, lv=lv, vpos=vpos):
                eng.wait_ge(sm["ld"], lv)
                eng.wait_ge(sm["v"], vpos)
                for ee in range(E):
                    eng.indirect_dma_start(
                        out=gath,
                        out_offset=bass.IndirectOffsetOnAxis(ap=posS[:, ds(i * 8 + ee, 1)], axis=0),
                        in_=sti[:], in_offset=None,
                        bounds_check=bound_reg(eng, ee, (ee + 1) * C_CAP - 1),
                        oob_is_err=False,
                    ).then_inc(sm["gq"], 16)
            p.on("gpsimd", f)
            p.bump("gq", 16 * 8)
        gq_scat = p.val("gq")

        t_up_done = 0
        t_dn_done = 0
        vgv_prev = 0
        st_out_done = st_zero
        for ee in range(E):
            for j in range(5):
                def f(eng, ee=ee, j=j, gq_scat=gq_scat, wv=t_up_done, wvv=vgv_prev):
                    if j == 0:
                        eng.wait_ge(sm["gq"], gq_scat)
                    if wv:
                        eng.wait_ge(sm["t"], wv)
                    if wvv:
                        eng.wait_ge(sm["v"], wvv)
                    eng.dma_start(out=gt[j][:], in_=gath[ds(ee * C_CAP + j * 128, 128), :]).then_inc(sm["ld"], 16)
                p.on("sync", f)
            lgt = p.bump("ld", 16 * 5)

            for k in range(8):
                def f(eng, ee=ee, k=k, wv=t_up_done, vpos=vpos):
                    if wv:
                        eng.wait_ge(sm["t"], wv)
                    if ee == 0 and k == 0:
                        eng.wait_ge(sm["v"], vpos)
                        eng.wait_ge(sm["gq"], gq_scat)
                    eng.dma_start(out=wexp[0][:, ds(k * 2816, 2816)],
                                  in_=wupP[ee // 2, ds(k * 128, 128), :]).then_inc(sm["lw"], 16)
                p.on("gpsimd", f)
            lwu = p.bump("lw", 16 * 8)

            vxg_base = p.val("v")
            for q in range(40):
                j, c = q // 8, q % 8

                def f(eng, j=j, c=c, q=q, lgt=lgt, vxg_base=vxg_base):
                    if q == 0:
                        eng.wait_ge(sm["ld"], lgt)
                        eng.wait_ge(sm["lw"], lw_id)
                        eng.wait_ge(sm["v"], vxg_base)
                    if q >= 2:
                        eng.wait_ge(sm["v"], vxg_base + q - 1)
                    eng.transpose(psT2[q % 2][:], gt[j][:, ds(c * 128, 128)],
                                  ident[:]).then_inc(sm["t"], 1)
                p.on("tensor", f)
                tn = p.bump("t")

                def f(eng, tn=tn, j=j, c=c, q=q):
                    eng.wait_ge(sm["t"], tn)
                    eng.tensor_copy(Xg[:, ds(c * 640 + j * 128, 128)],
                                    psT2[q % 2][:]).then_inc(sm["v"], 1)
                p.on("vector", f)
                p.bump("v")
            vxg = p.val("v")

            vgv = p.fenced("vector", "v", [
                (lambda eng, j=j2, ee=ee: eng.tensor_copy(
                    gv[:, ds(j, 1)], gt[j][:].bitcast(F32)[:, ds(512 + ee, 1)]))
                for j2 in range(5)
            ], pre_waits=[("ld", lgt)])
            vgv_prev = vgv

            s_silu_base = p.val("s")
            for m in range(22):
                def f(eng, m=m, lwu=lwu, vxg=vxg, s_silu_base=s_silu_base):
                    if m == 0:
                        eng.wait_ge(sm["lw"], lwu)
                        eng.wait_ge(sm["v"], vxg)
                    if m < 2:
                        eng.wait_ge(sm["s"], s_silu_base)
                    if m >= 2:
                        eng.wait_ge(sm["s"], s_silu_base + m - 1)
                    pu = psUP[m % 2]
                    for k in range(8):
                        eng.matmul(pu[:, 0:512], wexp[0][:, ds(k * 2816 + m * 128, 128)],
                                   Xg[:, ds(k * 640, 512)], start=(k == 0), stop=(k == 7))
                    for k in range(8):
                        inst = eng.matmul(pu[:, 512:640],
                                          wexp[0][:, ds(k * 2816 + m * 128, 128)],
                                          Xg[:, ds(k * 640 + 512, 128)],
                                          start=(k == 0), stop=(k == 7))
                    inst.then_inc(sm["t"], 1)
                p.on("tensor", f)
                tn = p.bump("t")
                if m == 21:
                    t_up_done = tn

                def f(eng, tn=tn, m=m, ee=ee):
                    eng.wait_ge(sm["t"], tn)
                    pu = psUP[m % 2]
                    eng.activation(midT[:, ds(m * 640, 512)], pu[:, 0:512], AF.Silu,
                                   bias=bup_sb[:, ds((ee // 2) * 22 + m, 1)], scale=1.0)
                    eng.activation(midT[:, ds(m * 640 + 512, 128)], pu[:, 512:640], AF.Silu,
                                   bias=bup_sb[:, ds((ee // 2) * 22 + m, 1)],
                                   scale=1.0).then_inc(sm["s"], 1)
                p.on("scalar", f)
                p.bump("s")
            sup = p.val("s")

            for k in range(22):
                def f(eng, ee=ee, k=k, wv=t_dn_done, vpos=vpos):
                    if wv:
                        eng.wait_ge(sm["t"], wv)
                    if ee == 0 and k == 0:
                        eng.wait_ge(sm["v"], vpos)
                    eng.dma_start(out=wexp[1][:, ds(k * 1024, 1024)],
                                  in_=wdnP[ee, ds(k * 128, 128), :]).then_inc(sm["lw"], 16)
                p.on("gpsimd", f)
            lwd = p.bump("lw", 16 * 22)

            s_dn_base = p.val("s")
            for o in range(8):
                def f(eng, o=o, lwd=lwd, sup=sup, s_dn_base=s_dn_base):
                    if o == 0:
                        eng.wait_ge(sm["lw"], lwd)
                        eng.wait_ge(sm["s"], sup)
                    if o >= 2:
                        eng.wait_ge(sm["s"], s_dn_base + o - 1)
                    pu = psUP[o % 2]
                    for k in range(22):
                        eng.matmul(pu[:, 0:512], wexp[1][:, ds(k * 1024 + o * 128, 128)],
                                   midT[:, ds(k * 640, 512)], start=(k == 0), stop=(k == 21))
                    for k in range(22):
                        inst = eng.matmul(pu[:, 512:640],
                                          wexp[1][:, ds(k * 1024 + o * 128, 128)],
                                          midT[:, ds(k * 640 + 512, 128)],
                                          start=(k == 0), stop=(k == 21))
                    inst.then_inc(sm["t"], 1)
                p.on("tensor", f)
                tn = p.bump("t")
                if o == 7:
                    t_dn_done = tn

                tr_prev = p.val("t")

                def f(eng, tn=tn, o=o, ee=ee, tr_prev=tr_prev):
                    eng.wait_ge(sm["t"], tr_prev)
                    pu = psUP[o % 2]
                    eng.activation(dnf[:, 0:512], pu[:, 0:512], AF.Identity,
                                   bias=bdn_sb[:, ds(ee * 8 + o, 1)], scale=1.0)
                    eng.activation(dnf[:, 512:640], pu[:, 512:640], AF.Identity,
                                   bias=bdn_sb[:, ds(ee * 8 + o, 1)], scale=1.0).then_inc(sm["s"], 1)
                p.on("scalar", f)
                sn = p.bump("s")

                vtrd_base = p.val("v")
                for j in range(5):
                    def f(eng, sn=sn, j=j, vtrd_base=vtrd_base):
                        if j == 0:
                            eng.wait_ge(sm["s"], sn)
                            eng.wait_ge(sm["v"], vtrd_base)
                        if j >= 2:
                            eng.wait_ge(sm["v"], vtrd_base + j - 1)
                        eng.transpose(psF2[j % 2], dnf[:, ds(j * 128, 128)],
                                      idf32[:]).then_inc(sm["t"], 1)
                    p.on("tensor", f)
                    tn2 = p.bump("t")

                    def f(eng, tn2=tn2, o=o, j=j, wst=st_out_done):
                        eng.wait_ge(sm["t"], tn2)
                        if o == 0 and j == 0:
                            eng.wait_ge(sm["st"], wst)
                        eng.tensor_copy(outT[:, ds(j * 1024 + o * 128, 128)],
                                        psF2[j % 2]).then_inc(sm["v"], 1)
                    p.on("vector", f)
                    p.bump("v")
            vdn = p.val("v")

            for j in range(5):
                vn = p.fenced("vector", "v", [
                    lambda eng, j=j: eng.tensor_scalar(tmp1[:], gv[:, ds(j, 1)], 0.5,
                                                       None, ALU.mult),
                    lambda eng, j=j: eng.tensor_scalar(outT[:, ds(j * 1024, 1024)],
                                                       outT[:, ds(j * 1024, 1024)],
                                                       tmp1[:], None, ALU.mult),
                ], pre_waits=[("v", vdn), ("v", vgv)] if j == 0 else ())

                def f(eng, vn=vn, ee=ee, j=j):
                    eng.wait_ge(sm["v"], vn)
                    eng.dma_start(out=outb[ds(ee * C_CAP + j * 128, 128), :],
                                  in_=outT[:, ds(j * 1024, 1024)]).then_inc(sm["st"], 16)
                p.on("sync", f)
                st_out_done = p.bump("st", 16)

        ga_base = 0
        st_x1_prev = 0
        v_ln2_prev = 0
        for g in range(8):
            tiles = [g * 4 + u for u in range(4)]
            for u, i in enumerate(tiles):
                def f(eng, i=i, u=u, st_out_done=st_out_done, wv=v_ln2_prev,
                      wst=st_x1_prev, first=(g == 0)):
                    if first and u == 0:
                        eng.wait_ge(sm["st"], st_out_done)
                    if wv:
                        eng.wait_ge(sm["v"], wv)
                    if wst:
                        eng.wait_ge(sm["st"], wst)
                    eng.dma_start(out=xq[:, ds(u * 1024, 1024)],
                                  in_=x0[ds(i * 128, 128), :]).then_inc(sm["ld"], 16)
                p.on("sync", f)
            lx = p.bump("ld", 16 * 4)

            ga_grp_base = p.val("ga")
            for ee in range(E):
                for u, i in enumerate(tiles):
                    def f(eng, ee=ee, i=i, u=u, lx=lx, ga_grp_base=ga_grp_base):
                        if ee == 0 and u == 0:
                            eng.wait_ge(sm["ld"], lx)
                        if ee > 0:
                            eng.wait_ge(sm["ga"], ga_grp_base + 16 * ((ee - 1) * 4 + u + 1))
                        eng.indirect_dma_start(
                            out=xq[:, ds(u * 1024, 1024)], out_offset=None,
                            in_=outb,
                            in_offset=bass.IndirectOffsetOnAxis(ap=posG[:, ds(i * 8 + ee, 1)], axis=0),
                            compute_op=ALU.add, oob_is_err=True,
                            bounds_check=bound_reg(eng, "ga", E * C_CAP + 127),
                        ).then_inc(sm["ga"], 16)
                    p.on("gpsimd", f)
                    p.bump("ga", 16)
            gdone = p.val("ga")

            for u, i in enumerate(tiles):
                def f(eng, u=u, i=i, gdone=gdone):
                    eng.wait_ge(sm["ga"], gdone)
                    eng.dma_start(out=x1_o[ds(i * 128, 128), :],
                                  in_=xq[:, ds(u * 1024, 1024)]).then_inc(sm["st"], 16)
                p.on("sync", f)
                st_x1_prev = p.bump("st", 16)

                xti = xq[:, ds(u * 1024, 1024)]
                vn = p.fenced("vector", "v", [
                    lambda eng, xti=xti: eng.reduce_sum(sum1[:], xti, axis=AX.X),
                ], pre_waits=[("ga", gdone)])
                sn = p.fenced("scalar", "s", [
                    lambda eng, xti=xti: eng.activation(sq[:], xti, AF.Square, bias=0.0,
                                                        scale=1.0, accum_out=ssq[:]),
                ], pre_waits=[("ga", gdone), ("v", p.val("v"))])
                vn = p.fenced("vector", "v", [
                    lambda eng: eng.tensor_scalar(mean[:], sum1[:], 1.0 / 1024, None, ALU.mult),
                    lambda eng: eng.tensor_mul(nmr[:], mean[:], mean[:]),
                    lambda eng: eng.scalar_tensor_tensor(var[:], ssq[:], 1.0 / 1024,
                                                         nmr[:], ALU.mult, ALU.subtract),
                ], pre_waits=[("v", vn), ("s", sn)])
                sn = p.fenced("scalar", "s", [
                    lambda eng: eng.activation(rstd[:], var[:], AF.Sqrt, bias=eps_sb[:],
                                               scale=1.0),
                ], pre_waits=[("v", vn)])
                vn = p.fenced("vector", "v", [
                    lambda eng: eng.reciprocal(rstd[:], rstd[:]),
                    lambda eng: eng.tensor_mul(nmr[:], mean[:], rstd[:]),
                    lambda eng: eng.tensor_scalar(nmr[:], nmr[:], -1.0, None, ALU.mult),
                    lambda eng, xti=xti: eng.tensor_scalar(lnt[:], xti, rstd[:], nmr[:],
                                                           ALU.mult, ALU.add),
                ], pre_waits=[("s", sn), ("t", p.val("t"))])

                vtl_base = p.val("v")
                for c in range(8):
                    def f(eng, c=c, vn=vn, vtl_base=vtl_base):
                        if c == 0:
                            eng.wait_ge(sm["v"], vn)
                        if c >= 2:
                            eng.wait_ge(sm["v"], vtl_base + c - 1)
                        eng.transpose(psT2[c % 2][:], lnt[:, ds(c * 128, 128)],
                                      ident[:]).then_inc(sm["t"], 1)
                    p.on("tensor", f)
                    tn = p.bump("t")

                    def f(eng, tn=tn, c=c, i=i):
                        eng.wait_ge(sm["t"], tn)
                        eng.tensor_copy(ln2fm[:, ds(c * 4096 + i * 128, 128)],
                                        psT2[c % 2][:]).then_inc(sm["v"], 1)
                    p.on("vector", f)
                    p.bump("v")
            v_ln2_prev = p.val("v")
        vln2 = p.val("v")

        vbq = p.fenced("vector", "v", [
            lambda eng: eng.tensor_scalar(bq8_sb[:], bq_sb[:], 0.125, None, ALU.mult),
        ], pre_waits=[("ld", ld_pro)])

        qkv_slot_free_st = [0, 0]
        wslot_free_t = [0, 0]
        projs = [(wqT, bq8_sb, 0.125, qT_o), (wkT, bk_sb, 1.0, kT_o),
                 (wvT, bv_sb, 1.0, vT_o)]
        lin = 0
        for pi, (wsrc, bias_sb, scale, out_dram) in enumerate(projs):
            for o in range(8):
                wslot = qkvw[o % 2]

                def f(eng, o=o, wslot=wslot, wsrc=wsrc, wv=wslot_free_t[o % 2],
                      vln2=vln2):
                    if wv:
                        eng.wait_ge(sm["t"], wv)
                    eng.dma_start(
                        out=wslot[:].rearrange("p (k f) -> p k f", f=128),
                        in_=wsrc[:, ds(o * 128, 128)].rearrange("(k p) f -> p k f", p=128),
                    ).then_inc(sm["lw"], 16)
                p.on("gpsimd", f)
                lw = p.bump("lw", 16)

                for n in range(8):
                    bank = lin % 4
                    s_prev_bank = p.val("s") - 3

                    def f(eng, o=o, n=n, bank=bank, lw=lw, vln2=vln2, wslot=wslot,
                          sv=s_prev_bank, lin=lin, sall=p.val("s")):
                        if n == 0:
                            eng.wait_ge(sm["lw"], lw)
                        if lin == 0:
                            eng.wait_ge(sm["v"], vln2)
                            eng.wait_ge(sm["s"], sall)
                        if lin >= 4:
                            eng.wait_ge(sm["s"], sv)
                        for k in range(8):
                            inst = eng.matmul(psQ[bank], wslot[:, ds(k * 128, 128)],
                                              ln2fm[:, ds(k * 4096 + n * 512, 512)],
                                              start=(k == 0), stop=(k == 7))
                        inst.then_inc(sm["t"], 1)
                    p.on("tensor", f)
                    tn = p.bump("t")
                    if n == 7:
                        wslot_free_t[o % 2] = tn

                    oslot = lin % 2

                    def f(eng, tn=tn, o=o, bank=bank, bias_sb=bias_sb, scale=scale,
                          oslot=oslot, wst=qkv_slot_free_st[oslot]):
                        eng.wait_ge(sm["t"], tn)
                        if wst:
                            eng.wait_ge(sm["st"], wst)
                        eng.activation(qkvo[oslot][:], psQ[bank], AF.Identity,
                                       bias=bias_sb[:, ds(o, 1)],
                                       scale=scale).then_inc(sm["s"], 1)
                    p.on("scalar", f)
                    sn = p.bump("s")

                    def f(eng, sn=sn, o=o, n=n, out_dram=out_dram, oslot=oslot):
                        eng.wait_ge(sm["s"], sn)
                        eng.dma_start(out=out_dram[ds(o * 128, 128), ds(n * 512, 512)],
                                      in_=qkvo[oslot][:]).then_inc(sm["st"], 16)
                    p.on("sync", f)
                    qkv_slot_free_st[oslot] = p.bump("st", 16)
                    lin += 1

        def f(eng, wv=ld_pro):
            eng.wait_ge(sm["ld"], wv)
            eng.matmul(psWa, gruW_sb[:], g48a_sb[:], start=True, stop=True)
            eng.matmul(psWb, gruW_sb[:], g48b_sb[:], start=True, stop=True)
            eng.matmul(psBa, gruB_sb[:], g48a_sb[:], start=True, stop=True)
            eng.matmul(psBb, gruB_sb[:], g48b_sb[:], start=True, stop=True).then_inc(sm["t"], 1)
        p.on("tensor", f)
        tn = p.bump("t")

        vn = p.fenced("vector", "v", [
            lambda eng: eng.tensor_copy(wa2[0:64, :], psWa),
            lambda eng: eng.tensor_copy(wa2[64:128, :], psWa),
            lambda eng: eng.tensor_copy(wb2[0:64, :], psWb),
            lambda eng: eng.tensor_copy(wb2[64:128, :], psWb),
            lambda eng: eng.tensor_copy(ba_sb[:], psBa),
            lambda eng: eng.tensor_copy(bb_sb[:], psBb),
        ], pre_waits=[("t", tn)])
        vgru = vn

        for h in range(NH):
            for n in range(8):
                base = (h % 2) * 64

                def f(eng, h=h, n=n, base=base, vgru=vgru, vln2=vln2, wv=p.val("s")):
                    if h == 0 and n == 0:
                        eng.wait_ge(sm["v"], vgru)
                        eng.wait_ge(sm["v"], vln2)
                    else:
                        eng.wait_ge(sm["s"], wv)
                    eng.matmul(psGa[:, 0:512], wa2[ds(base, 64), :],
                               ln2fm[ds(base, 64), ds((h // 2) * 4096 + n * 512, 512)],
                               start=True, stop=True)
                    eng.matmul(psGb, wb2[ds(base, 64), :],
                               ln2fm[ds(base, 64), ds((h // 2) * 4096 + n * 512, 512)],
                               start=True, stop=True).then_inc(sm["t"], 1)
                p.on("tensor", f)
                tn = p.bump("t")

                sn = p.fenced("scalar", "s", [
                    lambda eng: eng.activation(sga[:], psGa[:, 0:512], AF.Sigmoid,
                                               bias=ba_sb[:], scale=1.0),
                    lambda eng: eng.activation(sgb[:], psGb, AF.Sigmoid,
                                               bias=bb_sb[:], scale=1.0),
                ], pre_waits=[("t", tn), ("v", p.val("v"))])

                vn = p.fenced("vector", "v", [
                    lambda eng, h=h: eng.tensor_scalar(sgb[:], sgb[:], gruC_sb[0:1, ds(h, 1)],
                                                       -1.0, ALU.mult, ALU.add),
                    lambda eng: eng.tensor_mul(grow[:], sga[:], sgb[:]),
                    lambda eng: eng.tensor_scalar(grow[:], grow[:], 2.0, None, ALU.add),
                ], pre_waits=[("s", sn), ("st", p.val("st"))])

                def f(eng, vn=vn, h=h, n=n):
                    eng.wait_ge(sm["v"], vn)
                    eng.dma_start(out=gate_o[ds(h, 1), ds(n * 512, 512)],
                                  in_=grow[:]).then_inc(sm["st"], 16)
                p.on("sync", f)
                p.bump("st", 16)

        def f(eng, n=p.val("st")):
            eng.wait_ge(sm["st"], n)
        p.on("sync", f)
        def f2(eng, n=p.val("ga")):
            eng.wait_ge(sm["ga"], n)
        p.on("gpsimd", f2)

        p.replay(blk)
    return nc
def build_C():
    nc = bass.Bass("TRN2", target_bir_lowering=False, debug=False)
    x1 = nc.dram_tensor("x1", [T, D_MODEL], F32, kind="ExternalInput").ap()
    qt8 = nc.dram_tensor("qt8", [D_MODEL, T], BF16, kind="ExternalInput").ap()
    krevT = nc.dram_tensor("krevT", [D_MODEL, S], BF16, kind="ExternalInput").ap()
    vtok = nc.dram_tensor("vtok", [NH, 64 * 128 * 65], BF16, kind="ExternalInput").ap()
    gate_i = nc.dram_tensor("gate_i", [NH, T], F32, kind="ExternalInput").ap()
    pbrevc = nc.dram_tensor("pbrevc", [NH, 12416], F32, kind="ExternalInput").ap()
    woT = nc.dram_tensor("woT", [D_MODEL, D_MODEL], F32, kind="ExternalInput").ap()
    bo_i = nc.dram_tensor("bo_i", [1, D_MODEL], F32, kind="ExternalInput").ap()
    wr = nc.dram_tensor("wr", [D_MODEL, E], F32, kind="ExternalInput").ap()
    br = nc.dram_tensor("br", [1, E], F32, kind="ExternalInput").ap()
    identp = nc.dram_tensor("identp", [128, 128], BF16, kind="ExternalInput").ap()
    onesf_i = nc.dram_tensor("onesf_i", [1, 64], F32, kind="ExternalInput").ap()

    x2_o = nc.dram_tensor("x2_o", [T, D_MODEL], F32, kind="ExternalOutput").ap()
    ln3_o = nc.dram_tensor("ln3_o", [T, D_MODEL], BF16, kind="ExternalOutput").ap()
    sm2_o = nc.dram_tensor("sm2_o", [T, E], F32, kind="ExternalOutput").ap()

    attnfm = nc.dram_tensor("attnfm", [D_MODEL, T], BF16, kind="Internal").ap()

    NJC = S // 128
    NIB = T // 512

    with ExitStack() as ctx:
        e = ctx.enter_context
        shd = [e(nc.sbuf_tensor(f"shd{i}", [128, 12288], F32)) for i in range(2)]
        kq = e(nc.sbuf_tensor("kq", [128, S], BF16))
        qt = e(nc.sbuf_tensor("qt", [128, T], BF16))
        vt = [e(nc.sbuf_tensor(f"vt{i}", [128, 4160], BF16)) for i in range(2)]
        grep = [[e(nc.sbuf_tensor(f"grep{h}_{i}", [128, 512], BF16)) for i in range(2)]
                for h in range(2)]
        bt = [[e(nc.sbuf_tensor(f"bt{hh}_{i}", [128, 512], BF16)) for i in range(2)]
              for hh in range(2)]
        pt = [[e(nc.sbuf_tensor(f"pt{hh}_{i}", [128, 512], BF16)) for i in range(2)]
              for hh in range(2)]
        rsb = [e(nc.sbuf_tensor(f"rsb{hh}", [1, 512], F32)) for hh in range(2)]
        rrep = [e(nc.sbuf_tensor(f"rrep{hh}", [64, 512], F32)) for hh in range(2)]
        attn_t = e(nc.sbuf_tensor("attn_t", [128, 512], BF16))
        onesf = e(nc.sbuf_tensor("onesf", [1, 64], F32))
        ident = e(nc.sbuf_tensor("ident", [128, 128], BF16))
        wot = [e(nc.sbuf_tensor(f"wot{i}", [128, 1024], BF16)) for i in range(2)]
        wo_ev = [e(nc.sbuf_tensor(f"wo_ev{i}", [128, 512], BF16)) for i in range(2)]
        atok = [e(nc.sbuf_tensor(f"atok{i}", [128, 1024], BF16)) for i in range(4)]
        borep = e(nc.sbuf_tensor("borep", [128, 1024], F32))
        borow = e(nc.sbuf_tensor("borow", [1, 1024], F32))
        x2t = [e(nc.sbuf_tensor(f"x2t{i}", [128, 1024], F32)) for i in range(2)]
        wr_sb = e(nc.sbuf_tensor("wr_sb", [128, 64], BF16))
        brrep = e(nc.sbuf_tensor("brrep", [128, 8], F32))
        brrow = e(nc.sbuf_tensor("brrow", [1, 8], F32))
        ones1 = e(nc.sbuf_tensor("ones1", [1, 128], F32))
        eps_sb = e(nc.sbuf_tensor("eps_sb", [128, 1], F32))

        psA = e(nc.psum_tensor("psA", [128, 4096], F32))
        psS = [[psA[:, 0:512], psA[:, 512:1024]],
               [psA[:, 1024:1536], psA[:, 1536:2048]]]
        psO = [psA[0:65, 2048:2560], psA[0:65, 2560:3072]]
        psR = [psA[0:64, 3072:3584], psA[0:64, 3584:4096]]
        psW = [psA[:, 0:512], psA[:, 512:1024], psA[:, 1024:1536], psA[:, 1536:2048]]
        psT2 = [psA[:, ds(2048 + i * 512, 512)].bitcast(BF16)[:, 0:128] for i in range(2)]
        psB1 = psA[:, 3072:4096]
        psL = psA[:, 3072:3080]

        sm = sem_dict(ctx, nc, ["ld", "lw", "v", "t", "s", "g", "st"])
        blk = e(nc.Block())
        p = Prog(sm)

        def f(eng):
            for dst, src in [(onesf, onesf_i), (brrow, br), (borow, bo_i)]:
                eng.dma_start(out=dst[:], in_=src).then_inc(sm["ld"], 16)
        p.on("sync", f)
        ld_pro = p.bump("ld", 16 * 3)

        def f(eng):
            eng.dma_start(out=ident[:], in_=identp).then_inc(sm["lw"], 16)
            eng.dma_start(out=wr_sb[:].rearrange("p (c e) -> p c e", e=8),
                          in_=wr.rearrange("(c p) e -> p c e", p=128)).then_inc(sm["lw"], 16)
        p.on("gpsimd", f)
        lw_id = p.bump("lw", 32)

        vz = p.fenced("vector", "v", [
            lambda eng: eng.memset(eps_sb[:], 1e-5),
            lambda eng: eng.memset(ones1[:], 1.0),
        ])

        def f(eng, vz=vz, ld_pro=ld_pro):
            eng.wait_ge(sm["v"], vz)
            eng.wait_ge(sm["ld"], ld_pro)
            eng.matmul(psB1[:, 0:512], ones1[:], borow[:, 0:512], start=True, stop=True)
            eng.matmul(psB1[:, 512:1024], ones1[:], borow[:, 512:1024], start=True, stop=True)
            eng.matmul(psL, ones1[:], brrow[:], start=True, stop=True).then_inc(sm["t"], 1)
        p.on("tensor", f)
        tn = p.bump("t")
        vpro = p.fenced("vector", "v", [
            lambda eng: eng.tensor_copy(borep[:, 0:512], psB1[:, 0:512]),
            lambda eng: eng.tensor_copy(borep[:, 512:1024], psB1[:, 512:1024]),
            lambda eng: eng.tensor_copy(brrep[:], psL),
        ], pre_waits=[("t", tn)])

        lastExp = [[0, 0], [0, 0]]
        lastAV = [[0, 0], [0, 0]]
        lastSTT = [[0, 0], [0, 0]]
        lastOEvac = [vpro, vpro]
        lastREvac = [vpro, vpro]
        lastAttnSt = 0

        hp_loads = []
        for hp in range(8):
            lw_before = p.val("lw")

            def f(eng, hp=hp, w=p.val("t"), wg=p.val("g"), wv=p.val("v")):
                if hp > 0:
                    eng.wait_ge(sm["t"], w)
                    eng.wait_ge(sm["g"], wg)
                    eng.wait_ge(sm["v"], wv)
                eng.dma_start(out=qt[:], in_=qt8[ds(hp * 128, 128), :]).then_inc(sm["ld"], 16)
                eng.dma_start(out=kq[:], in_=krevT[ds(hp * 128, 128), :]).then_inc(sm["ld"], 16)
                eng.dma_start(out=shd[0][:],
                              in_=bass.AP(tensor=pbrevc.tensor, offset=(2 * hp) * 12416,
                                          ap=[[1, 128], [1, 12288]])).then_inc(sm["ld"], 16)
                eng.dma_start(out=shd[1][:],
                              in_=bass.AP(tensor=pbrevc.tensor, offset=(2 * hp + 1) * 12416,
                                          ap=[[1, 128], [1, 12288]])).then_inc(sm["ld"], 16)
                eng.dma_start(out=vt[0][:].rearrange("r (c x) -> r c x", x=65),
                              in_=vtok[ds(2 * hp, 1), :].rearrange("1 (c r x) -> r c x", r=128, x=65)
                              ).then_inc(sm["ld"], 16)
                eng.dma_start(out=vt[1][:].rearrange("r (c x) -> r c x", x=65),
                              in_=vtok[ds(2 * hp + 1, 1), :].rearrange("1 (c r x) -> r c x", r=128, x=65)
                              ).then_inc(sm["ld"], 16)
            p.on("sync", f)
            ld_hp = p.bump("ld", 16 * 6)


            for ib in range(NIB):
                ibp = ib % 2
                def f(eng, hp=hp, ib=ib, ibp=ibp, w=p.val("g")):
                    if hp + ib > 0:
                        eng.wait_ge(sm["g"], w)
                    for hh in range(2):
                        src = bass.AP(tensor=gate_i.tensor,
                                      offset=(2 * hp + hh) * T + ib * 512,
                                      ap=[[0, 128], [1, 512]])
                        eng.dma_start(out=grep[hh][ibp][:], in_=src).then_inc(sm["lw"], 16)
                p.on("gpsimd", f)
                lw_g = p.bump("lw", 32)

                u0s = [8064 - 128 * jc + ib * 512 for jc in range(NJC)]
                for jc in range(NJC):
                    par = jc % 2
                    u0 = u0s[jc]

                    def f(eng, hp=hp, ib=ib, jc=jc, par=par,
                          wA=lastExp[0][par], wB=lastExp[1][par], ld_hp=ld_hp):
                        if jc < 2:
                            eng.wait_ge(sm["ld"], ld_hp)
                            eng.wait_ge(sm["lw"], lw_id)
                        eng.wait_ge(sm["s"], max(wA, wB))
                        eng.matmul(psS[0][par], kq[0:64, ds(jc * 128, 128)],
                                   qt[0:64, ds(ib * 512, 512)], start=True, stop=True,
                                   tile_position=(0, 0)).then_inc(sm["t"], 1)
                        eng.matmul(psS[1][par], kq[64:128, ds(jc * 128, 128)],
                                   qt[64:128, ds(ib * 512, 512)], start=True, stop=True,
                                   tile_position=(64, 0)).then_inc(sm["t"], 1)
                    p.on("tensor", f)
                    tqk = [p.bump("t"), p.bump("t")]

                    for hh in range(2):
                        def f(eng, hh=hh, par=par, u0=u0, ibp=ibp,
                              wfree=lastSTT[hh][par], lw_g=lw_g, jc=jc, ld_hp=ld_hp):
                            if jc < 2:
                                eng.wait_ge(sm["lw"], lw_g)
                                eng.wait_ge(sm["ld"], ld_hp)
                            if wfree:
                                eng.wait_ge(sm["v"], wfree)
                            eng.tensor_tensor(bt[hh][par][:], shd[hh][:, ds(u0, 512)],
                                              grep[hh][ibp][:],
                                              ALU.mult).then_inc(sm["g"], 1)
                        p.on("gpsimd", f)
                    gbt = [p.bump("g") - 1, p.bump("g") - 0]
                    gbt = [p.val("g") - 1, p.val("g")]

                    for hh in range(2):
                        def f(eng, hh=hh, par=par, tq=tqk[hh], gb=gbt[hh]):
                            eng.wait_ge(sm["t"], tq)
                            eng.wait_ge(sm["g"], gb)
                            eng.scalar_tensor_tensor(psS[hh][par], bt[hh][par][:], 1.0,
                                                     psS[hh][par], ALU.mult,
                                                     ALU.add).then_inc(sm["v"], 1)
                        p.on("vector", f)
                    vstt = [p.bump("v"), p.bump("v")]
                    lastSTT[0][par] = vstt[0]
                    lastSTT[1][par] = vstt[1]

                    for hh in range(2):
                        def f(eng, hh=hh, par=par, vs=vstt[hh], wpt=lastAV[hh][par]):
                            eng.wait_ge(sm["v"], vs)
                            if wpt:
                                eng.wait_ge(sm["t"], wpt)
                            eng.activation(pt[hh][par][:], psS[hh][par], AF.Exp,
                                           bias=0.0, scale=1.0).then_inc(sm["s"], 1)
                        p.on("scalar", f)
                    sexp = [p.bump("s"), p.bump("s")]
                    lastExp[0][par] = sexp[0]
                    lastExp[1][par] = sexp[1]

                    for hh in range(2):
                        def f(eng, hh=hh, par=par, jc=jc, se=sexp[hh],
                              wO=lastOEvac[hh]):
                            eng.wait_ge(sm["s"], se)
                            if jc == 0:
                                eng.wait_ge(sm["v"], wO)
                            eng.matmul(psO[hh], vt[hh][:, ds(jc * 65, 65)],
                                       pt[hh][par][:], start=(jc == 0),
                                       stop=(jc == NJC - 1)).then_inc(sm["t"], 1)
                        p.on("tensor", f)
                    tav = [p.bump("t"), p.bump("t")]
                    lastAV[0][par] = tav[0]
                    lastAV[1][par] = tav[1]

                vrec = p.fenced("vector", "v", [
                    lambda eng: eng.reciprocal(rsb[0][:], psO[0][64:65, :]),
                    lambda eng: eng.reciprocal(rsb[1][:], psO[1][64:65, :]),
                ], pre_waits=[("t", tav[1])])

                def f(eng, vrec=vrec, wR=max(lastREvac)):
                    eng.wait_ge(sm["v"], vrec)
                    eng.matmul(psR[0], onesf[:], rsb[0][:], start=True, stop=True).then_inc(sm["t"], 1)
                    eng.matmul(psR[1], onesf[:], rsb[1][:], start=True, stop=True).then_inc(sm["t"], 1)
                p.on("tensor", f)
                trep = [p.bump("t"), p.bump("t")]

                vev = p.fenced("vector", "v", [
                    lambda eng: eng.tensor_copy(rrep[0][:], psR[0]),
                    lambda eng: eng.tensor_copy(rrep[1][:], psR[1]),
                    lambda eng: eng.tensor_tensor(attn_t[0:64, :], psO[0][0:64, :],
                                                  rrep[0][:], ALU.mult),
                    lambda eng: eng.tensor_tensor(attn_t[64:128, :], psO[1][0:64, :],
                                                  rrep[1][:], ALU.mult),
                ], pre_waits=[("t", trep[1]), ("st", lastAttnSt)])
                lastOEvac = [vev, vev]
                lastREvac = [vev, vev]

                def f(eng, vev=vev, hp=hp, ib=ib):
                    eng.wait_ge(sm["v"], vev)
                    eng.dma_start(out=attnfm[ds(hp * 128, 128), ds(ib * 512, 512)],
                                  in_=attn_t[:]).then_inc(sm["st"], 16)
                p.on("sync", f)
                lastAttnSt = p.bump("st", 16)

        st_flash = p.val("st")

        afull = kq[:, 0:4096]
        x1t4 = [e(nc.sbuf_tensor(f"x1t4_{i}", [128, 1024], F32)) for i in range(4)]

        wot_free = [0, 0]
        woev_free = [0, 0]
        afull_free = 0
        atok_free = [0, 0, 0, 0]
        x1t_free = [0, 0, 0, 0]
        x2t_free = [0, 0]
        st_x2 = 0
        for n in range(8):
            def f(eng, n=n, wld=afull_free, st_flash=st_flash):
                if n == 0:
                    eng.wait_ge(sm["st"], st_flash)
                if wld:
                    eng.wait_ge(sm["t"], wld)
                for k in range(8):
                    eng.dma_start(out=afull[:, ds(k * 512, 512)],
                                  in_=attnfm[ds(k * 128, 128), ds(n * 512, 512)]).then_inc(sm["ld"], 16)
                for tt in range(4):
                    pass
            p.on("sync", f)
            ld_af = p.bump("ld", 16 * 8)

            def f(eng, n=n, wv=[x1t_free[i] for i in range(4)]):
                for tt in range(4):
                    if wv[tt]:
                        eng.wait_ge(sm["v"], wv[tt])
                    eng.dma_start(out=x1t4[tt][:],
                                  in_=x1[ds(n * 512 + tt * 128, 128), :]).then_inc(sm["ld"], 16)
            p.on("sync", f)
            ld_x1 = p.bump("ld", 16 * 4)

            copies_start = p.val("v")
            for fo in range(8):
                wslot = fo % 2

                def f(eng, n=n, fo=fo, wslot=wslot, wv=wot_free[wslot]):
                    if wv:
                        eng.wait_ge(sm["t"], wv)
                    eng.dma_start(
                        out=wot[wslot][:].rearrange("p (k f) -> p k f", f=128),
                        in_=woT[:, ds(fo * 128, 128)].rearrange("(k p) f -> p k f", p=128),
                    ).then_inc(sm["lw"], 16)
                p.on("gpsimd", f)
                lww = p.bump("lw", 16)

                sfree = p.val("s") - 3
                def f(eng, n=n, fo=fo, wslot=wslot, lww=lww, ld_af=ld_af, sfree=sfree):
                    eng.wait_ge(sm["lw"], lww)
                    if fo == 0:
                        eng.wait_ge(sm["ld"], ld_af)
                    if fo >= 4 or n > 0:
                        eng.wait_ge(sm["s"], sfree)
                    for k in range(8):
                        inst = eng.matmul(psW[fo % 4], wot[wslot][:, ds(k * 128, 128)],
                                          afull[:, ds(k * 512, 512)],
                                          start=(k == 0), stop=(k == 7))
                    inst.then_inc(sm["t"], 1)
                p.on("tensor", f)
                tmm = p.bump("t")
                wot_free[wslot] = tmm
                if fo == 7:
                    afull_free = tmm

                def f(eng, tmm=tmm, fo=fo, wev=woev_free[fo % 2]):
                    eng.wait_ge(sm["t"], tmm)
                    if wev:
                        eng.wait_ge(sm["t"], wev)
                    eng.activation(wo_ev[fo % 2][:], psW[fo % 4], AF.Identity,
                                   bias=0.0, scale=1.0).then_inc(sm["s"], 1)
                p.on("scalar", f)
                sev = p.bump("s")

                vtr_base = p.val("v")
                for tt in range(4):
                    def f(eng, sev=sev, fo=fo, tt=tt, vtr_base=vtr_base):
                        if tt == 0:
                            eng.wait_ge(sm["s"], sev)
                            eng.wait_ge(sm["v"], vtr_base)
                        if tt >= 2:
                            eng.wait_ge(sm["v"], vtr_base + tt - 1)
                        eng.transpose(psT2[tt % 2][:], wo_ev[fo % 2][:, ds(tt * 128, 128)],
                                      ident[:]).then_inc(sm["t"], 1)
                    p.on("tensor", f)
                    ttr = p.bump("t")
                    if tt == 3:
                        woev_free[fo % 2] = ttr

                    def f(eng, ttr=ttr, fo=fo, tt=tt, wv=atok_free[tt]):
                        eng.wait_ge(sm["t"], ttr)
                        if fo == 0 and wv:
                            eng.wait_ge(sm["v"], wv)
                        eng.tensor_copy(atok[tt][:, ds(fo * 128, 128)],
                                        psT2[tt % 2][:]).then_inc(sm["v"], 1)
                    p.on("vector", f)
                    p.bump("v")
            v_copies = p.val("v")

            for tt in range(4):
                par = tt % 2
                vn = p.fenced("vector", "v", [
                    lambda eng, tt=tt, par=par: eng.tensor_tensor(
                        x2t[par][:], atok[tt][:], borep[:], ALU.add),
                    lambda eng, tt=tt, par=par: eng.tensor_add(
                        x2t[par][:], x2t[par][:], x1t4[tt][:]),
                ], pre_waits=[("v", v_copies), ("ld", ld_x1), ("st", x2t_free[par])])
                atok_free[tt] = vn
                x1t_free[tt] = vn

                def f(eng, vn=vn, n=n, tt=tt, par=par):
                    eng.wait_ge(sm["v"], vn)
                    eng.dma_start(out=x2_o[ds(n * 512 + tt * 128, 128), :],
                                  in_=x2t[par][:]).then_inc(sm["st"], 16)
                p.on("sync", f)
                x2t_free[par] = p.bump("st", 16)
        st_wo = p.val("st")

        em = LnRouterEmitter(p, sm, nc, ctx, "c", psT=psT2, psL=psL,
                             xt=[x1t4[0][:], x1t4[1][:]],
                             lnb=[atok[0][:], atok[1][:]],
                             fmb=[atok[2][:], atok[3][:]],
                             sq=afull[:, 0:1024])

        def f(eng, st_wo=st_wo):
            eng.wait_ge(sm["st"], st_wo)
        p.on("sync", f)
        def f(eng, st_wo=st_wo, lw_id=lw_id):
            eng.wait_ge(sm["st"], st_wo)
            eng.wait_ge(sm["lw"], lw_id)
        p.on("tensor", f)
        def f(eng, st_wo=st_wo):
            eng.wait_ge(sm["st"], st_wo)
        p.on("vector", f)
        def f(eng, st_wo=st_wo):
            eng.wait_ge(sm["st"], st_wo)
        p.on("scalar", f)
        for i in range(NT):
            em.tile(i, x2_o, ln3_o, sm2_o, wr_sb, brrep, eps_sb, ident)

        def f(eng, nst=p.val("st")):
            eng.wait_ge(sm["st"], nst)
        p.on("sync", f)

        p.replay(blk)
    return nc
def _shard_x(x):
    out = []
    for c in range(NCORES):
        b, h = c // 2, c % 2
        out.append(np.ascontiguousarray(x[b, h * T:(h + 1) * T, :]))
    return out


def _prep_B_host(inputs):
    hp = {}
    wup = np.asarray(inputs["L0_ffn_W_up"], np.float32)
    hp["wupP"] = np.ascontiguousarray(np.pad(wup, [(0, 0), (0, 0), (0, FPAD - 2730)]))
    bup = np.pad(np.asarray(inputs["L0_ffn_b_up"], np.float32), [(0, 0), (0, FPAD - 2730)])
    hp["bupP"] = np.ascontiguousarray(bup.reshape(4, 22, 128).transpose(2, 0, 1).reshape(128, 88))
    wdn = np.asarray(inputs["L0_ffn_W_down"], np.float32)
    hp["wdnP"] = np.ascontiguousarray(np.pad(wdn, [(0, 0), (0, FPAD - 2730), (0, 0)]))
    bdn = np.asarray(inputs["L0_ffn_b_down"], np.float32)
    hp["bdnP"] = np.ascontiguousarray(bdn.reshape(8, 8, 128).transpose(2, 0, 1).reshape(128, 64))
    for nm, key in [("wqT", "L0_attn_Wq"), ("wkT", "L0_attn_Wk"), ("wvT", "L0_attn_Wv")]:
        hp[nm] = np.ascontiguousarray(np.asarray(inputs[key], np.float32).T)
    for nm, key in [("bq_r", "L0_attn_bq"), ("bk_r", "L0_attn_bk"), ("bv_r", "L0_attn_bv")]:
        hp[nm] = np.ascontiguousarray(
            np.asarray(inputs[key], np.float32).reshape(8, 128).T)
    hp["gruW"] = np.ascontiguousarray(np.asarray(inputs["L0_attn_gru_W"], np.float32))
    hp["gruB"] = np.ascontiguousarray(np.asarray(inputs["L0_attn_gru_b"], np.float32).reshape(8, 1))
    hp["gruC"] = np.ascontiguousarray(np.asarray(inputs["L0_attn_gru_const"], np.float32).reshape(1, 16))
    hp["g48a_i"] = np.array([[1], [1], [1], [1], [0], [0], [0], [0]], np.float32)
    hp["g48b_i"] = np.array([[0], [0], [0], [0], [1], [1], [1], [1]], np.float32)
    M = np.zeros((128, 128), np.float32)
    for g in range(8):
        M[g * 16:(g + 1) * 16, g * 16:(g + 1) * 16] = 1.0
    hp["grpsum"] = M
    Ssel = np.zeros((128, 8), np.float32)
    for ee in range(8):
        Ssel[ee * 16, ee] = 1.0
    hp["thrsel"] = Ssel
    hp["eoff_i"] = (np.arange(8, dtype=np.float32) * C_CAP).reshape(8, 1)
    hp["bnd_i"] = ((np.arange(8, dtype=np.float32) + 1) * C_CAP).reshape(8, 1)
    hp["idf32p"] = np.eye(128, dtype=np.float32)
    return hp


def _sm_global(sm_cores):
    out = np.empty((B * S, E), np.float32)
    for c in range(NCORES):
        b, h = c // 2, c % 2
        out[b * S + h * T:(b * S + h * T) + T] = sm_cores[c]
    return out


def kernel(**inputs):
    x0 = np.asarray(inputs["L0_attn_hidden_states"], np.float32)
    consts = _host_consts()
    shards = _shard_x(x0)

    ncA = build_A()
    wr = np.ascontiguousarray(inputs["L0_ffn_Wr"], dtype=np.float32)
    br = np.ascontiguousarray(inputs["L0_ffn_br"], dtype=np.float32).reshape(1, E)
    mapsA = [dict(x=shards[c], wr=wr, br=br, identp=consts["ident"]) for c in range(NCORES)]
    resA = run_bass_kernel_spmd(ncA, mapsA, core_ids=list(range(NCORES)))

    sm_cores = [np.asarray(resA.results[c]["sm_o"]) for c in range(NCORES)]
    ln_cores = [np.asarray(resA.results[c]["ln_o"]) for c in range(NCORES)]
    sm_all = _sm_global(sm_cores)
    sm_resh = np.ascontiguousarray(
        sm_all.reshape(16, 2048, 8).transpose(0, 2, 1).reshape(16, 8, 2048)
    )
    sm_resh = np.empty((128, 2048), np.float32)
    for pp in range(128):
        sm_resh[pp] = sm_all[(pp % 16) * 2048:(pp % 16 + 1) * 2048, pp // 16]

    hp = _prep_B_host(inputs)
    import os
    ncB = build_B(dbg=bool(os.environ.get('KDBG')))
    mapsB = []
    for c in range(NCORES):
        d = dict(hp)
        d["x0"] = shards[c]
        d["ln1"] = ln_cores[c]
        d["sm_resh"] = sm_resh
        d["smT_loc"] = np.ascontiguousarray(sm_cores[c].T)
        d["sm_loc"] = sm_cores[c]
        d["identp"] = consts["ident"]
        mapsB.append(d)
    resB = run_bass_kernel_spmd(ncB, mapsB, core_ids=list(range(NCORES)))

    def as_u16(a):
        a = np.asarray(a)
        if a.dtype == np.uint16:
            return a
        return a.view(np.uint16) if a.dtype == bf16 else a.astype(bf16).view(np.uint16)

    kTs = [as_u16(resB.results[c]["kT_o"]) for c in range(NCORES)]
    vTs = [as_u16(resB.results[c]["vT_o"]) for c in range(NCORES)]
    qTs = [as_u16(resB.results[c]["qT_o"]) for c in range(NCORES)]
    gates = [np.asarray(resB.results[c]["gate_o"], np.float32) for c in range(NCORES)]
    x1s = [np.asarray(resB.results[c]["x1_o"], np.float32) for c in range(NCORES)]

    ones_bf16_u16 = np.full((NH, 64, 128, 1), 0x3F80, np.uint16)
    krevs, vtoks = {}, {}
    for bb in range(B):
        kfull = np.concatenate([kTs[2 * bb], kTs[2 * bb + 1]], axis=1)
        krevs[bb] = np.ascontiguousarray(
            kfull.reshape(D_MODEL, 64, 128)[:, :, ::-1].reshape(D_MODEL, S))
        vfull = np.concatenate([vTs[2 * bb], vTs[2 * bb + 1]], axis=1)
        vh = vfull.reshape(NH, 64, S).transpose(0, 2, 1)
        vrev = vh.reshape(NH, 64, 128, 64)[:, :, ::-1, :]
        vtoks[bb] = np.ascontiguousarray(
            np.concatenate([vrev, ones_bf16_u16], axis=3).reshape(NH, 64 * 128 * 65))

    btab = _bucket_table()
    re_in = np.asarray(inputs["L0_attn_rel_embed"], np.float32)
    pb = re_in[btab]
    pbrev = pb[::-1]
    woT = np.ascontiguousarray(np.asarray(inputs["L0_attn_Wo"], np.float32).T)
    bo_row = np.ascontiguousarray(np.asarray(inputs["L0_attn_bo"], np.float32).reshape(1, D_MODEL))
    onesf = np.ones((1, 64), np.float32)

    ncC = build_C()
    mapsC = []
    for c in range(NCORES):
        bb, hh = c // 2, c % 2
        q0 = hh * T
        pbc = np.zeros((NH, 12416), np.float32)
        avail = min(12416, 16383 - q0)
        pbc[:, 0:avail] = pbrev[q0:q0 + avail, :].T
        d = dict(
            x1=x1s[c], qt8=qTs[c], krevT=krevs[bb], vtok=vtoks[bb],
            gate_i=np.ascontiguousarray(gates[c]),
            pbrevc=np.ascontiguousarray(pbc), woT=woT, bo_i=bo_row,
            wr=wr, br=br, identp=consts["ident"], onesf_i=onesf,
        )
        mapsC.append(d)
    resC = run_bass_kernel_spmd(ncC, mapsC, core_ids=list(range(NCORES)))

    sm2_cores = [np.asarray(resC.results[c]["sm2_o"], np.float32) for c in range(NCORES)]
    ln3_cores = [np.asarray(resC.results[c]["ln3_o"]) for c in range(NCORES)]
    x2_cores = [np.asarray(resC.results[c]["x2_o"], np.float32) for c in range(NCORES)]
    sm2_all = _sm_global(sm2_cores)
    sm2_resh = np.empty((128, 2048), np.float32)
    for pp in range(128):
        sm2_resh[pp] = sm2_all[(pp % 16) * 2048:(pp % 16 + 1) * 2048, pp // 16]
    mapsD = []
    for c in range(NCORES):
        d = dict(mapsB[c])
        d["x0"] = np.ascontiguousarray(x2_cores[c])
        d["ln1"] = np.ascontiguousarray(ln3_cores[c])
        d["sm_resh"] = sm2_resh
        d["smT_loc"] = np.ascontiguousarray(sm2_cores[c].T)
        d["sm_loc"] = sm2_cores[c]
        mapsD.append(d)
    resD = run_bass_kernel_spmd(ncB, mapsD, core_ids=list(range(NCORES)))

    out = np.empty((B, S, D_MODEL), np.float32)
    for c in range(NCORES):
        bb, hh = c // 2, c % 2
        out[bb, hh * T:(hh + 1) * T, :] = np.asarray(resD.results[c]["x1_o"], np.float32)
    return out


# version 02
# speedup vs baseline: 1.0000x; 1.0000x over previous
import math
from contextlib import ExitStack

import numpy as np
import ml_dtypes
import concourse.bass as bass
import concourse.mybir as mybir
from concourse.bass_utils import run_bass_kernel_spmd

F32 = mybir.dt.float32
BF16 = mybir.dt.bfloat16
I32 = mybir.dt.int32
U32 = mybir.dt.uint32
AF = mybir.ActivationFunctionType
ALU = mybir.AluOpType
ds = bass.ds
AX = mybir.AxisListType

B, S, D_MODEL = 4, 8192, 1024
NCORES = 8
T = 4096
NT = T // 128
E = 8
KSEL = 4096
C_CAP = 640
FPAD = 2816
NH, HD = 16, 64
NB, MAXD = 320, 800

bf16 = ml_dtypes.bfloat16


class Prog:

    ENGS = ("sync", "gpsimd", "vector", "scalar", "tensor")

    def __init__(self, sm):
        self.streams = {e: [] for e in self.ENGS}
        self.counts = {k: 0 for k in sm}
        self.sm = sm

    def on(self, eng, fn):
        self.streams[eng].append(fn)

    def bump(self, sem, n=1):
        self.counts[sem] += n
        return self.counts[sem]

    def val(self, sem):
        return self.counts[sem]

    def fenced(self, engname, semname, ops, pre_waits=()):
        base = self.val(semname)
        smd = self.sm

        def fn(eng, ops=tuple(ops), base=base, pre=tuple(pre_waits), semname=semname):
            for ws, wv in pre:
                eng.wait_ge(smd[ws], wv)
            if base:
                eng.wait_ge(smd[semname], base)
            for j, op in enumerate(ops):
                if j > 0:
                    eng.wait_ge(smd[semname], base + j)
                op(eng).then_inc(smd[semname], 1)
        self.on(engname, fn)
        return self.bump(semname, len(ops))

    def replay(self, blk):
        for name in self.ENGS:
            stream = self.streams[name]
            if not stream:
                continue
            dec = getattr(blk, name)

            def make(stream):
                def _body(eng):
                    for fn in stream:
                        fn(eng)
                return _body
            dec(make(stream))


def sem_dict(ctx, nc, names):
    return {n: ctx.enter_context(nc.semaphore(n)) for n in names}


def _host_consts():
    t = {}
    t["ident"] = np.eye(128, dtype=bf16)
    return t


def _bucket_table():
    nb = NB // 2
    rel = np.arange(-(S - 1), S)
    buckets = (rel > 0).astype(np.int64) * nb
    rel_abs = np.abs(rel)
    mx = nb // 2
    small = rel_abs < mx
    large = np.log(np.maximum(rel_abs, 1).astype(np.float32) / mx) / math.log(MAXD / mx) * (nb - mx)
    large = (mx + large).astype(np.int64)
    large = np.minimum(large, nb - 1)
    return buckets + np.where(small, rel_abs, large)


class LnRouterEmitter:

    def __init__(self, p, sm, nc, ctx, tag, psT=None, psL=None, xt=None, lnb=None,
                 fmb=None, sq=None):
        e = ctx.enter_context
        self.p, self.sm = p, sm
        self.xt = xt or [e(nc.sbuf_tensor(f"{tag}xt{i}", [128, 1024], F32)) for i in range(2)]
        self.lnb = lnb or [e(nc.sbuf_tensor(f"{tag}lnb{i}", [128, 1024], BF16)) for i in range(2)]
        self.fmb = fmb or [e(nc.sbuf_tensor(f"{tag}fmb{i}", [128, 1024], BF16)) for i in range(2)]
        self.sum1 = e(nc.sbuf_tensor(f"{tag}sum1", [128, 1], F32))
        self.mean = e(nc.sbuf_tensor(f"{tag}mean", [128, 1], F32))
        self.sq = sq if sq is not None else e(nc.sbuf_tensor(f"{tag}sq", [128, 1024], BF16))
        self.ssq = e(nc.sbuf_tensor(f"{tag}ssq", [128, 1], F32))
        self.var = e(nc.sbuf_tensor(f"{tag}var", [128, 1], F32))
        self.rstd = e(nc.sbuf_tensor(f"{tag}rstd", [128, 1], F32))
        self.nmr = e(nc.sbuf_tensor(f"{tag}nmr", [128, 1], F32))
        self.smt = e(nc.sbuf_tensor(f"{tag}smt", [128, 8], F32))
        self.mx1 = e(nc.sbuf_tensor(f"{tag}mx1", [128, 1], F32))
        self.se1 = e(nc.sbuf_tensor(f"{tag}se1", [128, 1], F32))
        if psT is None:
            self.psT = [e(nc.psum_tensor(f"{tag}psT{i}", [128, 128], BF16)) for i in range(2)]
            self.psL = e(nc.psum_tensor(f"{tag}psL", [128, 8], F32))
        else:
            self.psT = psT
            self.psL = psL
        self.xt_free = [0, 0]
        self.lnb_free = [0, 0]
        self.fmb_free = [0, 0]

    def tile(self, i, x_dram, ln_out, sm_out, wr_sb, brrep, eps_sb, ident,
             extra_ln_sink=None):
        p, sm = self.p, self.sm
        par = i % 2
        xti, lnbi, fmbi = self.xt[par], self.lnb[par], self.fmb[par]

        def f(eng, i=i, xti=xti, wv=self.xt_free[par]):
            if wv:
                eng.wait_ge(sm["v"], wv)
            eng.dma_start(out=xti[:], in_=x_dram[ds(i * 128, 128), :]).then_inc(sm["ld"], 16)
        p.on("sync", f)
        lv = p.bump("ld", 16)

        vn = p.fenced("vector", "v", [
            lambda eng, xti=xti: eng.reduce_sum(self.sum1[:], xti[:], axis=AX.X),
        ], pre_waits=[("ld", lv)])

        sn = p.fenced("scalar", "s", [
            lambda eng, xti=xti: eng.activation(self.sq[:], xti[:], AF.Square,
                                                bias=0.0, scale=1.0,
                                                accum_out=self.ssq[:]),
        ], pre_waits=[("ld", lv), ("v", p.val("v"))])

        vn = p.fenced("vector", "v", [
            lambda eng: eng.tensor_scalar(self.mean[:], self.sum1[:], 1.0 / 1024,
                                          None, ALU.mult),
            lambda eng: eng.tensor_mul(self.nmr[:], self.mean[:], self.mean[:]),
            lambda eng: eng.scalar_tensor_tensor(self.var[:], self.ssq[:], 1.0 / 1024,
                                                 self.nmr[:], ALU.mult, ALU.subtract),
        ], pre_waits=[("v", vn), ("s", sn)])

        sn = p.fenced("scalar", "s", [
            lambda eng: eng.activation(self.rstd[:], self.var[:], AF.Sqrt,
                                       bias=eps_sb[:], scale=1.0),
        ], pre_waits=[("v", vn)])

        vn = p.fenced("vector", "v", [
            lambda eng: eng.reciprocal(self.rstd[:], self.rstd[:]),
            lambda eng: eng.tensor_mul(self.nmr[:], self.mean[:], self.rstd[:]),
            lambda eng: eng.tensor_scalar(self.nmr[:], self.nmr[:], -1.0, None, ALU.mult),
            lambda eng, xti=xti, lnbi=lnbi: eng.tensor_scalar(
                lnbi[:], xti[:], self.rstd[:], self.nmr[:], ALU.mult, ALU.add),
        ], pre_waits=[("s", sn), ("t", self.lnb_free[par])])
        self.xt_free[par] = vn

        def f(eng, vn=vn, lnbi=lnbi, i=i):
            eng.wait_ge(sm["v"], vn)
            eng.dma_start(out=ln_out[ds(i * 128, 128), :], in_=lnbi[:]).then_inc(sm["st"], 16)
        p.on("sync", f)
        p.bump("st", 16)
        if extra_ln_sink is not None:
            extra_ln_sink(lnbi, vn)

        tbase = p.val("t")
        vbase0 = p.val("v")

        def f(eng, vn=vn, lnbi=lnbi, tbase=tbase, vbase0=vbase0, ident=ident):
            eng.wait_ge(sm["v"], vn)
            for c in range(8):
                if c >= 2:
                    eng.wait_ge(sm["v"], vbase0 + c - 1)
                eng.transpose(self.psT[c % 2][:], lnbi[:, ds(c * 128, 128)],
                              ident[:]).then_inc(sm["t"], 1)
        p.on("tensor", f)
        p.bump("t", 8)

        def f(eng, tbase=tbase, fmbi=fmbi):
            for c in range(8):
                eng.wait_ge(sm["t"], tbase + c + 1)
                eng.tensor_copy(fmbi[:, ds(c * 128, 128)], self.psT[c % 2][:]).then_inc(sm["v"], 1)
        p.on("vector", f)
        p.bump("v", 8)
        vafter = p.val("v")

        def f(eng, vafter=vafter, fmbi=fmbi, wr_sb=wr_sb):
            eng.wait_ge(sm["v"], vafter)
            for c in range(8):
                inst = eng.matmul(self.psL[:], fmbi[:, ds(c * 128, 128)],
                                  wr_sb[:, ds(c * 8, 8)], start=(c == 0), stop=(c == 7))
            inst.then_inc(sm["t"], 1)
        p.on("tensor", f)
        tn = p.bump("t")
        self.lnb_free[par] = tn
        self.fmb_free[par] = tn

        vn = p.fenced("vector", "v", [
            lambda eng: eng.tensor_add(self.smt[:], self.psL[:], brrep[:]),
            lambda eng: eng.reduce_max(self.mx1[:], self.smt[:], axis=AX.X, negate=True),
        ], pre_waits=[("t", tn)])

        def f(eng, vn=vn):
            eng.wait_ge(sm["v"], vn)
            eng.activation(self.smt[:], self.smt[:], AF.Exp, bias=self.mx1[:], scale=1.0,
                           accum_out=self.se1[:]).then_inc(sm["s"], 1)
        p.on("scalar", f)
        sn = p.bump("s")

        vn = p.fenced("vector", "v", [
            lambda eng: eng.reciprocal(self.se1[:], self.se1[:]),
            lambda eng: eng.tensor_scalar(self.smt[:], self.smt[:], self.se1[:],
                                          None, ALU.mult),
        ], pre_waits=[("s", sn)])

        def f(eng, vn=vn, i=i):
            eng.wait_ge(sm["v"], vn)
            eng.dma_start(out=sm_out[ds(i * 128, 128), :], in_=self.smt[:]).then_inc(sm["st"], 16)
        p.on("sync", f)
        p.bump("st", 16)


def build_A():
    nc = bass.Bass("TRN2", target_bir_lowering=False, debug=False)
    x = nc.dram_tensor("x", [T, D_MODEL], F32, kind="ExternalInput").ap()
    wr = nc.dram_tensor("wr", [D_MODEL, E], F32, kind="ExternalInput").ap()
    br = nc.dram_tensor("br", [1, E], F32, kind="ExternalInput").ap()
    identp = nc.dram_tensor("identp", [128, 128], BF16, kind="ExternalInput").ap()
    ln_o = nc.dram_tensor("ln_o", [T, D_MODEL], BF16, kind="ExternalOutput").ap()
    sm_o = nc.dram_tensor("sm_o", [T, E], F32, kind="ExternalOutput").ap()

    with ExitStack() as ctx:
        e = ctx.enter_context
        wr_sb = e(nc.sbuf_tensor("wr_sb", [128, 64], BF16))
        brrep = e(nc.sbuf_tensor("brrep", [128, 8], F32))
        brrow = e(nc.sbuf_tensor("brrow", [1, 8], F32))
        ones1 = e(nc.sbuf_tensor("ones1", [1, 128], F32))
        eps_sb = e(nc.sbuf_tensor("eps_sb", [128, 1], F32))
        ident = e(nc.sbuf_tensor("ident", [128, 128], BF16))
        psB = e(nc.psum_tensor("psB", [128, 8], F32))

        sm = sem_dict(ctx, nc, ["ld", "v", "t", "s", "st"])
        blk = e(nc.Block())
        p = Prog(sm)

        def f(eng):
            eng.dma_start(out=brrow[:], in_=br).then_inc(sm["ld"], 16)
            eng.dma_start(out=ident[:], in_=identp).then_inc(sm["ld"], 16)
            eng.dma_start(out=wr_sb[:].rearrange("p (c e) -> p c e", e=8),
                          in_=wr.rearrange("(c p) e -> p c e", p=128)).then_inc(sm["ld"], 16)
        p.on("gpsimd", f)
        ldn = p.bump("ld", 48)

        def f(eng):
            eng.memset(eps_sb[:], 1e-5)
            eng.memset(ones1[:], 1.0).then_inc(sm["v"], 1)
        p.on("vector", f)
        vn = p.bump("v")

        def f(eng, ldn=ldn, vn=vn):
            eng.wait_ge(sm["ld"], ldn)
            eng.wait_ge(sm["v"], vn)
            eng.matmul(psB[:], ones1[:], brrow[:], start=True, stop=True).then_inc(sm["t"], 1)
        p.on("tensor", f)
        tn = p.bump("t")

        def f(eng, tn=tn):
            eng.wait_ge(sm["t"], tn)
            eng.tensor_copy(brrep[:], psB[:]).then_inc(sm["v"], 1)
        p.on("vector", f)
        p.bump("v")

        em = LnRouterEmitter(p, sm, nc, ctx, "a")
        for i in range(NT):
            em.tile(i, x, ln_o, sm_o, wr_sb, brrep, eps_sb, ident)

        def f(eng, n=p.val("st")):
            eng.wait_ge(sm["st"], n)
        p.on("sync", f)

        p.replay(blk)
    return nc



def build_B(dbg=False):
    nc = bass.Bass("TRN2", target_bir_lowering=False, debug=False)
    x0 = nc.dram_tensor("x0", [T, D_MODEL], F32, kind="ExternalInput").ap()
    ln1 = nc.dram_tensor("ln1", [T, D_MODEL], BF16, kind="ExternalInput").ap()
    sm_resh = nc.dram_tensor("sm_resh", [128, 2048], F32, kind="ExternalInput").ap()
    smT_loc = nc.dram_tensor("smT_loc", [E, T], F32, kind="ExternalInput").ap()
    sm_loc = nc.dram_tensor("sm_loc", [T, E], F32, kind="ExternalInput").ap()
    wupP = nc.dram_tensor("wupP", [4, D_MODEL, FPAD], F32, kind="ExternalInput").ap()
    bupP = nc.dram_tensor("bupP", [128, 88], F32, kind="ExternalInput").ap()
    wdnP = nc.dram_tensor("wdnP", [E, FPAD, D_MODEL], F32, kind="ExternalInput").ap()
    bdnP = nc.dram_tensor("bdnP", [128, 64], F32, kind="ExternalInput").ap()
    wqT = nc.dram_tensor("wqT", [D_MODEL, D_MODEL], F32, kind="ExternalInput").ap()
    wkT = nc.dram_tensor("wkT", [D_MODEL, D_MODEL], F32, kind="ExternalInput").ap()
    wvT = nc.dram_tensor("wvT", [D_MODEL, D_MODEL], F32, kind="ExternalInput").ap()
    bq_r = nc.dram_tensor("bq_r", [128, 8], F32, kind="ExternalInput").ap()
    bk_r = nc.dram_tensor("bk_r", [128, 8], F32, kind="ExternalInput").ap()
    bv_r = nc.dram_tensor("bv_r", [128, 8], F32, kind="ExternalInput").ap()
    gruW = nc.dram_tensor("gruW", [8, 64], F32, kind="ExternalInput").ap()
    gruB = nc.dram_tensor("gruB", [8, 1], F32, kind="ExternalInput").ap()
    gruC = nc.dram_tensor("gruC", [1, 16], F32, kind="ExternalInput").ap()
    g48a_i = nc.dram_tensor("g48a_i", [8, 1], F32, kind="ExternalInput").ap()
    g48b_i = nc.dram_tensor("g48b_i", [8, 1], F32, kind="ExternalInput").ap()
    grpsum = nc.dram_tensor("grpsum", [128, 128], F32, kind="ExternalInput").ap()
    thrsel = nc.dram_tensor("thrsel", [128, 8], F32, kind="ExternalInput").ap()
    eoff_i = nc.dram_tensor("eoff_i", [8, 1], F32, kind="ExternalInput").ap()
    bnd_i = nc.dram_tensor("bnd_i", [8, 1], F32, kind="ExternalInput").ap()
    identp = nc.dram_tensor("identp", [128, 128], BF16, kind="ExternalInput").ap()
    idf32p = nc.dram_tensor("idf32p", [128, 128], F32, kind="ExternalInput").ap()

    x1_o = nc.dram_tensor("x1_o", [T, D_MODEL], F32, kind="ExternalOutput").ap()
    qT_o = nc.dram_tensor("qT_o", [D_MODEL, T], BF16, kind="ExternalOutput").ap()
    kT_o = nc.dram_tensor("kT_o", [D_MODEL, T], BF16, kind="ExternalOutput").ap()
    vT_o = nc.dram_tensor("vT_o", [D_MODEL, T], BF16, kind="ExternalOutput").ap()
    gate_o = nc.dram_tensor("gate_o", [NH, T], F32, kind="ExternalOutput").ap()
    thr_o = nc.dram_tensor("thr_o", [8, 1], F32, kind="ExternalOutput").ap()

    kind_dbg = "ExternalOutput" if dbg else "Internal"
    gath = nc.dram_tensor("gath", [E * C_CAP, 1040], BF16, kind=kind_dbg).ap()
    posS_o = nc.dram_tensor("posS_o", [128, 8 * NT], U32, kind=kind_dbg).ap()
    posG_o = nc.dram_tensor("posG_o", [128, 8 * NT], U32, kind=kind_dbg).ap()
    outb = nc.dram_tensor("outb", [E * C_CAP + 128, D_MODEL], F32, kind=kind_dbg).ap()

    with ExitStack() as ctx:
        e = ctx.enter_context
        lo = e(nc.sbuf_tensor("lo", [128, 1], F32))
        hi = e(nc.sbuf_tensor("hi", [128, 1], F32))
        mid = e(nc.sbuf_tensor("mid", [128, 1], F32))
        cnt = e(nc.sbuf_tensor("cnt", [128, 1], F32))
        cond = e(nc.sbuf_tensor("cond", [128, 1], F32))
        ncond = e(nc.sbuf_tensor("ncond", [128, 1], F32))
        tmp1 = e(nc.sbuf_tensor("tmp1", [128, 1], F32))
        tmp2 = e(nc.sbuf_tensor("tmp2", [128, 1], F32))
        grp_sb = e(nc.sbuf_tensor("grp_sb", [128, 128], F32))
        tsel_sb = e(nc.sbuf_tensor("tsel_sb", [128, 8], F32))
        thr8 = e(nc.sbuf_tensor("thr8", [8, 1], F32))
        eoff_sb = e(nc.sbuf_tensor("eoff_sb", [8, 1], F32))
        bnd_sb = e(nc.sbuf_tensor("bnd_sb", [8, 1], F32))
        posS = e(nc.sbuf_tensor("posS", [128, 8 * NT], U32))
        posG = e(nc.sbuf_tensor("posG", [128, 8 * NT], U32))
        ident = e(nc.sbuf_tensor("ident", [128, 128], BF16))
        idf32 = e(nc.sbuf_tensor("idf32", [128, 128], F32))
        stg = [e(nc.sbuf_tensor(f"stg{i}", [128, 1040], BF16)) for i in range(2)]

        arena = e(nc.sbuf_tensor("arena", [128, 90112], mybir.dt.uint8))
        wexp = [arena[:, ds(i * 45056, 45056)].bitcast(BF16) for i in range(2)]
        ln2fm = arena[:, 0:65536].bitcast(BF16)
        xq = arena[:, 65536:81920].bitcast(F32)
        qkvw = [arena[:, ds(81920 + i * 2048, 2048)].bitcast(BF16) for i in range(2)]
        a4 = arena[:, 0:65536].bitcast(F32)
        smT = a4[0:8, 0:4096]
        maskT = a4[0:8, 4096:8192]
        psc = a4[0:8, 8192:12288]
        pgc = a4[0:8, 12288:16384]

        Xg = e(nc.sbuf_tensor("Xg", [128, 5120], BF16))
        midT = e(nc.sbuf_tensor("midT", [128, 14080], BF16))
        vals = midT[:, 0:4096].bitcast(F32)
        cmpd = Xg[:, 0:2048]
        dnf = e(nc.sbuf_tensor("dnf", [128, 640], F32))
        outT = e(nc.sbuf_tensor("outT", [128, 5120], F32))
        gt = [e(nc.sbuf_tensor(f"gt{j}", [128, 1040], BF16)) for j in range(5)]
        gv = e(nc.sbuf_tensor("gv", [128, 8], F32))
        zrow = e(nc.sbuf_tensor("zrow", [128, 1024], F32))
        bup_sb = e(nc.sbuf_tensor("bup_sb", [128, 88], F32))
        bdn_sb = e(nc.sbuf_tensor("bdn_sb", [128, 64], F32))
        bq_sb = e(nc.sbuf_tensor("bq_sb", [128, 8], F32))
        bk_sb = e(nc.sbuf_tensor("bk_sb", [128, 8], F32))
        bv_sb = e(nc.sbuf_tensor("bv_sb", [128, 8], F32))
        bq8_sb = e(nc.sbuf_tensor("bq8_sb", [128, 8], F32))
        lnt = e(nc.sbuf_tensor("lnt", [128, 1024], BF16))
        qkvo = [e(nc.sbuf_tensor(f"qkvo{i}", [128, 512], BF16)) for i in range(2)]

        gruW_sb = e(nc.sbuf_tensor("gruW_sb", [8, 64], F32))
        gruB_sb = e(nc.sbuf_tensor("gruB_sb", [8, 1], F32))
        gruC_sb = e(nc.sbuf_tensor("gruC_sb", [1, 16], F32))
        g48a_sb = e(nc.sbuf_tensor("g48a_sb", [8, 1], F32))
        g48b_sb = e(nc.sbuf_tensor("g48b_sb", [8, 1], F32))
        wa2 = e(nc.sbuf_tensor("wa2", [128, 1], BF16))
        wb2 = e(nc.sbuf_tensor("wb2", [128, 1], BF16))
        ba_sb = e(nc.sbuf_tensor("ba_sb", [1, 1], F32))
        bb_sb = e(nc.sbuf_tensor("bb_sb", [1, 1], F32))
        sga = e(nc.sbuf_tensor("sga", [1, 512], F32))
        sgb = e(nc.sbuf_tensor("sgb", [1, 512], F32))
        grow = e(nc.sbuf_tensor("grow", [1, 512], F32))

        sum1 = e(nc.sbuf_tensor("sum1", [128, 1], F32))
        mean = e(nc.sbuf_tensor("mean", [128, 1], F32))
        sq = e(nc.sbuf_tensor("sq", [128, 1024], BF16))
        ssq = e(nc.sbuf_tensor("ssq", [128, 1], F32))
        var = e(nc.sbuf_tensor("var", [128, 1], F32))
        rstd = e(nc.sbuf_tensor("rstd", [128, 1], F32))
        nmr = e(nc.sbuf_tensor("nmr", [128, 1], F32))
        eps_sb = e(nc.sbuf_tensor("eps_sb", [128, 1], F32))

        psA = e(nc.psum_tensor("psA", [128, 4096], F32))
        psUP = [psA[:, 0:1024], psA[:, 1024:2048]]
        psT2 = [psA[:, ds(2048 + i * 512, 512)].bitcast(BF16)[:, 0:128] for i in range(2)]
        psF2 = [psA[:, 2048:2176], psA[:, 2560:2688]]
        psC = psA[:, 3072:3073]
        psT8 = psA[0:8, 3074:3075]
        psWa = psA[0:64, 3076:3077]
        psWb = psA[0:64, 3078:3079]
        psBa = psA[0:1, 3080:3081]
        psBb = psA[0:1, 3082:3083]
        psGa = psA[0:1, 3072:3584]
        psGb = psA[0:1, 3584:4096]
        psQ = [psA[:, ds(i * 512, 512)] for i in range(4)]

        sm = sem_dict(ctx, nc, ["ld", "lw", "v", "t", "s", "st", "gq", "ga"])
        blk = e(nc.Block())
        p = Prog(sm)
        REGS = {}

        def bound_reg(eng, key, value):
            if key not in REGS:
                REGS[key] = eng.to_reg(value)
            return REGS[key]

        def f(eng):
            for dst, src in [(vals, sm_resh), (smT, smT_loc), (grp_sb, grpsum),
                             (tsel_sb, thrsel), (eoff_sb, eoff_i), (bnd_sb, bnd_i),
                             (bup_sb, bupP), (bdn_sb, bdnP), (bq_sb, bq_r),
                             (bk_sb, bk_r), (bv_sb, bv_r), (gruW_sb, gruW),
                             (gruB_sb, gruB), (gruC_sb, gruC), (g48a_sb, g48a_i),
                             (g48b_sb, g48b_i), (idf32, idf32p)]:
                eng.dma_start(out=dst[:], in_=src).then_inc(sm["ld"], 16)
        p.on("sync", f)
        ld_pro = p.bump("ld", 16 * 17)

        def f(eng):
            eng.dma_start(out=ident[:], in_=identp).then_inc(sm["lw"], 16)
        p.on("gpsimd", f)
        lw_id = p.bump("lw", 16)

        vz = p.fenced("vector", "v", [
            lambda eng: eng.memset(lo[:], 0.0),
            lambda eng: eng.memset(hi[:], 1.0000001),
            lambda eng: eng.memset(mid[:], 0.50000006),
            lambda eng: eng.memset(zrow[:], 0.0),
            lambda eng: eng.memset(eps_sb[:], 1e-5),
        ])

        def f(eng, vz=vz):
            eng.wait_ge(sm["v"], vz)
            eng.dma_start(out=outb[ds(E * C_CAP, 128), :], in_=zrow[:]).then_inc(sm["st"], 16)
        p.on("sync", f)
        st_zero = p.bump("st", 16)

        NITER = 48
        for it in range(NITER):
            vn = p.fenced("vector", "v", [
                lambda eng: eng.tensor_scalar(cmpd[:], vals[:], mid[:], 0.0,
                                              ALU.is_ge, ALU.add, accum_out=cnt[:]),
            ], pre_waits=[("ld", 16 * 2)] if it == 0 else ())

            def f(eng, vn=vn, first=(it == 0)):
                eng.wait_ge(sm["v"], vn)
                if first:
                    eng.wait_ge(sm["ld"], ld_pro)
                eng.matmul(psC, grp_sb[:], cnt[:], start=True, stop=True).then_inc(sm["t"], 1)
            p.on("tensor", f)
            tn = p.bump("t")

            p.fenced("vector", "v", [
                lambda eng: eng.tensor_scalar(cond[:], psC, 4096.0, None, ALU.is_ge),
                lambda eng: eng.tensor_scalar(ncond[:], cond[:], -1.0, 1.0, ALU.mult, ALU.add),
                lambda eng: eng.tensor_sub(tmp1[:], mid[:], lo[:]),
                lambda eng: eng.tensor_mul(tmp1[:], tmp1[:], cond[:]),
                lambda eng: eng.tensor_add(lo[:], lo[:], tmp1[:]),
                lambda eng: eng.tensor_sub(tmp2[:], mid[:], hi[:]),
                lambda eng: eng.tensor_mul(tmp2[:], tmp2[:], ncond[:]),
                lambda eng: eng.tensor_add(hi[:], hi[:], tmp2[:]),
                lambda eng: eng.tensor_add(mid[:], lo[:], hi[:]),
                lambda eng: eng.tensor_scalar(mid[:], mid[:], 0.5, None, ALU.mult),
            ], pre_waits=[("t", tn)])

        vlast = p.val("v")

        def f(eng, vlast=vlast):
            eng.wait_ge(sm["v"], vlast)
            eng.matmul(psT8, tsel_sb[:], lo[:], start=True, stop=True).then_inc(sm["t"], 1)
        p.on("tensor", f)
        tn = p.bump("t")
        vn = p.fenced("vector", "v", [
            lambda eng: eng.tensor_copy(thr8[:], psT8),
        ], pre_waits=[("t", tn)])

        def f(eng, vn=vn):
            eng.wait_ge(sm["v"], vn)
            eng.dma_start(out=thr_o, in_=thr8[:]).then_inc(sm["st"], 16)
        p.on("sync", f)
        p.bump("st", 16)

        DUMP = float(E * C_CAP)
        BIGF = float(1 << 26)
        vpos_chain = p.fenced("vector", "v", [
            lambda eng: eng.tensor_scalar(maskT, smT, thr8[:], None, ALU.is_ge),
            lambda eng: eng.memset(psc, 0.0),
            lambda eng: eng.tensor_tensor_scan(smT, maskT, psc, 0.0, ALU.add, ALU.add),
            lambda eng: eng.tensor_scalar(psc, smT, eoff_sb[:], -1.0, ALU.add, ALU.add),
            lambda eng: eng.tensor_mul(pgc, psc, maskT),
            lambda eng: eng.scalar_tensor_tensor(pgc, maskT, -DUMP, pgc, ALU.mult, ALU.add),
            lambda eng: eng.tensor_scalar(pgc, pgc, DUMP, None, ALU.add),
            lambda eng: eng.tensor_scalar(smT, pgc, bnd_sb[:], None, ALU.is_ge),
            lambda eng: eng.tensor_scalar(psc, psc, 0.0, None, ALU.add),
            lambda eng: eng.tensor_scalar(maskT, maskT, -1.0, 1.0, ALU.mult, ALU.add),
            lambda eng: eng.scalar_tensor_tensor(psc, maskT, BIGF, psc, ALU.mult, ALU.add),
            lambda eng: eng.tensor_scalar(maskT, smT, 0.0, None, ALU.add),
            lambda eng: eng.tensor_scalar(smT, pgc, -1.0, DUMP, ALU.mult, ALU.add),
            lambda eng: eng.tensor_mul(smT, smT, maskT),
            lambda eng: eng.tensor_add(pgc, pgc, smT),
        ], pre_waits=[("v", vn)])

        vtr_copies = p.val("v")
        for i in range(NT):
            for which, src, dstp in ((0, psc, posS), (1, pgc, posG)):
                q = i * 2 + which
                par = q % 2

                def f(eng, i=i, src=src, par=par, q=q, vpos_chain=vpos_chain,
                      vtr_copies=vtr_copies):
                    if q == 0:
                        eng.wait_ge(sm["v"], vpos_chain)
                    if q >= 2:
                        eng.wait_ge(sm["v"], vtr_copies + q - 1)
                    eng.transpose(psF2[par][:, 0:8], src[:, ds(i * 128, 128)],
                                  idf32[0:8, 0:8]).then_inc(sm["t"], 1)
                p.on("tensor", f)
                tn = p.bump("t")

                def f(eng, tn=tn, i=i, par=par, dstp=dstp):
                    eng.wait_ge(sm["t"], tn)
                    eng.tensor_copy(dstp[:, ds(i * 8, 8)], psF2[par][:, 0:8]).then_inc(sm["v"], 1)
                p.on("vector", f)
                p.bump("v")
        vpos = p.val("v")

        if True:
            def f(eng, vpos=vpos):
                eng.wait_ge(sm["v"], vpos)
                eng.dma_start(out=posS_o, in_=posS[:]).then_inc(sm["st"], 16)
                eng.dma_start(out=posG_o, in_=posG[:]).then_inc(sm["st"], 16)
            p.on("sync", f)
            p.bump("st", 32)

        for i in range(NT):
            par = i % 2
            sti = stg[par]
            gq_before = p.val("gq")

            def f(eng, i=i, sti=sti, wv=gq_before - 16 * 8):
                if wv > 0:
                    eng.wait_ge(sm["gq"], wv)
                eng.dma_start(out=sti[:, 0:1024], in_=ln1[ds(i * 128, 128), :]).then_inc(sm["ld"], 16)
                eng.dma_start(out=sti[:].bitcast(F32)[:, 512:520],
                              in_=sm_loc[ds(i * 128, 128), :]).then_inc(sm["ld"], 16)
            p.on("sync", f)
            lv = p.bump("ld", 32)

            def f(eng, i=i, sti=sti, lv=lv, vpos=vpos):
                eng.wait_ge(sm["ld"], lv)
                eng.wait_ge(sm["v"], vpos)
                for ee in range(E):
                    eng.indirect_dma_start(
                        out=gath,
                        out_offset=bass.IndirectOffsetOnAxis(ap=posS[:, ds(i * 8 + ee, 1)], axis=0),
                        in_=sti[:], in_offset=None,
                        bounds_check=bound_reg(eng, ee, (ee + 1) * C_CAP - 1),
                        oob_is_err=False,
                    ).then_inc(sm["gq"], 16)
            p.on("gpsimd", f)
            p.bump("gq", 16 * 8)
        gq_scat = p.val("gq")

        t_up_done = 0
        t_dn_done = 0
        vgv_prev = 0
        st_out_done = st_zero
        for ee in range(E):
            for j in range(5):
                def f(eng, ee=ee, j=j, gq_scat=gq_scat, wv=t_up_done, wvv=vgv_prev):
                    if j == 0:
                        eng.wait_ge(sm["gq"], gq_scat)
                    if wv:
                        eng.wait_ge(sm["t"], wv)
                    if wvv:
                        eng.wait_ge(sm["v"], wvv)
                    eng.dma_start(out=gt[j][:], in_=gath[ds(ee * C_CAP + j * 128, 128), :]).then_inc(sm["ld"], 16)
                p.on("sync", f)
            lgt = p.bump("ld", 16 * 5)

            for k in range(8):
                def f(eng, ee=ee, k=k, wv=t_up_done, vpos=vpos):
                    if wv:
                        eng.wait_ge(sm["t"], wv)
                    if ee == 0 and k == 0:
                        eng.wait_ge(sm["v"], vpos)
                        eng.wait_ge(sm["gq"], gq_scat)
                    eng.dma_start(out=wexp[0][:, ds(k * 2816, 2816)],
                                  in_=wupP[ee // 2, ds(k * 128, 128), :]).then_inc(sm["lw"], 16)
                p.on("gpsimd", f)
            lwu = p.bump("lw", 16 * 8)

            vxg_base = p.val("v")
            for q in range(40):
                j, c = q // 8, q % 8

                def f(eng, j=j, c=c, q=q, lgt=lgt, vxg_base=vxg_base):
                    if q == 0:
                        eng.wait_ge(sm["ld"], lgt)
                        eng.wait_ge(sm["lw"], lw_id)
                        eng.wait_ge(sm["v"], vxg_base)
                    if q >= 2:
                        eng.wait_ge(sm["v"], vxg_base + q - 1)
                    eng.transpose(psT2[q % 2][:], gt[j][:, ds(c * 128, 128)],
                                  ident[:]).then_inc(sm["t"], 1)
                p.on("tensor", f)
                tn = p.bump("t")

                def f(eng, tn=tn, j=j, c=c, q=q):
                    eng.wait_ge(sm["t"], tn)
                    eng.tensor_copy(Xg[:, ds(c * 640 + j * 128, 128)],
                                    psT2[q % 2][:]).then_inc(sm["v"], 1)
                p.on("vector", f)
                p.bump("v")
            vxg = p.val("v")

            vgv = p.fenced("vector", "v", [
                (lambda eng, j=j2, ee=ee: eng.tensor_copy(
                    gv[:, ds(j, 1)], gt[j][:].bitcast(F32)[:, ds(512 + ee, 1)]))
                for j2 in range(5)
            ], pre_waits=[("ld", lgt)])
            vgv_prev = vgv

            s_silu_base = p.val("s")
            for m in range(22):
                def f(eng, m=m, lwu=lwu, vxg=vxg, s_silu_base=s_silu_base):
                    if m == 0:
                        eng.wait_ge(sm["lw"], lwu)
                        eng.wait_ge(sm["v"], vxg)
                    if m < 2:
                        eng.wait_ge(sm["s"], s_silu_base)
                    if m >= 2:
                        eng.wait_ge(sm["s"], s_silu_base + m - 1)
                    pu = psUP[m % 2]
                    for k in range(8):
                        eng.matmul(pu[:, 0:512], wexp[0][:, ds(k * 2816 + m * 128, 128)],
                                   Xg[:, ds(k * 640, 512)], start=(k == 0), stop=(k == 7))
                    for k in range(8):
                        inst = eng.matmul(pu[:, 512:640],
                                          wexp[0][:, ds(k * 2816 + m * 128, 128)],
                                          Xg[:, ds(k * 640 + 512, 128)],
                                          start=(k == 0), stop=(k == 7))
                    inst.then_inc(sm["t"], 1)
                p.on("tensor", f)
                tn = p.bump("t")
                if m == 21:
                    t_up_done = tn

                def f(eng, tn=tn, m=m, ee=ee):
                    eng.wait_ge(sm["t"], tn)
                    pu = psUP[m % 2]
                    eng.activation(midT[:, ds(m * 640, 512)], pu[:, 0:512], AF.Silu,
                                   bias=bup_sb[:, ds((ee // 2) * 22 + m, 1)], scale=1.0)
                    eng.activation(midT[:, ds(m * 640 + 512, 128)], pu[:, 512:640], AF.Silu,
                                   bias=bup_sb[:, ds((ee // 2) * 22 + m, 1)],
                                   scale=1.0).then_inc(sm["s"], 1)
                p.on("scalar", f)
                p.bump("s")
            sup = p.val("s")

            for k in range(22):
                def f(eng, ee=ee, k=k, wv=t_dn_done, vpos=vpos):
                    if wv:
                        eng.wait_ge(sm["t"], wv)
                    if ee == 0 and k == 0:
                        eng.wait_ge(sm["v"], vpos)
                    eng.dma_start(out=wexp[1][:, ds(k * 1024, 1024)],
                                  in_=wdnP[ee, ds(k * 128, 128), :]).then_inc(sm["lw"], 16)
                p.on("gpsimd", f)
            lwd = p.bump("lw", 16 * 22)

            s_dn_base = p.val("s")
            for o in range(8):
                def f(eng, o=o, lwd=lwd, sup=sup, s_dn_base=s_dn_base):
                    if o == 0:
                        eng.wait_ge(sm["lw"], lwd)
                        eng.wait_ge(sm["s"], sup)
                    if o >= 2:
                        eng.wait_ge(sm["s"], s_dn_base + o - 1)
                    pu = psUP[o % 2]
                    for k in range(22):
                        eng.matmul(pu[:, 0:512], wexp[1][:, ds(k * 1024 + o * 128, 128)],
                                   midT[:, ds(k * 640, 512)], start=(k == 0), stop=(k == 21))
                    for k in range(22):
                        inst = eng.matmul(pu[:, 512:640],
                                          wexp[1][:, ds(k * 1024 + o * 128, 128)],
                                          midT[:, ds(k * 640 + 512, 128)],
                                          start=(k == 0), stop=(k == 21))
                    inst.then_inc(sm["t"], 1)
                p.on("tensor", f)
                tn = p.bump("t")
                if o == 7:
                    t_dn_done = tn

                tr_prev = p.val("t")

                def f(eng, tn=tn, o=o, ee=ee, tr_prev=tr_prev):
                    eng.wait_ge(sm["t"], tr_prev)
                    pu = psUP[o % 2]
                    eng.activation(dnf[:, 0:512], pu[:, 0:512], AF.Identity,
                                   bias=bdn_sb[:, ds(ee * 8 + o, 1)], scale=1.0)
                    eng.activation(dnf[:, 512:640], pu[:, 512:640], AF.Identity,
                                   bias=bdn_sb[:, ds(ee * 8 + o, 1)], scale=1.0).then_inc(sm["s"], 1)
                p.on("scalar", f)
                sn = p.bump("s")

                vtrd_base = p.val("v")
                for j in range(5):
                    def f(eng, sn=sn, j=j, vtrd_base=vtrd_base):
                        if j == 0:
                            eng.wait_ge(sm["s"], sn)
                            eng.wait_ge(sm["v"], vtrd_base)
                        if j >= 2:
                            eng.wait_ge(sm["v"], vtrd_base + j - 1)
                        eng.transpose(psF2[j % 2], dnf[:, ds(j * 128, 128)],
                                      idf32[:]).then_inc(sm["t"], 1)
                    p.on("tensor", f)
                    tn2 = p.bump("t")

                    def f(eng, tn2=tn2, o=o, j=j, wst=st_out_done):
                        eng.wait_ge(sm["t"], tn2)
                        if o == 0 and j == 0:
                            eng.wait_ge(sm["st"], wst)
                        eng.tensor_copy(outT[:, ds(j * 1024 + o * 128, 128)],
                                        psF2[j % 2]).then_inc(sm["v"], 1)
                    p.on("vector", f)
                    p.bump("v")
            vdn = p.val("v")

            for j in range(5):
                vn = p.fenced("vector", "v", [
                    lambda eng, j=j: eng.tensor_scalar(tmp1[:], gv[:, ds(j, 1)], 0.5,
                                                       None, ALU.mult),
                    lambda eng, j=j: eng.tensor_scalar(outT[:, ds(j * 1024, 1024)],
                                                       outT[:, ds(j * 1024, 1024)],
                                                       tmp1[:], None, ALU.mult),
                ], pre_waits=[("v", vdn), ("v", vgv)] if j == 0 else ())

                def f(eng, vn=vn, ee=ee, j=j):
                    eng.wait_ge(sm["v"], vn)
                    eng.dma_start(out=outb[ds(ee * C_CAP + j * 128, 128), :],
                                  in_=outT[:, ds(j * 1024, 1024)]).then_inc(sm["st"], 16)
                p.on("sync", f)
                st_out_done = p.bump("st", 16)

        ga_base = 0
        st_x1_prev = 0
        v_ln2_prev = 0
        for g in range(8):
            tiles = [g * 4 + u for u in range(4)]
            for u, i in enumerate(tiles):
                def f(eng, i=i, u=u, st_out_done=st_out_done, wv=v_ln2_prev,
                      wst=st_x1_prev, first=(g == 0)):
                    if first and u == 0:
                        eng.wait_ge(sm["st"], st_out_done)
                    if wv:
                        eng.wait_ge(sm["v"], wv)
                    if wst:
                        eng.wait_ge(sm["st"], wst)
                    eng.dma_start(out=xq[:, ds(u * 1024, 1024)],
                                  in_=x0[ds(i * 128, 128), :]).then_inc(sm["ld"], 16)
                p.on("sync", f)
            lx = p.bump("ld", 16 * 4)

            ga_grp_base = p.val("ga")
            for ee in range(E):
                for u, i in enumerate(tiles):
                    def f(eng, ee=ee, i=i, u=u, lx=lx, ga_grp_base=ga_grp_base):
                        if ee == 0 and u == 0:
                            eng.wait_ge(sm["ld"], lx)
                        if ee > 0:
                            eng.wait_ge(sm["ga"], ga_grp_base + 16 * ((ee - 1) * 4 + u + 1))
                        eng.indirect_dma_start(
                            out=xq[:, ds(u * 1024, 1024)], out_offset=None,
                            in_=outb,
                            in_offset=bass.IndirectOffsetOnAxis(ap=posG[:, ds(i * 8 + ee, 1)], axis=0),
                            compute_op=ALU.add, oob_is_err=True,
                            bounds_check=bound_reg(eng, "ga", E * C_CAP + 127),
                        ).then_inc(sm["ga"], 16)
                    p.on("gpsimd", f)
                    p.bump("ga", 16)
            gdone = p.val("ga")

            for u, i in enumerate(tiles):
                def f(eng, u=u, i=i, gdone=gdone):
                    eng.wait_ge(sm["ga"], gdone)
                    eng.dma_start(out=x1_o[ds(i * 128, 128), :],
                                  in_=xq[:, ds(u * 1024, 1024)]).then_inc(sm["st"], 16)
                p.on("sync", f)
                st_x1_prev = p.bump("st", 16)

                xti = xq[:, ds(u * 1024, 1024)]
                vn = p.fenced("vector", "v", [
                    lambda eng, xti=xti: eng.reduce_sum(sum1[:], xti, axis=AX.X),
                ], pre_waits=[("ga", gdone)])
                sn = p.fenced("scalar", "s", [
                    lambda eng, xti=xti: eng.activation(sq[:], xti, AF.Square, bias=0.0,
                                                        scale=1.0, accum_out=ssq[:]),
                ], pre_waits=[("ga", gdone), ("v", p.val("v"))])
                vn = p.fenced("vector", "v", [
                    lambda eng: eng.tensor_scalar(mean[:], sum1[:], 1.0 / 1024, None, ALU.mult),
                    lambda eng: eng.tensor_mul(nmr[:], mean[:], mean[:]),
                    lambda eng: eng.scalar_tensor_tensor(var[:], ssq[:], 1.0 / 1024,
                                                         nmr[:], ALU.mult, ALU.subtract),
                ], pre_waits=[("v", vn), ("s", sn)])
                sn = p.fenced("scalar", "s", [
                    lambda eng: eng.activation(rstd[:], var[:], AF.Sqrt, bias=eps_sb[:],
                                               scale=1.0),
                ], pre_waits=[("v", vn)])
                vn = p.fenced("vector", "v", [
                    lambda eng: eng.reciprocal(rstd[:], rstd[:]),
                    lambda eng: eng.tensor_mul(nmr[:], mean[:], rstd[:]),
                    lambda eng: eng.tensor_scalar(nmr[:], nmr[:], -1.0, None, ALU.mult),
                    lambda eng, xti=xti: eng.tensor_scalar(lnt[:], xti, rstd[:], nmr[:],
                                                           ALU.mult, ALU.add),
                ], pre_waits=[("s", sn), ("t", p.val("t"))])

                vtl_base = p.val("v")
                for c in range(8):
                    def f(eng, c=c, vn=vn, vtl_base=vtl_base):
                        if c == 0:
                            eng.wait_ge(sm["v"], vn)
                        if c >= 2:
                            eng.wait_ge(sm["v"], vtl_base + c - 1)
                        eng.transpose(psT2[c % 2][:], lnt[:, ds(c * 128, 128)],
                                      ident[:]).then_inc(sm["t"], 1)
                    p.on("tensor", f)
                    tn = p.bump("t")

                    def f(eng, tn=tn, c=c, i=i):
                        eng.wait_ge(sm["t"], tn)
                        eng.tensor_copy(ln2fm[:, ds(c * 4096 + i * 128, 128)],
                                        psT2[c % 2][:]).then_inc(sm["v"], 1)
                    p.on("vector", f)
                    p.bump("v")
            v_ln2_prev = p.val("v")
        vln2 = p.val("v")

        vbq = p.fenced("vector", "v", [
            lambda eng: eng.tensor_scalar(bq8_sb[:], bq_sb[:], 0.125, None, ALU.mult),
        ], pre_waits=[("ld", ld_pro)])

        qkv_slot_free_st = [0, 0]
        wslot_free_t = [0, 0]
        projs = [(wqT, bq8_sb, 0.125, qT_o), (wkT, bk_sb, 1.0, kT_o),
                 (wvT, bv_sb, 1.0, vT_o)]
        lin = 0
        for pi, (wsrc, bias_sb, scale, out_dram) in enumerate(projs):
            for o in range(8):
                wslot = qkvw[o % 2]

                def f(eng, o=o, wslot=wslot, wsrc=wsrc, wv=wslot_free_t[o % 2],
                      vln2=vln2):
                    if wv:
                        eng.wait_ge(sm["t"], wv)
                    eng.dma_start(
                        out=wslot[:].rearrange("p (k f) -> p k f", f=128),
                        in_=wsrc[:, ds(o * 128, 128)].rearrange("(k p) f -> p k f", p=128),
                    ).then_inc(sm["lw"], 16)
                p.on("gpsimd", f)
                lw = p.bump("lw", 16)

                for n in range(8):
                    bank = lin % 4
                    s_prev_bank = p.val("s") - 3

                    def f(eng, o=o, n=n, bank=bank, lw=lw, vln2=vln2, wslot=wslot,
                          sv=s_prev_bank, lin=lin, sall=p.val("s")):
                        if n == 0:
                            eng.wait_ge(sm["lw"], lw)
                        if lin == 0:
                            eng.wait_ge(sm["v"], vln2)
                            eng.wait_ge(sm["s"], sall)
                        if lin >= 4:
                            eng.wait_ge(sm["s"], sv)
                        for k in range(8):
                            inst = eng.matmul(psQ[bank], wslot[:, ds(k * 128, 128)],
                                              ln2fm[:, ds(k * 4096 + n * 512, 512)],
                                              start=(k == 0), stop=(k == 7))
                        inst.then_inc(sm["t"], 1)
                    p.on("tensor", f)
                    tn = p.bump("t")
                    if n == 7:
                        wslot_free_t[o % 2] = tn

                    oslot = lin % 2

                    def f(eng, tn=tn, o=o, bank=bank, bias_sb=bias_sb, scale=scale,
                          oslot=oslot, wst=qkv_slot_free_st[oslot]):
                        eng.wait_ge(sm["t"], tn)
                        if wst:
                            eng.wait_ge(sm["st"], wst)
                        eng.activation(qkvo[oslot][:], psQ[bank], AF.Identity,
                                       bias=bias_sb[:, ds(o, 1)],
                                       scale=scale).then_inc(sm["s"], 1)
                    p.on("scalar", f)
                    sn = p.bump("s")

                    def f(eng, sn=sn, o=o, n=n, out_dram=out_dram, oslot=oslot):
                        eng.wait_ge(sm["s"], sn)
                        eng.dma_start(out=out_dram[ds(o * 128, 128), ds(n * 512, 512)],
                                      in_=qkvo[oslot][:]).then_inc(sm["st"], 16)
                    p.on("sync", f)
                    qkv_slot_free_st[oslot] = p.bump("st", 16)
                    lin += 1

        def f(eng, wv=ld_pro):
            eng.wait_ge(sm["ld"], wv)
            eng.matmul(psWa, gruW_sb[:], g48a_sb[:], start=True, stop=True)
            eng.matmul(psWb, gruW_sb[:], g48b_sb[:], start=True, stop=True)
            eng.matmul(psBa, gruB_sb[:], g48a_sb[:], start=True, stop=True)
            eng.matmul(psBb, gruB_sb[:], g48b_sb[:], start=True, stop=True).then_inc(sm["t"], 1)
        p.on("tensor", f)
        tn = p.bump("t")

        vn = p.fenced("vector", "v", [
            lambda eng: eng.tensor_copy(wa2[0:64, :], psWa),
            lambda eng: eng.tensor_copy(wa2[64:128, :], psWa),
            lambda eng: eng.tensor_copy(wb2[0:64, :], psWb),
            lambda eng: eng.tensor_copy(wb2[64:128, :], psWb),
            lambda eng: eng.tensor_copy(ba_sb[:], psBa),
            lambda eng: eng.tensor_copy(bb_sb[:], psBb),
        ], pre_waits=[("t", tn)])
        vgru = vn

        for h in range(NH):
            for n in range(8):
                base = (h % 2) * 64

                def f(eng, h=h, n=n, base=base, vgru=vgru, vln2=vln2, wv=p.val("s")):
                    if h == 0 and n == 0:
                        eng.wait_ge(sm["v"], vgru)
                        eng.wait_ge(sm["v"], vln2)
                    else:
                        eng.wait_ge(sm["s"], wv)
                    eng.matmul(psGa[:, 0:512], wa2[ds(base, 64), :],
                               ln2fm[ds(base, 64), ds((h // 2) * 4096 + n * 512, 512)],
                               start=True, stop=True)
                    eng.matmul(psGb, wb2[ds(base, 64), :],
                               ln2fm[ds(base, 64), ds((h // 2) * 4096 + n * 512, 512)],
                               start=True, stop=True).then_inc(sm["t"], 1)
                p.on("tensor", f)
                tn = p.bump("t")

                sn = p.fenced("scalar", "s", [
                    lambda eng: eng.activation(sga[:], psGa[:, 0:512], AF.Sigmoid,
                                               bias=ba_sb[:], scale=1.0),
                    lambda eng: eng.activation(sgb[:], psGb, AF.Sigmoid,
                                               bias=bb_sb[:], scale=1.0),
                ], pre_waits=[("t", tn), ("v", p.val("v"))])

                vn = p.fenced("vector", "v", [
                    lambda eng, h=h: eng.tensor_scalar(sgb[:], sgb[:], gruC_sb[0:1, ds(h, 1)],
                                                       -1.0, ALU.mult, ALU.add),
                    lambda eng: eng.tensor_mul(grow[:], sga[:], sgb[:]),
                    lambda eng: eng.tensor_scalar(grow[:], grow[:], 2.0, None, ALU.add),
                ], pre_waits=[("s", sn), ("st", p.val("st"))])

                def f(eng, vn=vn, h=h, n=n):
                    eng.wait_ge(sm["v"], vn)
                    eng.dma_start(out=gate_o[ds(h, 1), ds(n * 512, 512)],
                                  in_=grow[:]).then_inc(sm["st"], 16)
                p.on("sync", f)
                p.bump("st", 16)

        def f(eng, n=p.val("st")):
            eng.wait_ge(sm["st"], n)
        p.on("sync", f)
        def f2(eng, n=p.val("ga")):
            eng.wait_ge(sm["ga"], n)
        p.on("gpsimd", f2)

        p.replay(blk)
    return nc
def build_C(dbg=False):
    nc = bass.Bass("TRN2", target_bir_lowering=False, debug=False)
    x1 = nc.dram_tensor("x1", [T, D_MODEL], F32, kind="ExternalInput").ap()
    qt8 = nc.dram_tensor("qt8", [D_MODEL, T], BF16, kind="ExternalInput").ap()
    krevT = nc.dram_tensor("krevT", [D_MODEL, S], BF16, kind="ExternalInput").ap()
    vtok = nc.dram_tensor("vtok", [NH, 64 * 128 * 65], BF16, kind="ExternalInput").ap()
    gate_i = nc.dram_tensor("gate_i", [NH, T], F32, kind="ExternalInput").ap()
    pbrevc = nc.dram_tensor("pbrevc", [NH, 12416], F32, kind="ExternalInput").ap()
    woT = nc.dram_tensor("woT", [D_MODEL, D_MODEL], F32, kind="ExternalInput").ap()
    bo_i = nc.dram_tensor("bo_i", [1, D_MODEL], F32, kind="ExternalInput").ap()
    wr = nc.dram_tensor("wr", [D_MODEL, E], F32, kind="ExternalInput").ap()
    br = nc.dram_tensor("br", [1, E], F32, kind="ExternalInput").ap()
    identp = nc.dram_tensor("identp", [128, 128], BF16, kind="ExternalInput").ap()
    onesf_i = nc.dram_tensor("onesf_i", [1, 64], F32, kind="ExternalInput").ap()

    x2_o = nc.dram_tensor("x2_o", [T, D_MODEL], F32, kind="ExternalOutput").ap()
    ln3_o = nc.dram_tensor("ln3_o", [T, D_MODEL], BF16, kind="ExternalOutput").ap()
    sm2_o = nc.dram_tensor("sm2_o", [T, E], F32, kind="ExternalOutput").ap()

    attnfm = nc.dram_tensor("attnfm", [D_MODEL, T], BF16,
                            kind="ExternalOutput" if dbg else "Internal").ap()

    NJC = S // 128
    NIB = T // 512

    with ExitStack() as ctx:
        e = ctx.enter_context
        shd = [e(nc.sbuf_tensor(f"shd{i}", [128, 12288], F32)) for i in range(2)]
        kq = e(nc.sbuf_tensor("kq", [128, S], BF16))
        qt = e(nc.sbuf_tensor("qt", [128, T], BF16))
        vt = [e(nc.sbuf_tensor(f"vt{i}", [128, 4160], BF16)) for i in range(2)]
        grep = [[e(nc.sbuf_tensor(f"grep{h}_{i}", [128, 512], BF16)) for i in range(2)]
                for h in range(2)]
        bt = [[e(nc.sbuf_tensor(f"bt{hh}_{i}", [128, 512], BF16)) for i in range(2)]
              for hh in range(2)]
        pt = [[e(nc.sbuf_tensor(f"pt{hh}_{i}", [128, 512], BF16)) for i in range(2)]
              for hh in range(2)]
        rsb = [e(nc.sbuf_tensor(f"rsb{hh}", [1, 512], F32)) for hh in range(2)]
        rrep = [e(nc.sbuf_tensor(f"rrep{hh}", [64, 512], F32)) for hh in range(2)]
        attn_t = e(nc.sbuf_tensor("attn_t", [128, 512], BF16))
        onesf = e(nc.sbuf_tensor("onesf", [1, 64], F32))
        ident = e(nc.sbuf_tensor("ident", [128, 128], BF16))
        wot = [e(nc.sbuf_tensor(f"wot{i}", [128, 1024], BF16)) for i in range(2)]
        wo_ev = [e(nc.sbuf_tensor(f"wo_ev{i}", [128, 512], BF16)) for i in range(2)]
        atok = [e(nc.sbuf_tensor(f"atok{i}", [128, 1024], BF16)) for i in range(4)]
        borep = e(nc.sbuf_tensor("borep", [128, 1024], F32))
        borow = e(nc.sbuf_tensor("borow", [1, 1024], F32))
        x2t = [e(nc.sbuf_tensor(f"x2t{i}", [128, 1024], F32)) for i in range(2)]
        wr_sb = e(nc.sbuf_tensor("wr_sb", [128, 64], BF16))
        brrep = e(nc.sbuf_tensor("brrep", [128, 8], F32))
        brrow = e(nc.sbuf_tensor("brrow", [1, 8], F32))
        ones1 = e(nc.sbuf_tensor("ones1", [1, 128], F32))
        eps_sb = e(nc.sbuf_tensor("eps_sb", [128, 1], F32))

        psA = e(nc.psum_tensor("psA", [128, 4096], F32))
        psS = [[psA[:, 0:512], psA[:, 512:1024]],
               [psA[:, 1024:1536], psA[:, 1536:2048]]]
        psO = [psA[0:65, 2048:2560], psA[0:65, 2560:3072]]
        psR = [psA[0:64, 3072:3584], psA[0:64, 3584:4096]]
        psW = [psA[:, 0:512], psA[:, 512:1024], psA[:, 1024:1536], psA[:, 1536:2048]]
        psT2 = [psA[:, ds(2048 + i * 512, 512)].bitcast(BF16)[:, 0:128] for i in range(2)]
        psB1 = psA[:, 3072:4096]
        psL = psA[:, 3072:3080]

        sm = sem_dict(ctx, nc, ["ld", "lw", "v", "t", "s", "g", "st"])
        blk = e(nc.Block())
        p = Prog(sm)

        def f(eng):
            for dst, src in [(onesf, onesf_i), (brrow, br), (borow, bo_i)]:
                eng.dma_start(out=dst[:], in_=src).then_inc(sm["ld"], 16)
        p.on("sync", f)
        ld_pro = p.bump("ld", 16 * 3)

        def f(eng):
            eng.dma_start(out=ident[:], in_=identp).then_inc(sm["lw"], 16)
            eng.dma_start(out=wr_sb[:].rearrange("p (c e) -> p c e", e=8),
                          in_=wr.rearrange("(c p) e -> p c e", p=128)).then_inc(sm["lw"], 16)
        p.on("gpsimd", f)
        lw_id = p.bump("lw", 32)

        vz = p.fenced("vector", "v", [
            lambda eng: eng.memset(eps_sb[:], 1e-5),
            lambda eng: eng.memset(ones1[:], 1.0),
        ])

        def f(eng, vz=vz, ld_pro=ld_pro):
            eng.wait_ge(sm["v"], vz)
            eng.wait_ge(sm["ld"], ld_pro)
            eng.matmul(psL, ones1[:], brrow[:], start=True, stop=True).then_inc(sm["t"], 1)
        p.on("tensor", f)
        tn = p.bump("t")
        vbr = p.fenced("vector", "v", [
            lambda eng: eng.tensor_copy(brrep[:], psL),
        ], pre_waits=[("t", tn)])

        def f(eng, vbr=vbr):
            eng.wait_ge(sm["v"], vbr)
            eng.matmul(psB1[:, 0:512], ones1[:], borow[:, 0:512], start=True, stop=True)
            eng.matmul(psB1[:, 512:1024], ones1[:], borow[:, 512:1024],
                       start=True, stop=True).then_inc(sm["t"], 1)
        p.on("tensor", f)
        tn = p.bump("t")
        vpro = p.fenced("vector", "v", [
            lambda eng: eng.tensor_copy(borep[:, 0:512], psB1[:, 0:512]),
            lambda eng: eng.tensor_copy(borep[:, 512:1024], psB1[:, 512:1024]),
        ], pre_waits=[("t", tn)])

        lastExp = [[0, 0], [0, 0]]
        lastAV = [[0, 0], [0, 0]]
        lastSTT = [[0, 0], [0, 0]]
        lastOEvac = [vpro, vpro]
        lastREvac = [vpro, vpro]
        lastAttnSt = 0

        hp_loads = []
        for hp in range(8):
            lw_before = p.val("lw")

            def f(eng, hp=hp, w=p.val("t"), wg=p.val("g"), wv=p.val("v")):
                if hp > 0:
                    eng.wait_ge(sm["t"], w)
                    eng.wait_ge(sm["g"], wg)
                    eng.wait_ge(sm["v"], wv)
                eng.dma_start(out=qt[:], in_=qt8[ds(hp * 128, 128), :]).then_inc(sm["ld"], 16)
                eng.dma_start(out=kq[:], in_=krevT[ds(hp * 128, 128), :]).then_inc(sm["ld"], 16)
                eng.dma_start(out=shd[0][:],
                              in_=bass.AP(tensor=pbrevc.tensor, offset=(2 * hp) * 12416,
                                          ap=[[1, 128], [1, 12288]])).then_inc(sm["ld"], 16)
                eng.dma_start(out=shd[1][:],
                              in_=bass.AP(tensor=pbrevc.tensor, offset=(2 * hp + 1) * 12416,
                                          ap=[[1, 128], [1, 12288]])).then_inc(sm["ld"], 16)
                eng.dma_start(out=vt[0][:].rearrange("r (c x) -> r c x", x=65),
                              in_=vtok[ds(2 * hp, 1), :].rearrange("1 (c r x) -> r c x", r=128, x=65)
                              ).then_inc(sm["ld"], 16)
                eng.dma_start(out=vt[1][:].rearrange("r (c x) -> r c x", x=65),
                              in_=vtok[ds(2 * hp + 1, 1), :].rearrange("1 (c r x) -> r c x", r=128, x=65)
                              ).then_inc(sm["ld"], 16)
            p.on("sync", f)
            ld_hp = p.bump("ld", 16 * 6)


            for ib in range(NIB):
                ibp = ib % 2
                def f(eng, hp=hp, ib=ib, ibp=ibp, w=p.val("g")):
                    if hp + ib > 0:
                        eng.wait_ge(sm["g"], w)
                    for hh in range(2):
                        src = bass.AP(tensor=gate_i.tensor,
                                      offset=(2 * hp + hh) * T + ib * 512,
                                      ap=[[0, 128], [1, 512]])
                        eng.dma_start(out=grep[hh][ibp][:], in_=src).then_inc(sm["lw"], 16)
                p.on("gpsimd", f)
                lw_g = p.bump("lw", 32)

                u0s = [8064 - 128 * jc + ib * 512 for jc in range(NJC)]
                for jc in range(NJC):
                    par = jc % 2
                    u0 = u0s[jc]

                    def f(eng, hp=hp, ib=ib, jc=jc, par=par,
                          wA=lastExp[0][par], wB=lastExp[1][par], ld_hp=ld_hp):
                        if jc < 2:
                            eng.wait_ge(sm["ld"], ld_hp)
                            eng.wait_ge(sm["lw"], lw_id)
                        eng.wait_ge(sm["s"], max(wA, wB))
                        eng.matmul(psS[0][par], kq[0:64, ds(jc * 128, 128)],
                                   qt[0:64, ds(ib * 512, 512)], start=True, stop=True,
                                   tile_position=(0, 0)).then_inc(sm["t"], 1)
                        eng.matmul(psS[1][par], kq[64:128, ds(jc * 128, 128)],
                                   qt[64:128, ds(ib * 512, 512)], start=True, stop=True,
                                   tile_position=(64, 0)).then_inc(sm["t"], 1)
                    p.on("tensor", f)
                    tqk = [p.bump("t"), p.bump("t")]

                    for hh in range(2):
                        def f(eng, hh=hh, par=par, u0=u0, ibp=ibp,
                              wfree=lastSTT[hh][par], lw_g=lw_g, jc=jc, ld_hp=ld_hp):
                            if jc < 2:
                                eng.wait_ge(sm["lw"], lw_g)
                                eng.wait_ge(sm["ld"], ld_hp)
                            if wfree:
                                eng.wait_ge(sm["v"], wfree)
                            eng.tensor_tensor(bt[hh][par][:], shd[hh][:, ds(u0, 512)],
                                              grep[hh][ibp][:],
                                              ALU.mult).then_inc(sm["g"], 1)
                        p.on("gpsimd", f)
                    gbt = [p.bump("g") - 1, p.bump("g") - 0]
                    gbt = [p.val("g") - 1, p.val("g")]

                    for hh in range(2):
                        def f(eng, hh=hh, par=par, tq=tqk[hh], gb=gbt[hh]):
                            eng.wait_ge(sm["t"], tq)
                            eng.wait_ge(sm["g"], gb)
                            eng.scalar_tensor_tensor(psS[hh][par], bt[hh][par][:], 1.0,
                                                     psS[hh][par], ALU.mult,
                                                     ALU.add).then_inc(sm["v"], 1)
                        p.on("vector", f)
                    vstt = [p.bump("v"), p.bump("v")]
                    lastSTT[0][par] = vstt[0]
                    lastSTT[1][par] = vstt[1]

                    for hh in range(2):
                        def f(eng, hh=hh, par=par, vs=vstt[hh], wpt=lastAV[hh][par]):
                            eng.wait_ge(sm["v"], vs)
                            if wpt:
                                eng.wait_ge(sm["t"], wpt)
                            eng.activation(pt[hh][par][:], psS[hh][par], AF.Exp,
                                           bias=0.0, scale=1.0).then_inc(sm["s"], 1)
                        p.on("scalar", f)
                    sexp = [p.bump("s"), p.bump("s")]
                    lastExp[0][par] = sexp[0]
                    lastExp[1][par] = sexp[1]

                    for hh in range(2):
                        def f(eng, hh=hh, par=par, jc=jc, se=sexp[hh],
                              wO=lastOEvac[hh]):
                            eng.wait_ge(sm["s"], se)
                            if jc == 0:
                                eng.wait_ge(sm["v"], wO)
                            eng.matmul(psO[hh], vt[hh][:, ds(jc * 65, 65)],
                                       pt[hh][par][:], start=(jc == 0),
                                       stop=(jc == NJC - 1)).then_inc(sm["t"], 1)
                        p.on("tensor", f)
                    tav = [p.bump("t"), p.bump("t")]
                    lastAV[0][par] = tav[0]
                    lastAV[1][par] = tav[1]

                vrec = p.fenced("vector", "v", [
                    lambda eng: eng.reciprocal(rsb[0][:], psO[0][64:65, :]),
                    lambda eng: eng.reciprocal(rsb[1][:], psO[1][64:65, :]),
                ], pre_waits=[("t", tav[1])])

                def f(eng, vrec=vrec, wR=max(lastREvac)):
                    eng.wait_ge(sm["v"], vrec)
                    eng.matmul(psR[0], onesf[:], rsb[0][:], start=True, stop=True).then_inc(sm["t"], 1)
                    eng.matmul(psR[1], onesf[:], rsb[1][:], start=True, stop=True).then_inc(sm["t"], 1)
                p.on("tensor", f)
                trep = [p.bump("t"), p.bump("t")]

                vev = p.fenced("vector", "v", [
                    lambda eng: eng.tensor_copy(rrep[0][:], psR[0]),
                    lambda eng: eng.tensor_copy(rrep[1][:], psR[1]),
                    lambda eng: eng.tensor_tensor(attn_t[0:64, :], psO[0][0:64, :],
                                                  rrep[0][:], ALU.mult),
                    lambda eng: eng.tensor_tensor(attn_t[64:128, :], psO[1][0:64, :],
                                                  rrep[1][:], ALU.mult),
                ], pre_waits=[("t", trep[1]), ("st", lastAttnSt)])
                lastOEvac = [vev, vev]
                lastREvac = [vev, vev]

                def f(eng, vev=vev, hp=hp, ib=ib):
                    eng.wait_ge(sm["v"], vev)
                    eng.dma_start(out=attnfm[ds(hp * 128, 128), ds(ib * 512, 512)],
                                  in_=attn_t[:]).then_inc(sm["st"], 16)
                p.on("sync", f)
                lastAttnSt = p.bump("st", 16)

        st_flash = p.val("st")

        afull = kq[:, 0:4096]
        x1t4 = [e(nc.sbuf_tensor(f"x1t4_{i}", [128, 1024], F32)) for i in range(4)]

        wot_free = [0, 0]
        woev_free = [0, 0]
        afull_free = 0
        atok_free = [0, 0, 0, 0]
        x1t_free = [0, 0, 0, 0]
        x2t_free = [0, 0]
        st_x2 = 0
        for n in range(8):
            def f(eng, n=n, wld=afull_free, st_flash=st_flash):
                if n == 0:
                    eng.wait_ge(sm["st"], st_flash)
                if wld:
                    eng.wait_ge(sm["t"], wld)
                for k in range(8):
                    eng.dma_start(out=afull[:, ds(k * 512, 512)],
                                  in_=attnfm[ds(k * 128, 128), ds(n * 512, 512)]).then_inc(sm["ld"], 16)
                for tt in range(4):
                    pass
            p.on("sync", f)
            ld_af = p.bump("ld", 16 * 8)

            def f(eng, n=n, wv=[x1t_free[i] for i in range(4)]):
                for tt in range(4):
                    if wv[tt]:
                        eng.wait_ge(sm["v"], wv[tt])
                    eng.dma_start(out=x1t4[tt][:],
                                  in_=x1[ds(n * 512 + tt * 128, 128), :]).then_inc(sm["ld"], 16)
            p.on("sync", f)
            ld_x1 = p.bump("ld", 16 * 4)

            copies_start = p.val("v")
            for fo in range(8):
                wslot = fo % 2

                def f(eng, n=n, fo=fo, wslot=wslot, wv=wot_free[wslot]):
                    if wv:
                        eng.wait_ge(sm["t"], wv)
                    eng.dma_start(
                        out=wot[wslot][:].rearrange("p (k f) -> p k f", f=128),
                        in_=woT[:, ds(fo * 128, 128)].rearrange("(k p) f -> p k f", p=128),
                    ).then_inc(sm["lw"], 16)
                p.on("gpsimd", f)
                lww = p.bump("lw", 16)

                sfree = p.val("s") - 3
                def f(eng, n=n, fo=fo, wslot=wslot, lww=lww, ld_af=ld_af, sfree=sfree):
                    eng.wait_ge(sm["lw"], lww)
                    if fo == 0:
                        eng.wait_ge(sm["ld"], ld_af)
                    if fo >= 4 or n > 0:
                        eng.wait_ge(sm["s"], sfree)
                    for k in range(8):
                        inst = eng.matmul(psW[fo % 4], wot[wslot][:, ds(k * 128, 128)],
                                          afull[:, ds(k * 512, 512)],
                                          start=(k == 0), stop=(k == 7))
                    inst.then_inc(sm["t"], 1)
                p.on("tensor", f)
                tmm = p.bump("t")
                wot_free[wslot] = tmm
                if fo == 7:
                    afull_free = tmm

                def f(eng, tmm=tmm, fo=fo, wev=woev_free[fo % 2]):
                    eng.wait_ge(sm["t"], tmm)
                    if wev:
                        eng.wait_ge(sm["t"], wev)
                    eng.activation(wo_ev[fo % 2][:], psW[fo % 4], AF.Identity,
                                   bias=0.0, scale=1.0).then_inc(sm["s"], 1)
                p.on("scalar", f)
                sev = p.bump("s")

                vtr_base = p.val("v")
                for tt in range(4):
                    def f(eng, sev=sev, fo=fo, tt=tt, vtr_base=vtr_base):
                        if tt == 0:
                            eng.wait_ge(sm["s"], sev)
                            eng.wait_ge(sm["v"], vtr_base)
                        if tt >= 2:
                            eng.wait_ge(sm["v"], vtr_base + tt - 1)
                        eng.transpose(psT2[tt % 2][:], wo_ev[fo % 2][:, ds(tt * 128, 128)],
                                      ident[:]).then_inc(sm["t"], 1)
                    p.on("tensor", f)
                    ttr = p.bump("t")
                    if tt == 3:
                        woev_free[fo % 2] = ttr

                    def f(eng, ttr=ttr, fo=fo, tt=tt, wv=atok_free[tt]):
                        eng.wait_ge(sm["t"], ttr)
                        if fo == 0 and wv:
                            eng.wait_ge(sm["v"], wv)
                        eng.tensor_copy(atok[tt][:, ds(fo * 128, 128)],
                                        psT2[tt % 2][:]).then_inc(sm["v"], 1)
                    p.on("vector", f)
                    p.bump("v")
            v_copies = p.val("v")

            for tt in range(4):
                par = tt % 2
                vn = p.fenced("vector", "v", [
                    lambda eng, tt=tt, par=par: eng.tensor_tensor(
                        x2t[par][:], atok[tt][:], borep[:], ALU.add),
                    lambda eng, tt=tt, par=par: eng.tensor_add(
                        x2t[par][:], x2t[par][:], x1t4[tt][:]),
                ], pre_waits=[("v", v_copies), ("ld", ld_x1), ("st", x2t_free[par])])
                atok_free[tt] = vn
                x1t_free[tt] = vn

                def f(eng, vn=vn, n=n, tt=tt, par=par):
                    eng.wait_ge(sm["v"], vn)
                    eng.dma_start(out=x2_o[ds(n * 512 + tt * 128, 128), :],
                                  in_=x2t[par][:]).then_inc(sm["st"], 16)
                p.on("sync", f)
                x2t_free[par] = p.bump("st", 16)
        st_wo = p.val("st")

        em = LnRouterEmitter(p, sm, nc, ctx, "c", psT=psT2, psL=psL,
                             xt=[x1t4[0][:], x1t4[1][:]],
                             lnb=[atok[0][:], atok[1][:]],
                             fmb=[atok[2][:], atok[3][:]],
                             sq=afull[:, 0:1024])

        def f(eng, st_wo=st_wo):
            eng.wait_ge(sm["st"], st_wo)
        p.on("sync", f)
        def f(eng, st_wo=st_wo, lw_id=lw_id):
            eng.wait_ge(sm["st"], st_wo)
            eng.wait_ge(sm["lw"], lw_id)
        p.on("tensor", f)
        def f(eng, st_wo=st_wo):
            eng.wait_ge(sm["st"], st_wo)
        p.on("vector", f)
        def f(eng, st_wo=st_wo):
            eng.wait_ge(sm["st"], st_wo)
        p.on("scalar", f)
        for i in range(NT):
            em.tile(i, x2_o, ln3_o, sm2_o, wr_sb, brrep, eps_sb, ident)

        def f(eng, nst=p.val("st")):
            eng.wait_ge(sm["st"], nst)
        p.on("sync", f)

        p.replay(blk)
    return nc
def _shard_x(x):
    out = []
    for c in range(NCORES):
        b, h = c // 2, c % 2
        out.append(np.ascontiguousarray(x[b, h * T:(h + 1) * T, :]))
    return out


def _prep_B_host(inputs):
    hp = {}
    wup = np.asarray(inputs["L0_ffn_W_up"], np.float32)
    hp["wupP"] = np.ascontiguousarray(np.pad(wup, [(0, 0), (0, 0), (0, FPAD - 2730)]))
    bup = np.pad(np.asarray(inputs["L0_ffn_b_up"], np.float32), [(0, 0), (0, FPAD - 2730)])
    hp["bupP"] = np.ascontiguousarray(bup.reshape(4, 22, 128).transpose(2, 0, 1).reshape(128, 88))
    wdn = np.asarray(inputs["L0_ffn_W_down"], np.float32)
    hp["wdnP"] = np.ascontiguousarray(np.pad(wdn, [(0, 0), (0, FPAD - 2730), (0, 0)]))
    bdn = np.asarray(inputs["L0_ffn_b_down"], np.float32)
    hp["bdnP"] = np.ascontiguousarray(bdn.reshape(8, 8, 128).transpose(2, 0, 1).reshape(128, 64))
    for nm, key in [("wqT", "L0_attn_Wq"), ("wkT", "L0_attn_Wk"), ("wvT", "L0_attn_Wv")]:
        hp[nm] = np.ascontiguousarray(np.asarray(inputs[key], np.float32).T)
    for nm, key in [("bq_r", "L0_attn_bq"), ("bk_r", "L0_attn_bk"), ("bv_r", "L0_attn_bv")]:
        hp[nm] = np.ascontiguousarray(
            np.asarray(inputs[key], np.float32).reshape(8, 128).T)
    hp["gruW"] = np.ascontiguousarray(np.asarray(inputs["L0_attn_gru_W"], np.float32))
    hp["gruB"] = np.ascontiguousarray(np.asarray(inputs["L0_attn_gru_b"], np.float32).reshape(8, 1))
    hp["gruC"] = np.ascontiguousarray(np.asarray(inputs["L0_attn_gru_const"], np.float32).reshape(1, 16))
    hp["g48a_i"] = np.array([[1], [1], [1], [1], [0], [0], [0], [0]], np.float32)
    hp["g48b_i"] = np.array([[0], [0], [0], [0], [1], [1], [1], [1]], np.float32)
    M = np.zeros((128, 128), np.float32)
    for g in range(8):
        M[g * 16:(g + 1) * 16, g * 16:(g + 1) * 16] = 1.0
    hp["grpsum"] = M
    Ssel = np.zeros((128, 8), np.float32)
    for ee in range(8):
        Ssel[ee * 16, ee] = 1.0
    hp["thrsel"] = Ssel
    hp["eoff_i"] = (np.arange(8, dtype=np.float32) * C_CAP).reshape(8, 1)
    hp["bnd_i"] = ((np.arange(8, dtype=np.float32) + 1) * C_CAP).reshape(8, 1)
    hp["idf32p"] = np.eye(128, dtype=np.float32)
    return hp


def _sm_global(sm_cores):
    out = np.empty((B * S, E), np.float32)
    for c in range(NCORES):
        b, h = c // 2, c % 2
        out[b * S + h * T:(b * S + h * T) + T] = sm_cores[c]
    return out


def kernel(**inputs):
    x0 = np.asarray(inputs["L0_attn_hidden_states"], np.float32)
    consts = _host_consts()
    shards = _shard_x(x0)

    ncA = build_A()
    wr = np.ascontiguousarray(inputs["L0_ffn_Wr"], dtype=np.float32)
    br = np.ascontiguousarray(inputs["L0_ffn_br"], dtype=np.float32).reshape(1, E)
    mapsA = [dict(x=shards[c], wr=wr, br=br, identp=consts["ident"]) for c in range(NCORES)]
    resA = run_bass_kernel_spmd(ncA, mapsA, core_ids=list(range(NCORES)))

    sm_cores = [np.asarray(resA.results[c]["sm_o"]) for c in range(NCORES)]
    ln_cores = [np.asarray(resA.results[c]["ln_o"]) for c in range(NCORES)]
    sm_all = _sm_global(sm_cores)
    sm_resh = np.ascontiguousarray(
        sm_all.reshape(16, 2048, 8).transpose(0, 2, 1).reshape(16, 8, 2048)
    )
    sm_resh = np.empty((128, 2048), np.float32)
    for pp in range(128):
        sm_resh[pp] = sm_all[(pp % 16) * 2048:(pp % 16 + 1) * 2048, pp // 16]

    hp = _prep_B_host(inputs)
    import os
    ncB = build_B(dbg=bool(os.environ.get('KDBG')))
    mapsB = []
    for c in range(NCORES):
        d = dict(hp)
        d["x0"] = shards[c]
        d["ln1"] = ln_cores[c]
        d["sm_resh"] = sm_resh
        d["smT_loc"] = np.ascontiguousarray(sm_cores[c].T)
        d["sm_loc"] = sm_cores[c]
        d["identp"] = consts["ident"]
        mapsB.append(d)
    resB = run_bass_kernel_spmd(ncB, mapsB, core_ids=list(range(NCORES)))

    def as_u16(a):
        a = np.asarray(a)
        if a.dtype == np.uint16:
            return a
        return a.view(np.uint16) if a.dtype == bf16 else a.astype(bf16).view(np.uint16)

    kTs = [as_u16(resB.results[c]["kT_o"]) for c in range(NCORES)]
    vTs = [as_u16(resB.results[c]["vT_o"]) for c in range(NCORES)]
    qTs = [as_u16(resB.results[c]["qT_o"]) for c in range(NCORES)]
    gates = [np.asarray(resB.results[c]["gate_o"], np.float32) for c in range(NCORES)]
    x1s = [np.asarray(resB.results[c]["x1_o"], np.float32) for c in range(NCORES)]

    ones_bf16_u16 = np.full((NH, 64, 128, 1), 0x3F80, np.uint16)
    krevs, vtoks = {}, {}
    for bb in range(B):
        kfull = np.concatenate([kTs[2 * bb], kTs[2 * bb + 1]], axis=1)
        krevs[bb] = np.ascontiguousarray(
            kfull.reshape(D_MODEL, 64, 128)[:, :, ::-1].reshape(D_MODEL, S))
        vfull = np.concatenate([vTs[2 * bb], vTs[2 * bb + 1]], axis=1)
        vh = vfull.reshape(NH, 64, S).transpose(0, 2, 1)
        vrev = vh.reshape(NH, 64, 128, 64)[:, :, ::-1, :]
        vtoks[bb] = np.ascontiguousarray(
            np.concatenate([vrev, ones_bf16_u16], axis=3).reshape(NH, 64 * 128 * 65))

    btab = _bucket_table()
    re_in = np.asarray(inputs["L0_attn_rel_embed"], np.float32)
    pb = re_in[btab]
    pbrev = pb[::-1]
    woT = np.ascontiguousarray(np.asarray(inputs["L0_attn_Wo"], np.float32).T)
    bo_row = np.ascontiguousarray(np.asarray(inputs["L0_attn_bo"], np.float32).reshape(1, D_MODEL))
    onesf = np.ones((1, 64), np.float32)

    import os as _os
    ncC = build_C(dbg=bool(_os.environ.get('KDBG')))
    mapsC = []
    for c in range(NCORES):
        bb, hh = c // 2, c % 2
        q0 = hh * T
        pbc = np.zeros((NH, 12416), np.float32)
        avail = min(12416, 16383 - q0)
        pbc[:, 0:avail] = pbrev[q0:q0 + avail, :].T
        d = dict(
            x1=x1s[c], qt8=qTs[c], krevT=krevs[bb], vtok=vtoks[bb],
            gate_i=np.ascontiguousarray(gates[c]),
            pbrevc=np.ascontiguousarray(pbc), woT=woT, bo_i=bo_row,
            wr=wr, br=br, identp=consts["ident"], onesf_i=onesf,
        )
        mapsC.append(d)
    resC = run_bass_kernel_spmd(ncC, mapsC, core_ids=list(range(NCORES)))

    sm2_cores = [np.asarray(resC.results[c]["sm2_o"], np.float32) for c in range(NCORES)]
    ln3_cores = [np.asarray(resC.results[c]["ln3_o"]) for c in range(NCORES)]
    x2_cores = [np.asarray(resC.results[c]["x2_o"], np.float32) for c in range(NCORES)]
    sm2_all = _sm_global(sm2_cores)
    sm2_resh = np.empty((128, 2048), np.float32)
    for pp in range(128):
        sm2_resh[pp] = sm2_all[(pp % 16) * 2048:(pp % 16 + 1) * 2048, pp // 16]
    mapsD = []
    for c in range(NCORES):
        d = dict(mapsB[c])
        d["x0"] = np.ascontiguousarray(x2_cores[c])
        d["ln1"] = np.ascontiguousarray(ln3_cores[c])
        d["sm_resh"] = sm2_resh
        d["smT_loc"] = np.ascontiguousarray(sm2_cores[c].T)
        d["sm_loc"] = sm2_cores[c]
        mapsD.append(d)
    resD = run_bass_kernel_spmd(ncB, mapsD, core_ids=list(range(NCORES)))

    out = np.empty((B, S, D_MODEL), np.float32)
    for c in range(NCORES):
        bb, hh = c // 2, c % 2
        out[bb, hh * T:(hh + 1) * T, :] = np.asarray(resD.results[c]["x1_o"], np.float32)
    if _os.environ.get("KDBG"):
        kernel._dbg = dict(resA=resA, resB=resB, resC=resC, resD=resD, mapsC=mapsC)
    return out


# version 03
# speedup vs baseline: 1.0429x; 1.0429x over previous
import math
from contextlib import ExitStack

import numpy as np
import ml_dtypes
import concourse.bass as bass
import concourse.mybir as mybir
from concourse.bass_utils import run_bass_kernel_spmd

F32 = mybir.dt.float32
BF16 = mybir.dt.bfloat16
I32 = mybir.dt.int32
U32 = mybir.dt.uint32
AF = mybir.ActivationFunctionType
ALU = mybir.AluOpType
ds = bass.ds
AX = mybir.AxisListType

B, S, D_MODEL = 4, 8192, 1024
NCORES = 8
T = 4096
NT = T // 128
E = 8
KSEL = 4096
C_CAP = 640
FPAD = 2816
NH, HD = 16, 64
NB, MAXD = 320, 800

bf16 = ml_dtypes.bfloat16


class Prog:

    ENGS = ("sync", "gpsimd", "vector", "scalar", "tensor")

    def __init__(self, sm):
        self.streams = {e: [] for e in self.ENGS}
        self.counts = {k: 0 for k in sm}
        self.sm = sm

    def on(self, eng, fn):
        self.streams[eng].append(fn)

    def bump(self, sem, n=1):
        self.counts[sem] += n
        return self.counts[sem]

    def val(self, sem):
        return self.counts[sem]

    def fenced(self, engname, semname, ops, pre_waits=()):
        base = self.val(semname)
        smd = self.sm

        def fn(eng, ops=tuple(ops), base=base, pre=tuple(pre_waits), semname=semname):
            for ws, wv in pre:
                eng.wait_ge(smd[ws], wv)
            if base:
                eng.wait_ge(smd[semname], base)
            for j, op in enumerate(ops):
                if j > 0:
                    eng.wait_ge(smd[semname], base + j)
                op(eng).then_inc(smd[semname], 1)
        self.on(engname, fn)
        return self.bump(semname, len(ops))

    def replay(self, blk):
        for name in self.ENGS:
            stream = self.streams[name]
            if not stream:
                continue
            dec = getattr(blk, name)

            def make(stream):
                def _body(eng):
                    for fn in stream:
                        fn(eng)
                return _body
            dec(make(stream))


def sem_dict(ctx, nc, names):
    return {n: ctx.enter_context(nc.semaphore(n)) for n in names}


def _host_consts():
    t = {}
    t["ident"] = np.eye(128, dtype=bf16)
    return t


def _bucket_table():
    nb = NB // 2
    rel = np.arange(-(S - 1), S)
    buckets = (rel > 0).astype(np.int64) * nb
    rel_abs = np.abs(rel)
    mx = nb // 2
    small = rel_abs < mx
    large = np.log(np.maximum(rel_abs, 1).astype(np.float32) / mx) / math.log(MAXD / mx) * (nb - mx)
    large = (mx + large).astype(np.int64)
    large = np.minimum(large, nb - 1)
    return buckets + np.where(small, rel_abs, large)


class LnRouterEmitter:

    def __init__(self, p, sm, nc, ctx, tag, psT=None, psL=None, xt=None, lnb=None,
                 fmb=None, sq=None):
        e = ctx.enter_context
        self.p, self.sm = p, sm
        self.xt = xt or [e(nc.sbuf_tensor(f"{tag}xt{i}", [128, 1024], F32)) for i in range(2)]
        self.lnb = lnb or [e(nc.sbuf_tensor(f"{tag}lnb{i}", [128, 1024], BF16)) for i in range(2)]
        self.fmb = fmb or [e(nc.sbuf_tensor(f"{tag}fmb{i}", [128, 1024], BF16)) for i in range(2)]
        self.sum1 = e(nc.sbuf_tensor(f"{tag}sum1", [128, 1], F32))
        self.mean = e(nc.sbuf_tensor(f"{tag}mean", [128, 1], F32))
        self.sq = sq if sq is not None else e(nc.sbuf_tensor(f"{tag}sq", [128, 1024], BF16))
        self.ssq = e(nc.sbuf_tensor(f"{tag}ssq", [128, 1], F32))
        self.var = e(nc.sbuf_tensor(f"{tag}var", [128, 1], F32))
        self.rstd = e(nc.sbuf_tensor(f"{tag}rstd", [128, 1], F32))
        self.nmr = e(nc.sbuf_tensor(f"{tag}nmr", [128, 1], F32))
        self.smt = e(nc.sbuf_tensor(f"{tag}smt", [128, 8], F32))
        self.mx1 = e(nc.sbuf_tensor(f"{tag}mx1", [128, 1], F32))
        self.se1 = e(nc.sbuf_tensor(f"{tag}se1", [128, 1], F32))
        if psT is None:
            self.psT = [e(nc.psum_tensor(f"{tag}psT{i}", [128, 128], BF16)) for i in range(2)]
            self.psL = e(nc.psum_tensor(f"{tag}psL", [128, 8], F32))
        else:
            self.psT = psT
            self.psL = psL
        self.xt_free = [0, 0]
        self.lnb_free = [0, 0]
        self.fmb_free = [0, 0]

    def tile(self, i, x_dram, ln_out, sm_out, wr_sb, brrep, eps_sb, ident,
             extra_ln_sink=None):
        p, sm = self.p, self.sm
        par = i % 2
        xti, lnbi, fmbi = self.xt[par], self.lnb[par], self.fmb[par]

        def f(eng, i=i, xti=xti, wv=self.xt_free[par]):
            if wv:
                eng.wait_ge(sm["v"], wv)
            eng.dma_start(out=xti[:], in_=x_dram[ds(i * 128, 128), :]).then_inc(sm["ld"], 16)
        p.on("sync", f)
        lv = p.bump("ld", 16)

        vn = p.fenced("vector", "v", [
            lambda eng, xti=xti: eng.reduce_sum(self.sum1[:], xti[:], axis=AX.X),
        ], pre_waits=[("ld", lv)])

        sn = p.fenced("scalar", "s", [
            lambda eng, xti=xti: eng.activation(self.sq[:], xti[:], AF.Square,
                                                bias=0.0, scale=1.0,
                                                accum_out=self.ssq[:]),
        ], pre_waits=[("ld", lv), ("v", p.val("v"))])

        vn = p.fenced("vector", "v", [
            lambda eng: eng.tensor_scalar(self.mean[:], self.sum1[:], 1.0 / 1024,
                                          None, ALU.mult),
            lambda eng: eng.tensor_mul(self.nmr[:], self.mean[:], self.mean[:]),
            lambda eng: eng.scalar_tensor_tensor(self.var[:], self.ssq[:], 1.0 / 1024,
                                                 self.nmr[:], ALU.mult, ALU.subtract),
        ], pre_waits=[("v", vn), ("s", sn)])

        sn = p.fenced("scalar", "s", [
            lambda eng: eng.activation(self.rstd[:], self.var[:], AF.Sqrt,
                                       bias=eps_sb[:], scale=1.0),
        ], pre_waits=[("v", vn)])

        vn = p.fenced("vector", "v", [
            lambda eng: eng.reciprocal(self.rstd[:], self.rstd[:]),
            lambda eng: eng.tensor_mul(self.nmr[:], self.mean[:], self.rstd[:]),
            lambda eng: eng.tensor_scalar(self.nmr[:], self.nmr[:], -1.0, None, ALU.mult),
            lambda eng, xti=xti, lnbi=lnbi: eng.tensor_scalar(
                lnbi[:], xti[:], self.rstd[:], self.nmr[:], ALU.mult, ALU.add),
        ], pre_waits=[("s", sn), ("t", self.lnb_free[par])])
        self.xt_free[par] = vn

        def f(eng, vn=vn, lnbi=lnbi, i=i):
            eng.wait_ge(sm["v"], vn)
            eng.dma_start(out=ln_out[ds(i * 128, 128), :], in_=lnbi[:]).then_inc(sm["st"], 16)
        p.on("sync", f)
        p.bump("st", 16)
        if extra_ln_sink is not None:
            extra_ln_sink(lnbi, vn)

        tbase = p.val("t")
        vbase0 = p.val("v")

        def f(eng, vn=vn, lnbi=lnbi, tbase=tbase, vbase0=vbase0, ident=ident):
            eng.wait_ge(sm["v"], vn)
            for c in range(8):
                if c >= 2:
                    eng.wait_ge(sm["v"], vbase0 + c - 1)
                eng.transpose(self.psT[c % 2][:], lnbi[:, ds(c * 128, 128)],
                              ident[:]).then_inc(sm["t"], 1)
        p.on("tensor", f)
        p.bump("t", 8)

        def f(eng, tbase=tbase, fmbi=fmbi):
            for c in range(8):
                eng.wait_ge(sm["t"], tbase + c + 1)
                eng.tensor_copy(fmbi[:, ds(c * 128, 128)], self.psT[c % 2][:]).then_inc(sm["v"], 1)
        p.on("vector", f)
        p.bump("v", 8)
        vafter = p.val("v")

        def f(eng, vafter=vafter, fmbi=fmbi, wr_sb=wr_sb):
            eng.wait_ge(sm["v"], vafter)
            for c in range(8):
                inst = eng.matmul(self.psL[:], fmbi[:, ds(c * 128, 128)],
                                  wr_sb[:, ds(c * 8, 8)], start=(c == 0), stop=(c == 7))
            inst.then_inc(sm["t"], 1)
        p.on("tensor", f)
        tn = p.bump("t")
        self.lnb_free[par] = tn
        self.fmb_free[par] = tn

        vn = p.fenced("vector", "v", [
            lambda eng: eng.tensor_add(self.smt[:], self.psL[:], brrep[:]),
            lambda eng: eng.reduce_max(self.mx1[:], self.smt[:], axis=AX.X, negate=True),
        ], pre_waits=[("t", tn)])

        def f(eng, vn=vn):
            eng.wait_ge(sm["v"], vn)
            eng.activation(self.smt[:], self.smt[:], AF.Exp, bias=self.mx1[:], scale=1.0,
                           accum_out=self.se1[:]).then_inc(sm["s"], 1)
        p.on("scalar", f)
        sn = p.bump("s")

        vn = p.fenced("vector", "v", [
            lambda eng: eng.reciprocal(self.se1[:], self.se1[:]),
            lambda eng: eng.tensor_scalar(self.smt[:], self.smt[:], self.se1[:],
                                          None, ALU.mult),
        ], pre_waits=[("s", sn)])

        def f(eng, vn=vn, i=i):
            eng.wait_ge(sm["v"], vn)
            eng.dma_start(out=sm_out[ds(i * 128, 128), :], in_=self.smt[:]).then_inc(sm["st"], 16)
        p.on("sync", f)
        p.bump("st", 16)


def build_A():
    nc = bass.Bass("TRN2", target_bir_lowering=False, debug=False)
    x = nc.dram_tensor("x", [T, D_MODEL], F32, kind="ExternalInput").ap()
    wr = nc.dram_tensor("wr", [D_MODEL, E], F32, kind="ExternalInput").ap()
    br = nc.dram_tensor("br", [1, E], F32, kind="ExternalInput").ap()
    identp = nc.dram_tensor("identp", [128, 128], BF16, kind="ExternalInput").ap()
    ln_o = nc.dram_tensor("ln_o", [T, D_MODEL], BF16, kind="ExternalOutput").ap()
    sm_o = nc.dram_tensor("sm_o", [T, E], F32, kind="ExternalOutput").ap()

    with ExitStack() as ctx:
        e = ctx.enter_context
        wr_sb = e(nc.sbuf_tensor("wr_sb", [128, 64], BF16))
        brrep = e(nc.sbuf_tensor("brrep", [128, 8], F32))
        brrow = e(nc.sbuf_tensor("brrow", [1, 8], F32))
        ones1 = e(nc.sbuf_tensor("ones1", [1, 128], F32))
        eps_sb = e(nc.sbuf_tensor("eps_sb", [128, 1], F32))
        ident = e(nc.sbuf_tensor("ident", [128, 128], BF16))
        psB = e(nc.psum_tensor("psB", [128, 8], F32))

        sm = sem_dict(ctx, nc, ["ld", "v", "t", "s", "st"])
        blk = e(nc.Block())
        p = Prog(sm)

        def f(eng):
            eng.dma_start(out=brrow[:], in_=br).then_inc(sm["ld"], 16)
            eng.dma_start(out=ident[:], in_=identp).then_inc(sm["ld"], 16)
            eng.dma_start(out=wr_sb[:].rearrange("p (c e) -> p c e", e=8),
                          in_=wr.rearrange("(c p) e -> p c e", p=128)).then_inc(sm["ld"], 16)
        p.on("gpsimd", f)
        ldn = p.bump("ld", 48)

        def f(eng):
            eng.memset(eps_sb[:], 1e-5)
            eng.memset(ones1[:], 1.0).then_inc(sm["v"], 1)
        p.on("vector", f)
        vn = p.bump("v")

        def f(eng, ldn=ldn, vn=vn):
            eng.wait_ge(sm["ld"], ldn)
            eng.wait_ge(sm["v"], vn)
            eng.matmul(psB[:], ones1[:], brrow[:], start=True, stop=True).then_inc(sm["t"], 1)
        p.on("tensor", f)
        tn = p.bump("t")

        def f(eng, tn=tn):
            eng.wait_ge(sm["t"], tn)
            eng.tensor_copy(brrep[:], psB[:]).then_inc(sm["v"], 1)
        p.on("vector", f)
        p.bump("v")

        em = LnRouterEmitter(p, sm, nc, ctx, "a")
        for i in range(NT):
            em.tile(i, x, ln_o, sm_o, wr_sb, brrep, eps_sb, ident)

        def f(eng, n=p.val("st")):
            eng.wait_ge(sm["st"], n)
        p.on("sync", f)

        p.replay(blk)
    return nc



def build_B(dbg=False, mode="B"):
    nc = bass.Bass("TRN2", target_bir_lowering=False, debug=False)
    x0 = nc.dram_tensor("x0", [T, D_MODEL], F32, kind="ExternalInput").ap()
    ln1 = nc.dram_tensor("ln1", [T, D_MODEL], BF16, kind="ExternalInput").ap()
    sm_resh = nc.dram_tensor("sm_resh", [128, 2048], F32, kind="ExternalInput").ap()
    smT_loc = nc.dram_tensor("smT_loc", [E, T], F32, kind="ExternalInput").ap()
    sm_loc = nc.dram_tensor("sm_loc", [T, E], F32, kind="ExternalInput").ap()
    wupP = nc.dram_tensor("wupP", [4, D_MODEL, FPAD], F32, kind="ExternalInput").ap()
    bupP = nc.dram_tensor("bupP", [128, 88], F32, kind="ExternalInput").ap()
    wdnP = nc.dram_tensor("wdnP", [E, FPAD, D_MODEL], F32, kind="ExternalInput").ap()
    bdnP = nc.dram_tensor("bdnP", [128, 64], F32, kind="ExternalInput").ap()
    wqT = nc.dram_tensor("wqT", [D_MODEL, D_MODEL], F32, kind="ExternalInput").ap()
    wkT = nc.dram_tensor("wkT", [D_MODEL, D_MODEL], F32, kind="ExternalInput").ap()
    wvT = nc.dram_tensor("wvT", [D_MODEL, D_MODEL], F32, kind="ExternalInput").ap()
    bq_r = nc.dram_tensor("bq_r", [128, 8], F32, kind="ExternalInput").ap()
    bk_r = nc.dram_tensor("bk_r", [128, 8], F32, kind="ExternalInput").ap()
    bv_r = nc.dram_tensor("bv_r", [128, 8], F32, kind="ExternalInput").ap()
    gruW = nc.dram_tensor("gruW", [8, 64], F32, kind="ExternalInput").ap()
    gruB = nc.dram_tensor("gruB", [8, 1], F32, kind="ExternalInput").ap()
    gruC = nc.dram_tensor("gruC", [1, 16], F32, kind="ExternalInput").ap()
    g48a_i = nc.dram_tensor("g48a_i", [8, 1], F32, kind="ExternalInput").ap()
    g48b_i = nc.dram_tensor("g48b_i", [8, 1], F32, kind="ExternalInput").ap()
    grpsum = nc.dram_tensor("grpsum", [128, 128], F32, kind="ExternalInput").ap()
    thrsel = nc.dram_tensor("thrsel", [128, 8], F32, kind="ExternalInput").ap()
    eoff_i = nc.dram_tensor("eoff_i", [8, 1], F32, kind="ExternalInput").ap()
    bnd_i = nc.dram_tensor("bnd_i", [8, 1], F32, kind="ExternalInput").ap()
    identp = nc.dram_tensor("identp", [128, 128], BF16, kind="ExternalInput").ap()
    idf32p = nc.dram_tensor("idf32p", [128, 128], F32, kind="ExternalInput").ap()

    x1_o = nc.dram_tensor("x1_o", [T, D_MODEL], F32, kind="ExternalOutput").ap()
    qT_o = nc.dram_tensor("qT_o", [D_MODEL, T], BF16, kind="ExternalOutput").ap()
    kT_o = nc.dram_tensor("kT_o", [D_MODEL, T], BF16, kind="ExternalOutput").ap()
    vT_o = nc.dram_tensor("vT_o", [D_MODEL, T], BF16, kind="ExternalOutput").ap()
    gate_o = nc.dram_tensor("gate_o", [NH, T], F32, kind="ExternalOutput").ap()
    thr_o = nc.dram_tensor("thr_o", [8, 1], F32, kind="ExternalOutput").ap()

    kind_dbg = "ExternalOutput" if dbg else "Internal"
    gath = nc.dram_tensor("gath", [E * C_CAP, 1040], BF16, kind=kind_dbg).ap()
    posS_o = nc.dram_tensor("posS_o", [128, 8 * NT], U32, kind=kind_dbg).ap()
    posG_o = nc.dram_tensor("posG_o", [128, 8 * NT], U32, kind=kind_dbg).ap()
    outb = nc.dram_tensor("outb", [E * C_CAP + 128, D_MODEL], F32, kind=kind_dbg).ap()

    with ExitStack() as ctx:
        e = ctx.enter_context
        lo = e(nc.sbuf_tensor("lo", [128, 1], F32))
        hi = e(nc.sbuf_tensor("hi", [128, 1], F32))
        mid = e(nc.sbuf_tensor("mid", [128, 1], F32))
        cnt = e(nc.sbuf_tensor("cnt", [128, 1], F32))
        cond = e(nc.sbuf_tensor("cond", [128, 1], F32))
        ncond = e(nc.sbuf_tensor("ncond", [128, 1], F32))
        tmp1 = e(nc.sbuf_tensor("tmp1", [128, 1], F32))
        tmp2 = e(nc.sbuf_tensor("tmp2", [128, 1], F32))
        grp_sb = e(nc.sbuf_tensor("grp_sb", [128, 128], F32))
        tsel_sb = e(nc.sbuf_tensor("tsel_sb", [128, 8], F32))
        thr8 = e(nc.sbuf_tensor("thr8", [8, 1], F32))
        eoff_sb = e(nc.sbuf_tensor("eoff_sb", [8, 1], F32))
        bnd_sb = e(nc.sbuf_tensor("bnd_sb", [8, 1], F32))
        posS = e(nc.sbuf_tensor("posS", [128, 8 * NT], U32))
        posG = e(nc.sbuf_tensor("posG", [128, 8 * NT], U32))
        ident = e(nc.sbuf_tensor("ident", [128, 128], BF16))
        idf32 = e(nc.sbuf_tensor("idf32", [128, 128], F32))
        stg = [e(nc.sbuf_tensor(f"stg{i}", [128, 1040], BF16)) for i in range(2)]

        arena = e(nc.sbuf_tensor("arena", [128, 90112], mybir.dt.uint8))
        wexp = [arena[:, ds(i * 45056, 45056)].bitcast(BF16) for i in range(2)]
        ln2fm = arena[:, 0:65536].bitcast(BF16)
        xq = arena[:, 65536:81920].bitcast(F32)
        qkvw = [arena[:, ds(81920 + i * 2048, 2048)].bitcast(BF16) for i in range(2)]
        a4 = arena[:, 0:65536].bitcast(F32)
        smT = a4[0:8, 0:4096]
        maskT = a4[0:8, 4096:8192]
        psc = a4[0:8, 8192:12288]
        pgc = a4[0:8, 12288:16384]

        Xg = e(nc.sbuf_tensor("Xg", [128, 5120], BF16))
        midT = e(nc.sbuf_tensor("midT", [128, 14080], BF16))
        vals = midT[:, 0:4096].bitcast(F32)
        cmpd = Xg[:, 0:2048]
        dnf = e(nc.sbuf_tensor("dnf", [128, 640], F32))
        outT = e(nc.sbuf_tensor("outT", [128, 5120], F32))
        gt = [e(nc.sbuf_tensor(f"gt{j}", [128, 1040], BF16)) for j in range(5)]
        gv = e(nc.sbuf_tensor("gv", [128, 8], F32))
        zrow = e(nc.sbuf_tensor("zrow", [128, 1024], F32))
        bup_sb = e(nc.sbuf_tensor("bup_sb", [128, 88], F32))
        bdn_sb = e(nc.sbuf_tensor("bdn_sb", [128, 64], F32))
        bq_sb = e(nc.sbuf_tensor("bq_sb", [128, 8], F32))
        bk_sb = e(nc.sbuf_tensor("bk_sb", [128, 8], F32))
        bv_sb = e(nc.sbuf_tensor("bv_sb", [128, 8], F32))
        bq8_sb = e(nc.sbuf_tensor("bq8_sb", [128, 8], F32))
        lnt = e(nc.sbuf_tensor("lnt", [128, 1024], BF16))
        qkvo = [e(nc.sbuf_tensor(f"qkvo{i}", [128, 512], BF16)) for i in range(2)]

        gruW_sb = e(nc.sbuf_tensor("gruW_sb", [8, 64], F32))
        gruB_sb = e(nc.sbuf_tensor("gruB_sb", [8, 1], F32))
        gruC_sb = e(nc.sbuf_tensor("gruC_sb", [1, 16], F32))
        g48a_sb = e(nc.sbuf_tensor("g48a_sb", [8, 1], F32))
        g48b_sb = e(nc.sbuf_tensor("g48b_sb", [8, 1], F32))
        wa2 = e(nc.sbuf_tensor("wa2", [128, 1], BF16))
        wb2 = e(nc.sbuf_tensor("wb2", [128, 1], BF16))
        ba_sb = e(nc.sbuf_tensor("ba_sb", [1, 1], F32))
        bb_sb = e(nc.sbuf_tensor("bb_sb", [1, 1], F32))
        sga = e(nc.sbuf_tensor("sga", [1, 512], F32))
        sgb = e(nc.sbuf_tensor("sgb", [1, 512], F32))
        grow = e(nc.sbuf_tensor("grow", [1, 512], F32))

        sum1 = e(nc.sbuf_tensor("sum1", [128, 1], F32))
        mean = e(nc.sbuf_tensor("mean", [128, 1], F32))
        sq = e(nc.sbuf_tensor("sq", [128, 1024], BF16))
        ssq = e(nc.sbuf_tensor("ssq", [128, 1], F32))
        var = e(nc.sbuf_tensor("var", [128, 1], F32))
        rstd = e(nc.sbuf_tensor("rstd", [128, 1], F32))
        nmr = e(nc.sbuf_tensor("nmr", [128, 1], F32))
        eps_sb = e(nc.sbuf_tensor("eps_sb", [128, 1], F32))

        psA = e(nc.psum_tensor("psA", [128, 4096], F32))
        psUP = [psA[:, 0:1024], psA[:, 1024:2048]]
        psT2 = [psA[:, ds(2048 + i * 512, 512)].bitcast(BF16)[:, 0:128] for i in range(2)]
        psF2 = [psA[:, 2048:2176], psA[:, 2560:2688]]
        psC = psA[:, 3072:3073]
        psT8 = psA[0:8, 3074:3075]
        psWa = psA[0:64, 3076:3077]
        psWb = psA[0:64, 3078:3079]
        psBa = psA[0:1, 3080:3081]
        psBb = psA[0:1, 3082:3083]
        psGa = psA[0:1, 3072:3584]
        psGb = psA[0:1, 3584:4096]
        psQ = [psA[:, ds(i * 512, 512)] for i in range(4)]

        sm = sem_dict(ctx, nc, ["ld", "lw", "v", "t", "s", "st", "gq", "ga"])
        blk = e(nc.Block())
        p = Prog(sm)
        REGS = {}

        def bound_reg(eng, key, value):
            if key not in REGS:
                REGS[key] = eng.to_reg(value)
            return REGS[key]

        def f(eng):
            for dst, src in [(vals, sm_resh), (smT, smT_loc), (grp_sb, grpsum),
                             (tsel_sb, thrsel), (eoff_sb, eoff_i), (bnd_sb, bnd_i),
                             (bup_sb, bupP), (bdn_sb, bdnP), (bq_sb, bq_r),
                             (bk_sb, bk_r), (bv_sb, bv_r), (gruW_sb, gruW),
                             (gruB_sb, gruB), (gruC_sb, gruC), (g48a_sb, g48a_i),
                             (g48b_sb, g48b_i), (idf32, idf32p)]:
                eng.dma_start(out=dst[:], in_=src).then_inc(sm["ld"], 16)
        p.on("sync", f)
        ld_pro = p.bump("ld", 16 * 17)

        def f(eng):
            eng.dma_start(out=ident[:], in_=identp).then_inc(sm["lw"], 16)
        p.on("gpsimd", f)
        lw_id = p.bump("lw", 16)

        vz = p.fenced("vector", "v", [
            lambda eng: eng.memset(lo[:], 0.0),
            lambda eng: eng.memset(hi[:], 1.0000001),
            lambda eng: eng.memset(mid[:], 0.50000006),
            lambda eng: eng.memset(zrow[:], 0.0),
            lambda eng: eng.memset(eps_sb[:], 1e-5),
        ])

        def f(eng, vz=vz):
            eng.wait_ge(sm["v"], vz)
            eng.dma_start(out=outb[ds(E * C_CAP, 128), :], in_=zrow[:]).then_inc(sm["st"], 16)
        p.on("sync", f)
        st_zero = p.bump("st", 16)

        NITER = 48
        for it in range(NITER):
            vn = p.fenced("vector", "v", [
                lambda eng: eng.tensor_scalar(cmpd[:], vals[:], mid[:], 0.0,
                                              ALU.is_ge, ALU.add, accum_out=cnt[:]),
            ], pre_waits=[("ld", 16 * 2)] if it == 0 else ())

            def f(eng, vn=vn, first=(it == 0)):
                eng.wait_ge(sm["v"], vn)
                if first:
                    eng.wait_ge(sm["ld"], ld_pro)
                eng.matmul(psC, grp_sb[:], cnt[:], start=True, stop=True).then_inc(sm["t"], 1)
            p.on("tensor", f)
            tn = p.bump("t")

            p.fenced("vector", "v", [
                lambda eng: eng.tensor_scalar(cond[:], psC, 4096.0, None, ALU.is_ge),
                lambda eng: eng.tensor_scalar(ncond[:], cond[:], -1.0, 1.0, ALU.mult, ALU.add),
                lambda eng: eng.tensor_sub(tmp1[:], mid[:], lo[:]),
                lambda eng: eng.tensor_mul(tmp1[:], tmp1[:], cond[:]),
                lambda eng: eng.tensor_add(lo[:], lo[:], tmp1[:]),
                lambda eng: eng.tensor_sub(tmp2[:], mid[:], hi[:]),
                lambda eng: eng.tensor_mul(tmp2[:], tmp2[:], ncond[:]),
                lambda eng: eng.tensor_add(hi[:], hi[:], tmp2[:]),
                lambda eng: eng.tensor_add(mid[:], lo[:], hi[:]),
                lambda eng: eng.tensor_scalar(mid[:], mid[:], 0.5, None, ALU.mult),
            ], pre_waits=[("t", tn)])

        vlast = p.val("v")

        def f(eng, vlast=vlast):
            eng.wait_ge(sm["v"], vlast)
            eng.matmul(psT8, tsel_sb[:], lo[:], start=True, stop=True).then_inc(sm["t"], 1)
        p.on("tensor", f)
        tn = p.bump("t")
        vn = p.fenced("vector", "v", [
            lambda eng: eng.tensor_copy(thr8[:], psT8),
        ], pre_waits=[("t", tn)])

        def f(eng, vn=vn):
            eng.wait_ge(sm["v"], vn)
            eng.dma_start(out=thr_o, in_=thr8[:]).then_inc(sm["st"], 16)
        p.on("sync", f)
        p.bump("st", 16)

        DUMP = float(E * C_CAP)
        BIGF = float(1 << 26)
        vpos_chain = p.fenced("vector", "v", [
            lambda eng: eng.tensor_scalar(maskT, smT, thr8[:], None, ALU.is_ge),
            lambda eng: eng.memset(psc, 0.0),
            lambda eng: eng.tensor_tensor_scan(smT, maskT, psc, 0.0, ALU.add, ALU.add),
            lambda eng: eng.tensor_scalar(psc, smT, eoff_sb[:], -1.0, ALU.add, ALU.add),
            lambda eng: eng.tensor_mul(pgc, psc, maskT),
            lambda eng: eng.scalar_tensor_tensor(pgc, maskT, -DUMP, pgc, ALU.mult, ALU.add),
            lambda eng: eng.tensor_scalar(pgc, pgc, DUMP, None, ALU.add),
            lambda eng: eng.tensor_scalar(smT, pgc, bnd_sb[:], None, ALU.is_ge),
            lambda eng: eng.tensor_scalar(psc, psc, 0.0, None, ALU.add),
            lambda eng: eng.tensor_scalar(maskT, maskT, -1.0, 1.0, ALU.mult, ALU.add),
            lambda eng: eng.scalar_tensor_tensor(psc, maskT, BIGF, psc, ALU.mult, ALU.add),
            lambda eng: eng.tensor_scalar(maskT, smT, 0.0, None, ALU.add),
            lambda eng: eng.tensor_scalar(smT, pgc, -1.0, DUMP, ALU.mult, ALU.add),
            lambda eng: eng.tensor_mul(smT, smT, maskT),
            lambda eng: eng.tensor_add(pgc, pgc, smT),
        ], pre_waits=[("v", vn)])

        vtr_copies = p.val("v")
        for i in range(NT):
            for which, src, dstp in ((0, psc, posS), (1, pgc, posG)):
                q = i * 2 + which
                par = q % 2

                def f(eng, i=i, src=src, par=par, q=q, vpos_chain=vpos_chain,
                      vtr_copies=vtr_copies):
                    if q == 0:
                        eng.wait_ge(sm["v"], vpos_chain)
                    if q >= 2:
                        eng.wait_ge(sm["v"], vtr_copies + q - 1)
                    eng.transpose(psF2[par][:, 0:8], src[:, ds(i * 128, 128)],
                                  idf32[0:8, 0:8]).then_inc(sm["t"], 1)
                p.on("tensor", f)
                tn = p.bump("t")

                def f(eng, tn=tn, i=i, par=par, dstp=dstp):
                    eng.wait_ge(sm["t"], tn)
                    eng.tensor_copy(dstp[:, ds(i * 8, 8)], psF2[par][:, 0:8]).then_inc(sm["v"], 1)
                p.on("vector", f)
                p.bump("v")
        vpos = p.val("v")

        if True:
            def f(eng, vpos=vpos):
                eng.wait_ge(sm["v"], vpos)
                eng.dma_start(out=posS_o, in_=posS[:]).then_inc(sm["st"], 16)
                eng.dma_start(out=posG_o, in_=posG[:]).then_inc(sm["st"], 16)
            p.on("sync", f)
            p.bump("st", 32)

        for i in range(NT):
            par = i % 2
            sti = stg[par]
            gq_before = p.val("gq")

            def f(eng, i=i, sti=sti, wv=gq_before - 16 * 8):
                if wv > 0:
                    eng.wait_ge(sm["gq"], wv)
                eng.dma_start(out=sti[:, 0:1024], in_=ln1[ds(i * 128, 128), :]).then_inc(sm["ld"], 16)
                eng.dma_start(out=sti[:].bitcast(F32)[:, 512:520],
                              in_=sm_loc[ds(i * 128, 128), :]).then_inc(sm["ld"], 16)
            p.on("sync", f)
            lv = p.bump("ld", 32)

            def f(eng, i=i, sti=sti, lv=lv, vpos=vpos):
                eng.wait_ge(sm["ld"], lv)
                eng.wait_ge(sm["v"], vpos)
                for ee in range(E):
                    eng.indirect_dma_start(
                        out=gath,
                        out_offset=bass.IndirectOffsetOnAxis(ap=posS[:, ds(i * 8 + ee, 1)], axis=0),
                        in_=sti[:], in_offset=None,
                        bounds_check=bound_reg(eng, ee, (ee + 1) * C_CAP - 1),
                        oob_is_err=False,
                    ).then_inc(sm["gq"], 16)
            p.on("gpsimd", f)
            p.bump("gq", 16 * 8)
        gq_scat = p.val("gq")

        t_up_done = 0
        t_dn_done = 0
        vgv_prev = 0
        st_out_done = st_zero
        for ee in range(E):
            for j in range(5):
                def f(eng, ee=ee, j=j, gq_scat=gq_scat, wv=t_up_done, wvv=vgv_prev):
                    if j == 0:
                        eng.wait_ge(sm["gq"], gq_scat)
                    if wv:
                        eng.wait_ge(sm["t"], wv)
                    if wvv:
                        eng.wait_ge(sm["v"], wvv)
                    eng.dma_start(out=gt[j][:], in_=gath[ds(ee * C_CAP + j * 128, 128), :]).then_inc(sm["ld"], 16)
                p.on("sync", f)
            lgt = p.bump("ld", 16 * 5)

            for k in range(8):
                def f(eng, ee=ee, k=k, wv=t_up_done, vpos=vpos):
                    if wv:
                        eng.wait_ge(sm["t"], wv)
                    if ee == 0 and k == 0:
                        eng.wait_ge(sm["v"], vpos)
                        eng.wait_ge(sm["gq"], gq_scat)
                    eng.dma_start(out=wexp[0][:, ds(k * 2816, 2816)],
                                  in_=wupP[ee // 2, ds(k * 128, 128), :]).then_inc(sm["lw"], 16)
                p.on("gpsimd", f)
            lwu = p.bump("lw", 16 * 8)

            vxg_base = p.val("v")
            for q in range(40):
                j, c = q // 8, q % 8

                def f(eng, j=j, c=c, q=q, lgt=lgt, vxg_base=vxg_base):
                    if q == 0:
                        eng.wait_ge(sm["ld"], lgt)
                        eng.wait_ge(sm["lw"], lw_id)
                        eng.wait_ge(sm["v"], vxg_base)
                    if q >= 2:
                        eng.wait_ge(sm["v"], vxg_base + q - 1)
                    eng.transpose(psT2[q % 2][:], gt[j][:, ds(c * 128, 128)],
                                  ident[:]).then_inc(sm["t"], 1)
                p.on("tensor", f)
                tn = p.bump("t")

                def f(eng, tn=tn, j=j, c=c, q=q):
                    eng.wait_ge(sm["t"], tn)
                    eng.tensor_copy(Xg[:, ds(c * 640 + j * 128, 128)],
                                    psT2[q % 2][:]).then_inc(sm["v"], 1)
                p.on("vector", f)
                p.bump("v")
            vxg = p.val("v")

            vgv = p.fenced("vector", "v", [
                (lambda eng, j=j2, ee=ee: eng.tensor_copy(
                    gv[:, ds(j, 1)], gt[j][:].bitcast(F32)[:, ds(512 + ee, 1)]))
                for j2 in range(5)
            ], pre_waits=[("ld", lgt)])
            vgv_prev = vgv

            s_silu_base = p.val("s")
            for m in range(22):
                def f(eng, m=m, lwu=lwu, vxg=vxg, s_silu_base=s_silu_base):
                    if m == 0:
                        eng.wait_ge(sm["lw"], lwu)
                        eng.wait_ge(sm["v"], vxg)
                    if m < 2:
                        eng.wait_ge(sm["s"], s_silu_base)
                    if m >= 2:
                        eng.wait_ge(sm["s"], s_silu_base + m - 1)
                    pu = psUP[m % 2]
                    for k in range(8):
                        eng.matmul(pu[:, 0:512], wexp[0][:, ds(k * 2816 + m * 128, 128)],
                                   Xg[:, ds(k * 640, 512)], start=(k == 0), stop=(k == 7))
                    for k in range(8):
                        inst = eng.matmul(pu[:, 512:640],
                                          wexp[0][:, ds(k * 2816 + m * 128, 128)],
                                          Xg[:, ds(k * 640 + 512, 128)],
                                          start=(k == 0), stop=(k == 7))
                    inst.then_inc(sm["t"], 1)
                p.on("tensor", f)
                tn = p.bump("t")
                if m == 21:
                    t_up_done = tn

                def f(eng, tn=tn, m=m, ee=ee):
                    eng.wait_ge(sm["t"], tn)
                    pu = psUP[m % 2]
                    eng.activation(midT[:, ds(m * 640, 512)], pu[:, 0:512], AF.Silu,
                                   bias=bup_sb[:, ds((ee // 2) * 22 + m, 1)], scale=1.0)
                    eng.activation(midT[:, ds(m * 640 + 512, 128)], pu[:, 512:640], AF.Silu,
                                   bias=bup_sb[:, ds((ee // 2) * 22 + m, 1)],
                                   scale=1.0).then_inc(sm["s"], 1)
                p.on("scalar", f)
                p.bump("s")
            sup = p.val("s")

            for k in range(22):
                def f(eng, ee=ee, k=k, wv=t_dn_done, vpos=vpos):
                    if wv:
                        eng.wait_ge(sm["t"], wv)
                    if ee == 0 and k == 0:
                        eng.wait_ge(sm["v"], vpos)
                    eng.dma_start(out=wexp[1][:, ds(k * 1024, 1024)],
                                  in_=wdnP[ee, ds(k * 128, 128), :]).then_inc(sm["lw"], 16)
                p.on("gpsimd", f)
            lwd = p.bump("lw", 16 * 22)

            s_dn_base = p.val("s")
            for o in range(8):
                def f(eng, o=o, lwd=lwd, sup=sup, s_dn_base=s_dn_base):
                    if o == 0:
                        eng.wait_ge(sm["lw"], lwd)
                        eng.wait_ge(sm["s"], sup)
                    if o >= 2:
                        eng.wait_ge(sm["s"], s_dn_base + o - 1)
                    pu = psUP[o % 2]
                    for k in range(22):
                        eng.matmul(pu[:, 0:512], wexp[1][:, ds(k * 1024 + o * 128, 128)],
                                   midT[:, ds(k * 640, 512)], start=(k == 0), stop=(k == 21))
                    for k in range(22):
                        inst = eng.matmul(pu[:, 512:640],
                                          wexp[1][:, ds(k * 1024 + o * 128, 128)],
                                          midT[:, ds(k * 640 + 512, 128)],
                                          start=(k == 0), stop=(k == 21))
                    inst.then_inc(sm["t"], 1)
                p.on("tensor", f)
                tn = p.bump("t")
                if o == 7:
                    t_dn_done = tn

                tr_prev = p.val("t")

                def f(eng, tn=tn, o=o, ee=ee, tr_prev=tr_prev):
                    eng.wait_ge(sm["t"], tr_prev)
                    pu = psUP[o % 2]
                    eng.activation(dnf[:, 0:512], pu[:, 0:512], AF.Identity,
                                   bias=bdn_sb[:, ds(ee * 8 + o, 1)], scale=1.0)
                    eng.activation(dnf[:, 512:640], pu[:, 512:640], AF.Identity,
                                   bias=bdn_sb[:, ds(ee * 8 + o, 1)], scale=1.0).then_inc(sm["s"], 1)
                p.on("scalar", f)
                sn = p.bump("s")

                vtrd_base = p.val("v")
                for j in range(5):
                    def f(eng, sn=sn, j=j, vtrd_base=vtrd_base):
                        if j == 0:
                            eng.wait_ge(sm["s"], sn)
                            eng.wait_ge(sm["v"], vtrd_base)
                        if j >= 2:
                            eng.wait_ge(sm["v"], vtrd_base + j - 1)
                        eng.transpose(psF2[j % 2], dnf[:, ds(j * 128, 128)],
                                      idf32[:]).then_inc(sm["t"], 1)
                    p.on("tensor", f)
                    tn2 = p.bump("t")

                    def f(eng, tn2=tn2, o=o, j=j, wst=st_out_done):
                        eng.wait_ge(sm["t"], tn2)
                        if o == 0 and j == 0:
                            eng.wait_ge(sm["st"], wst)
                        eng.tensor_copy(outT[:, ds(j * 1024 + o * 128, 128)],
                                        psF2[j % 2]).then_inc(sm["v"], 1)
                    p.on("vector", f)
                    p.bump("v")
            vdn = p.val("v")

            for j in range(5):
                vn = p.fenced("vector", "v", [
                    lambda eng, j=j: eng.tensor_scalar(tmp1[:], gv[:, ds(j, 1)], 0.5,
                                                       None, ALU.mult),
                    lambda eng, j=j: eng.tensor_scalar(outT[:, ds(j * 1024, 1024)],
                                                       outT[:, ds(j * 1024, 1024)],
                                                       tmp1[:], None, ALU.mult),
                ], pre_waits=[("v", vdn), ("v", vgv)] if j == 0 else ())

                def f(eng, vn=vn, ee=ee, j=j):
                    eng.wait_ge(sm["v"], vn)
                    eng.dma_start(out=outb[ds(ee * C_CAP + j * 128, 128), :],
                                  in_=outT[:, ds(j * 1024, 1024)]).then_inc(sm["st"], 16)
                p.on("sync", f)
                st_out_done = p.bump("st", 16)

        ga_base = 0
        st_x1_prev = 0
        v_ln2_prev = 0
        for g in range(8):
            tiles = [g * 4 + u for u in range(4)]
            for u, i in enumerate(tiles):
                def f(eng, i=i, u=u, st_out_done=st_out_done, wv=v_ln2_prev,
                      wst=st_x1_prev, first=(g == 0)):
                    if first and u == 0:
                        eng.wait_ge(sm["st"], st_out_done)
                    if wv:
                        eng.wait_ge(sm["v"], wv)
                    if wst:
                        eng.wait_ge(sm["st"], wst)
                    eng.dma_start(out=xq[:, ds(u * 1024, 1024)],
                                  in_=x0[ds(i * 128, 128), :]).then_inc(sm["ld"], 16)
                p.on("sync", f)
            lx = p.bump("ld", 16 * 4)

            ga_grp_base = p.val("ga")
            for ee in range(E):
                for u, i in enumerate(tiles):
                    def f(eng, ee=ee, i=i, u=u, lx=lx, ga_grp_base=ga_grp_base):
                        if ee == 0 and u == 0:
                            eng.wait_ge(sm["ld"], lx)
                        if ee > 0:
                            eng.wait_ge(sm["ga"], ga_grp_base + 16 * ((ee - 1) * 4 + u + 1))
                        eng.indirect_dma_start(
                            out=xq[:, ds(u * 1024, 1024)], out_offset=None,
                            in_=outb,
                            in_offset=bass.IndirectOffsetOnAxis(ap=posG[:, ds(i * 8 + ee, 1)], axis=0),
                            compute_op=ALU.add, oob_is_err=True,
                            bounds_check=bound_reg(eng, "ga", E * C_CAP + 127),
                        ).then_inc(sm["ga"], 16)
                    p.on("gpsimd", f)
                    p.bump("ga", 16)
            gdone = p.val("ga")

            for u, i in enumerate(tiles):
                def f(eng, u=u, i=i, gdone=gdone):
                    eng.wait_ge(sm["ga"], gdone)
                    eng.dma_start(out=x1_o[ds(i * 128, 128), :],
                                  in_=xq[:, ds(u * 1024, 1024)]).then_inc(sm["st"], 16)
                p.on("sync", f)
                st_x1_prev = p.bump("st", 16)

                if mode == "D":
                    continue
                xti = xq[:, ds(u * 1024, 1024)]
                vn = p.fenced("vector", "v", [
                    lambda eng, xti=xti: eng.reduce_sum(sum1[:], xti, axis=AX.X),
                ], pre_waits=[("ga", gdone)])
                sn = p.fenced("scalar", "s", [
                    lambda eng, xti=xti: eng.activation(sq[:], xti, AF.Square, bias=0.0,
                                                        scale=1.0, accum_out=ssq[:]),
                ], pre_waits=[("ga", gdone), ("v", p.val("v"))])
                vn = p.fenced("vector", "v", [
                    lambda eng: eng.tensor_scalar(mean[:], sum1[:], 1.0 / 1024, None, ALU.mult),
                    lambda eng: eng.tensor_mul(nmr[:], mean[:], mean[:]),
                    lambda eng: eng.scalar_tensor_tensor(var[:], ssq[:], 1.0 / 1024,
                                                         nmr[:], ALU.mult, ALU.subtract),
                ], pre_waits=[("v", vn), ("s", sn)])
                sn = p.fenced("scalar", "s", [
                    lambda eng: eng.activation(rstd[:], var[:], AF.Sqrt, bias=eps_sb[:],
                                               scale=1.0),
                ], pre_waits=[("v", vn)])
                vn = p.fenced("vector", "v", [
                    lambda eng: eng.reciprocal(rstd[:], rstd[:]),
                    lambda eng: eng.tensor_mul(nmr[:], mean[:], rstd[:]),
                    lambda eng: eng.tensor_scalar(nmr[:], nmr[:], -1.0, None, ALU.mult),
                    lambda eng, xti=xti: eng.tensor_scalar(lnt[:], xti, rstd[:], nmr[:],
                                                           ALU.mult, ALU.add),
                ], pre_waits=[("s", sn), ("t", p.val("t"))])

                vtl_base = p.val("v")
                for c in range(8):
                    def f(eng, c=c, vn=vn, vtl_base=vtl_base):
                        if c == 0:
                            eng.wait_ge(sm["v"], vn)
                        if c >= 2:
                            eng.wait_ge(sm["v"], vtl_base + c - 1)
                        eng.transpose(psT2[c % 2][:], lnt[:, ds(c * 128, 128)],
                                      ident[:]).then_inc(sm["t"], 1)
                    p.on("tensor", f)
                    tn = p.bump("t")

                    def f(eng, tn=tn, c=c, i=i):
                        eng.wait_ge(sm["t"], tn)
                        eng.tensor_copy(ln2fm[:, ds(c * 4096 + i * 128, 128)],
                                        psT2[c % 2][:]).then_inc(sm["v"], 1)
                    p.on("vector", f)
                    p.bump("v")
            v_ln2_prev = p.val("v")
        vln2 = p.val("v")

        if mode == "D":
            def f(eng, n=p.val("st")):
                eng.wait_ge(sm["st"], n)
            p.on("sync", f)
            def f2(eng, n=p.val("ga")):
                eng.wait_ge(sm["ga"], n)
            p.on("gpsimd", f2)
            p.replay(blk)
            return nc
        vbq = p.fenced("vector", "v", [
            lambda eng: eng.tensor_scalar(bq8_sb[:], bq_sb[:], 0.125, None, ALU.mult),
        ], pre_waits=[("ld", ld_pro)])

        qkv_slot_free_st = [0, 0]
        wslot_free_t = [0, 0]
        projs = [(wqT, bq8_sb, 0.125, qT_o), (wkT, bk_sb, 1.0, kT_o),
                 (wvT, bv_sb, 1.0, vT_o)]
        lin = 0
        for pi, (wsrc, bias_sb, scale, out_dram) in enumerate(projs):
            for o in range(8):
                wslot = qkvw[o % 2]

                def f(eng, o=o, wslot=wslot, wsrc=wsrc, wv=wslot_free_t[o % 2],
                      vln2=vln2):
                    if wv:
                        eng.wait_ge(sm["t"], wv)
                    eng.dma_start(
                        out=wslot[:].rearrange("p (k f) -> p k f", f=128),
                        in_=wsrc[:, ds(o * 128, 128)].rearrange("(k p) f -> p k f", p=128),
                    ).then_inc(sm["lw"], 16)
                p.on("gpsimd", f)
                lw = p.bump("lw", 16)

                for n in range(8):
                    bank = lin % 4
                    s_prev_bank = p.val("s") - 3

                    def f(eng, o=o, n=n, bank=bank, lw=lw, vln2=vln2, wslot=wslot,
                          sv=s_prev_bank, lin=lin, sall=p.val("s")):
                        if n == 0:
                            eng.wait_ge(sm["lw"], lw)
                        if lin == 0:
                            eng.wait_ge(sm["v"], vln2)
                            eng.wait_ge(sm["s"], sall)
                        if lin >= 4:
                            eng.wait_ge(sm["s"], sv)
                        for k in range(8):
                            inst = eng.matmul(psQ[bank], wslot[:, ds(k * 128, 128)],
                                              ln2fm[:, ds(k * 4096 + n * 512, 512)],
                                              start=(k == 0), stop=(k == 7))
                        inst.then_inc(sm["t"], 1)
                    p.on("tensor", f)
                    tn = p.bump("t")
                    if n == 7:
                        wslot_free_t[o % 2] = tn

                    oslot = lin % 2

                    def f(eng, tn=tn, o=o, bank=bank, bias_sb=bias_sb, scale=scale,
                          oslot=oslot, wst=qkv_slot_free_st[oslot]):
                        eng.wait_ge(sm["t"], tn)
                        if wst:
                            eng.wait_ge(sm["st"], wst)
                        eng.activation(qkvo[oslot][:], psQ[bank], AF.Identity,
                                       bias=bias_sb[:, ds(o, 1)],
                                       scale=scale).then_inc(sm["s"], 1)
                    p.on("scalar", f)
                    sn = p.bump("s")

                    def f(eng, sn=sn, o=o, n=n, out_dram=out_dram, oslot=oslot):
                        eng.wait_ge(sm["s"], sn)
                        eng.dma_start(out=out_dram[ds(o * 128, 128), ds(n * 512, 512)],
                                      in_=qkvo[oslot][:]).then_inc(sm["st"], 16)
                    p.on("sync", f)
                    qkv_slot_free_st[oslot] = p.bump("st", 16)
                    lin += 1

        def f(eng, wv=ld_pro):
            eng.wait_ge(sm["ld"], wv)
            eng.matmul(psWa, gruW_sb[:], g48a_sb[:], start=True, stop=True)
            eng.matmul(psWb, gruW_sb[:], g48b_sb[:], start=True, stop=True)
            eng.matmul(psBa, gruB_sb[:], g48a_sb[:], start=True, stop=True)
            eng.matmul(psBb, gruB_sb[:], g48b_sb[:], start=True, stop=True).then_inc(sm["t"], 1)
        p.on("tensor", f)
        tn = p.bump("t")

        vn = p.fenced("vector", "v", [
            lambda eng: eng.tensor_copy(wa2[0:64, :], psWa),
            lambda eng: eng.tensor_copy(wa2[64:128, :], psWa),
            lambda eng: eng.tensor_copy(wb2[0:64, :], psWb),
            lambda eng: eng.tensor_copy(wb2[64:128, :], psWb),
            lambda eng: eng.tensor_copy(ba_sb[:], psBa),
            lambda eng: eng.tensor_copy(bb_sb[:], psBb),
        ], pre_waits=[("t", tn)])
        vgru = vn

        for h in range(NH):
            for n in range(8):
                base = (h % 2) * 64

                def f(eng, h=h, n=n, base=base, vgru=vgru, vln2=vln2, wv=p.val("s")):
                    if h == 0 and n == 0:
                        eng.wait_ge(sm["v"], vgru)
                        eng.wait_ge(sm["v"], vln2)
                    else:
                        eng.wait_ge(sm["s"], wv)
                    eng.matmul(psGa[:, 0:512], wa2[ds(base, 64), :],
                               ln2fm[ds(base, 64), ds((h // 2) * 4096 + n * 512, 512)],
                               start=True, stop=True)
                    eng.matmul(psGb, wb2[ds(base, 64), :],
                               ln2fm[ds(base, 64), ds((h // 2) * 4096 + n * 512, 512)],
                               start=True, stop=True).then_inc(sm["t"], 1)
                p.on("tensor", f)
                tn = p.bump("t")

                sn = p.fenced("scalar", "s", [
                    lambda eng: eng.activation(sga[:], psGa[:, 0:512], AF.Sigmoid,
                                               bias=ba_sb[:], scale=1.0),
                    lambda eng: eng.activation(sgb[:], psGb, AF.Sigmoid,
                                               bias=bb_sb[:], scale=1.0),
                ], pre_waits=[("t", tn), ("v", p.val("v"))])

                vn = p.fenced("vector", "v", [
                    lambda eng, h=h: eng.tensor_scalar(sgb[:], sgb[:], gruC_sb[0:1, ds(h, 1)],
                                                       -1.0, ALU.mult, ALU.add),
                    lambda eng: eng.tensor_mul(grow[:], sga[:], sgb[:]),
                    lambda eng: eng.tensor_scalar(grow[:], grow[:], 2.0, None, ALU.add),
                ], pre_waits=[("s", sn), ("st", p.val("st"))])

                def f(eng, vn=vn, h=h, n=n):
                    eng.wait_ge(sm["v"], vn)
                    eng.dma_start(out=gate_o[ds(h, 1), ds(n * 512, 512)],
                                  in_=grow[:]).then_inc(sm["st"], 16)
                p.on("sync", f)
                p.bump("st", 16)

        def f(eng, n=p.val("st")):
            eng.wait_ge(sm["st"], n)
        p.on("sync", f)
        def f2(eng, n=p.val("ga")):
            eng.wait_ge(sm["ga"], n)
        p.on("gpsimd", f2)

        p.replay(blk)
    return nc
def build_C(dbg=False):
    nc = bass.Bass("TRN2", target_bir_lowering=False, debug=False)
    x1 = nc.dram_tensor("x1", [T, D_MODEL], F32, kind="ExternalInput").ap()
    qt8 = nc.dram_tensor("qt8", [D_MODEL, T], BF16, kind="ExternalInput").ap()
    krevT = nc.dram_tensor("krevT", [D_MODEL, S], BF16, kind="ExternalInput").ap()
    vtok = nc.dram_tensor("vtok", [NH, 64 * 128 * 65], BF16, kind="ExternalInput").ap()
    gate_i = nc.dram_tensor("gate_i", [NH, T], F32, kind="ExternalInput").ap()
    pbrevc = nc.dram_tensor("pbrevc", [NH, 12416], F32, kind="ExternalInput").ap()
    woT = nc.dram_tensor("woT", [D_MODEL, D_MODEL], F32, kind="ExternalInput").ap()
    bo_i = nc.dram_tensor("bo_i", [1, D_MODEL], F32, kind="ExternalInput").ap()
    wr = nc.dram_tensor("wr", [D_MODEL, E], F32, kind="ExternalInput").ap()
    br = nc.dram_tensor("br", [1, E], F32, kind="ExternalInput").ap()
    identp = nc.dram_tensor("identp", [128, 128], BF16, kind="ExternalInput").ap()
    onesf_i = nc.dram_tensor("onesf_i", [1, 64], F32, kind="ExternalInput").ap()

    x2_o = nc.dram_tensor("x2_o", [T, D_MODEL], F32, kind="ExternalOutput").ap()
    ln3_o = nc.dram_tensor("ln3_o", [T, D_MODEL], BF16, kind="ExternalOutput").ap()
    sm2_o = nc.dram_tensor("sm2_o", [T, E], F32, kind="ExternalOutput").ap()

    attnfm = nc.dram_tensor("attnfm", [D_MODEL, T], BF16,
                            kind="ExternalOutput" if dbg else "Internal").ap()

    NJC = S // 128
    NIB = T // 512

    with ExitStack() as ctx:
        e = ctx.enter_context
        shd = [e(nc.sbuf_tensor(f"shd{i}", [128, 12288], F32)) for i in range(2)]
        kq = e(nc.sbuf_tensor("kq", [128, S], BF16))
        qt = e(nc.sbuf_tensor("qt", [128, T], BF16))
        vt = [e(nc.sbuf_tensor(f"vt{i}", [128, 4160], BF16)) for i in range(2)]
        grep = [[e(nc.sbuf_tensor(f"grep{h}_{i}", [128, 512], BF16)) for i in range(2)]
                for h in range(2)]
        bt = [[e(nc.sbuf_tensor(f"bt{hh}_{i}", [128, 512], BF16)) for i in range(2)]
              for hh in range(2)]
        pt = [[e(nc.sbuf_tensor(f"pt{hh}_{i}", [128, 512], BF16)) for i in range(2)]
              for hh in range(2)]
        rsb = [e(nc.sbuf_tensor(f"rsb{hh}", [1, 512], F32)) for hh in range(2)]
        rrep = [e(nc.sbuf_tensor(f"rrep{hh}", [64, 512], F32)) for hh in range(2)]
        attn_t = e(nc.sbuf_tensor("attn_t", [128, 512], BF16))
        onesf = e(nc.sbuf_tensor("onesf", [1, 64], F32))
        ident = e(nc.sbuf_tensor("ident", [128, 128], BF16))
        wot = [e(nc.sbuf_tensor(f"wot{i}", [128, 1024], BF16)) for i in range(2)]
        wo_ev = [e(nc.sbuf_tensor(f"wo_ev{i}", [128, 512], BF16)) for i in range(2)]
        atok = [e(nc.sbuf_tensor(f"atok{i}", [128, 1024], BF16)) for i in range(4)]
        borep = e(nc.sbuf_tensor("borep", [128, 1024], F32))
        borow = e(nc.sbuf_tensor("borow", [1, 1024], F32))
        x2t = [e(nc.sbuf_tensor(f"x2t{i}", [128, 1024], F32)) for i in range(2)]
        wr_sb = e(nc.sbuf_tensor("wr_sb", [128, 64], BF16))
        brrep = e(nc.sbuf_tensor("brrep", [128, 8], F32))
        brrow = e(nc.sbuf_tensor("brrow", [1, 8], F32))
        ones1 = e(nc.sbuf_tensor("ones1", [1, 128], F32))
        eps_sb = e(nc.sbuf_tensor("eps_sb", [128, 1], F32))

        psA = e(nc.psum_tensor("psA", [128, 4096], F32))
        psS = [[psA[:, 0:512], psA[:, 512:1024]],
               [psA[:, 1024:1536], psA[:, 1536:2048]]]
        psO = [psA[0:65, 2048:2560], psA[0:65, 2560:3072]]
        psR = [psA[0:64, 3072:3584], psA[0:64, 3584:4096]]
        psW = [psA[:, 0:512], psA[:, 512:1024], psA[:, 1024:1536], psA[:, 1536:2048]]
        psT2 = [psA[:, ds(2048 + i * 512, 512)].bitcast(BF16)[:, 0:128] for i in range(2)]
        psB1 = psA[:, 3072:4096]
        psL = psA[:, 3072:3080]

        sm = sem_dict(ctx, nc, ["ld", "lw", "v", "t", "s", "g", "st"])
        blk = e(nc.Block())
        p = Prog(sm)

        def f(eng):
            for dst, src in [(onesf, onesf_i), (brrow, br), (borow, bo_i)]:
                eng.dma_start(out=dst[:], in_=src).then_inc(sm["ld"], 16)
        p.on("sync", f)
        ld_pro = p.bump("ld", 16 * 3)

        def f(eng):
            eng.dma_start(out=ident[:], in_=identp).then_inc(sm["lw"], 16)
            eng.dma_start(out=wr_sb[:].rearrange("p (c e) -> p c e", e=8),
                          in_=wr.rearrange("(c p) e -> p c e", p=128)).then_inc(sm["lw"], 16)
        p.on("gpsimd", f)
        lw_id = p.bump("lw", 32)

        vz = p.fenced("vector", "v", [
            lambda eng: eng.memset(eps_sb[:], 1e-5),
            lambda eng: eng.memset(ones1[:], 1.0),
        ])

        def f(eng, vz=vz, ld_pro=ld_pro):
            eng.wait_ge(sm["v"], vz)
            eng.wait_ge(sm["ld"], ld_pro)
            eng.matmul(psL, ones1[:], brrow[:], start=True, stop=True).then_inc(sm["t"], 1)
        p.on("tensor", f)
        tn = p.bump("t")
        vbr = p.fenced("vector", "v", [
            lambda eng: eng.tensor_copy(brrep[:], psL),
        ], pre_waits=[("t", tn)])

        def f(eng, vbr=vbr):
            eng.wait_ge(sm["v"], vbr)
            eng.matmul(psB1[:, 0:512], ones1[:], borow[:, 0:512], start=True, stop=True)
            eng.matmul(psB1[:, 512:1024], ones1[:], borow[:, 512:1024],
                       start=True, stop=True).then_inc(sm["t"], 1)
        p.on("tensor", f)
        tn = p.bump("t")
        vpro = p.fenced("vector", "v", [
            lambda eng: eng.tensor_copy(borep[:, 0:512], psB1[:, 0:512]),
            lambda eng: eng.tensor_copy(borep[:, 512:1024], psB1[:, 512:1024]),
        ], pre_waits=[("t", tn)])

        lastExp = [[0, 0], [0, 0]]
        lastAV = [[0, 0], [0, 0]]
        lastSTT = [[0, 0], [0, 0]]
        lastOEvac = [vpro, vpro]
        lastREvac = [vpro, vpro]
        lastAttnSt = 0

        hp_loads = []
        for hp in range(8):
            lw_before = p.val("lw")

            def f(eng, hp=hp, w=p.val("t"), wg=p.val("g"), wv=p.val("v")):
                if hp > 0:
                    eng.wait_ge(sm["t"], w)
                    eng.wait_ge(sm["g"], wg)
                    eng.wait_ge(sm["v"], wv)
                eng.dma_start(out=qt[:], in_=qt8[ds(hp * 128, 128), :]).then_inc(sm["ld"], 16)
                eng.dma_start(out=kq[:], in_=krevT[ds(hp * 128, 128), :]).then_inc(sm["ld"], 16)
                eng.dma_start(out=shd[0][:],
                              in_=bass.AP(tensor=pbrevc.tensor, offset=(2 * hp) * 12416,
                                          ap=[[1, 128], [1, 12288]])).then_inc(sm["ld"], 16)
                eng.dma_start(out=shd[1][:],
                              in_=bass.AP(tensor=pbrevc.tensor, offset=(2 * hp + 1) * 12416,
                                          ap=[[1, 128], [1, 12288]])).then_inc(sm["ld"], 16)
                eng.dma_start(out=vt[0][:].rearrange("r (c x) -> r c x", x=65),
                              in_=vtok[ds(2 * hp, 1), :].rearrange("1 (c r x) -> r c x", r=128, x=65)
                              ).then_inc(sm["ld"], 16)
                eng.dma_start(out=vt[1][:].rearrange("r (c x) -> r c x", x=65),
                              in_=vtok[ds(2 * hp + 1, 1), :].rearrange("1 (c r x) -> r c x", r=128, x=65)
                              ).then_inc(sm["ld"], 16)
            p.on("sync", f)
            ld_hp = p.bump("ld", 16 * 6)


            for ib in range(NIB):
                ibp = ib % 2
                def f(eng, hp=hp, ib=ib, ibp=ibp, w=p.val("g")):
                    if hp + ib > 0:
                        eng.wait_ge(sm["g"], w)
                    for hh in range(2):
                        src = bass.AP(tensor=gate_i.tensor,
                                      offset=(2 * hp + hh) * T + ib * 512,
                                      ap=[[0, 128], [1, 512]])
                        eng.dma_start(out=grep[hh][ibp][:], in_=src).then_inc(sm["lw"], 16)
                p.on("gpsimd", f)
                lw_g = p.bump("lw", 32)

                u0s = [8064 - 128 * jc + ib * 512 for jc in range(NJC)]
                for jc in range(NJC):
                    par = jc % 2
                    u0 = u0s[jc]

                    def f(eng, hp=hp, ib=ib, jc=jc, par=par,
                          wA=lastExp[0][par], wB=lastExp[1][par], ld_hp=ld_hp):
                        if jc < 2:
                            eng.wait_ge(sm["ld"], ld_hp)
                            eng.wait_ge(sm["lw"], lw_id)
                        eng.wait_ge(sm["s"], max(wA, wB))
                        eng.matmul(psS[0][par], kq[0:64, ds(jc * 128, 128)],
                                   qt[0:64, ds(ib * 512, 512)], start=True, stop=True,
                                   tile_position=(0, 0)).then_inc(sm["t"], 1)
                        eng.matmul(psS[1][par], kq[64:128, ds(jc * 128, 128)],
                                   qt[64:128, ds(ib * 512, 512)], start=True, stop=True,
                                   tile_position=(64, 0)).then_inc(sm["t"], 1)
                    p.on("tensor", f)
                    tqk = [p.bump("t"), p.bump("t")]

                    for hh in range(2):
                        def f(eng, hh=hh, par=par, u0=u0, ibp=ibp,
                              wfree=lastSTT[hh][par], lw_g=lw_g, jc=jc, ld_hp=ld_hp):
                            if jc < 2:
                                eng.wait_ge(sm["lw"], lw_g)
                                eng.wait_ge(sm["ld"], ld_hp)
                            if wfree:
                                eng.wait_ge(sm["v"], wfree)
                            eng.tensor_tensor(bt[hh][par][:], shd[hh][:, ds(u0, 512)],
                                              grep[hh][ibp][:],
                                              ALU.mult).then_inc(sm["g"], 1)
                        p.on("gpsimd", f)
                    gbt = [p.bump("g") - 1, p.bump("g") - 0]
                    gbt = [p.val("g") - 1, p.val("g")]

                    for hh in range(2):
                        def f(eng, hh=hh, par=par, tq=tqk[hh], gb=gbt[hh]):
                            eng.wait_ge(sm["t"], tq)
                            eng.wait_ge(sm["g"], gb)
                            eng.scalar_tensor_tensor(psS[hh][par], bt[hh][par][:], 1.0,
                                                     psS[hh][par], ALU.mult,
                                                     ALU.add).then_inc(sm["v"], 1)
                        p.on("vector", f)
                    vstt = [p.bump("v"), p.bump("v")]
                    lastSTT[0][par] = vstt[0]
                    lastSTT[1][par] = vstt[1]

                    for hh in range(2):
                        def f(eng, hh=hh, par=par, vs=vstt[hh], wpt=lastAV[hh][par]):
                            eng.wait_ge(sm["v"], vs)
                            if wpt:
                                eng.wait_ge(sm["t"], wpt)
                            eng.activation(pt[hh][par][:], psS[hh][par], AF.Exp,
                                           bias=0.0, scale=1.0).then_inc(sm["s"], 1)
                        p.on("scalar", f)
                    sexp = [p.bump("s"), p.bump("s")]
                    lastExp[0][par] = sexp[0]
                    lastExp[1][par] = sexp[1]

                    for hh in range(2):
                        def f(eng, hh=hh, par=par, jc=jc, se=sexp[hh],
                              wO=lastOEvac[hh]):
                            eng.wait_ge(sm["s"], se)
                            if jc == 0:
                                eng.wait_ge(sm["v"], wO)
                            eng.matmul(psO[hh], vt[hh][:, ds(jc * 65, 65)],
                                       pt[hh][par][:], start=(jc == 0),
                                       stop=(jc == NJC - 1)).then_inc(sm["t"], 1)
                        p.on("tensor", f)
                    tav = [p.bump("t"), p.bump("t")]
                    lastAV[0][par] = tav[0]
                    lastAV[1][par] = tav[1]

                vrec = p.fenced("vector", "v", [
                    lambda eng: eng.reciprocal(rsb[0][:], psO[0][64:65, :]),
                    lambda eng: eng.reciprocal(rsb[1][:], psO[1][64:65, :]),
                ], pre_waits=[("t", tav[1])])

                def f(eng, vrec=vrec, wR=max(lastREvac)):
                    eng.wait_ge(sm["v"], vrec)
                    eng.matmul(psR[0], onesf[:], rsb[0][:], start=True, stop=True).then_inc(sm["t"], 1)
                    eng.matmul(psR[1], onesf[:], rsb[1][:], start=True, stop=True).then_inc(sm["t"], 1)
                p.on("tensor", f)
                trep = [p.bump("t"), p.bump("t")]

                vev = p.fenced("vector", "v", [
                    lambda eng: eng.tensor_copy(rrep[0][:], psR[0]),
                    lambda eng: eng.tensor_copy(rrep[1][:], psR[1]),
                    lambda eng: eng.tensor_tensor(attn_t[0:64, :], psO[0][0:64, :],
                                                  rrep[0][:], ALU.mult),
                    lambda eng: eng.tensor_tensor(attn_t[64:128, :], psO[1][0:64, :],
                                                  rrep[1][:], ALU.mult),
                ], pre_waits=[("t", trep[1]), ("st", lastAttnSt)])
                lastOEvac = [vev, vev]
                lastREvac = [vev, vev]

                def f(eng, vev=vev, hp=hp, ib=ib):
                    eng.wait_ge(sm["v"], vev)
                    eng.dma_start(out=attnfm[ds(hp * 128, 128), ds(ib * 512, 512)],
                                  in_=attn_t[:]).then_inc(sm["st"], 16)
                p.on("sync", f)
                lastAttnSt = p.bump("st", 16)

        st_flash = p.val("st")

        afull = kq[:, 0:4096]
        x1t4 = [e(nc.sbuf_tensor(f"x1t4_{i}", [128, 1024], F32)) for i in range(4)]

        wot_free = [0, 0]
        woev_free = [0, 0]
        afull_free = 0
        atok_free = [0, 0, 0, 0]
        x1t_free = [0, 0, 0, 0]
        x2t_free = [0, 0]
        st_x2 = 0
        for n in range(8):
            def f(eng, n=n, wld=afull_free, st_flash=st_flash):
                if n == 0:
                    eng.wait_ge(sm["st"], st_flash)
                if wld:
                    eng.wait_ge(sm["t"], wld)
                for k in range(8):
                    eng.dma_start(out=afull[:, ds(k * 512, 512)],
                                  in_=attnfm[ds(k * 128, 128), ds(n * 512, 512)]).then_inc(sm["ld"], 16)
                for tt in range(4):
                    pass
            p.on("sync", f)
            ld_af = p.bump("ld", 16 * 8)

            def f(eng, n=n, wv=[x1t_free[i] for i in range(4)]):
                for tt in range(4):
                    if wv[tt]:
                        eng.wait_ge(sm["v"], wv[tt])
                    eng.dma_start(out=x1t4[tt][:],
                                  in_=x1[ds(n * 512 + tt * 128, 128), :]).then_inc(sm["ld"], 16)
            p.on("sync", f)
            ld_x1 = p.bump("ld", 16 * 4)

            copies_start = p.val("v")
            for fo in range(8):
                wslot = fo % 2

                def f(eng, n=n, fo=fo, wslot=wslot, wv=wot_free[wslot]):
                    if wv:
                        eng.wait_ge(sm["t"], wv)
                    eng.dma_start(
                        out=wot[wslot][:].rearrange("p (k f) -> p k f", f=128),
                        in_=woT[:, ds(fo * 128, 128)].rearrange("(k p) f -> p k f", p=128),
                    ).then_inc(sm["lw"], 16)
                p.on("gpsimd", f)
                lww = p.bump("lw", 16)

                sfree = p.val("s") - 3
                def f(eng, n=n, fo=fo, wslot=wslot, lww=lww, ld_af=ld_af, sfree=sfree):
                    eng.wait_ge(sm["lw"], lww)
                    if fo == 0:
                        eng.wait_ge(sm["ld"], ld_af)
                    if fo >= 4 or n > 0:
                        eng.wait_ge(sm["s"], sfree)
                    for k in range(8):
                        inst = eng.matmul(psW[fo % 4], wot[wslot][:, ds(k * 128, 128)],
                                          afull[:, ds(k * 512, 512)],
                                          start=(k == 0), stop=(k == 7))
                    inst.then_inc(sm["t"], 1)
                p.on("tensor", f)
                tmm = p.bump("t")
                wot_free[wslot] = tmm
                if fo == 7:
                    afull_free = tmm

                def f(eng, tmm=tmm, fo=fo, wev=woev_free[fo % 2]):
                    eng.wait_ge(sm["t"], tmm)
                    if wev:
                        eng.wait_ge(sm["t"], wev)
                    eng.activation(wo_ev[fo % 2][:], psW[fo % 4], AF.Identity,
                                   bias=0.0, scale=1.0).then_inc(sm["s"], 1)
                p.on("scalar", f)
                sev = p.bump("s")

                vtr_base = p.val("v")
                for tt in range(4):
                    def f(eng, sev=sev, fo=fo, tt=tt, vtr_base=vtr_base):
                        if tt == 0:
                            eng.wait_ge(sm["s"], sev)
                            eng.wait_ge(sm["v"], vtr_base)
                        if tt >= 2:
                            eng.wait_ge(sm["v"], vtr_base + tt - 1)
                        eng.transpose(psT2[tt % 2][:], wo_ev[fo % 2][:, ds(tt * 128, 128)],
                                      ident[:]).then_inc(sm["t"], 1)
                    p.on("tensor", f)
                    ttr = p.bump("t")
                    if tt == 3:
                        woev_free[fo % 2] = ttr

                    def f(eng, ttr=ttr, fo=fo, tt=tt, wv=atok_free[tt]):
                        eng.wait_ge(sm["t"], ttr)
                        if fo == 0 and wv:
                            eng.wait_ge(sm["v"], wv)
                        eng.tensor_copy(atok[tt][:, ds(fo * 128, 128)],
                                        psT2[tt % 2][:]).then_inc(sm["v"], 1)
                    p.on("vector", f)
                    p.bump("v")
            v_copies = p.val("v")

            for tt in range(4):
                par = tt % 2
                vn = p.fenced("vector", "v", [
                    lambda eng, tt=tt, par=par: eng.tensor_tensor(
                        x2t[par][:], atok[tt][:], borep[:], ALU.add),
                    lambda eng, tt=tt, par=par: eng.tensor_add(
                        x2t[par][:], x2t[par][:], x1t4[tt][:]),
                ], pre_waits=[("v", v_copies), ("ld", ld_x1), ("st", x2t_free[par])])
                atok_free[tt] = vn
                x1t_free[tt] = vn

                def f(eng, vn=vn, n=n, tt=tt, par=par):
                    eng.wait_ge(sm["v"], vn)
                    eng.dma_start(out=x2_o[ds(n * 512 + tt * 128, 128), :],
                                  in_=x2t[par][:]).then_inc(sm["st"], 16)
                p.on("sync", f)
                x2t_free[par] = p.bump("st", 16)
        st_wo = p.val("st")

        em = LnRouterEmitter(p, sm, nc, ctx, "c", psT=psT2, psL=psL,
                             xt=[x1t4[0][:], x1t4[1][:]],
                             lnb=[atok[0][:], atok[1][:]],
                             fmb=[atok[2][:], atok[3][:]],
                             sq=afull[:, 0:1024])

        def f(eng, st_wo=st_wo):
            eng.wait_ge(sm["st"], st_wo)
        p.on("sync", f)
        def f(eng, st_wo=st_wo, lw_id=lw_id):
            eng.wait_ge(sm["st"], st_wo)
            eng.wait_ge(sm["lw"], lw_id)
        p.on("tensor", f)
        def f(eng, st_wo=st_wo):
            eng.wait_ge(sm["st"], st_wo)
        p.on("vector", f)
        def f(eng, st_wo=st_wo):
            eng.wait_ge(sm["st"], st_wo)
        p.on("scalar", f)
        for i in range(NT):
            em.tile(i, x2_o, ln3_o, sm2_o, wr_sb, brrep, eps_sb, ident)

        def f(eng, nst=p.val("st")):
            eng.wait_ge(sm["st"], nst)
        p.on("sync", f)

        p.replay(blk)
    return nc
def _shard_x(x):
    out = []
    for c in range(NCORES):
        b, h = c // 2, c % 2
        out.append(np.ascontiguousarray(x[b, h * T:(h + 1) * T, :]))
    return out


def _prep_B_host(inputs):
    hp = {}
    wup = np.asarray(inputs["L0_ffn_W_up"], np.float32)
    hp["wupP"] = np.ascontiguousarray(np.pad(wup, [(0, 0), (0, 0), (0, FPAD - 2730)]))
    bup = np.pad(np.asarray(inputs["L0_ffn_b_up"], np.float32), [(0, 0), (0, FPAD - 2730)])
    hp["bupP"] = np.ascontiguousarray(bup.reshape(4, 22, 128).transpose(2, 0, 1).reshape(128, 88))
    wdn = np.asarray(inputs["L0_ffn_W_down"], np.float32)
    hp["wdnP"] = np.ascontiguousarray(np.pad(wdn, [(0, 0), (0, FPAD - 2730), (0, 0)]))
    bdn = np.asarray(inputs["L0_ffn_b_down"], np.float32)
    hp["bdnP"] = np.ascontiguousarray(bdn.reshape(8, 8, 128).transpose(2, 0, 1).reshape(128, 64))
    for nm, key in [("wqT", "L0_attn_Wq"), ("wkT", "L0_attn_Wk"), ("wvT", "L0_attn_Wv")]:
        hp[nm] = np.ascontiguousarray(np.asarray(inputs[key], np.float32).T)
    for nm, key in [("bq_r", "L0_attn_bq"), ("bk_r", "L0_attn_bk"), ("bv_r", "L0_attn_bv")]:
        hp[nm] = np.ascontiguousarray(
            np.asarray(inputs[key], np.float32).reshape(8, 128).T)
    hp["gruW"] = np.ascontiguousarray(np.asarray(inputs["L0_attn_gru_W"], np.float32))
    hp["gruB"] = np.ascontiguousarray(np.asarray(inputs["L0_attn_gru_b"], np.float32).reshape(8, 1))
    hp["gruC"] = np.ascontiguousarray(np.asarray(inputs["L0_attn_gru_const"], np.float32).reshape(1, 16))
    hp["g48a_i"] = np.array([[1], [1], [1], [1], [0], [0], [0], [0]], np.float32)
    hp["g48b_i"] = np.array([[0], [0], [0], [0], [1], [1], [1], [1]], np.float32)
    M = np.zeros((128, 128), np.float32)
    for g in range(8):
        M[g * 16:(g + 1) * 16, g * 16:(g + 1) * 16] = 1.0
    hp["grpsum"] = M
    Ssel = np.zeros((128, 8), np.float32)
    for ee in range(8):
        Ssel[ee * 16, ee] = 1.0
    hp["thrsel"] = Ssel
    hp["eoff_i"] = (np.arange(8, dtype=np.float32) * C_CAP).reshape(8, 1)
    hp["bnd_i"] = ((np.arange(8, dtype=np.float32) + 1) * C_CAP).reshape(8, 1)
    hp["idf32p"] = np.eye(128, dtype=np.float32)
    return hp


def _sm_global(sm_cores):
    out = np.empty((B * S, E), np.float32)
    for c in range(NCORES):
        b, h = c // 2, c % 2
        out[b * S + h * T:(b * S + h * T) + T] = sm_cores[c]
    return out


def _tlog(tag, t0):
    import os, time, sys
    if os.environ.get("KTIME"):
        print(f"[ktime] {tag}: {time.perf_counter() - t0:.2f}s", file=sys.stderr)
    return time.perf_counter()


def kernel(**inputs):
    import time as _time
    _t0 = _time.perf_counter()
    x0 = np.asarray(inputs["L0_attn_hidden_states"], np.float32)
    consts = _host_consts()
    shards = _shard_x(x0)

    ncA = build_A()
    wr = np.ascontiguousarray(inputs["L0_ffn_Wr"], dtype=np.float32)
    br = np.ascontiguousarray(inputs["L0_ffn_br"], dtype=np.float32).reshape(1, E)
    mapsA = [dict(x=shards[c], wr=wr, br=br, identp=consts["ident"]) for c in range(NCORES)]
    resA = run_bass_kernel_spmd(ncA, mapsA, core_ids=list(range(NCORES)))
    _t0 = _tlog("A", _t0)

    sm_cores = [np.asarray(resA.results[c]["sm_o"]) for c in range(NCORES)]
    ln_cores = [np.asarray(resA.results[c]["ln_o"]) for c in range(NCORES)]
    sm_all = _sm_global(sm_cores)
    sm_resh = np.ascontiguousarray(
        sm_all.reshape(16, 2048, 8).transpose(0, 2, 1).reshape(16, 8, 2048)
    )
    sm_resh = np.empty((128, 2048), np.float32)
    for pp in range(128):
        sm_resh[pp] = sm_all[(pp % 16) * 2048:(pp % 16 + 1) * 2048, pp // 16]

    hp = _prep_B_host(inputs)
    import os
    ncB = build_B(dbg=bool(os.environ.get('KDBG')))
    mapsB = []
    for c in range(NCORES):
        d = dict(hp)
        d["x0"] = shards[c]
        d["ln1"] = ln_cores[c]
        d["sm_resh"] = sm_resh
        d["smT_loc"] = np.ascontiguousarray(sm_cores[c].T)
        d["sm_loc"] = sm_cores[c]
        d["identp"] = consts["ident"]
        mapsB.append(d)
    resB = run_bass_kernel_spmd(ncB, mapsB, core_ids=list(range(NCORES)))
    _t0 = _tlog("B", _t0)

    def as_u16(a):
        a = np.asarray(a)
        if a.dtype == np.uint16:
            return a
        return a.view(np.uint16) if a.dtype == bf16 else a.astype(bf16).view(np.uint16)

    kTs = [as_u16(resB.results[c]["kT_o"]) for c in range(NCORES)]
    vTs = [as_u16(resB.results[c]["vT_o"]) for c in range(NCORES)]
    qTs = [as_u16(resB.results[c]["qT_o"]) for c in range(NCORES)]
    gates = [np.asarray(resB.results[c]["gate_o"], np.float32) for c in range(NCORES)]
    x1s = [np.asarray(resB.results[c]["x1_o"], np.float32) for c in range(NCORES)]

    ones_bf16_u16 = np.full((NH, 64, 128, 1), 0x3F80, np.uint16)
    krevs, vtoks = {}, {}
    for bb in range(B):
        kfull = np.concatenate([kTs[2 * bb], kTs[2 * bb + 1]], axis=1)
        krevs[bb] = np.ascontiguousarray(
            kfull.reshape(D_MODEL, 64, 128)[:, :, ::-1].reshape(D_MODEL, S))
        vfull = np.concatenate([vTs[2 * bb], vTs[2 * bb + 1]], axis=1)
        vh = vfull.reshape(NH, 64, S).transpose(0, 2, 1)
        vrev = vh.reshape(NH, 64, 128, 64)[:, :, ::-1, :]
        vtoks[bb] = np.ascontiguousarray(
            np.concatenate([vrev, ones_bf16_u16], axis=3).reshape(NH, 64 * 128 * 65))

    btab = _bucket_table()
    re_in = np.asarray(inputs["L0_attn_rel_embed"], np.float32)
    pb = re_in[btab]
    pbrev = pb[::-1]
    woT = np.ascontiguousarray(np.asarray(inputs["L0_attn_Wo"], np.float32).T)
    bo_row = np.ascontiguousarray(np.asarray(inputs["L0_attn_bo"], np.float32).reshape(1, D_MODEL))
    onesf = np.ones((1, 64), np.float32)

    import os as _os
    ncC = build_C(dbg=bool(_os.environ.get('KDBG')))
    mapsC = []
    for c in range(NCORES):
        bb, hh = c // 2, c % 2
        q0 = hh * T
        pbc = np.zeros((NH, 12416), np.float32)
        avail = min(12416, 16383 - q0)
        pbc[:, 0:avail] = pbrev[q0:q0 + avail, :].T
        d = dict(
            x1=x1s[c], qt8=qTs[c], krevT=krevs[bb], vtok=vtoks[bb],
            gate_i=np.ascontiguousarray(gates[c]),
            pbrevc=np.ascontiguousarray(pbc), woT=woT, bo_i=bo_row,
            wr=wr, br=br, identp=consts["ident"], onesf_i=onesf,
        )
        mapsC.append(d)
    resC = run_bass_kernel_spmd(ncC, mapsC, core_ids=list(range(NCORES)))
    _t0 = _tlog("C", _t0)

    sm2_cores = [np.asarray(resC.results[c]["sm2_o"], np.float32) for c in range(NCORES)]
    ln3_cores = [np.asarray(resC.results[c]["ln3_o"]) for c in range(NCORES)]
    x2_cores = [np.asarray(resC.results[c]["x2_o"], np.float32) for c in range(NCORES)]
    sm2_all = _sm_global(sm2_cores)
    sm2_resh = np.empty((128, 2048), np.float32)
    for pp in range(128):
        sm2_resh[pp] = sm2_all[(pp % 16) * 2048:(pp % 16 + 1) * 2048, pp // 16]
    mapsD = []
    for c in range(NCORES):
        d = dict(mapsB[c])
        d["x0"] = np.ascontiguousarray(x2_cores[c])
        d["ln1"] = np.ascontiguousarray(ln3_cores[c])
        d["sm_resh"] = sm2_resh
        d["smT_loc"] = np.ascontiguousarray(sm2_cores[c].T)
        d["sm_loc"] = sm2_cores[c]
        mapsD.append(d)
    ncD = build_B(mode="D")
    resD = run_bass_kernel_spmd(ncD, mapsD, core_ids=list(range(NCORES)))
    _t0 = _tlog("D", _t0)

    out = np.empty((B, S, D_MODEL), np.float32)
    for c in range(NCORES):
        bb, hh = c // 2, c % 2
        out[bb, hh * T:(hh + 1) * T, :] = np.asarray(resD.results[c]["x1_o"], np.float32)
    if _os.environ.get("KDBG"):
        kernel._dbg = dict(resA=resA, resB=resB, resC=resC, resD=resD, mapsC=mapsC)
    return out


# version 04 (final)
# speedup vs baseline: 1.3536x; 1.2979x over previous
import math
from contextlib import ExitStack

import numpy as np
import ml_dtypes
import concourse.bass as bass
import concourse.mybir as mybir
from concourse.bass_utils import run_bass_kernel_spmd

F32 = mybir.dt.float32
BF16 = mybir.dt.bfloat16
I32 = mybir.dt.int32
U32 = mybir.dt.uint32
AF = mybir.ActivationFunctionType
ALU = mybir.AluOpType
ds = bass.ds
AX = mybir.AxisListType

B, S, D_MODEL = 4, 8192, 1024
NCORES = 8
T = 4096
NT = T // 128
E = 8
KSEL = 4096
C_CAP = 640
FPAD = 2816
NH, HD = 16, 64
NB, MAXD = 320, 800

bf16 = ml_dtypes.bfloat16


class Prog:

    ENGS = ("sync", "gpsimd", "vector", "scalar", "tensor")

    def __init__(self, sm):
        self.streams = {e: [] for e in self.ENGS}
        self.counts = {k: 0 for k in sm}
        self.sm = sm

    def on(self, eng, fn):
        self.streams[eng].append(fn)

    def bump(self, sem, n=1):
        self.counts[sem] += n
        return self.counts[sem]

    def val(self, sem):
        return self.counts[sem]

    def fenced(self, engname, semname, ops, pre_waits=()):
        base = self.val(semname)
        smd = self.sm

        def fn(eng, ops=tuple(ops), base=base, pre=tuple(pre_waits), semname=semname):
            for ws, wv in pre:
                eng.wait_ge(smd[ws], wv)
            if base:
                eng.wait_ge(smd[semname], base)
            for j, op in enumerate(ops):
                if j > 0:
                    eng.wait_ge(smd[semname], base + j)
                op(eng).then_inc(smd[semname], 1)
        self.on(engname, fn)
        return self.bump(semname, len(ops))

    def replay(self, blk):
        for name in self.ENGS:
            stream = self.streams[name]
            if not stream:
                continue
            dec = getattr(blk, name)

            def make(stream):
                def _body(eng):
                    for fn in stream:
                        fn(eng)
                return _body
            dec(make(stream))


def sem_dict(ctx, nc, names):
    return {n: ctx.enter_context(nc.semaphore(n)) for n in names}


def _host_consts():
    t = {}
    t["ident"] = np.eye(128, dtype=bf16)
    return t


def _bucket_table():
    nb = NB // 2
    rel = np.arange(-(S - 1), S)
    buckets = (rel > 0).astype(np.int64) * nb
    rel_abs = np.abs(rel)
    mx = nb // 2
    small = rel_abs < mx
    large = np.log(np.maximum(rel_abs, 1).astype(np.float32) / mx) / math.log(MAXD / mx) * (nb - mx)
    large = (mx + large).astype(np.int64)
    large = np.minimum(large, nb - 1)
    return buckets + np.where(small, rel_abs, large)


class LnRouterEmitter:

    def __init__(self, p, sm, nc, ctx, tag, psT=None, psL=None, xt=None, lnb=None,
                 fmb=None, sq=None):
        e = ctx.enter_context
        self.p, self.sm = p, sm
        self.xt = xt or [e(nc.sbuf_tensor(f"{tag}xt{i}", [128, 1024], F32)) for i in range(2)]
        self.lnb = lnb or [e(nc.sbuf_tensor(f"{tag}lnb{i}", [128, 1024], BF16)) for i in range(2)]
        self.fmb = fmb or [e(nc.sbuf_tensor(f"{tag}fmb{i}", [128, 1024], BF16)) for i in range(2)]
        self.sum1 = e(nc.sbuf_tensor(f"{tag}sum1", [128, 1], F32))
        self.mean = e(nc.sbuf_tensor(f"{tag}mean", [128, 1], F32))
        self.sq = sq if sq is not None else e(nc.sbuf_tensor(f"{tag}sq", [128, 1024], BF16))
        self.ssq = e(nc.sbuf_tensor(f"{tag}ssq", [128, 1], F32))
        self.var = e(nc.sbuf_tensor(f"{tag}var", [128, 1], F32))
        self.rstd = e(nc.sbuf_tensor(f"{tag}rstd", [128, 1], F32))
        self.nmr = e(nc.sbuf_tensor(f"{tag}nmr", [128, 1], F32))
        self.smt = e(nc.sbuf_tensor(f"{tag}smt", [128, 8], F32))
        self.mx1 = e(nc.sbuf_tensor(f"{tag}mx1", [128, 1], F32))
        self.se1 = e(nc.sbuf_tensor(f"{tag}se1", [128, 1], F32))
        if psT is None:
            self.psT = [e(nc.psum_tensor(f"{tag}psT{i}", [128, 128], BF16)) for i in range(2)]
            self.psL = e(nc.psum_tensor(f"{tag}psL", [128, 8], F32))
        else:
            self.psT = psT
            self.psL = psL
        self.xt_free = [0, 0]
        self.lnb_free = [0, 0]
        self.fmb_free = [0, 0]

    def tile(self, i, x_dram, ln_out, sm_out, wr_sb, brrep, eps_sb, ident,
             extra_ln_sink=None):
        p, sm = self.p, self.sm
        par = i % 2
        xti, lnbi, fmbi = self.xt[par], self.lnb[par], self.fmb[par]

        def f(eng, i=i, xti=xti, wv=self.xt_free[par]):
            if wv:
                eng.wait_ge(sm["v"], wv)
            eng.dma_start(out=xti[:], in_=x_dram[ds(i * 128, 128), :]).then_inc(sm["ld"], 16)
        p.on("sync", f)
        lv = p.bump("ld", 16)

        vn = p.fenced("vector", "v", [
            lambda eng, xti=xti: eng.reduce_sum(self.sum1[:], xti[:], axis=AX.X),
        ], pre_waits=[("ld", lv)])

        sn = p.fenced("scalar", "s", [
            lambda eng, xti=xti: eng.activation(self.sq[:], xti[:], AF.Square,
                                                bias=0.0, scale=1.0,
                                                accum_out=self.ssq[:]),
        ], pre_waits=[("ld", lv), ("v", p.val("v"))])

        vn = p.fenced("vector", "v", [
            lambda eng: eng.tensor_scalar(self.mean[:], self.sum1[:], 1.0 / 1024,
                                          None, ALU.mult),
            lambda eng: eng.tensor_mul(self.nmr[:], self.mean[:], self.mean[:]),
            lambda eng: eng.scalar_tensor_tensor(self.var[:], self.ssq[:], 1.0 / 1024,
                                                 self.nmr[:], ALU.mult, ALU.subtract),
        ], pre_waits=[("v", vn), ("s", sn)])

        sn = p.fenced("scalar", "s", [
            lambda eng: eng.activation(self.rstd[:], self.var[:], AF.Sqrt,
                                       bias=eps_sb[:], scale=1.0),
        ], pre_waits=[("v", vn)])

        vn = p.fenced("vector", "v", [
            lambda eng: eng.reciprocal(self.rstd[:], self.rstd[:]),
            lambda eng: eng.tensor_mul(self.nmr[:], self.mean[:], self.rstd[:]),
            lambda eng: eng.tensor_scalar(self.nmr[:], self.nmr[:], -1.0, None, ALU.mult),
            lambda eng, xti=xti, lnbi=lnbi: eng.tensor_scalar(
                lnbi[:], xti[:], self.rstd[:], self.nmr[:], ALU.mult, ALU.add),
        ], pre_waits=[("s", sn), ("t", self.lnb_free[par])])
        self.xt_free[par] = vn

        def f(eng, vn=vn, lnbi=lnbi, i=i):
            eng.wait_ge(sm["v"], vn)
            eng.dma_start(out=ln_out[ds(i * 128, 128), :], in_=lnbi[:]).then_inc(sm["st"], 16)
        p.on("sync", f)
        p.bump("st", 16)
        if extra_ln_sink is not None:
            extra_ln_sink(lnbi, vn)

        tbase = p.val("t")
        vbase0 = p.val("v")

        def f(eng, vn=vn, lnbi=lnbi, tbase=tbase, vbase0=vbase0, ident=ident):
            eng.wait_ge(sm["v"], vn)
            for c in range(8):
                if c >= 2:
                    eng.wait_ge(sm["v"], vbase0 + c - 1)
                eng.transpose(self.psT[c % 2][:], lnbi[:, ds(c * 128, 128)],
                              ident[:]).then_inc(sm["t"], 1)
        p.on("tensor", f)
        p.bump("t", 8)

        def f(eng, tbase=tbase, fmbi=fmbi):
            for c in range(8):
                eng.wait_ge(sm["t"], tbase + c + 1)
                eng.tensor_copy(fmbi[:, ds(c * 128, 128)], self.psT[c % 2][:]).then_inc(sm["v"], 1)
        p.on("vector", f)
        p.bump("v", 8)
        vafter = p.val("v")

        def f(eng, vafter=vafter, fmbi=fmbi, wr_sb=wr_sb):
            eng.wait_ge(sm["v"], vafter)
            for c in range(8):
                inst = eng.matmul(self.psL[:], fmbi[:, ds(c * 128, 128)],
                                  wr_sb[:, ds(c * 8, 8)], start=(c == 0), stop=(c == 7))
            inst.then_inc(sm["t"], 1)
        p.on("tensor", f)
        tn = p.bump("t")
        self.lnb_free[par] = tn
        self.fmb_free[par] = tn

        vn = p.fenced("vector", "v", [
            lambda eng: eng.tensor_add(self.smt[:], self.psL[:], brrep[:]),
            lambda eng: eng.reduce_max(self.mx1[:], self.smt[:], axis=AX.X, negate=True),
        ], pre_waits=[("t", tn)])

        def f(eng, vn=vn):
            eng.wait_ge(sm["v"], vn)
            eng.activation(self.smt[:], self.smt[:], AF.Exp, bias=self.mx1[:], scale=1.0,
                           accum_out=self.se1[:]).then_inc(sm["s"], 1)
        p.on("scalar", f)
        sn = p.bump("s")

        vn = p.fenced("vector", "v", [
            lambda eng: eng.reciprocal(self.se1[:], self.se1[:]),
            lambda eng: eng.tensor_scalar(self.smt[:], self.smt[:], self.se1[:],
                                          None, ALU.mult),
        ], pre_waits=[("s", sn)])

        def f(eng, vn=vn, i=i):
            eng.wait_ge(sm["v"], vn)
            eng.dma_start(out=sm_out[ds(i * 128, 128), :], in_=self.smt[:]).then_inc(sm["st"], 16)
        p.on("sync", f)
        p.bump("st", 16)


def build_A():
    nc = bass.Bass("TRN2", target_bir_lowering=False, debug=False)
    x = nc.dram_tensor("x", [T, D_MODEL], F32, kind="ExternalInput").ap()
    wr = nc.dram_tensor("wr", [D_MODEL, E], F32, kind="ExternalInput").ap()
    br = nc.dram_tensor("br", [1, E], F32, kind="ExternalInput").ap()
    identp = nc.dram_tensor("identp", [128, 128], BF16, kind="ExternalInput").ap()
    ln_o = nc.dram_tensor("ln_o", [T, D_MODEL], BF16, kind="ExternalOutput").ap()
    sm_o = nc.dram_tensor("sm_o", [T, E], F32, kind="ExternalOutput").ap()

    with ExitStack() as ctx:
        e = ctx.enter_context
        wr_sb = e(nc.sbuf_tensor("wr_sb", [128, 64], BF16))
        brrep = e(nc.sbuf_tensor("brrep", [128, 8], F32))
        brrow = e(nc.sbuf_tensor("brrow", [1, 8], F32))
        ones1 = e(nc.sbuf_tensor("ones1", [1, 128], F32))
        eps_sb = e(nc.sbuf_tensor("eps_sb", [128, 1], F32))
        ident = e(nc.sbuf_tensor("ident", [128, 128], BF16))
        psB = e(nc.psum_tensor("psB", [128, 8], F32))

        sm = sem_dict(ctx, nc, ["ld", "v", "t", "s", "st"])
        blk = e(nc.Block())
        p = Prog(sm)

        def f(eng):
            eng.dma_start(out=brrow[:], in_=br).then_inc(sm["ld"], 16)
            eng.dma_start(out=ident[:], in_=identp).then_inc(sm["ld"], 16)
            eng.dma_start(out=wr_sb[:].rearrange("p (c e) -> p c e", e=8),
                          in_=wr.rearrange("(c p) e -> p c e", p=128)).then_inc(sm["ld"], 16)
        p.on("gpsimd", f)
        ldn = p.bump("ld", 48)

        def f(eng):
            eng.memset(eps_sb[:], 1e-5)
            eng.memset(ones1[:], 1.0).then_inc(sm["v"], 1)
        p.on("vector", f)
        vn = p.bump("v")

        def f(eng, ldn=ldn, vn=vn):
            eng.wait_ge(sm["ld"], ldn)
            eng.wait_ge(sm["v"], vn)
            eng.matmul(psB[:], ones1[:], brrow[:], start=True, stop=True).then_inc(sm["t"], 1)
        p.on("tensor", f)
        tn = p.bump("t")

        def f(eng, tn=tn):
            eng.wait_ge(sm["t"], tn)
            eng.tensor_copy(brrep[:], psB[:]).then_inc(sm["v"], 1)
        p.on("vector", f)
        p.bump("v")

        em = LnRouterEmitter(p, sm, nc, ctx, "a")
        for i in range(NT):
            em.tile(i, x, ln_o, sm_o, wr_sb, brrep, eps_sb, ident)

        def f(eng, n=p.val("st")):
            eng.wait_ge(sm["st"], n)
        p.on("sync", f)

        p.replay(blk)
    return nc



def build_B(dbg=False, mode="B"):
    nc = bass.Bass("TRN2", target_bir_lowering=False, debug=False)
    x0 = nc.dram_tensor("x0", [T, D_MODEL], F32, kind="ExternalInput").ap()
    ln1 = nc.dram_tensor("ln1", [T, D_MODEL], BF16, kind="ExternalInput").ap()
    sm_resh = nc.dram_tensor("sm_resh", [128, 2048], F32, kind="ExternalInput").ap()
    smT_loc = nc.dram_tensor("smT_loc", [E, T], F32, kind="ExternalInput").ap()
    sm_loc = nc.dram_tensor("sm_loc", [T, E], F32, kind="ExternalInput").ap()
    wupP = nc.dram_tensor("wupP", [4, D_MODEL, FPAD], F32, kind="ExternalInput").ap()
    bupP = nc.dram_tensor("bupP", [128, 88], F32, kind="ExternalInput").ap()
    wdnP = nc.dram_tensor("wdnP", [E, FPAD, D_MODEL], F32, kind="ExternalInput").ap()
    bdnP = nc.dram_tensor("bdnP", [128, 64], F32, kind="ExternalInput").ap()
    wqT = nc.dram_tensor("wqT", [D_MODEL, D_MODEL], F32, kind="ExternalInput").ap()
    wkT = nc.dram_tensor("wkT", [D_MODEL, D_MODEL], F32, kind="ExternalInput").ap()
    wvT = nc.dram_tensor("wvT", [D_MODEL, D_MODEL], F32, kind="ExternalInput").ap()
    bq_r = nc.dram_tensor("bq_r", [128, 8], F32, kind="ExternalInput").ap()
    bk_r = nc.dram_tensor("bk_r", [128, 8], F32, kind="ExternalInput").ap()
    bv_r = nc.dram_tensor("bv_r", [128, 8], F32, kind="ExternalInput").ap()
    gruW = nc.dram_tensor("gruW", [8, 64], F32, kind="ExternalInput").ap()
    gruB = nc.dram_tensor("gruB", [8, 1], F32, kind="ExternalInput").ap()
    gruC = nc.dram_tensor("gruC", [1, 16], F32, kind="ExternalInput").ap()
    g48a_i = nc.dram_tensor("g48a_i", [8, 1], F32, kind="ExternalInput").ap()
    g48b_i = nc.dram_tensor("g48b_i", [8, 1], F32, kind="ExternalInput").ap()
    grpsum = nc.dram_tensor("grpsum", [128, 128], F32, kind="ExternalInput").ap()
    thrsel = nc.dram_tensor("thrsel", [128, 8], F32, kind="ExternalInput").ap()
    eoff_i = nc.dram_tensor("eoff_i", [8, 1], F32, kind="ExternalInput").ap()
    bnd_i = nc.dram_tensor("bnd_i", [8, 1], F32, kind="ExternalInput").ap()
    identp = nc.dram_tensor("identp", [128, 128], BF16, kind="ExternalInput").ap()
    idf32p = nc.dram_tensor("idf32p", [128, 128], F32, kind="ExternalInput").ap()

    x1_o = nc.dram_tensor("x1_o", [T, D_MODEL], F32, kind="ExternalOutput").ap()
    qT_o = nc.dram_tensor("qT_o", [D_MODEL, T], BF16, kind="ExternalOutput").ap()
    kT_o = nc.dram_tensor("kT_o", [D_MODEL, T], BF16, kind="ExternalOutput").ap()
    vT_o = nc.dram_tensor("vT_o", [D_MODEL, T], BF16, kind="ExternalOutput").ap()
    gate_o = nc.dram_tensor("gate_o", [NH, T], F32, kind="ExternalOutput").ap()
    thr_o = nc.dram_tensor("thr_o", [8, 1], F32, kind="ExternalOutput").ap()

    kind_dbg = "ExternalOutput" if dbg else "Internal"
    gath = nc.dram_tensor("gath", [E * C_CAP, 1040], BF16, kind=kind_dbg).ap()
    posS_o = nc.dram_tensor("posS_o", [128, 8 * NT], U32, kind=kind_dbg).ap()
    posG_o = nc.dram_tensor("posG_o", [128, 8 * NT], U32, kind=kind_dbg).ap()
    outb = nc.dram_tensor("outb", [E * C_CAP + 128, D_MODEL], F32, kind=kind_dbg).ap()

    with ExitStack() as ctx:
        e = ctx.enter_context
        lo = e(nc.sbuf_tensor("lo", [128, 1], F32))
        hi = e(nc.sbuf_tensor("hi", [128, 1], F32))
        mid = e(nc.sbuf_tensor("mid", [128, 1], F32))
        cnt = e(nc.sbuf_tensor("cnt", [128, 1], F32))
        cond = e(nc.sbuf_tensor("cond", [128, 1], F32))
        ncond = e(nc.sbuf_tensor("ncond", [128, 1], F32))
        tmp1 = e(nc.sbuf_tensor("tmp1", [128, 1], F32))
        tmp2 = e(nc.sbuf_tensor("tmp2", [128, 1], F32))
        grp_sb = e(nc.sbuf_tensor("grp_sb", [128, 128], F32))
        tsel_sb = e(nc.sbuf_tensor("tsel_sb", [128, 8], F32))
        thr8 = e(nc.sbuf_tensor("thr8", [8, 1], F32))
        eoff_sb = e(nc.sbuf_tensor("eoff_sb", [8, 1], F32))
        bnd_sb = e(nc.sbuf_tensor("bnd_sb", [8, 1], F32))
        posS = e(nc.sbuf_tensor("posS", [128, 8 * NT], U32))
        posG = e(nc.sbuf_tensor("posG", [128, 8 * NT], U32))
        ident = e(nc.sbuf_tensor("ident", [128, 128], BF16))
        idf32 = e(nc.sbuf_tensor("idf32", [128, 128], F32))
        stg = [e(nc.sbuf_tensor(f"stg{i}", [128, 1040], BF16)) for i in range(2)]

        arena = e(nc.sbuf_tensor("arena", [128, 90112], mybir.dt.uint8))
        wexp = [arena[:, ds(i * 45056, 45056)].bitcast(BF16) for i in range(2)]
        ln2fm = arena[:, 0:65536].bitcast(BF16)
        xq = arena[:, 65536:81920].bitcast(F32)
        qkvw = [arena[:, ds(81920 + i * 2048, 2048)].bitcast(BF16) for i in range(2)]
        a4 = arena[:, 0:65536].bitcast(F32)
        smT = a4[0:8, 0:4096]
        maskT = a4[0:8, 4096:8192]
        psc = a4[0:8, 8192:12288]
        pgc = a4[0:8, 12288:16384]

        Xg = e(nc.sbuf_tensor("Xg", [128, 5120], BF16))
        midT = e(nc.sbuf_tensor("midT", [128, 14080], BF16))
        vals = midT[:, 0:4096].bitcast(F32)
        cmpd = Xg[:, 0:2048]
        dnf = e(nc.sbuf_tensor("dnf", [128, 640], F32))
        outT = e(nc.sbuf_tensor("outT", [128, 5120], F32))
        gt = [e(nc.sbuf_tensor(f"gt{j}", [128, 1040], BF16)) for j in range(5)]
        gv = e(nc.sbuf_tensor("gv", [128, 8], F32))
        zrow = e(nc.sbuf_tensor("zrow", [128, 1024], F32))
        bup_sb = e(nc.sbuf_tensor("bup_sb", [128, 88], F32))
        bdn_sb = e(nc.sbuf_tensor("bdn_sb", [128, 64], F32))
        bq_sb = e(nc.sbuf_tensor("bq_sb", [128, 8], F32))
        bk_sb = e(nc.sbuf_tensor("bk_sb", [128, 8], F32))
        bv_sb = e(nc.sbuf_tensor("bv_sb", [128, 8], F32))
        bq8_sb = e(nc.sbuf_tensor("bq8_sb", [128, 8], F32))
        lnt = e(nc.sbuf_tensor("lnt", [128, 1024], BF16))
        qkvo = [e(nc.sbuf_tensor(f"qkvo{i}", [128, 512], BF16)) for i in range(2)]

        gruW_sb = e(nc.sbuf_tensor("gruW_sb", [8, 64], F32))
        gruB_sb = e(nc.sbuf_tensor("gruB_sb", [8, 1], F32))
        gruC_sb = e(nc.sbuf_tensor("gruC_sb", [1, 16], F32))
        g48a_sb = e(nc.sbuf_tensor("g48a_sb", [8, 1], F32))
        g48b_sb = e(nc.sbuf_tensor("g48b_sb", [8, 1], F32))
        wa2 = e(nc.sbuf_tensor("wa2", [128, 1], BF16))
        wb2 = e(nc.sbuf_tensor("wb2", [128, 1], BF16))
        ba_sb = e(nc.sbuf_tensor("ba_sb", [1, 1], F32))
        bb_sb = e(nc.sbuf_tensor("bb_sb", [1, 1], F32))
        sga = e(nc.sbuf_tensor("sga", [1, 512], F32))
        sgb = e(nc.sbuf_tensor("sgb", [1, 512], F32))
        grow = e(nc.sbuf_tensor("grow", [1, 512], F32))

        sum1 = e(nc.sbuf_tensor("sum1", [128, 1], F32))
        mean = e(nc.sbuf_tensor("mean", [128, 1], F32))
        sq = e(nc.sbuf_tensor("sq", [128, 1024], BF16))
        ssq = e(nc.sbuf_tensor("ssq", [128, 1], F32))
        var = e(nc.sbuf_tensor("var", [128, 1], F32))
        rstd = e(nc.sbuf_tensor("rstd", [128, 1], F32))
        nmr = e(nc.sbuf_tensor("nmr", [128, 1], F32))
        eps_sb = e(nc.sbuf_tensor("eps_sb", [128, 1], F32))

        psA = e(nc.psum_tensor("psA", [128, 4096], F32))
        psUP = [psA[:, 0:1024], psA[:, 1024:2048]]
        psT2 = [psA[:, ds(2048 + i * 512, 512)].bitcast(BF16)[:, 0:128] for i in range(2)]
        psF2 = [psA[:, 2048:2176], psA[:, 2560:2688]]
        psC = psA[:, 3072:3073]
        psT8 = psA[0:8, 3074:3075]
        psWa = psA[0:64, 3076:3077]
        psWb = psA[0:64, 3078:3079]
        psBa = psA[0:1, 3080:3081]
        psBb = psA[0:1, 3082:3083]
        psGa = psA[0:1, 3072:3584]
        psGb = psA[0:1, 3584:4096]
        psQ = [psA[:, ds(i * 512, 512)] for i in range(4)]

        sm = sem_dict(ctx, nc, ["ld", "lw", "v", "t", "s", "st", "gq", "ga"])
        blk = e(nc.Block())
        p = Prog(sm)
        REGS = {}

        def bound_reg(eng, key, value):
            if key not in REGS:
                REGS[key] = eng.to_reg(value)
            return REGS[key]

        def f(eng):
            for dst, src in [(vals, sm_resh), (smT, smT_loc), (grp_sb, grpsum),
                             (tsel_sb, thrsel), (eoff_sb, eoff_i), (bnd_sb, bnd_i),
                             (bup_sb, bupP), (bdn_sb, bdnP), (bq_sb, bq_r),
                             (bk_sb, bk_r), (bv_sb, bv_r), (gruW_sb, gruW),
                             (gruB_sb, gruB), (gruC_sb, gruC), (g48a_sb, g48a_i),
                             (g48b_sb, g48b_i), (idf32, idf32p)]:
                eng.dma_start(out=dst[:], in_=src).then_inc(sm["ld"], 16)
        p.on("sync", f)
        ld_pro = p.bump("ld", 16 * 17)

        def f(eng):
            eng.dma_start(out=ident[:], in_=identp).then_inc(sm["lw"], 16)
        p.on("gpsimd", f)
        lw_id = p.bump("lw", 16)

        vz = p.fenced("vector", "v", [
            lambda eng: eng.memset(lo[:], 0.0),
            lambda eng: eng.memset(hi[:], 1.0000001),
            lambda eng: eng.memset(mid[:], 0.50000006),
            lambda eng: eng.memset(zrow[:], 0.0),
            lambda eng: eng.memset(eps_sb[:], 1e-5),
        ])

        def f(eng, vz=vz):
            eng.wait_ge(sm["v"], vz)
            eng.dma_start(out=outb[ds(E * C_CAP, 128), :], in_=zrow[:]).then_inc(sm["st"], 16)
        p.on("sync", f)
        st_zero = p.bump("st", 16)

        NITER = 48
        for it in range(NITER):
            vn = p.fenced("vector", "v", [
                lambda eng: eng.tensor_scalar(cmpd[:], vals[:], mid[:], 0.0,
                                              ALU.is_ge, ALU.add, accum_out=cnt[:]),
            ], pre_waits=[("ld", 16 * 2)] if it == 0 else ())

            def f(eng, vn=vn, first=(it == 0)):
                eng.wait_ge(sm["v"], vn)
                if first:
                    eng.wait_ge(sm["ld"], ld_pro)
                eng.matmul(psC, grp_sb[:], cnt[:], start=True, stop=True).then_inc(sm["t"], 1)
            p.on("tensor", f)
            tn = p.bump("t")

            p.fenced("vector", "v", [
                lambda eng: eng.tensor_scalar(cond[:], psC, 4096.0, None, ALU.is_ge),
                lambda eng: eng.tensor_scalar(ncond[:], cond[:], -1.0, 1.0, ALU.mult, ALU.add),
                lambda eng: eng.tensor_sub(tmp1[:], mid[:], lo[:]),
                lambda eng: eng.tensor_mul(tmp1[:], tmp1[:], cond[:]),
                lambda eng: eng.tensor_add(lo[:], lo[:], tmp1[:]),
                lambda eng: eng.tensor_sub(tmp2[:], mid[:], hi[:]),
                lambda eng: eng.tensor_mul(tmp2[:], tmp2[:], ncond[:]),
                lambda eng: eng.tensor_add(hi[:], hi[:], tmp2[:]),
                lambda eng: eng.tensor_add(mid[:], lo[:], hi[:]),
                lambda eng: eng.tensor_scalar(mid[:], mid[:], 0.5, None, ALU.mult),
            ], pre_waits=[("t", tn)])

        vlast = p.val("v")

        def f(eng, vlast=vlast):
            eng.wait_ge(sm["v"], vlast)
            eng.matmul(psT8, tsel_sb[:], lo[:], start=True, stop=True).then_inc(sm["t"], 1)
        p.on("tensor", f)
        tn = p.bump("t")
        vn = p.fenced("vector", "v", [
            lambda eng: eng.tensor_copy(thr8[:], psT8),
        ], pre_waits=[("t", tn)])

        def f(eng, vn=vn):
            eng.wait_ge(sm["v"], vn)
            eng.dma_start(out=thr_o, in_=thr8[:]).then_inc(sm["st"], 16)
        p.on("sync", f)
        p.bump("st", 16)

        DUMP = float(E * C_CAP)
        BIGF = float(1 << 26)
        vpos_chain = p.fenced("vector", "v", [
            lambda eng: eng.tensor_scalar(maskT, smT, thr8[:], None, ALU.is_ge),
            lambda eng: eng.memset(psc, 0.0),
            lambda eng: eng.tensor_tensor_scan(smT, maskT, psc, 0.0, ALU.add, ALU.add),
            lambda eng: eng.tensor_scalar(psc, smT, eoff_sb[:], -1.0, ALU.add, ALU.add),
            lambda eng: eng.tensor_mul(pgc, psc, maskT),
            lambda eng: eng.scalar_tensor_tensor(pgc, maskT, -DUMP, pgc, ALU.mult, ALU.add),
            lambda eng: eng.tensor_scalar(pgc, pgc, DUMP, None, ALU.add),
            lambda eng: eng.tensor_scalar(smT, pgc, bnd_sb[:], None, ALU.is_ge),
            lambda eng: eng.tensor_scalar(psc, psc, 0.0, None, ALU.add),
            lambda eng: eng.tensor_scalar(maskT, maskT, -1.0, 1.0, ALU.mult, ALU.add),
            lambda eng: eng.scalar_tensor_tensor(psc, maskT, BIGF, psc, ALU.mult, ALU.add),
            lambda eng: eng.tensor_scalar(maskT, smT, 0.0, None, ALU.add),
            lambda eng: eng.tensor_scalar(smT, pgc, -1.0, DUMP, ALU.mult, ALU.add),
            lambda eng: eng.tensor_mul(smT, smT, maskT),
            lambda eng: eng.tensor_add(pgc, pgc, smT),
        ], pre_waits=[("v", vn)])

        vtr_copies = p.val("v")
        for i in range(NT):
            for which, src, dstp in ((0, psc, posS), (1, pgc, posG)):
                q = i * 2 + which
                par = q % 2

                def f(eng, i=i, src=src, par=par, q=q, vpos_chain=vpos_chain,
                      vtr_copies=vtr_copies):
                    if q == 0:
                        eng.wait_ge(sm["v"], vpos_chain)
                    if q >= 2:
                        eng.wait_ge(sm["v"], vtr_copies + q - 1)
                    eng.transpose(psF2[par][:, 0:8], src[:, ds(i * 128, 128)],
                                  idf32[0:8, 0:8]).then_inc(sm["t"], 1)
                p.on("tensor", f)
                tn = p.bump("t")

                def f(eng, tn=tn, i=i, par=par, dstp=dstp):
                    eng.wait_ge(sm["t"], tn)
                    eng.tensor_copy(dstp[:, ds(i * 8, 8)], psF2[par][:, 0:8]).then_inc(sm["v"], 1)
                p.on("vector", f)
                p.bump("v")
        vpos = p.val("v")

        if True:
            def f(eng, vpos=vpos):
                eng.wait_ge(sm["v"], vpos)
                eng.dma_start(out=posS_o, in_=posS[:]).then_inc(sm["st"], 16)
                eng.dma_start(out=posG_o, in_=posG[:]).then_inc(sm["st"], 16)
            p.on("sync", f)
            p.bump("st", 32)

        for i in range(NT):
            par = i % 2
            sti = stg[par]
            gq_before = p.val("gq")

            def f(eng, i=i, sti=sti, wv=gq_before - 16 * 8):
                if wv > 0:
                    eng.wait_ge(sm["gq"], wv)
                eng.dma_start(out=sti[:, 0:1024], in_=ln1[ds(i * 128, 128), :]).then_inc(sm["ld"], 16)
                eng.dma_start(out=sti[:].bitcast(F32)[:, 512:520],
                              in_=sm_loc[ds(i * 128, 128), :]).then_inc(sm["ld"], 16)
            p.on("sync", f)
            lv = p.bump("ld", 32)

            def f(eng, i=i, sti=sti, lv=lv, vpos=vpos):
                eng.wait_ge(sm["ld"], lv)
                eng.wait_ge(sm["v"], vpos)
                for ee in range(E):
                    eng.indirect_dma_start(
                        out=gath,
                        out_offset=bass.IndirectOffsetOnAxis(ap=posS[:, ds(i * 8 + ee, 1)], axis=0),
                        in_=sti[:], in_offset=None,
                        bounds_check=bound_reg(eng, ee, (ee + 1) * C_CAP - 1),
                        oob_is_err=False,
                    ).then_inc(sm["gq"], 16)
            p.on("gpsimd", f)
            p.bump("gq", 16 * 8)
        gq_scat = p.val("gq")

        t_up_done = 0
        t_dn_done = 0
        vgv_prev = 0
        st_out_done = st_zero
        for ee in range(E):
            for j in range(5):
                def f(eng, ee=ee, j=j, gq_scat=gq_scat, wv=t_up_done, wvv=vgv_prev):
                    if j == 0:
                        eng.wait_ge(sm["gq"], gq_scat)
                    if wv:
                        eng.wait_ge(sm["t"], wv)
                    if wvv:
                        eng.wait_ge(sm["v"], wvv)
                    eng.dma_start(out=gt[j][:], in_=gath[ds(ee * C_CAP + j * 128, 128), :]).then_inc(sm["ld"], 16)
                p.on("sync", f)
            lgt = p.bump("ld", 16 * 5)

            for k in range(8):
                def f(eng, ee=ee, k=k, wv=t_up_done, vpos=vpos):
                    if wv:
                        eng.wait_ge(sm["t"], wv)
                    if ee == 0 and k == 0:
                        eng.wait_ge(sm["v"], vpos)
                        eng.wait_ge(sm["gq"], gq_scat)
                    eng.dma_start(out=wexp[0][:, ds(k * 2816, 2816)],
                                  in_=wupP[ee // 2, ds(k * 128, 128), :]).then_inc(sm["lw"], 16)
                p.on("gpsimd", f)
            lwu = p.bump("lw", 16 * 8)

            vxg_base = p.val("v")
            for q in range(40):
                j, c = q // 8, q % 8

                def f(eng, j=j, c=c, q=q, lgt=lgt, vxg_base=vxg_base):
                    if q == 0:
                        eng.wait_ge(sm["ld"], lgt)
                        eng.wait_ge(sm["lw"], lw_id)
                        eng.wait_ge(sm["v"], vxg_base)
                    if q >= 2:
                        eng.wait_ge(sm["v"], vxg_base + q - 1)
                    eng.transpose(psT2[q % 2][:], gt[j][:, ds(c * 128, 128)],
                                  ident[:]).then_inc(sm["t"], 1)
                p.on("tensor", f)
                tn = p.bump("t")

                def f(eng, tn=tn, j=j, c=c, q=q):
                    eng.wait_ge(sm["t"], tn)
                    eng.tensor_copy(Xg[:, ds(c * 640 + j * 128, 128)],
                                    psT2[q % 2][:]).then_inc(sm["v"], 1)
                p.on("vector", f)
                p.bump("v")
            vxg = p.val("v")

            vgv = p.fenced("vector", "v", [
                (lambda eng, j=j2, ee=ee: eng.tensor_copy(
                    gv[:, ds(j, 1)], gt[j][:].bitcast(F32)[:, ds(512 + ee, 1)]))
                for j2 in range(5)
            ], pre_waits=[("ld", lgt)])
            vgv_prev = vgv

            s_silu_base = p.val("s")
            for m in range(22):
                def f(eng, m=m, lwu=lwu, vxg=vxg, s_silu_base=s_silu_base):
                    if m == 0:
                        eng.wait_ge(sm["lw"], lwu)
                        eng.wait_ge(sm["v"], vxg)
                    if m < 2:
                        eng.wait_ge(sm["s"], s_silu_base)
                    if m >= 2:
                        eng.wait_ge(sm["s"], s_silu_base + m - 1)
                    pu = psUP[m % 2]
                    for k in range(8):
                        eng.matmul(pu[:, 0:512], wexp[0][:, ds(k * 2816 + m * 128, 128)],
                                   Xg[:, ds(k * 640, 512)], start=(k == 0), stop=(k == 7))
                    for k in range(8):
                        inst = eng.matmul(pu[:, 512:640],
                                          wexp[0][:, ds(k * 2816 + m * 128, 128)],
                                          Xg[:, ds(k * 640 + 512, 128)],
                                          start=(k == 0), stop=(k == 7))
                    inst.then_inc(sm["t"], 1)
                p.on("tensor", f)
                tn = p.bump("t")
                if m == 21:
                    t_up_done = tn

                def f(eng, tn=tn, m=m, ee=ee):
                    eng.wait_ge(sm["t"], tn)
                    pu = psUP[m % 2]
                    eng.activation(midT[:, ds(m * 640, 512)], pu[:, 0:512], AF.Silu,
                                   bias=bup_sb[:, ds((ee // 2) * 22 + m, 1)], scale=1.0)
                    eng.activation(midT[:, ds(m * 640 + 512, 128)], pu[:, 512:640], AF.Silu,
                                   bias=bup_sb[:, ds((ee // 2) * 22 + m, 1)],
                                   scale=1.0).then_inc(sm["s"], 1)
                p.on("scalar", f)
                p.bump("s")
            sup = p.val("s")

            for k in range(22):
                def f(eng, ee=ee, k=k, wv=t_dn_done, vpos=vpos):
                    if wv:
                        eng.wait_ge(sm["t"], wv)
                    if ee == 0 and k == 0:
                        eng.wait_ge(sm["v"], vpos)
                    eng.dma_start(out=wexp[1][:, ds(k * 1024, 1024)],
                                  in_=wdnP[ee, ds(k * 128, 128), :]).then_inc(sm["lw"], 16)
                p.on("gpsimd", f)
            lwd = p.bump("lw", 16 * 22)

            s_dn_base = p.val("s")
            for o in range(8):
                def f(eng, o=o, lwd=lwd, sup=sup, s_dn_base=s_dn_base):
                    if o == 0:
                        eng.wait_ge(sm["lw"], lwd)
                        eng.wait_ge(sm["s"], sup)
                    if o >= 2:
                        eng.wait_ge(sm["s"], s_dn_base + o - 1)
                    pu = psUP[o % 2]
                    for k in range(22):
                        eng.matmul(pu[:, 0:512], wexp[1][:, ds(k * 1024 + o * 128, 128)],
                                   midT[:, ds(k * 640, 512)], start=(k == 0), stop=(k == 21))
                    for k in range(22):
                        inst = eng.matmul(pu[:, 512:640],
                                          wexp[1][:, ds(k * 1024 + o * 128, 128)],
                                          midT[:, ds(k * 640 + 512, 128)],
                                          start=(k == 0), stop=(k == 21))
                    inst.then_inc(sm["t"], 1)
                p.on("tensor", f)
                tn = p.bump("t")
                if o == 7:
                    t_dn_done = tn

                tr_prev = p.val("t")

                def f(eng, tn=tn, o=o, ee=ee, tr_prev=tr_prev):
                    eng.wait_ge(sm["t"], tr_prev)
                    pu = psUP[o % 2]
                    eng.activation(dnf[:, 0:512], pu[:, 0:512], AF.Identity,
                                   bias=bdn_sb[:, ds(ee * 8 + o, 1)], scale=1.0)
                    eng.activation(dnf[:, 512:640], pu[:, 512:640], AF.Identity,
                                   bias=bdn_sb[:, ds(ee * 8 + o, 1)], scale=1.0).then_inc(sm["s"], 1)
                p.on("scalar", f)
                sn = p.bump("s")

                vtrd_base = p.val("v")
                for j in range(5):
                    def f(eng, sn=sn, j=j, vtrd_base=vtrd_base):
                        if j == 0:
                            eng.wait_ge(sm["s"], sn)
                            eng.wait_ge(sm["v"], vtrd_base)
                        if j >= 2:
                            eng.wait_ge(sm["v"], vtrd_base + j - 1)
                        eng.transpose(psF2[j % 2], dnf[:, ds(j * 128, 128)],
                                      idf32[:]).then_inc(sm["t"], 1)
                    p.on("tensor", f)
                    tn2 = p.bump("t")

                    def f(eng, tn2=tn2, o=o, j=j, wst=st_out_done):
                        eng.wait_ge(sm["t"], tn2)
                        if o == 0 and j == 0:
                            eng.wait_ge(sm["st"], wst)
                        eng.tensor_copy(outT[:, ds(j * 1024 + o * 128, 128)],
                                        psF2[j % 2]).then_inc(sm["v"], 1)
                    p.on("vector", f)
                    p.bump("v")
            vdn = p.val("v")

            for j in range(5):
                vn = p.fenced("vector", "v", [
                    lambda eng, j=j: eng.tensor_scalar(tmp1[:], gv[:, ds(j, 1)], 0.5,
                                                       None, ALU.mult),
                    lambda eng, j=j: eng.tensor_scalar(outT[:, ds(j * 1024, 1024)],
                                                       outT[:, ds(j * 1024, 1024)],
                                                       tmp1[:], None, ALU.mult),
                ], pre_waits=[("v", vdn), ("v", vgv)] if j == 0 else ())

                def f(eng, vn=vn, ee=ee, j=j):
                    eng.wait_ge(sm["v"], vn)
                    eng.dma_start(out=outb[ds(ee * C_CAP + j * 128, 128), :],
                                  in_=outT[:, ds(j * 1024, 1024)]).then_inc(sm["st"], 16)
                p.on("sync", f)
                st_out_done = p.bump("st", 16)

        ga_base = 0
        st_x1_prev = 0
        v_ln2_prev = 0
        for g in range(8):
            tiles = [g * 4 + u for u in range(4)]
            for u, i in enumerate(tiles):
                def f(eng, i=i, u=u, st_out_done=st_out_done, wv=v_ln2_prev,
                      wst=st_x1_prev, first=(g == 0)):
                    if first and u == 0:
                        eng.wait_ge(sm["st"], st_out_done)
                    if wv:
                        eng.wait_ge(sm["v"], wv)
                    if wst:
                        eng.wait_ge(sm["st"], wst)
                    eng.dma_start(out=xq[:, ds(u * 1024, 1024)],
                                  in_=x0[ds(i * 128, 128), :]).then_inc(sm["ld"], 16)
                p.on("sync", f)
            lx = p.bump("ld", 16 * 4)

            ga_grp_base = p.val("ga")
            for ee in range(E):
                for u, i in enumerate(tiles):
                    def f(eng, ee=ee, i=i, u=u, lx=lx, ga_grp_base=ga_grp_base):
                        if ee == 0 and u == 0:
                            eng.wait_ge(sm["ld"], lx)
                        if ee > 0:
                            eng.wait_ge(sm["ga"], ga_grp_base + 16 * ((ee - 1) * 4 + u + 1))
                        eng.indirect_dma_start(
                            out=xq[:, ds(u * 1024, 1024)], out_offset=None,
                            in_=outb,
                            in_offset=bass.IndirectOffsetOnAxis(ap=posG[:, ds(i * 8 + ee, 1)], axis=0),
                            compute_op=ALU.add, oob_is_err=True,
                            bounds_check=bound_reg(eng, "ga", E * C_CAP + 127),
                        ).then_inc(sm["ga"], 16)
                    p.on("gpsimd", f)
                    p.bump("ga", 16)
            gdone = p.val("ga")

            for u, i in enumerate(tiles):
                def f(eng, u=u, i=i, gdone=gdone):
                    eng.wait_ge(sm["ga"], gdone)
                    eng.dma_start(out=x1_o[ds(i * 128, 128), :],
                                  in_=xq[:, ds(u * 1024, 1024)]).then_inc(sm["st"], 16)
                p.on("sync", f)
                st_x1_prev = p.bump("st", 16)

                if mode == "D":
                    continue
                xti = xq[:, ds(u * 1024, 1024)]
                vn = p.fenced("vector", "v", [
                    lambda eng, xti=xti: eng.reduce_sum(sum1[:], xti, axis=AX.X),
                ], pre_waits=[("ga", gdone)])
                sn = p.fenced("scalar", "s", [
                    lambda eng, xti=xti: eng.activation(sq[:], xti, AF.Square, bias=0.0,
                                                        scale=1.0, accum_out=ssq[:]),
                ], pre_waits=[("ga", gdone), ("v", p.val("v"))])
                vn = p.fenced("vector", "v", [
                    lambda eng: eng.tensor_scalar(mean[:], sum1[:], 1.0 / 1024, None, ALU.mult),
                    lambda eng: eng.tensor_mul(nmr[:], mean[:], mean[:]),
                    lambda eng: eng.scalar_tensor_tensor(var[:], ssq[:], 1.0 / 1024,
                                                         nmr[:], ALU.mult, ALU.subtract),
                ], pre_waits=[("v", vn), ("s", sn)])
                sn = p.fenced("scalar", "s", [
                    lambda eng: eng.activation(rstd[:], var[:], AF.Sqrt, bias=eps_sb[:],
                                               scale=1.0),
                ], pre_waits=[("v", vn)])
                vn = p.fenced("vector", "v", [
                    lambda eng: eng.reciprocal(rstd[:], rstd[:]),
                    lambda eng: eng.tensor_mul(nmr[:], mean[:], rstd[:]),
                    lambda eng: eng.tensor_scalar(nmr[:], nmr[:], -1.0, None, ALU.mult),
                    lambda eng, xti=xti: eng.tensor_scalar(lnt[:], xti, rstd[:], nmr[:],
                                                           ALU.mult, ALU.add),
                ], pre_waits=[("s", sn), ("t", p.val("t"))])

                vtl_base = p.val("v")
                for c in range(8):
                    def f(eng, c=c, vn=vn, vtl_base=vtl_base):
                        if c == 0:
                            eng.wait_ge(sm["v"], vn)
                        if c >= 2:
                            eng.wait_ge(sm["v"], vtl_base + c - 1)
                        eng.transpose(psT2[c % 2][:], lnt[:, ds(c * 128, 128)],
                                      ident[:]).then_inc(sm["t"], 1)
                    p.on("tensor", f)
                    tn = p.bump("t")

                    def f(eng, tn=tn, c=c, i=i):
                        eng.wait_ge(sm["t"], tn)
                        eng.tensor_copy(ln2fm[:, ds(c * 4096 + i * 128, 128)],
                                        psT2[c % 2][:]).then_inc(sm["v"], 1)
                    p.on("vector", f)
                    p.bump("v")
            v_ln2_prev = p.val("v")
        vln2 = p.val("v")

        if mode == "D":
            def f(eng, n=p.val("st")):
                eng.wait_ge(sm["st"], n)
            p.on("sync", f)
            def f2(eng, n=p.val("ga")):
                eng.wait_ge(sm["ga"], n)
            p.on("gpsimd", f2)
            p.replay(blk)
            return nc
        vbq = p.fenced("vector", "v", [
            lambda eng: eng.tensor_scalar(bq8_sb[:], bq_sb[:], 0.125, None, ALU.mult),
        ], pre_waits=[("ld", ld_pro)])

        qkv_slot_free_st = [0, 0]
        wslot_free_t = [0, 0]
        projs = [(wqT, bq8_sb, 0.125, qT_o), (wkT, bk_sb, 1.0, kT_o),
                 (wvT, bv_sb, 1.0, vT_o)]
        lin = 0
        for pi, (wsrc, bias_sb, scale, out_dram) in enumerate(projs):
            for o in range(8):
                wslot = qkvw[o % 2]

                def f(eng, o=o, wslot=wslot, wsrc=wsrc, wv=wslot_free_t[o % 2],
                      vln2=vln2):
                    if wv:
                        eng.wait_ge(sm["t"], wv)
                    eng.dma_start(
                        out=wslot[:].rearrange("p (k f) -> p k f", f=128),
                        in_=wsrc[:, ds(o * 128, 128)].rearrange("(k p) f -> p k f", p=128),
                    ).then_inc(sm["lw"], 16)
                p.on("gpsimd", f)
                lw = p.bump("lw", 16)

                for n in range(8):
                    bank = lin % 4
                    s_prev_bank = p.val("s") - 3

                    def f(eng, o=o, n=n, bank=bank, lw=lw, vln2=vln2, wslot=wslot,
                          sv=s_prev_bank, lin=lin, sall=p.val("s")):
                        if n == 0:
                            eng.wait_ge(sm["lw"], lw)
                        if lin == 0:
                            eng.wait_ge(sm["v"], vln2)
                            eng.wait_ge(sm["s"], sall)
                        if lin >= 4:
                            eng.wait_ge(sm["s"], sv)
                        for k in range(8):
                            inst = eng.matmul(psQ[bank], wslot[:, ds(k * 128, 128)],
                                              ln2fm[:, ds(k * 4096 + n * 512, 512)],
                                              start=(k == 0), stop=(k == 7))
                        inst.then_inc(sm["t"], 1)
                    p.on("tensor", f)
                    tn = p.bump("t")
                    if n == 7:
                        wslot_free_t[o % 2] = tn

                    oslot = lin % 2

                    def f(eng, tn=tn, o=o, bank=bank, bias_sb=bias_sb, scale=scale,
                          oslot=oslot, wst=qkv_slot_free_st[oslot]):
                        eng.wait_ge(sm["t"], tn)
                        if wst:
                            eng.wait_ge(sm["st"], wst)
                        eng.activation(qkvo[oslot][:], psQ[bank], AF.Identity,
                                       bias=bias_sb[:, ds(o, 1)],
                                       scale=scale).then_inc(sm["s"], 1)
                    p.on("scalar", f)
                    sn = p.bump("s")

                    def f(eng, sn=sn, o=o, n=n, out_dram=out_dram, oslot=oslot):
                        eng.wait_ge(sm["s"], sn)
                        eng.dma_start(out=out_dram[ds(o * 128, 128), ds(n * 512, 512)],
                                      in_=qkvo[oslot][:]).then_inc(sm["st"], 16)
                    p.on("sync", f)
                    qkv_slot_free_st[oslot] = p.bump("st", 16)
                    lin += 1

        def f(eng, wv=ld_pro):
            eng.wait_ge(sm["ld"], wv)
            eng.matmul(psWa, gruW_sb[:], g48a_sb[:], start=True, stop=True)
            eng.matmul(psWb, gruW_sb[:], g48b_sb[:], start=True, stop=True)
            eng.matmul(psBa, gruB_sb[:], g48a_sb[:], start=True, stop=True)
            eng.matmul(psBb, gruB_sb[:], g48b_sb[:], start=True, stop=True).then_inc(sm["t"], 1)
        p.on("tensor", f)
        tn = p.bump("t")

        vn = p.fenced("vector", "v", [
            lambda eng: eng.tensor_copy(wa2[0:64, :], psWa),
            lambda eng: eng.tensor_copy(wa2[64:128, :], psWa),
            lambda eng: eng.tensor_copy(wb2[0:64, :], psWb),
            lambda eng: eng.tensor_copy(wb2[64:128, :], psWb),
            lambda eng: eng.tensor_copy(ba_sb[:], psBa),
            lambda eng: eng.tensor_copy(bb_sb[:], psBb),
        ], pre_waits=[("t", tn)])
        vgru = vn

        for h in range(NH):
            for n in range(8):
                base = (h % 2) * 64

                def f(eng, h=h, n=n, base=base, vgru=vgru, vln2=vln2, wv=p.val("s")):
                    if h == 0 and n == 0:
                        eng.wait_ge(sm["v"], vgru)
                        eng.wait_ge(sm["v"], vln2)
                    else:
                        eng.wait_ge(sm["s"], wv)
                    eng.matmul(psGa[:, 0:512], wa2[ds(base, 64), :],
                               ln2fm[ds(base, 64), ds((h // 2) * 4096 + n * 512, 512)],
                               start=True, stop=True)
                    eng.matmul(psGb, wb2[ds(base, 64), :],
                               ln2fm[ds(base, 64), ds((h // 2) * 4096 + n * 512, 512)],
                               start=True, stop=True).then_inc(sm["t"], 1)
                p.on("tensor", f)
                tn = p.bump("t")

                sn = p.fenced("scalar", "s", [
                    lambda eng: eng.activation(sga[:], psGa[:, 0:512], AF.Sigmoid,
                                               bias=ba_sb[:], scale=1.0),
                    lambda eng: eng.activation(sgb[:], psGb, AF.Sigmoid,
                                               bias=bb_sb[:], scale=1.0),
                ], pre_waits=[("t", tn), ("v", p.val("v"))])

                vn = p.fenced("vector", "v", [
                    lambda eng, h=h: eng.tensor_scalar(sgb[:], sgb[:], gruC_sb[0:1, ds(h, 1)],
                                                       -1.0, ALU.mult, ALU.add),
                    lambda eng: eng.tensor_mul(grow[:], sga[:], sgb[:]),
                    lambda eng: eng.tensor_scalar(grow[:], grow[:], 2.0, None, ALU.add),
                ], pre_waits=[("s", sn), ("st", p.val("st"))])

                def f(eng, vn=vn, h=h, n=n):
                    eng.wait_ge(sm["v"], vn)
                    eng.dma_start(out=gate_o[ds(h, 1), ds(n * 512, 512)],
                                  in_=grow[:]).then_inc(sm["st"], 16)
                p.on("sync", f)
                p.bump("st", 16)

        def f(eng, n=p.val("st")):
            eng.wait_ge(sm["st"], n)
        p.on("sync", f)
        def f2(eng, n=p.val("ga")):
            eng.wait_ge(sm["ga"], n)
        p.on("gpsimd", f2)

        p.replay(blk)
    return nc
def build_C(dbg=False):
    nc = bass.Bass("TRN2", target_bir_lowering=False, debug=False)
    x1 = nc.dram_tensor("x1", [T, D_MODEL], F32, kind="ExternalInput").ap()
    qt8 = nc.dram_tensor("qt8", [D_MODEL, T], BF16, kind="ExternalInput").ap()
    krevT = nc.dram_tensor("krevT", [D_MODEL, S], BF16, kind="ExternalInput").ap()
    vtok = nc.dram_tensor("vtok", [NH, 64 * 128 * 65], BF16, kind="ExternalInput").ap()
    gate_i = nc.dram_tensor("gate_i", [NH, T], F32, kind="ExternalInput").ap()
    pbrevc = nc.dram_tensor("pbrevc", [NH, 12416], F32, kind="ExternalInput").ap()
    woT = nc.dram_tensor("woT", [D_MODEL, D_MODEL], F32, kind="ExternalInput").ap()
    bo_i = nc.dram_tensor("bo_i", [1, D_MODEL], F32, kind="ExternalInput").ap()
    wr = nc.dram_tensor("wr", [D_MODEL, E], F32, kind="ExternalInput").ap()
    br = nc.dram_tensor("br", [1, E], F32, kind="ExternalInput").ap()
    identp = nc.dram_tensor("identp", [128, 128], BF16, kind="ExternalInput").ap()
    onesf_i = nc.dram_tensor("onesf_i", [1, 64], F32, kind="ExternalInput").ap()

    x2_o = nc.dram_tensor("x2_o", [T, D_MODEL], F32, kind="ExternalOutput").ap()
    ln3_o = nc.dram_tensor("ln3_o", [T, D_MODEL], BF16, kind="ExternalOutput").ap()
    sm2_o = nc.dram_tensor("sm2_o", [T, E], F32, kind="ExternalOutput").ap()

    attnfm = nc.dram_tensor("attnfm", [D_MODEL, T], BF16,
                            kind="ExternalOutput" if dbg else "Internal").ap()

    NJC = S // 128
    NIB = T // 512

    with ExitStack() as ctx:
        e = ctx.enter_context
        shd = [e(nc.sbuf_tensor(f"shd{i}", [128, 12288], F32)) for i in range(2)]
        kq = e(nc.sbuf_tensor("kq", [128, S], BF16))
        qt = e(nc.sbuf_tensor("qt", [128, T], BF16))
        vt = [e(nc.sbuf_tensor(f"vt{i}", [128, 4160], BF16)) for i in range(2)]
        grep = [[e(nc.sbuf_tensor(f"grep{h}_{i}", [128, 512], BF16)) for i in range(2)]
                for h in range(2)]
        bt2 = [e(nc.sbuf_tensor(f"bt2_{i}", [128, 1024], BF16)) for i in range(2)]
        pt3 = [e(nc.sbuf_tensor(f"pt3_{i}", [128, 1024], BF16)) for i in range(3)]
        rsb = [e(nc.sbuf_tensor(f"rsb{hh}", [1, 512], F32)) for hh in range(2)]
        rrep = [e(nc.sbuf_tensor(f"rrep{hh}", [64, 512], F32)) for hh in range(2)]
        attn_t = e(nc.sbuf_tensor("attn_t", [128, 512], BF16))
        onesf = e(nc.sbuf_tensor("onesf", [1, 64], F32))
        ident = e(nc.sbuf_tensor("ident", [128, 128], BF16))
        wot = [e(nc.sbuf_tensor(f"wot{i}", [128, 1024], BF16)) for i in range(2)]
        wo_ev = [e(nc.sbuf_tensor(f"wo_ev{i}", [128, 512], BF16)) for i in range(2)]
        atok = [e(nc.sbuf_tensor(f"atok{i}", [128, 1024], BF16)) for i in range(4)]
        borep = e(nc.sbuf_tensor("borep", [128, 1024], F32))
        borow = e(nc.sbuf_tensor("borow", [1, 1024], F32))
        x2t = [e(nc.sbuf_tensor(f"x2t{i}", [128, 1024], F32)) for i in range(2)]
        wr_sb = e(nc.sbuf_tensor("wr_sb", [128, 64], BF16))
        brrep = e(nc.sbuf_tensor("brrep", [128, 8], F32))
        brrow = e(nc.sbuf_tensor("brrow", [1, 8], F32))
        ones1 = e(nc.sbuf_tensor("ones1", [1, 128], F32))
        eps_sb = e(nc.sbuf_tensor("eps_sb", [128, 1], F32))

        psA = e(nc.psum_tensor("psA", [128, 4096], F32))
        psS = [[psA[:, 0:512], psA[:, 512:1024]],
               [psA[:, 1024:1536], psA[:, 1536:2048]]]
        psO = [psA[0:65, 2048:2560], psA[0:65, 2560:3072]]
        psR = [psA[0:64, 3072:3584], psA[0:64, 3584:4096]]
        psW = [psA[:, 0:512], psA[:, 512:1024], psA[:, 1024:1536], psA[:, 1536:2048]]
        psT2 = [psA[:, ds(2048 + i * 512, 512)].bitcast(BF16)[:, 0:128] for i in range(2)]
        psB1 = psA[:, 3072:4096]
        psL = psA[:, 3072:3080]

        sm = sem_dict(ctx, nc, ["ld", "lw", "v", "t", "s", "g", "st"])
        blk = e(nc.Block())
        p = Prog(sm)

        def f(eng):
            for dst, src in [(onesf, onesf_i), (brrow, br), (borow, bo_i)]:
                eng.dma_start(out=dst[:], in_=src).then_inc(sm["ld"], 16)
        p.on("sync", f)
        ld_pro = p.bump("ld", 16 * 3)

        def f(eng):
            eng.dma_start(out=ident[:], in_=identp).then_inc(sm["lw"], 16)
            eng.dma_start(out=wr_sb[:].rearrange("p (c e) -> p c e", e=8),
                          in_=wr.rearrange("(c p) e -> p c e", p=128)).then_inc(sm["lw"], 16)
        p.on("gpsimd", f)
        lw_id = p.bump("lw", 32)

        vz = p.fenced("vector", "v", [
            lambda eng: eng.memset(eps_sb[:], 1e-5),
            lambda eng: eng.memset(ones1[:], 1.0),
        ])

        def f(eng, vz=vz, ld_pro=ld_pro):
            eng.wait_ge(sm["v"], vz)
            eng.wait_ge(sm["ld"], ld_pro)
            eng.matmul(psL, ones1[:], brrow[:], start=True, stop=True).then_inc(sm["t"], 1)
        p.on("tensor", f)
        tn = p.bump("t")
        vbr = p.fenced("vector", "v", [
            lambda eng: eng.tensor_copy(brrep[:], psL),
        ], pre_waits=[("t", tn)])

        def f(eng, vbr=vbr):
            eng.wait_ge(sm["v"], vbr)
            eng.matmul(psB1[:, 0:512], ones1[:], borow[:, 0:512], start=True, stop=True)
            eng.matmul(psB1[:, 512:1024], ones1[:], borow[:, 512:1024],
                       start=True, stop=True).then_inc(sm["t"], 1)
        p.on("tensor", f)
        tn = p.bump("t")
        vpro = p.fenced("vector", "v", [
            lambda eng: eng.tensor_copy(borep[:, 0:512], psB1[:, 0:512]),
            lambda eng: eng.tensor_copy(borep[:, 512:1024], psB1[:, 512:1024]),
        ], pre_waits=[("t", tn)])

        lastExp = [0, 0]
        lastAV = [0, 0, 0]
        lastSTT = [0, 0]
        lastOEvac = vpro
        lastREvac = vpro
        lastAttnSt = 0

        psPair = [psA[:, 0:1024], psA[:, 1024:2048]]

        for hp in range(8):
            def f(eng, hp=hp, w=p.val("t"), wg=p.val("g"), wv=p.val("v")):
                if hp > 0:
                    eng.wait_ge(sm["t"], w)
                    eng.wait_ge(sm["g"], wg)
                    eng.wait_ge(sm["v"], wv)
                eng.dma_start(out=qt[:], in_=qt8[ds(hp * 128, 128), :]).then_inc(sm["ld"], 16)
                eng.dma_start(out=kq[:], in_=krevT[ds(hp * 128, 128), :]).then_inc(sm["ld"], 16)
                eng.dma_start(out=shd[0][:],
                              in_=bass.AP(tensor=pbrevc.tensor, offset=(2 * hp) * 12416,
                                          ap=[[1, 128], [1, 12288]])).then_inc(sm["ld"], 16)
                eng.dma_start(out=shd[1][:],
                              in_=bass.AP(tensor=pbrevc.tensor, offset=(2 * hp + 1) * 12416,
                                          ap=[[1, 128], [1, 12288]])).then_inc(sm["ld"], 16)
                eng.dma_start(out=vt[0][:].rearrange("r (c x) -> r c x", x=65),
                              in_=vtok[ds(2 * hp, 1), :].rearrange("1 (c r x) -> r c x", r=128, x=65)
                              ).then_inc(sm["ld"], 16)
                eng.dma_start(out=vt[1][:].rearrange("r (c x) -> r c x", x=65),
                              in_=vtok[ds(2 * hp + 1, 1), :].rearrange("1 (c r x) -> r c x", r=128, x=65)
                              ).then_inc(sm["ld"], 16)
            p.on("sync", f)
            ld_hp = p.bump("ld", 16 * 6)

            for ib in range(NIB):
                ibp = ib % 2
                def f(eng, hp=hp, ib=ib, ibp=ibp, w=p.val("g")):
                    if hp + ib > 0:
                        eng.wait_ge(sm["g"], w)
                    for hh in range(2):
                        src = bass.AP(tensor=gate_i.tensor,
                                      offset=(2 * hp + hh) * T + ib * 512,
                                      ap=[[0, 128], [1, 512]])
                        eng.dma_start(out=grep[hh][ibp][:], in_=src).then_inc(sm["lw"], 16)
                p.on("gpsimd", f)
                lw_g = p.bump("lw", 32)

                u0s = [8064 - 128 * jc + ib * 512 for jc in range(NJC)]
                pend_av = []
                for jc in range(NJC):
                    par = jc % 2
                    p3 = jc % 3
                    u0 = u0s[jc]

                    def f(eng, hp=hp, ib=ib, jc=jc, par=par,
                          wA=lastExp[par], ld_hp=ld_hp):
                        if jc < 2:
                            eng.wait_ge(sm["ld"], ld_hp)
                            eng.wait_ge(sm["lw"], lw_id)
                        if wA:
                            eng.wait_ge(sm["s"], wA)
                        eng.matmul(psPair[par][:, 0:512], kq[0:64, ds(jc * 128, 128)],
                                   qt[0:64, ds(ib * 512, 512)], start=True, stop=True,
                                   tile_position=(0, 0)).then_inc(sm["t"], 1)
                        eng.matmul(psPair[par][:, 512:1024], kq[64:128, ds(jc * 128, 128)],
                                   qt[64:128, ds(ib * 512, 512)], start=True, stop=True,
                                   tile_position=(64, 0)).then_inc(sm["t"], 1)
                    p.on("tensor", f)
                    p.bump("t")
                    tqk = p.bump("t")

                    for hh in range(2):
                        def f(eng, hh=hh, par=par, u0=u0, ibp=ibp,
                              wfree=lastSTT[par], lw_g=lw_g, jc=jc, ld_hp=ld_hp):
                            if jc < 2:
                                eng.wait_ge(sm["lw"], lw_g)
                                eng.wait_ge(sm["ld"], ld_hp)
                            if hh == 0 and wfree:
                                eng.wait_ge(sm["v"], wfree)
                            eng.tensor_tensor(bt2[par][:, ds(hh * 512, 512)],
                                              shd[hh][:, ds(u0, 512)],
                                              grep[hh][ibp][:],
                                              ALU.mult).then_inc(sm["g"], 1)
                        p.on("gpsimd", f)
                        p.bump("g")
                    gbt = p.val("g")

                    def f(eng, par=par, tq=tqk, gb=gbt):
                        eng.wait_ge(sm["t"], tq)
                        eng.wait_ge(sm["g"], gb)
                        eng.scalar_tensor_tensor(psPair[par], bt2[par][:], 1.0,
                                                 psPair[par], ALU.mult,
                                                 ALU.add).then_inc(sm["v"], 1)
                    p.on("vector", f)
                    vstt = p.bump("v")
                    lastSTT[par] = vstt

                    def f(eng, par=par, p3=p3, vs=vstt, wpt=lastAV[p3]):
                        eng.wait_ge(sm["v"], vs)
                        if wpt:
                            eng.wait_ge(sm["t"], wpt)
                        eng.activation(pt3[p3][:], psPair[par], AF.Exp,
                                       bias=0.0, scale=1.0).then_inc(sm["s"], 1)
                    p.on("scalar", f)
                    sexp = p.bump("s")
                    lastExp[par] = sexp

                    def mkav(jc=jc, p3=p3, se=sexp, wO=lastOEvac):
                        def f(eng):
                            eng.wait_ge(sm["s"], se)
                            if jc == 0:
                                eng.wait_ge(sm["v"], wO)
                            eng.matmul(psO[0], vt[0][:, ds(jc * 65, 65)],
                                       pt3[p3][:, 0:512], start=(jc == 0),
                                       stop=(jc == NJC - 1)).then_inc(sm["t"], 1)
                            eng.matmul(psO[1], vt[1][:, ds(jc * 65, 65)],
                                       pt3[p3][:, 512:1024], start=(jc == 0),
                                       stop=(jc == NJC - 1)).then_inc(sm["t"], 1)
                        return f
                    pend_av.append(mkav())
                    if len(pend_av) > 2:
                        p.on("tensor", pend_av.pop(0))
                        p.bump("t")
                        lastAV[(jc - 2) % 3] = p.bump("t")
                for qi, f_ in enumerate(pend_av):
                    p.on("tensor", f_)
                    p.bump("t")
                    lastAV[(NJC - 2 + qi) % 3] = p.bump("t")
                tav = [p.val("t"), p.val("t")]

                vrec = p.fenced("vector", "v", [
                    lambda eng: eng.reciprocal(rsb[0][:], psO[0][64:65, :]),
                    lambda eng: eng.reciprocal(rsb[1][:], psO[1][64:65, :]),
                ], pre_waits=[("t", tav[1])])

                def f(eng, vrec=vrec, wR=lastREvac):
                    eng.wait_ge(sm["v"], vrec)
                    eng.matmul(psR[0], onesf[:], rsb[0][:], start=True, stop=True).then_inc(sm["t"], 1)
                    eng.matmul(psR[1], onesf[:], rsb[1][:], start=True, stop=True).then_inc(sm["t"], 1)
                p.on("tensor", f)
                trep = [p.bump("t"), p.bump("t")]

                vev = p.fenced("vector", "v", [
                    lambda eng: eng.tensor_copy(rrep[0][:], psR[0]),
                    lambda eng: eng.tensor_copy(rrep[1][:], psR[1]),
                    lambda eng: eng.tensor_tensor(attn_t[0:64, :], psO[0][0:64, :],
                                                  rrep[0][:], ALU.mult),
                    lambda eng: eng.tensor_tensor(attn_t[64:128, :], psO[1][0:64, :],
                                                  rrep[1][:], ALU.mult),
                ], pre_waits=[("t", trep[1]), ("st", lastAttnSt)])
                lastOEvac = vev
                lastREvac = vev

                def f(eng, vev=vev, hp=hp, ib=ib):
                    eng.wait_ge(sm["v"], vev)
                    eng.dma_start(out=attnfm[ds(hp * 128, 128), ds(ib * 512, 512)],
                                  in_=attn_t[:]).then_inc(sm["st"], 16)
                p.on("sync", f)
                lastAttnSt = p.bump("st", 16)

        st_flash = p.val("st")

        afull = kq[:, 0:4096]
        x1t4 = [e(nc.sbuf_tensor(f"x1t4_{i}", [128, 1024], F32)) for i in range(4)]

        wot_free = [0, 0]
        woev_free = [0, 0]
        afull_free = 0
        atok_free = [0, 0, 0, 0]
        x1t_free = [0, 0, 0, 0]
        x2t_free = [0, 0]
        st_x2 = 0
        for n in range(8):
            def f(eng, n=n, wld=afull_free, st_flash=st_flash):
                if n == 0:
                    eng.wait_ge(sm["st"], st_flash)
                if wld:
                    eng.wait_ge(sm["t"], wld)
                for k in range(8):
                    eng.dma_start(out=afull[:, ds(k * 512, 512)],
                                  in_=attnfm[ds(k * 128, 128), ds(n * 512, 512)]).then_inc(sm["ld"], 16)
                for tt in range(4):
                    pass
            p.on("sync", f)
            ld_af = p.bump("ld", 16 * 8)

            def f(eng, n=n, wv=[x1t_free[i] for i in range(4)]):
                for tt in range(4):
                    if wv[tt]:
                        eng.wait_ge(sm["v"], wv[tt])
                    eng.dma_start(out=x1t4[tt][:],
                                  in_=x1[ds(n * 512 + tt * 128, 128), :]).then_inc(sm["ld"], 16)
            p.on("sync", f)
            ld_x1 = p.bump("ld", 16 * 4)

            copies_start = p.val("v")
            for fo in range(8):
                wslot = fo % 2

                def f(eng, n=n, fo=fo, wslot=wslot, wv=wot_free[wslot]):
                    if wv:
                        eng.wait_ge(sm["t"], wv)
                    eng.dma_start(
                        out=wot[wslot][:].rearrange("p (k f) -> p k f", f=128),
                        in_=woT[:, ds(fo * 128, 128)].rearrange("(k p) f -> p k f", p=128),
                    ).then_inc(sm["lw"], 16)
                p.on("gpsimd", f)
                lww = p.bump("lw", 16)

                sfree = p.val("s") - 3
                def f(eng, n=n, fo=fo, wslot=wslot, lww=lww, ld_af=ld_af, sfree=sfree):
                    eng.wait_ge(sm["lw"], lww)
                    if fo == 0:
                        eng.wait_ge(sm["ld"], ld_af)
                    if fo >= 4 or n > 0:
                        eng.wait_ge(sm["s"], sfree)
                    for k in range(8):
                        inst = eng.matmul(psW[fo % 4], wot[wslot][:, ds(k * 128, 128)],
                                          afull[:, ds(k * 512, 512)],
                                          start=(k == 0), stop=(k == 7))
                    inst.then_inc(sm["t"], 1)
                p.on("tensor", f)
                tmm = p.bump("t")
                wot_free[wslot] = tmm
                if fo == 7:
                    afull_free = tmm

                def f(eng, tmm=tmm, fo=fo, wev=woev_free[fo % 2]):
                    eng.wait_ge(sm["t"], tmm)
                    if wev:
                        eng.wait_ge(sm["t"], wev)
                    eng.activation(wo_ev[fo % 2][:], psW[fo % 4], AF.Identity,
                                   bias=0.0, scale=1.0).then_inc(sm["s"], 1)
                p.on("scalar", f)
                sev = p.bump("s")

                vtr_base = p.val("v")
                for tt in range(4):
                    def f(eng, sev=sev, fo=fo, tt=tt, vtr_base=vtr_base):
                        if tt == 0:
                            eng.wait_ge(sm["s"], sev)
                            eng.wait_ge(sm["v"], vtr_base)
                        if tt >= 2:
                            eng.wait_ge(sm["v"], vtr_base + tt - 1)
                        eng.transpose(psT2[tt % 2][:], wo_ev[fo % 2][:, ds(tt * 128, 128)],
                                      ident[:]).then_inc(sm["t"], 1)
                    p.on("tensor", f)
                    ttr = p.bump("t")
                    if tt == 3:
                        woev_free[fo % 2] = ttr

                    def f(eng, ttr=ttr, fo=fo, tt=tt, wv=atok_free[tt]):
                        eng.wait_ge(sm["t"], ttr)
                        if fo == 0 and wv:
                            eng.wait_ge(sm["v"], wv)
                        eng.tensor_copy(atok[tt][:, ds(fo * 128, 128)],
                                        psT2[tt % 2][:]).then_inc(sm["v"], 1)
                    p.on("vector", f)
                    p.bump("v")
            v_copies = p.val("v")

            for tt in range(4):
                par = tt % 2
                vn = p.fenced("vector", "v", [
                    lambda eng, tt=tt, par=par: eng.tensor_tensor(
                        x2t[par][:], atok[tt][:], borep[:], ALU.add),
                    lambda eng, tt=tt, par=par: eng.tensor_add(
                        x2t[par][:], x2t[par][:], x1t4[tt][:]),
                ], pre_waits=[("v", v_copies), ("ld", ld_x1), ("st", x2t_free[par])])
                atok_free[tt] = vn
                x1t_free[tt] = vn

                def f(eng, vn=vn, n=n, tt=tt, par=par):
                    eng.wait_ge(sm["v"], vn)
                    eng.dma_start(out=x2_o[ds(n * 512 + tt * 128, 128), :],
                                  in_=x2t[par][:]).then_inc(sm["st"], 16)
                p.on("sync", f)
                x2t_free[par] = p.bump("st", 16)
        st_wo = p.val("st")

        em = LnRouterEmitter(p, sm, nc, ctx, "c", psT=psT2, psL=psL,
                             xt=[x1t4[0][:], x1t4[1][:]],
                             lnb=[atok[0][:], atok[1][:]],
                             fmb=[atok[2][:], atok[3][:]],
                             sq=afull[:, 0:1024])

        def f(eng, st_wo=st_wo):
            eng.wait_ge(sm["st"], st_wo)
        p.on("sync", f)
        def f(eng, st_wo=st_wo, lw_id=lw_id):
            eng.wait_ge(sm["st"], st_wo)
            eng.wait_ge(sm["lw"], lw_id)
        p.on("tensor", f)
        def f(eng, st_wo=st_wo):
            eng.wait_ge(sm["st"], st_wo)
        p.on("vector", f)
        def f(eng, st_wo=st_wo):
            eng.wait_ge(sm["st"], st_wo)
        p.on("scalar", f)
        for i in range(NT):
            em.tile(i, x2_o, ln3_o, sm2_o, wr_sb, brrep, eps_sb, ident)

        def f(eng, nst=p.val("st")):
            eng.wait_ge(sm["st"], nst)
        p.on("sync", f)

        p.replay(blk)
    return nc
def _shard_x(x):
    out = []
    for c in range(NCORES):
        b, h = c // 2, c % 2
        out.append(np.ascontiguousarray(x[b, h * T:(h + 1) * T, :]))
    return out


def _prep_B_host(inputs):
    hp = {}
    wup = np.asarray(inputs["L0_ffn_W_up"], np.float32)
    hp["wupP"] = np.ascontiguousarray(np.pad(wup, [(0, 0), (0, 0), (0, FPAD - 2730)]))
    bup = np.pad(np.asarray(inputs["L0_ffn_b_up"], np.float32), [(0, 0), (0, FPAD - 2730)])
    hp["bupP"] = np.ascontiguousarray(bup.reshape(4, 22, 128).transpose(2, 0, 1).reshape(128, 88))
    wdn = np.asarray(inputs["L0_ffn_W_down"], np.float32)
    hp["wdnP"] = np.ascontiguousarray(np.pad(wdn, [(0, 0), (0, FPAD - 2730), (0, 0)]))
    bdn = np.asarray(inputs["L0_ffn_b_down"], np.float32)
    hp["bdnP"] = np.ascontiguousarray(bdn.reshape(8, 8, 128).transpose(2, 0, 1).reshape(128, 64))
    for nm, key in [("wqT", "L0_attn_Wq"), ("wkT", "L0_attn_Wk"), ("wvT", "L0_attn_Wv")]:
        hp[nm] = np.ascontiguousarray(np.asarray(inputs[key], np.float32).T)
    for nm, key in [("bq_r", "L0_attn_bq"), ("bk_r", "L0_attn_bk"), ("bv_r", "L0_attn_bv")]:
        hp[nm] = np.ascontiguousarray(
            np.asarray(inputs[key], np.float32).reshape(8, 128).T)
    hp["gruW"] = np.ascontiguousarray(np.asarray(inputs["L0_attn_gru_W"], np.float32))
    hp["gruB"] = np.ascontiguousarray(np.asarray(inputs["L0_attn_gru_b"], np.float32).reshape(8, 1))
    hp["gruC"] = np.ascontiguousarray(np.asarray(inputs["L0_attn_gru_const"], np.float32).reshape(1, 16))
    hp["g48a_i"] = np.array([[1], [1], [1], [1], [0], [0], [0], [0]], np.float32)
    hp["g48b_i"] = np.array([[0], [0], [0], [0], [1], [1], [1], [1]], np.float32)
    M = np.zeros((128, 128), np.float32)
    for g in range(8):
        M[g * 16:(g + 1) * 16, g * 16:(g + 1) * 16] = 1.0
    hp["grpsum"] = M
    Ssel = np.zeros((128, 8), np.float32)
    for ee in range(8):
        Ssel[ee * 16, ee] = 1.0
    hp["thrsel"] = Ssel
    hp["eoff_i"] = (np.arange(8, dtype=np.float32) * C_CAP).reshape(8, 1)
    hp["bnd_i"] = ((np.arange(8, dtype=np.float32) + 1) * C_CAP).reshape(8, 1)
    hp["idf32p"] = np.eye(128, dtype=np.float32)
    return hp


def _sm_global(sm_cores):
    out = np.empty((B * S, E), np.float32)
    for c in range(NCORES):
        b, h = c // 2, c % 2
        out[b * S + h * T:(b * S + h * T) + T] = sm_cores[c]
    return out


def _tlog(tag, t0, res=None):
    import os, time, sys
    if os.environ.get("KTIME"):
        ns = getattr(res, "exec_time_ns", None) if res is not None else None
        print(f"[ktime] {tag}: {time.perf_counter() - t0:.2f}s exec_ns={ns}", file=sys.stderr)
    return time.perf_counter()


def kernel(**inputs):
    import time as _time
    _t0 = _time.perf_counter()
    x0 = np.asarray(inputs["L0_attn_hidden_states"], np.float32)
    consts = _host_consts()
    shards = _shard_x(x0)

    ncA = build_A()
    wr = np.ascontiguousarray(inputs["L0_ffn_Wr"], dtype=np.float32)
    br = np.ascontiguousarray(inputs["L0_ffn_br"], dtype=np.float32).reshape(1, E)
    mapsA = [dict(x=shards[c], wr=wr, br=br, identp=consts["ident"]) for c in range(NCORES)]
    resA = run_bass_kernel_spmd(ncA, mapsA, core_ids=list(range(NCORES)))
    _t0 = _tlog("A", _t0, resA)

    sm_cores = [np.asarray(resA.results[c]["sm_o"]) for c in range(NCORES)]
    ln_cores = [np.asarray(resA.results[c]["ln_o"]) for c in range(NCORES)]
    sm_all = _sm_global(sm_cores)
    sm_resh = np.ascontiguousarray(
        sm_all.reshape(16, 2048, 8).transpose(0, 2, 1).reshape(16, 8, 2048)
    )
    sm_resh = np.empty((128, 2048), np.float32)
    for pp in range(128):
        sm_resh[pp] = sm_all[(pp % 16) * 2048:(pp % 16 + 1) * 2048, pp // 16]

    hp = _prep_B_host(inputs)
    import os
    ncB = build_B(dbg=bool(os.environ.get('KDBG')))
    mapsB = []
    for c in range(NCORES):
        d = dict(hp)
        d["x0"] = shards[c]
        d["ln1"] = ln_cores[c]
        d["sm_resh"] = sm_resh
        d["smT_loc"] = np.ascontiguousarray(sm_cores[c].T)
        d["sm_loc"] = sm_cores[c]
        d["identp"] = consts["ident"]
        mapsB.append(d)
    resB = run_bass_kernel_spmd(ncB, mapsB, core_ids=list(range(NCORES)))
    _t0 = _tlog("B", _t0, resB)

    def as_u16(a):
        a = np.asarray(a)
        if a.dtype == np.uint16:
            return a
        return a.view(np.uint16) if a.dtype == bf16 else a.astype(bf16).view(np.uint16)

    kTs = [as_u16(resB.results[c]["kT_o"]) for c in range(NCORES)]
    vTs = [as_u16(resB.results[c]["vT_o"]) for c in range(NCORES)]
    qTs = [as_u16(resB.results[c]["qT_o"]) for c in range(NCORES)]
    gates = [np.asarray(resB.results[c]["gate_o"], np.float32) for c in range(NCORES)]
    x1s = [np.asarray(resB.results[c]["x1_o"], np.float32) for c in range(NCORES)]

    ones_bf16_u16 = np.full((NH, 64, 128, 1), 0x3F80, np.uint16)
    krevs, vtoks = {}, {}
    for bb in range(B):
        kfull = np.concatenate([kTs[2 * bb], kTs[2 * bb + 1]], axis=1)
        krevs[bb] = np.ascontiguousarray(
            kfull.reshape(D_MODEL, 64, 128)[:, :, ::-1].reshape(D_MODEL, S))
        vfull = np.concatenate([vTs[2 * bb], vTs[2 * bb + 1]], axis=1)
        vh = vfull.reshape(NH, 64, S).transpose(0, 2, 1)
        vrev = vh.reshape(NH, 64, 128, 64)[:, :, ::-1, :]
        vtoks[bb] = np.ascontiguousarray(
            np.concatenate([vrev, ones_bf16_u16], axis=3).reshape(NH, 64 * 128 * 65))

    btab = _bucket_table()
    re_in = np.asarray(inputs["L0_attn_rel_embed"], np.float32)
    pb = re_in[btab]
    pbrev = pb[::-1]
    woT = np.ascontiguousarray(np.asarray(inputs["L0_attn_Wo"], np.float32).T)
    bo_row = np.ascontiguousarray(np.asarray(inputs["L0_attn_bo"], np.float32).reshape(1, D_MODEL))
    onesf = np.ones((1, 64), np.float32)

    import os as _os
    ncC = build_C(dbg=bool(_os.environ.get('KDBG')))
    mapsC = []
    for c in range(NCORES):
        bb, hh = c // 2, c % 2
        q0 = hh * T
        pbc = np.zeros((NH, 12416), np.float32)
        avail = min(12416, 16383 - q0)
        pbc[:, 0:avail] = pbrev[q0:q0 + avail, :].T
        d = dict(
            x1=x1s[c], qt8=qTs[c], krevT=krevs[bb], vtok=vtoks[bb],
            gate_i=np.ascontiguousarray(gates[c]),
            pbrevc=np.ascontiguousarray(pbc), woT=woT, bo_i=bo_row,
            wr=wr, br=br, identp=consts["ident"], onesf_i=onesf,
        )
        mapsC.append(d)
    resC = run_bass_kernel_spmd(ncC, mapsC, core_ids=list(range(NCORES)))
    _t0 = _tlog("C", _t0, resC)

    sm2_cores = [np.asarray(resC.results[c]["sm2_o"], np.float32) for c in range(NCORES)]
    ln3_cores = [np.asarray(resC.results[c]["ln3_o"]) for c in range(NCORES)]
    x2_cores = [np.asarray(resC.results[c]["x2_o"], np.float32) for c in range(NCORES)]
    sm2_all = _sm_global(sm2_cores)
    sm2_resh = np.empty((128, 2048), np.float32)
    for pp in range(128):
        sm2_resh[pp] = sm2_all[(pp % 16) * 2048:(pp % 16 + 1) * 2048, pp // 16]
    mapsD = []
    for c in range(NCORES):
        d = dict(mapsB[c])
        d["x0"] = np.ascontiguousarray(x2_cores[c])
        d["ln1"] = np.ascontiguousarray(ln3_cores[c])
        d["sm_resh"] = sm2_resh
        d["smT_loc"] = np.ascontiguousarray(sm2_cores[c].T)
        d["sm_loc"] = sm2_cores[c]
        mapsD.append(d)
    ncD = build_B(mode="D")
    resD = run_bass_kernel_spmd(ncD, mapsD, core_ids=list(range(NCORES)))
    _t0 = _tlog("D", _t0, resD)

    out = np.empty((B, S, D_MODEL), np.float32)
    for c in range(NCORES):
        bb, hh = c // 2, c % 2
        out[bb, hh * T:(hh + 1) * T, :] = np.asarray(resD.results[c]["x1_o"], np.float32)
    if _os.environ.get("KDBG"):
        kernel._dbg = dict(resA=resA, resB=resB, resC=resC, resD=resD, mapsC=mapsC)
    return out


# version 06
# speedup vs baseline: 1.3564x; 1.0020x over previous
import math
from contextlib import ExitStack

import numpy as np
import ml_dtypes
import concourse.bass as bass
import concourse.mybir as mybir
from concourse.bass_utils import run_bass_kernel_spmd

F32 = mybir.dt.float32
BF16 = mybir.dt.bfloat16
I32 = mybir.dt.int32
U32 = mybir.dt.uint32
AF = mybir.ActivationFunctionType
ALU = mybir.AluOpType
ds = bass.ds
AX = mybir.AxisListType

B, S, D_MODEL = 4, 8192, 1024
NCORES = 8
T = 4096
NT = T // 128
E = 8
KSEL = 4096
C_CAP = 640
FPAD = 2816
NH, HD = 16, 64
NB, MAXD = 320, 800

bf16 = ml_dtypes.bfloat16


class Prog:

    ENGS = ("sync", "gpsimd", "vector", "scalar", "tensor")

    def __init__(self, sm):
        self.streams = {e: [] for e in self.ENGS}
        self.counts = {k: 0 for k in sm}
        self.sm = sm

    def on(self, eng, fn):
        self.streams[eng].append(fn)

    def bump(self, sem, n=1):
        self.counts[sem] += n
        return self.counts[sem]

    def val(self, sem):
        return self.counts[sem]

    def fenced(self, engname, semname, ops, pre_waits=()):
        base = self.val(semname)
        smd = self.sm

        def fn(eng, ops=tuple(ops), base=base, pre=tuple(pre_waits), semname=semname):
            for ws, wv in pre:
                eng.wait_ge(smd[ws], wv)
            if base:
                eng.wait_ge(smd[semname], base)
            for j, op in enumerate(ops):
                if j > 0:
                    eng.wait_ge(smd[semname], base + j)
                op(eng).then_inc(smd[semname], 1)
        self.on(engname, fn)
        return self.bump(semname, len(ops))

    def replay(self, blk):
        for name in self.ENGS:
            stream = self.streams[name]
            if not stream:
                continue
            dec = getattr(blk, name)

            def make(stream):
                def _body(eng):
                    for fn in stream:
                        fn(eng)
                return _body
            dec(make(stream))


def sem_dict(ctx, nc, names):
    return {n: ctx.enter_context(nc.semaphore(n)) for n in names}


def _host_consts():
    t = {}
    t["ident"] = np.eye(128, dtype=bf16)
    return t


def _bucket_table():
    nb = NB // 2
    rel = np.arange(-(S - 1), S)
    buckets = (rel > 0).astype(np.int64) * nb
    rel_abs = np.abs(rel)
    mx = nb // 2
    small = rel_abs < mx
    large = np.log(np.maximum(rel_abs, 1).astype(np.float32) / mx) / math.log(MAXD / mx) * (nb - mx)
    large = (mx + large).astype(np.int64)
    large = np.minimum(large, nb - 1)
    return buckets + np.where(small, rel_abs, large)


class LnRouterEmitter:

    def __init__(self, p, sm, nc, ctx, tag, psT=None, psL=None, xt=None, lnb=None,
                 fmb=None, sq=None):
        e = ctx.enter_context
        self.p, self.sm = p, sm
        self.xt = xt or [e(nc.sbuf_tensor(f"{tag}xt{i}", [128, 1024], F32)) for i in range(2)]
        self.lnb = lnb or [e(nc.sbuf_tensor(f"{tag}lnb{i}", [128, 1024], BF16)) for i in range(2)]
        self.fmb = fmb or [e(nc.sbuf_tensor(f"{tag}fmb{i}", [128, 1024], BF16)) for i in range(2)]
        self.sum1 = e(nc.sbuf_tensor(f"{tag}sum1", [128, 1], F32))
        self.mean = e(nc.sbuf_tensor(f"{tag}mean", [128, 1], F32))
        self.sq = sq if sq is not None else e(nc.sbuf_tensor(f"{tag}sq", [128, 1024], BF16))
        self.ssq = e(nc.sbuf_tensor(f"{tag}ssq", [128, 1], F32))
        self.var = e(nc.sbuf_tensor(f"{tag}var", [128, 1], F32))
        self.rstd = e(nc.sbuf_tensor(f"{tag}rstd", [128, 1], F32))
        self.nmr = e(nc.sbuf_tensor(f"{tag}nmr", [128, 1], F32))
        self.smt = e(nc.sbuf_tensor(f"{tag}smt", [128, 8], F32))
        self.mx1 = e(nc.sbuf_tensor(f"{tag}mx1", [128, 1], F32))
        self.se1 = e(nc.sbuf_tensor(f"{tag}se1", [128, 1], F32))
        if psT is None:
            self.psT = [e(nc.psum_tensor(f"{tag}psT{i}", [128, 128], BF16)) for i in range(2)]
            self.psL = e(nc.psum_tensor(f"{tag}psL", [128, 8], F32))
        else:
            self.psT = psT
            self.psL = psL
        self.xt_free = [0, 0]
        self.lnb_free = [0, 0]
        self.fmb_free = [0, 0]

    def tile(self, i, x_dram, ln_out, sm_out, wr_sb, brrep, eps_sb, ident,
             extra_ln_sink=None):
        p, sm = self.p, self.sm
        par = i % 2
        xti, lnbi, fmbi = self.xt[par], self.lnb[par], self.fmb[par]

        def f(eng, i=i, xti=xti, wv=self.xt_free[par]):
            if wv:
                eng.wait_ge(sm["v"], wv)
            eng.dma_start(out=xti[:], in_=x_dram[ds(i * 128, 128), :]).then_inc(sm["ld"], 16)
        p.on("sync", f)
        lv = p.bump("ld", 16)

        vn = p.fenced("vector", "v", [
            lambda eng, xti=xti: eng.reduce_sum(self.sum1[:], xti[:], axis=AX.X),
        ], pre_waits=[("ld", lv)])

        sn = p.fenced("scalar", "s", [
            lambda eng, xti=xti: eng.activation(self.sq[:], xti[:], AF.Square,
                                                bias=0.0, scale=1.0,
                                                accum_out=self.ssq[:]),
        ], pre_waits=[("ld", lv), ("v", p.val("v"))])

        vn = p.fenced("vector", "v", [
            lambda eng: eng.tensor_scalar(self.mean[:], self.sum1[:], 1.0 / 1024,
                                          None, ALU.mult),
            lambda eng: eng.tensor_mul(self.nmr[:], self.mean[:], self.mean[:]),
            lambda eng: eng.scalar_tensor_tensor(self.var[:], self.ssq[:], 1.0 / 1024,
                                                 self.nmr[:], ALU.mult, ALU.subtract),
        ], pre_waits=[("v", vn), ("s", sn)])

        sn = p.fenced("scalar", "s", [
            lambda eng: eng.activation(self.rstd[:], self.var[:], AF.Sqrt,
                                       bias=eps_sb[:], scale=1.0),
        ], pre_waits=[("v", vn)])

        vn = p.fenced("vector", "v", [
            lambda eng: eng.reciprocal(self.rstd[:], self.rstd[:]),
            lambda eng: eng.tensor_mul(self.nmr[:], self.mean[:], self.rstd[:]),
            lambda eng: eng.tensor_scalar(self.nmr[:], self.nmr[:], -1.0, None, ALU.mult),
            lambda eng, xti=xti, lnbi=lnbi: eng.tensor_scalar(
                lnbi[:], xti[:], self.rstd[:], self.nmr[:], ALU.mult, ALU.add),
        ], pre_waits=[("s", sn), ("t", self.lnb_free[par])])
        self.xt_free[par] = vn

        def f(eng, vn=vn, lnbi=lnbi, i=i):
            eng.wait_ge(sm["v"], vn)
            eng.dma_start(out=ln_out[ds(i * 128, 128), :], in_=lnbi[:]).then_inc(sm["st"], 16)
        p.on("sync", f)
        p.bump("st", 16)
        if extra_ln_sink is not None:
            extra_ln_sink(lnbi, vn)

        tbase = p.val("t")
        vbase0 = p.val("v")

        def f(eng, vn=vn, lnbi=lnbi, tbase=tbase, vbase0=vbase0, ident=ident):
            eng.wait_ge(sm["v"], vn)
            for c in range(8):
                if c >= 2:
                    eng.wait_ge(sm["v"], vbase0 + c - 1)
                eng.transpose(self.psT[c % 2][:], lnbi[:, ds(c * 128, 128)],
                              ident[:]).then_inc(sm["t"], 1)
        p.on("tensor", f)
        p.bump("t", 8)

        def f(eng, tbase=tbase, fmbi=fmbi):
            for c in range(8):
                eng.wait_ge(sm["t"], tbase + c + 1)
                eng.tensor_copy(fmbi[:, ds(c * 128, 128)], self.psT[c % 2][:]).then_inc(sm["v"], 1)
        p.on("vector", f)
        p.bump("v", 8)
        vafter = p.val("v")

        def f(eng, vafter=vafter, fmbi=fmbi, wr_sb=wr_sb):
            eng.wait_ge(sm["v"], vafter)
            for c in range(8):
                inst = eng.matmul(self.psL[:], fmbi[:, ds(c * 128, 128)],
                                  wr_sb[:, ds(c * 8, 8)], start=(c == 0), stop=(c == 7))
            inst.then_inc(sm["t"], 1)
        p.on("tensor", f)
        tn = p.bump("t")
        self.lnb_free[par] = tn
        self.fmb_free[par] = tn

        vn = p.fenced("vector", "v", [
            lambda eng: eng.tensor_add(self.smt[:], self.psL[:], brrep[:]),
            lambda eng: eng.reduce_max(self.mx1[:], self.smt[:], axis=AX.X, negate=True),
        ], pre_waits=[("t", tn)])

        def f(eng, vn=vn):
            eng.wait_ge(sm["v"], vn)
            eng.activation(self.smt[:], self.smt[:], AF.Exp, bias=self.mx1[:], scale=1.0,
                           accum_out=self.se1[:]).then_inc(sm["s"], 1)
        p.on("scalar", f)
        sn = p.bump("s")

        vn = p.fenced("vector", "v", [
            lambda eng: eng.reciprocal(self.se1[:], self.se1[:]),
            lambda eng: eng.tensor_scalar(self.smt[:], self.smt[:], self.se1[:],
                                          None, ALU.mult),
        ], pre_waits=[("s", sn)])

        def f(eng, vn=vn, i=i):
            eng.wait_ge(sm["v"], vn)
            eng.dma_start(out=sm_out[ds(i * 128, 128), :], in_=self.smt[:]).then_inc(sm["st"], 16)
        p.on("sync", f)
        p.bump("st", 16)


def build_A():
    nc = bass.Bass("TRN2", target_bir_lowering=False, debug=False)
    x = nc.dram_tensor("x", [T, D_MODEL], F32, kind="ExternalInput").ap()
    wr = nc.dram_tensor("wr", [D_MODEL, E], F32, kind="ExternalInput").ap()
    br = nc.dram_tensor("br", [1, E], F32, kind="ExternalInput").ap()
    identp = nc.dram_tensor("identp", [128, 128], BF16, kind="ExternalInput").ap()
    ln_o = nc.dram_tensor("ln_o", [T, D_MODEL], BF16, kind="ExternalOutput").ap()
    sm_o = nc.dram_tensor("sm_o", [T, E], F32, kind="ExternalOutput").ap()

    with ExitStack() as ctx:
        e = ctx.enter_context
        wr_sb = e(nc.sbuf_tensor("wr_sb", [128, 64], BF16))
        brrep = e(nc.sbuf_tensor("brrep", [128, 8], F32))
        brrow = e(nc.sbuf_tensor("brrow", [1, 8], F32))
        ones1 = e(nc.sbuf_tensor("ones1", [1, 128], F32))
        eps_sb = e(nc.sbuf_tensor("eps_sb", [128, 1], F32))
        ident = e(nc.sbuf_tensor("ident", [128, 128], BF16))
        psB = e(nc.psum_tensor("psB", [128, 8], F32))

        sm = sem_dict(ctx, nc, ["ld", "v", "t", "s", "st"])
        blk = e(nc.Block())
        p = Prog(sm)

        def f(eng):
            eng.dma_start(out=brrow[:], in_=br).then_inc(sm["ld"], 16)
            eng.dma_start(out=ident[:], in_=identp).then_inc(sm["ld"], 16)
            eng.dma_start(out=wr_sb[:].rearrange("p (c e) -> p c e", e=8),
                          in_=wr.rearrange("(c p) e -> p c e", p=128)).then_inc(sm["ld"], 16)
        p.on("gpsimd", f)
        ldn = p.bump("ld", 48)

        def f(eng):
            eng.memset(eps_sb[:], 1e-5)
            eng.memset(ones1[:], 1.0).then_inc(sm["v"], 1)
        p.on("vector", f)
        vn = p.bump("v")

        def f(eng, ldn=ldn, vn=vn):
            eng.wait_ge(sm["ld"], ldn)
            eng.wait_ge(sm["v"], vn)
            eng.matmul(psB[:], ones1[:], brrow[:], start=True, stop=True).then_inc(sm["t"], 1)
        p.on("tensor", f)
        tn = p.bump("t")

        def f(eng, tn=tn):
            eng.wait_ge(sm["t"], tn)
            eng.tensor_copy(brrep[:], psB[:]).then_inc(sm["v"], 1)
        p.on("vector", f)
        p.bump("v")

        em = LnRouterEmitter(p, sm, nc, ctx, "a")
        for i in range(NT):
            em.tile(i, x, ln_o, sm_o, wr_sb, brrep, eps_sb, ident)

        def f(eng, n=p.val("st")):
            eng.wait_ge(sm["st"], n)
        p.on("sync", f)

        p.replay(blk)
    return nc



def build_B(dbg=False, mode="B"):
    nc = bass.Bass("TRN2", target_bir_lowering=False, debug=False)
    x0 = nc.dram_tensor("x0", [T, D_MODEL], F32, kind="ExternalInput").ap()
    ln1 = nc.dram_tensor("ln1", [T, D_MODEL], BF16, kind="ExternalInput").ap()
    sm_resh = nc.dram_tensor("sm_resh", [128, 2048], F32, kind="ExternalInput").ap()
    smT_loc = nc.dram_tensor("smT_loc", [E, T], F32, kind="ExternalInput").ap()
    sm_loc = nc.dram_tensor("sm_loc", [T, E], F32, kind="ExternalInput").ap()
    wupP = nc.dram_tensor("wupP", [4, D_MODEL, FPAD], F32, kind="ExternalInput").ap()
    bupP = nc.dram_tensor("bupP", [128, 88], F32, kind="ExternalInput").ap()
    wdnP = nc.dram_tensor("wdnP", [E, FPAD, D_MODEL], F32, kind="ExternalInput").ap()
    bdnP = nc.dram_tensor("bdnP", [128, 64], F32, kind="ExternalInput").ap()
    wqT = nc.dram_tensor("wqT", [D_MODEL, D_MODEL], F32, kind="ExternalInput").ap()
    wkT = nc.dram_tensor("wkT", [D_MODEL, D_MODEL], F32, kind="ExternalInput").ap()
    wvT = nc.dram_tensor("wvT", [D_MODEL, D_MODEL], F32, kind="ExternalInput").ap()
    bq_r = nc.dram_tensor("bq_r", [128, 8], F32, kind="ExternalInput").ap()
    bk_r = nc.dram_tensor("bk_r", [128, 8], F32, kind="ExternalInput").ap()
    bv_r = nc.dram_tensor("bv_r", [128, 8], F32, kind="ExternalInput").ap()
    gruW = nc.dram_tensor("gruW", [8, 64], F32, kind="ExternalInput").ap()
    gruB = nc.dram_tensor("gruB", [8, 1], F32, kind="ExternalInput").ap()
    gruC = nc.dram_tensor("gruC", [1, 16], F32, kind="ExternalInput").ap()
    g48a_i = nc.dram_tensor("g48a_i", [8, 1], F32, kind="ExternalInput").ap()
    g48b_i = nc.dram_tensor("g48b_i", [8, 1], F32, kind="ExternalInput").ap()
    grpsum = nc.dram_tensor("grpsum", [128, 128], F32, kind="ExternalInput").ap()
    thrsel = nc.dram_tensor("thrsel", [128, 8], F32, kind="ExternalInput").ap()
    eoff_i = nc.dram_tensor("eoff_i", [8, 1], F32, kind="ExternalInput").ap()
    bnd_i = nc.dram_tensor("bnd_i", [8, 1], F32, kind="ExternalInput").ap()
    identp = nc.dram_tensor("identp", [128, 128], BF16, kind="ExternalInput").ap()
    idf32p = nc.dram_tensor("idf32p", [128, 128], F32, kind="ExternalInput").ap()

    x1_o = nc.dram_tensor("x1_o", [T, D_MODEL], F32, kind="ExternalOutput").ap()
    qT_o = nc.dram_tensor("qT_o", [D_MODEL, T], BF16, kind="ExternalOutput").ap()
    kT_o = nc.dram_tensor("kT_o", [D_MODEL, T], BF16, kind="ExternalOutput").ap()
    vT_o = nc.dram_tensor("vT_o", [D_MODEL, T], BF16, kind="ExternalOutput").ap()
    gate_o = nc.dram_tensor("gate_o", [NH, T], F32, kind="ExternalOutput").ap()
    thr_o = nc.dram_tensor("thr_o", [8, 1], F32, kind="ExternalOutput").ap()

    kind_dbg = "ExternalOutput" if dbg else "Internal"
    gath = nc.dram_tensor("gath", [E * C_CAP, 1040], BF16, kind=kind_dbg).ap()
    posS_o = nc.dram_tensor("posS_o", [128, 8 * NT], U32, kind=kind_dbg).ap()
    posG_o = nc.dram_tensor("posG_o", [128, 8 * NT], U32, kind=kind_dbg).ap()
    outb = nc.dram_tensor("outb", [E * C_CAP + 128, D_MODEL], F32, kind=kind_dbg).ap()

    with ExitStack() as ctx:
        e = ctx.enter_context
        lo = e(nc.sbuf_tensor("lo", [128, 1], F32))
        hi = e(nc.sbuf_tensor("hi", [128, 1], F32))
        mid = e(nc.sbuf_tensor("mid", [128, 1], F32))
        cnt = e(nc.sbuf_tensor("cnt", [128, 1], F32))
        cond = e(nc.sbuf_tensor("cond", [128, 1], F32))
        ncond = e(nc.sbuf_tensor("ncond", [128, 1], F32))
        tmp1 = e(nc.sbuf_tensor("tmp1", [128, 1], F32))
        tmp2 = e(nc.sbuf_tensor("tmp2", [128, 1], F32))
        grp_sb = e(nc.sbuf_tensor("grp_sb", [128, 128], F32))
        tsel_sb = e(nc.sbuf_tensor("tsel_sb", [128, 8], F32))
        thr8 = e(nc.sbuf_tensor("thr8", [8, 1], F32))
        eoff_sb = e(nc.sbuf_tensor("eoff_sb", [8, 1], F32))
        bnd_sb = e(nc.sbuf_tensor("bnd_sb", [8, 1], F32))
        posS = e(nc.sbuf_tensor("posS", [128, 8 * NT], U32))
        posG = e(nc.sbuf_tensor("posG", [128, 8 * NT], U32))
        ident = e(nc.sbuf_tensor("ident", [128, 128], BF16))
        idf32 = e(nc.sbuf_tensor("idf32", [128, 128], F32))
        stg = [e(nc.sbuf_tensor(f"stg{i}", [128, 1040], BF16)) for i in range(2)]

        arena = e(nc.sbuf_tensor("arena", [128, 90112], mybir.dt.uint8))
        wexp = [arena[:, ds(i * 45056, 45056)].bitcast(BF16) for i in range(2)]
        ln2fm = arena[:, 0:65536].bitcast(BF16)
        xq = arena[:, 65536:81920].bitcast(F32)
        qkvw = [arena[:, ds(81920 + i * 2048, 2048)].bitcast(BF16) for i in range(2)]
        a4 = arena[:, 0:65536].bitcast(F32)
        smT = a4[0:8, 0:4096]
        maskT = a4[0:8, 4096:8192]
        psc = a4[0:8, 8192:12288]
        pgc = a4[0:8, 12288:16384]

        Xg = e(nc.sbuf_tensor("Xg", [128, 5120], BF16))
        midT = e(nc.sbuf_tensor("midT", [128, 14080], BF16))
        vals = midT[:, 0:4096].bitcast(F32)
        cmpd = Xg[:, 0:2048]
        dnf = e(nc.sbuf_tensor("dnf", [128, 640], F32))
        outT = e(nc.sbuf_tensor("outT", [128, 5120], F32))
        gt = [e(nc.sbuf_tensor(f"gt{j}", [128, 1040], BF16)) for j in range(5)]
        gv = e(nc.sbuf_tensor("gv", [128, 8], F32))
        zrow = e(nc.sbuf_tensor("zrow", [128, 1024], F32))
        bup_sb = e(nc.sbuf_tensor("bup_sb", [128, 88], F32))
        bdn_sb = e(nc.sbuf_tensor("bdn_sb", [128, 64], F32))
        bq_sb = e(nc.sbuf_tensor("bq_sb", [128, 8], F32))
        bk_sb = e(nc.sbuf_tensor("bk_sb", [128, 8], F32))
        bv_sb = e(nc.sbuf_tensor("bv_sb", [128, 8], F32))
        bq8_sb = e(nc.sbuf_tensor("bq8_sb", [128, 8], F32))
        lnt = e(nc.sbuf_tensor("lnt", [128, 1024], BF16))
        qkvo = [e(nc.sbuf_tensor(f"qkvo{i}", [128, 512], BF16)) for i in range(2)]

        gruW_sb = e(nc.sbuf_tensor("gruW_sb", [8, 64], F32))
        gruB_sb = e(nc.sbuf_tensor("gruB_sb", [8, 1], F32))
        gruC_sb = e(nc.sbuf_tensor("gruC_sb", [1, 16], F32))
        g48a_sb = e(nc.sbuf_tensor("g48a_sb", [8, 1], F32))
        g48b_sb = e(nc.sbuf_tensor("g48b_sb", [8, 1], F32))
        wa2 = e(nc.sbuf_tensor("wa2", [128, 1], BF16))
        wb2 = e(nc.sbuf_tensor("wb2", [128, 1], BF16))
        ba_sb = e(nc.sbuf_tensor("ba_sb", [1, 1], F32))
        bb_sb = e(nc.sbuf_tensor("bb_sb", [1, 1], F32))
        sga = e(nc.sbuf_tensor("sga", [1, 512], F32))
        sgb = e(nc.sbuf_tensor("sgb", [1, 512], F32))
        grow = e(nc.sbuf_tensor("grow", [1, 512], F32))

        sum1 = e(nc.sbuf_tensor("sum1", [128, 1], F32))
        mean = e(nc.sbuf_tensor("mean", [128, 1], F32))
        sq = e(nc.sbuf_tensor("sq", [128, 1024], BF16))
        ssq = e(nc.sbuf_tensor("ssq", [128, 1], F32))
        var = e(nc.sbuf_tensor("var", [128, 1], F32))
        rstd = e(nc.sbuf_tensor("rstd", [128, 1], F32))
        nmr = e(nc.sbuf_tensor("nmr", [128, 1], F32))
        eps_sb = e(nc.sbuf_tensor("eps_sb", [128, 1], F32))

        psA = e(nc.psum_tensor("psA", [128, 4096], F32))
        psUP = [psA[:, 0:1024], psA[:, 1024:2048]]
        psT2 = [psA[:, ds(2048 + i * 512, 512)].bitcast(BF16)[:, 0:128] for i in range(2)]
        psF2 = [psA[:, 2048:2176], psA[:, 2560:2688]]
        psC = psA[:, 3072:3073]
        psT8 = psA[0:8, 3074:3075]
        psWa = psA[0:64, 3076:3077]
        psWb = psA[0:64, 3078:3079]
        psBa = psA[0:1, 3080:3081]
        psBb = psA[0:1, 3082:3083]
        psGa = psA[0:1, 3072:3584]
        psGb = psA[0:1, 3584:4096]
        psQ = [psA[:, ds(i * 512, 512)] for i in range(4)]

        sm = sem_dict(ctx, nc, ["ld", "lw", "v", "t", "s", "st", "gq", "ga"])
        blk = e(nc.Block())
        p = Prog(sm)
        REGS = {}

        def bound_reg(eng, key, value):
            if key not in REGS:
                REGS[key] = eng.to_reg(value)
            return REGS[key]

        def f(eng):
            for dst, src in [(vals, sm_resh), (smT, smT_loc), (grp_sb, grpsum),
                             (tsel_sb, thrsel), (eoff_sb, eoff_i), (bnd_sb, bnd_i),
                             (bup_sb, bupP), (bdn_sb, bdnP), (bq_sb, bq_r),
                             (bk_sb, bk_r), (bv_sb, bv_r), (gruW_sb, gruW),
                             (gruB_sb, gruB), (gruC_sb, gruC), (g48a_sb, g48a_i),
                             (g48b_sb, g48b_i), (idf32, idf32p)]:
                eng.dma_start(out=dst[:], in_=src).then_inc(sm["ld"], 16)
        p.on("sync", f)
        ld_pro = p.bump("ld", 16 * 17)

        def f(eng):
            eng.dma_start(out=ident[:], in_=identp).then_inc(sm["lw"], 16)
        p.on("gpsimd", f)
        lw_id = p.bump("lw", 16)

        vz = p.fenced("vector", "v", [
            lambda eng: eng.memset(lo[:], 0.0),
            lambda eng: eng.memset(hi[:], 1.0000001),
            lambda eng: eng.memset(mid[:], 0.50000006),
            lambda eng: eng.memset(zrow[:], 0.0),
            lambda eng: eng.memset(eps_sb[:], 1e-5),
        ])

        def f(eng, vz=vz):
            eng.wait_ge(sm["v"], vz)
            eng.dma_start(out=outb[ds(E * C_CAP, 128), :], in_=zrow[:]).then_inc(sm["st"], 16)
        p.on("sync", f)
        st_zero = p.bump("st", 16)

        NITER = 48
        for it in range(NITER):
            vn = p.fenced("vector", "v", [
                lambda eng: eng.tensor_scalar(cmpd[:], vals[:], mid[:], 0.0,
                                              ALU.is_ge, ALU.add, accum_out=cnt[:]),
            ], pre_waits=[("ld", 16 * 2)] if it == 0 else ())

            def f(eng, vn=vn, first=(it == 0)):
                eng.wait_ge(sm["v"], vn)
                if first:
                    eng.wait_ge(sm["ld"], ld_pro)
                eng.matmul(psC, grp_sb[:], cnt[:], start=True, stop=True).then_inc(sm["t"], 1)
            p.on("tensor", f)
            tn = p.bump("t")

            p.fenced("vector", "v", [
                lambda eng: eng.tensor_scalar(cond[:], psC, 4096.0, None, ALU.is_ge),
                lambda eng: eng.tensor_scalar(ncond[:], cond[:], -1.0, 1.0, ALU.mult, ALU.add),
                lambda eng: eng.tensor_sub(tmp1[:], mid[:], lo[:]),
                lambda eng: eng.tensor_mul(tmp1[:], tmp1[:], cond[:]),
                lambda eng: eng.tensor_add(lo[:], lo[:], tmp1[:]),
                lambda eng: eng.tensor_sub(tmp2[:], mid[:], hi[:]),
                lambda eng: eng.tensor_mul(tmp2[:], tmp2[:], ncond[:]),
                lambda eng: eng.tensor_add(hi[:], hi[:], tmp2[:]),
                lambda eng: eng.tensor_add(mid[:], lo[:], hi[:]),
                lambda eng: eng.tensor_scalar(mid[:], mid[:], 0.5, None, ALU.mult),
            ], pre_waits=[("t", tn)])

        vlast = p.val("v")

        def f(eng, vlast=vlast):
            eng.wait_ge(sm["v"], vlast)
            eng.matmul(psT8, tsel_sb[:], lo[:], start=True, stop=True).then_inc(sm["t"], 1)
        p.on("tensor", f)
        tn = p.bump("t")
        vn = p.fenced("vector", "v", [
            lambda eng: eng.tensor_copy(thr8[:], psT8),
        ], pre_waits=[("t", tn)])

        def f(eng, vn=vn):
            eng.wait_ge(sm["v"], vn)
            eng.dma_start(out=thr_o, in_=thr8[:]).then_inc(sm["st"], 16)
        p.on("sync", f)
        p.bump("st", 16)

        DUMP = float(E * C_CAP)
        BIGF = float(1 << 26)
        vpos_chain = p.fenced("vector", "v", [
            lambda eng: eng.tensor_scalar(maskT, smT, thr8[:], None, ALU.is_ge),
            lambda eng: eng.memset(psc, 0.0),
            lambda eng: eng.tensor_tensor_scan(smT, maskT, psc, 0.0, ALU.add, ALU.add),
            lambda eng: eng.tensor_scalar(psc, smT, eoff_sb[:], -1.0, ALU.add, ALU.add),
            lambda eng: eng.tensor_mul(pgc, psc, maskT),
            lambda eng: eng.scalar_tensor_tensor(pgc, maskT, -DUMP, pgc, ALU.mult, ALU.add),
            lambda eng: eng.tensor_scalar(pgc, pgc, DUMP, None, ALU.add),
            lambda eng: eng.tensor_scalar(smT, pgc, bnd_sb[:], None, ALU.is_ge),
            lambda eng: eng.tensor_scalar(psc, psc, 0.0, None, ALU.add),
            lambda eng: eng.tensor_scalar(maskT, maskT, -1.0, 1.0, ALU.mult, ALU.add),
            lambda eng: eng.scalar_tensor_tensor(psc, maskT, BIGF, psc, ALU.mult, ALU.add),
            lambda eng: eng.tensor_scalar(maskT, smT, 0.0, None, ALU.add),
            lambda eng: eng.tensor_scalar(smT, pgc, -1.0, DUMP, ALU.mult, ALU.add),
            lambda eng: eng.tensor_mul(smT, smT, maskT),
            lambda eng: eng.tensor_add(pgc, pgc, smT),
        ], pre_waits=[("v", vn)])

        vtr_copies = p.val("v")
        for i in range(NT):
            for which, src, dstp in ((0, psc, posS), (1, pgc, posG)):
                q = i * 2 + which
                par = q % 2

                def f(eng, i=i, src=src, par=par, q=q, vpos_chain=vpos_chain,
                      vtr_copies=vtr_copies):
                    if q == 0:
                        eng.wait_ge(sm["v"], vpos_chain)
                    if q >= 2:
                        eng.wait_ge(sm["v"], vtr_copies + q - 1)
                    eng.transpose(psF2[par][:, 0:8], src[:, ds(i * 128, 128)],
                                  idf32[0:8, 0:8]).then_inc(sm["t"], 1)
                p.on("tensor", f)
                tn = p.bump("t")

                def f(eng, tn=tn, i=i, par=par, dstp=dstp):
                    eng.wait_ge(sm["t"], tn)
                    eng.tensor_copy(dstp[:, ds(i * 8, 8)], psF2[par][:, 0:8]).then_inc(sm["v"], 1)
                p.on("vector", f)
                p.bump("v")
        vpos = p.val("v")

        if True:
            def f(eng, vpos=vpos):
                eng.wait_ge(sm["v"], vpos)
                eng.dma_start(out=posS_o, in_=posS[:]).then_inc(sm["st"], 16)
                eng.dma_start(out=posG_o, in_=posG[:]).then_inc(sm["st"], 16)
            p.on("sync", f)
            p.bump("st", 32)

        for i in range(NT):
            par = i % 2
            sti = stg[par]
            gq_before = p.val("gq")

            def f(eng, i=i, sti=sti, wv=gq_before - 16 * 8):
                if wv > 0:
                    eng.wait_ge(sm["gq"], wv)
                eng.dma_start(out=sti[:, 0:1024], in_=ln1[ds(i * 128, 128), :]).then_inc(sm["ld"], 16)
                eng.dma_start(out=sti[:].bitcast(F32)[:, 512:520],
                              in_=sm_loc[ds(i * 128, 128), :]).then_inc(sm["ld"], 16)
            p.on("sync", f)
            lv = p.bump("ld", 32)

            def f(eng, i=i, sti=sti, lv=lv, vpos=vpos):
                eng.wait_ge(sm["ld"], lv)
                eng.wait_ge(sm["v"], vpos)
                for ee in range(E):
                    eng.indirect_dma_start(
                        out=gath,
                        out_offset=bass.IndirectOffsetOnAxis(ap=posS[:, ds(i * 8 + ee, 1)], axis=0),
                        in_=sti[:], in_offset=None,
                        bounds_check=bound_reg(eng, ee, (ee + 1) * C_CAP - 1),
                        oob_is_err=False,
                    ).then_inc(sm["gq"], 16)
            p.on("gpsimd", f)
            p.bump("gq", 16 * 8)
        gq_scat = p.val("gq")

        t_up_done = 0
        t_dn_done = 0
        vgv_prev = 0
        st_out_done = st_zero
        for ee in range(E):
            for j in range(5):
                def f(eng, ee=ee, j=j, gq_scat=gq_scat, wv=t_up_done, wvv=vgv_prev):
                    if j == 0:
                        eng.wait_ge(sm["gq"], gq_scat)
                    if wv:
                        eng.wait_ge(sm["t"], wv)
                    if wvv:
                        eng.wait_ge(sm["v"], wvv)
                    eng.dma_start(out=gt[j][:], in_=gath[ds(ee * C_CAP + j * 128, 128), :]).then_inc(sm["ld"], 16)
                p.on("sync", f)
            lgt = p.bump("ld", 16 * 5)

            for k in range(8):
                def f(eng, ee=ee, k=k, wv=t_up_done, vpos=vpos):
                    if wv:
                        eng.wait_ge(sm["t"], wv)
                    if ee == 0 and k == 0:
                        eng.wait_ge(sm["v"], vpos)
                        eng.wait_ge(sm["gq"], gq_scat)
                    eng.dma_start(out=wexp[0][:, ds(k * 2816, 2816)],
                                  in_=wupP[ee // 2, ds(k * 128, 128), :]).then_inc(sm["lw"], 16)
                p.on("gpsimd", f)
            lwu = p.bump("lw", 16 * 8)

            vxg_base = p.val("v")
            for q in range(40):
                j, c = q // 8, q % 8

                def f(eng, j=j, c=c, q=q, lgt=lgt, vxg_base=vxg_base):
                    if q == 0:
                        eng.wait_ge(sm["ld"], lgt)
                        eng.wait_ge(sm["lw"], lw_id)
                        eng.wait_ge(sm["v"], vxg_base)
                    if q >= 2:
                        eng.wait_ge(sm["v"], vxg_base + q - 1)
                    eng.transpose(psT2[q % 2][:], gt[j][:, ds(c * 128, 128)],
                                  ident[:]).then_inc(sm["t"], 1)
                p.on("tensor", f)
                tn = p.bump("t")

                def f(eng, tn=tn, j=j, c=c, q=q):
                    eng.wait_ge(sm["t"], tn)
                    eng.tensor_copy(Xg[:, ds(c * 640 + j * 128, 128)],
                                    psT2[q % 2][:]).then_inc(sm["v"], 1)
                p.on("vector", f)
                p.bump("v")
            vxg = p.val("v")

            vgv = p.fenced("vector", "v", [
                (lambda eng, j=j2, ee=ee: eng.tensor_copy(
                    gv[:, ds(j, 1)], gt[j][:].bitcast(F32)[:, ds(512 + ee, 1)]))
                for j2 in range(5)
            ], pre_waits=[("ld", lgt)])
            vgv_prev = vgv

            s_silu_base = p.val("s")
            for m in range(22):
                def f(eng, m=m, lwu=lwu, vxg=vxg, s_silu_base=s_silu_base):
                    if m == 0:
                        eng.wait_ge(sm["lw"], lwu)
                        eng.wait_ge(sm["v"], vxg)
                    if m < 2:
                        eng.wait_ge(sm["s"], s_silu_base)
                    if m >= 2:
                        eng.wait_ge(sm["s"], s_silu_base + m - 1)
                    pu = psUP[m % 2]
                    for k in range(8):
                        eng.matmul(pu[:, 0:512], wexp[0][:, ds(k * 2816 + m * 128, 128)],
                                   Xg[:, ds(k * 640, 512)], start=(k == 0), stop=(k == 7))
                    for k in range(8):
                        inst = eng.matmul(pu[:, 512:640],
                                          wexp[0][:, ds(k * 2816 + m * 128, 128)],
                                          Xg[:, ds(k * 640 + 512, 128)],
                                          start=(k == 0), stop=(k == 7))
                    inst.then_inc(sm["t"], 1)
                p.on("tensor", f)
                tn = p.bump("t")
                if m == 21:
                    t_up_done = tn

                def f(eng, tn=tn, m=m, ee=ee):
                    eng.wait_ge(sm["t"], tn)
                    pu = psUP[m % 2]
                    eng.activation(midT[:, ds(m * 640, 512)], pu[:, 0:512], AF.Silu,
                                   bias=bup_sb[:, ds((ee // 2) * 22 + m, 1)], scale=1.0)
                    eng.activation(midT[:, ds(m * 640 + 512, 128)], pu[:, 512:640], AF.Silu,
                                   bias=bup_sb[:, ds((ee // 2) * 22 + m, 1)],
                                   scale=1.0).then_inc(sm["s"], 1)
                p.on("scalar", f)
                p.bump("s")
            sup = p.val("s")

            for k in range(22):
                def f(eng, ee=ee, k=k, wv=t_dn_done, vpos=vpos):
                    if wv:
                        eng.wait_ge(sm["t"], wv)
                    if ee == 0 and k == 0:
                        eng.wait_ge(sm["v"], vpos)
                    eng.dma_start(out=wexp[1][:, ds(k * 1024, 1024)],
                                  in_=wdnP[ee, ds(k * 128, 128), :]).then_inc(sm["lw"], 16)
                p.on("gpsimd", f)
            lwd = p.bump("lw", 16 * 22)

            s_dn_base = p.val("s")
            for o in range(8):
                def f(eng, o=o, lwd=lwd, sup=sup, s_dn_base=s_dn_base):
                    if o == 0:
                        eng.wait_ge(sm["lw"], lwd)
                        eng.wait_ge(sm["s"], sup)
                    if o >= 2:
                        eng.wait_ge(sm["s"], s_dn_base + o - 1)
                    pu = psUP[o % 2]
                    for k in range(22):
                        eng.matmul(pu[:, 0:512], wexp[1][:, ds(k * 1024 + o * 128, 128)],
                                   midT[:, ds(k * 640, 512)], start=(k == 0), stop=(k == 21))
                    for k in range(22):
                        inst = eng.matmul(pu[:, 512:640],
                                          wexp[1][:, ds(k * 1024 + o * 128, 128)],
                                          midT[:, ds(k * 640 + 512, 128)],
                                          start=(k == 0), stop=(k == 21))
                    inst.then_inc(sm["t"], 1)
                p.on("tensor", f)
                tn = p.bump("t")
                if o == 7:
                    t_dn_done = tn

                tr_prev = p.val("t")

                def f(eng, tn=tn, o=o, ee=ee, tr_prev=tr_prev):
                    eng.wait_ge(sm["t"], tr_prev)
                    pu = psUP[o % 2]
                    eng.activation(dnf[:, 0:512], pu[:, 0:512], AF.Identity,
                                   bias=bdn_sb[:, ds(ee * 8 + o, 1)], scale=1.0)
                    eng.activation(dnf[:, 512:640], pu[:, 512:640], AF.Identity,
                                   bias=bdn_sb[:, ds(ee * 8 + o, 1)], scale=1.0).then_inc(sm["s"], 1)
                p.on("scalar", f)
                sn = p.bump("s")

                vtrd_base = p.val("v")
                for j in range(5):
                    def f(eng, sn=sn, j=j, vtrd_base=vtrd_base):
                        if j == 0:
                            eng.wait_ge(sm["s"], sn)
                            eng.wait_ge(sm["v"], vtrd_base)
                        if j >= 2:
                            eng.wait_ge(sm["v"], vtrd_base + j - 1)
                        eng.transpose(psF2[j % 2], dnf[:, ds(j * 128, 128)],
                                      idf32[:]).then_inc(sm["t"], 1)
                    p.on("tensor", f)
                    tn2 = p.bump("t")

                    def f(eng, tn2=tn2, o=o, j=j, wst=st_out_done):
                        eng.wait_ge(sm["t"], tn2)
                        if o == 0 and j == 0:
                            eng.wait_ge(sm["st"], wst)
                        eng.tensor_copy(outT[:, ds(j * 1024 + o * 128, 128)],
                                        psF2[j % 2]).then_inc(sm["v"], 1)
                    p.on("vector", f)
                    p.bump("v")
            vdn = p.val("v")

            for j in range(5):
                vn = p.fenced("vector", "v", [
                    lambda eng, j=j: eng.tensor_scalar(tmp1[:], gv[:, ds(j, 1)], 0.5,
                                                       None, ALU.mult),
                    lambda eng, j=j: eng.tensor_scalar(outT[:, ds(j * 1024, 1024)],
                                                       outT[:, ds(j * 1024, 1024)],
                                                       tmp1[:], None, ALU.mult),
                ], pre_waits=[("v", vdn), ("v", vgv)] if j == 0 else ())

                def f(eng, vn=vn, ee=ee, j=j):
                    eng.wait_ge(sm["v"], vn)
                    eng.dma_start(out=outb[ds(ee * C_CAP + j * 128, 128), :],
                                  in_=outT[:, ds(j * 1024, 1024)]).then_inc(sm["st"], 16)
                p.on("sync", f)
                st_out_done = p.bump("st", 16)

        ga_base = 0
        st_x1_prev = 0
        v_ln2_prev = 0
        for g in range(8):
            tiles = [g * 4 + u for u in range(4)]
            for u, i in enumerate(tiles):
                def f(eng, i=i, u=u, st_out_done=st_out_done, wv=v_ln2_prev,
                      wst=st_x1_prev, first=(g == 0)):
                    if first and u == 0:
                        eng.wait_ge(sm["st"], st_out_done)
                    if wv:
                        eng.wait_ge(sm["v"], wv)
                    if wst:
                        eng.wait_ge(sm["st"], wst)
                    eng.dma_start(out=xq[:, ds(u * 1024, 1024)],
                                  in_=x0[ds(i * 128, 128), :]).then_inc(sm["ld"], 16)
                p.on("sync", f)
            lx = p.bump("ld", 16 * 4)

            ga_grp_base = p.val("ga")
            for ee in range(E):
                for u, i in enumerate(tiles):
                    def f(eng, ee=ee, i=i, u=u, lx=lx, ga_grp_base=ga_grp_base):
                        if ee == 0 and u == 0:
                            eng.wait_ge(sm["ld"], lx)
                        if ee > 0:
                            eng.wait_ge(sm["ga"], ga_grp_base + 16 * ((ee - 1) * 4 + u + 1))
                        eng.indirect_dma_start(
                            out=xq[:, ds(u * 1024, 1024)], out_offset=None,
                            in_=outb,
                            in_offset=bass.IndirectOffsetOnAxis(ap=posG[:, ds(i * 8 + ee, 1)], axis=0),
                            compute_op=ALU.add, oob_is_err=True,
                            bounds_check=bound_reg(eng, "ga", E * C_CAP + 127),
                        ).then_inc(sm["ga"], 16)
                    p.on("gpsimd", f)
                    p.bump("ga", 16)
            gdone = p.val("ga")

            for u, i in enumerate(tiles):
                def f(eng, u=u, i=i, gdone=gdone):
                    eng.wait_ge(sm["ga"], gdone)
                    eng.dma_start(out=x1_o[ds(i * 128, 128), :],
                                  in_=xq[:, ds(u * 1024, 1024)]).then_inc(sm["st"], 16)
                p.on("sync", f)
                st_x1_prev = p.bump("st", 16)

                if mode == "D":
                    continue
                xti = xq[:, ds(u * 1024, 1024)]
                vn = p.fenced("vector", "v", [
                    lambda eng, xti=xti: eng.reduce_sum(sum1[:], xti, axis=AX.X),
                ], pre_waits=[("ga", gdone)])
                sn = p.fenced("scalar", "s", [
                    lambda eng, xti=xti: eng.activation(sq[:], xti, AF.Square, bias=0.0,
                                                        scale=1.0, accum_out=ssq[:]),
                ], pre_waits=[("ga", gdone), ("v", p.val("v"))])
                vn = p.fenced("vector", "v", [
                    lambda eng: eng.tensor_scalar(mean[:], sum1[:], 1.0 / 1024, None, ALU.mult),
                    lambda eng: eng.tensor_mul(nmr[:], mean[:], mean[:]),
                    lambda eng: eng.scalar_tensor_tensor(var[:], ssq[:], 1.0 / 1024,
                                                         nmr[:], ALU.mult, ALU.subtract),
                ], pre_waits=[("v", vn), ("s", sn)])
                sn = p.fenced("scalar", "s", [
                    lambda eng: eng.activation(rstd[:], var[:], AF.Sqrt, bias=eps_sb[:],
                                               scale=1.0),
                ], pre_waits=[("v", vn)])
                vn = p.fenced("vector", "v", [
                    lambda eng: eng.reciprocal(rstd[:], rstd[:]),
                    lambda eng: eng.tensor_mul(nmr[:], mean[:], rstd[:]),
                    lambda eng: eng.tensor_scalar(nmr[:], nmr[:], -1.0, None, ALU.mult),
                    lambda eng, xti=xti: eng.tensor_scalar(lnt[:], xti, rstd[:], nmr[:],
                                                           ALU.mult, ALU.add),
                ], pre_waits=[("s", sn), ("t", p.val("t"))])

                vtl_base = p.val("v")
                for c in range(8):
                    def f(eng, c=c, vn=vn, vtl_base=vtl_base):
                        if c == 0:
                            eng.wait_ge(sm["v"], vn)
                        if c >= 2:
                            eng.wait_ge(sm["v"], vtl_base + c - 1)
                        eng.transpose(psT2[c % 2][:], lnt[:, ds(c * 128, 128)],
                                      ident[:]).then_inc(sm["t"], 1)
                    p.on("tensor", f)
                    tn = p.bump("t")

                    def f(eng, tn=tn, c=c, i=i):
                        eng.wait_ge(sm["t"], tn)
                        eng.tensor_copy(ln2fm[:, ds(c * 4096 + i * 128, 128)],
                                        psT2[c % 2][:]).then_inc(sm["v"], 1)
                    p.on("vector", f)
                    p.bump("v")
            v_ln2_prev = p.val("v")
        vln2 = p.val("v")

        if mode == "D":
            def f(eng, n=p.val("st")):
                eng.wait_ge(sm["st"], n)
            p.on("sync", f)
            def f2(eng, n=p.val("ga")):
                eng.wait_ge(sm["ga"], n)
            p.on("gpsimd", f2)
            p.replay(blk)
            return nc
        vbq = p.fenced("vector", "v", [
            lambda eng: eng.tensor_scalar(bq8_sb[:], bq_sb[:], 0.125, None, ALU.mult),
        ], pre_waits=[("ld", ld_pro)])

        qkv_slot_free_st = [0, 0]
        wslot_free_t = [0, 0]
        projs = [(wqT, bq8_sb, 0.125, qT_o), (wkT, bk_sb, 1.0, kT_o),
                 (wvT, bv_sb, 1.0, vT_o)]
        lin = 0
        for pi, (wsrc, bias_sb, scale, out_dram) in enumerate(projs):
            for o in range(8):
                wslot = qkvw[o % 2]

                def f(eng, o=o, wslot=wslot, wsrc=wsrc, wv=wslot_free_t[o % 2],
                      vln2=vln2):
                    if wv:
                        eng.wait_ge(sm["t"], wv)
                    eng.dma_start(
                        out=wslot[:].rearrange("p (k f) -> p k f", f=128),
                        in_=wsrc[:, ds(o * 128, 128)].rearrange("(k p) f -> p k f", p=128),
                    ).then_inc(sm["lw"], 16)
                p.on("gpsimd", f)
                lw = p.bump("lw", 16)

                for n in range(8):
                    bank = lin % 4
                    s_prev_bank = p.val("s") - 3

                    def f(eng, o=o, n=n, bank=bank, lw=lw, vln2=vln2, wslot=wslot,
                          sv=s_prev_bank, lin=lin, sall=p.val("s")):
                        if n == 0:
                            eng.wait_ge(sm["lw"], lw)
                        if lin == 0:
                            eng.wait_ge(sm["v"], vln2)
                            eng.wait_ge(sm["s"], sall)
                        if lin >= 4:
                            eng.wait_ge(sm["s"], sv)
                        for k in range(8):
                            inst = eng.matmul(psQ[bank], wslot[:, ds(k * 128, 128)],
                                              ln2fm[:, ds(k * 4096 + n * 512, 512)],
                                              start=(k == 0), stop=(k == 7))
                        inst.then_inc(sm["t"], 1)
                    p.on("tensor", f)
                    tn = p.bump("t")
                    if n == 7:
                        wslot_free_t[o % 2] = tn

                    oslot = lin % 2

                    def f(eng, tn=tn, o=o, bank=bank, bias_sb=bias_sb, scale=scale,
                          oslot=oslot, wst=qkv_slot_free_st[oslot]):
                        eng.wait_ge(sm["t"], tn)
                        if wst:
                            eng.wait_ge(sm["st"], wst)
                        eng.activation(qkvo[oslot][:], psQ[bank], AF.Identity,
                                       bias=bias_sb[:, ds(o, 1)],
                                       scale=scale).then_inc(sm["s"], 1)
                    p.on("scalar", f)
                    sn = p.bump("s")

                    def f(eng, sn=sn, o=o, n=n, out_dram=out_dram, oslot=oslot):
                        eng.wait_ge(sm["s"], sn)
                        eng.dma_start(out=out_dram[ds(o * 128, 128), ds(n * 512, 512)],
                                      in_=qkvo[oslot][:]).then_inc(sm["st"], 16)
                    p.on("sync", f)
                    qkv_slot_free_st[oslot] = p.bump("st", 16)
                    lin += 1

        def f(eng, wv=ld_pro):
            eng.wait_ge(sm["ld"], wv)
            eng.matmul(psWa, gruW_sb[:], g48a_sb[:], start=True, stop=True)
            eng.matmul(psWb, gruW_sb[:], g48b_sb[:], start=True, stop=True)
            eng.matmul(psBa, gruB_sb[:], g48a_sb[:], start=True, stop=True)
            eng.matmul(psBb, gruB_sb[:], g48b_sb[:], start=True, stop=True).then_inc(sm["t"], 1)
        p.on("tensor", f)
        tn = p.bump("t")

        vn = p.fenced("vector", "v", [
            lambda eng: eng.tensor_copy(wa2[0:64, :], psWa),
            lambda eng: eng.tensor_copy(wa2[64:128, :], psWa),
            lambda eng: eng.tensor_copy(wb2[0:64, :], psWb),
            lambda eng: eng.tensor_copy(wb2[64:128, :], psWb),
            lambda eng: eng.tensor_copy(ba_sb[:], psBa),
            lambda eng: eng.tensor_copy(bb_sb[:], psBb),
        ], pre_waits=[("t", tn)])
        vgru = vn

        for h in range(NH):
            for n in range(8):
                base = (h % 2) * 64

                def f(eng, h=h, n=n, base=base, vgru=vgru, vln2=vln2, wv=p.val("s")):
                    if h == 0 and n == 0:
                        eng.wait_ge(sm["v"], vgru)
                        eng.wait_ge(sm["v"], vln2)
                    else:
                        eng.wait_ge(sm["s"], wv)
                    eng.matmul(psGa[:, 0:512], wa2[ds(base, 64), :],
                               ln2fm[ds(base, 64), ds((h // 2) * 4096 + n * 512, 512)],
                               start=True, stop=True)
                    eng.matmul(psGb, wb2[ds(base, 64), :],
                               ln2fm[ds(base, 64), ds((h // 2) * 4096 + n * 512, 512)],
                               start=True, stop=True).then_inc(sm["t"], 1)
                p.on("tensor", f)
                tn = p.bump("t")

                sn = p.fenced("scalar", "s", [
                    lambda eng: eng.activation(sga[:], psGa[:, 0:512], AF.Sigmoid,
                                               bias=ba_sb[:], scale=1.0),
                    lambda eng: eng.activation(sgb[:], psGb, AF.Sigmoid,
                                               bias=bb_sb[:], scale=1.0),
                ], pre_waits=[("t", tn), ("v", p.val("v"))])

                vn = p.fenced("vector", "v", [
                    lambda eng, h=h: eng.tensor_scalar(sgb[:], sgb[:], gruC_sb[0:1, ds(h, 1)],
                                                       -1.0, ALU.mult, ALU.add),
                    lambda eng: eng.tensor_mul(grow[:], sga[:], sgb[:]),
                    lambda eng: eng.tensor_scalar(grow[:], grow[:], 2.0, None, ALU.add),
                ], pre_waits=[("s", sn), ("st", p.val("st"))])

                def f(eng, vn=vn, h=h, n=n):
                    eng.wait_ge(sm["v"], vn)
                    eng.dma_start(out=gate_o[ds(h, 1), ds(n * 512, 512)],
                                  in_=grow[:]).then_inc(sm["st"], 16)
                p.on("sync", f)
                p.bump("st", 16)

        def f(eng, n=p.val("st")):
            eng.wait_ge(sm["st"], n)
        p.on("sync", f)
        def f2(eng, n=p.val("ga")):
            eng.wait_ge(sm["ga"], n)
        p.on("gpsimd", f2)

        p.replay(blk)
    return nc
def build_C(dbg=False):
    nc = bass.Bass("TRN2", target_bir_lowering=False, debug=False)
    x1 = nc.dram_tensor("x1", [T, D_MODEL], F32, kind="ExternalInput").ap()
    qt8 = nc.dram_tensor("qt8", [D_MODEL, T], BF16, kind="ExternalInput").ap()
    krevT = nc.dram_tensor("krevT", [D_MODEL, S], BF16, kind="ExternalInput").ap()
    vtok = nc.dram_tensor("vtok", [NH, 64 * 128 * 65], BF16, kind="ExternalInput").ap()
    gate_i = nc.dram_tensor("gate_i", [NH, T], F32, kind="ExternalInput").ap()
    pbrevc = nc.dram_tensor("pbrevc", [NH, 12416], F32, kind="ExternalInput").ap()
    woT = nc.dram_tensor("woT", [D_MODEL, D_MODEL], F32, kind="ExternalInput").ap()
    bo_i = nc.dram_tensor("bo_i", [1, D_MODEL], F32, kind="ExternalInput").ap()
    wr = nc.dram_tensor("wr", [D_MODEL, E], F32, kind="ExternalInput").ap()
    br = nc.dram_tensor("br", [1, E], F32, kind="ExternalInput").ap()
    identp = nc.dram_tensor("identp", [128, 128], BF16, kind="ExternalInput").ap()
    onesf_i = nc.dram_tensor("onesf_i", [1, 64], F32, kind="ExternalInput").ap()

    x2_o = nc.dram_tensor("x2_o", [T, D_MODEL], F32, kind="ExternalOutput").ap()
    ln3_o = nc.dram_tensor("ln3_o", [T, D_MODEL], BF16, kind="ExternalOutput").ap()
    sm2_o = nc.dram_tensor("sm2_o", [T, E], F32, kind="ExternalOutput").ap()

    attnfm = nc.dram_tensor("attnfm", [D_MODEL, T], BF16,
                            kind="ExternalOutput" if dbg else "Internal").ap()

    NJC = S // 128
    NIB = T // 512

    with ExitStack() as ctx:
        e = ctx.enter_context
        shd = [e(nc.sbuf_tensor(f"shd{i}", [128, 12288], F32)) for i in range(2)]
        kq = e(nc.sbuf_tensor("kq", [128, S], BF16))
        qt = e(nc.sbuf_tensor("qt", [128, T], BF16))
        vt = [e(nc.sbuf_tensor(f"vt{i}", [128, 4160], BF16)) for i in range(2)]
        grep = [[e(nc.sbuf_tensor(f"grep{h}_{i}", [128, 512], BF16)) for i in range(2)]
                for h in range(2)]
        bt2 = [e(nc.sbuf_tensor(f"bt2_{i}", [128, 1024], BF16)) for i in range(3)]
        pt3 = [e(nc.sbuf_tensor(f"pt3_{i}", [128, 1024], BF16)) for i in range(3)]
        rsb = [e(nc.sbuf_tensor(f"rsb{hh}", [1, 512], F32)) for hh in range(2)]
        rrep = [e(nc.sbuf_tensor(f"rrep{hh}", [64, 512], F32)) for hh in range(2)]
        attn_t = e(nc.sbuf_tensor("attn_t", [128, 512], BF16))
        onesf = e(nc.sbuf_tensor("onesf", [1, 64], F32))
        ident = e(nc.sbuf_tensor("ident", [128, 128], BF16))
        wot = [e(nc.sbuf_tensor(f"wot{i}", [128, 1024], BF16)) for i in range(2)]
        atok = [e(nc.sbuf_tensor(f"atok{i}", [128, 1024], BF16)) for i in range(4)]
        wo_ev = [bt2[i][:, 0:512] for i in range(2)]
        borep = e(nc.sbuf_tensor("borep", [128, 1024], F32))
        borow = e(nc.sbuf_tensor("borow", [1, 1024], F32))
        x2t = [e(nc.sbuf_tensor(f"x2t{i}", [128, 1024], F32)) for i in range(2)]
        wr_sb = e(nc.sbuf_tensor("wr_sb", [128, 64], BF16))
        brrep = e(nc.sbuf_tensor("brrep", [128, 8], F32))
        brrow = e(nc.sbuf_tensor("brrow", [1, 8], F32))
        ones1 = e(nc.sbuf_tensor("ones1", [1, 128], F32))
        eps_sb = e(nc.sbuf_tensor("eps_sb", [128, 1], F32))

        psA = e(nc.psum_tensor("psA", [128, 4096], F32))
        psS = [[psA[:, 0:512], psA[:, 512:1024]],
               [psA[:, 1024:1536], psA[:, 1536:2048]]]
        psO = [psA[0:65, 3072:3584], psA[0:65, 3584:4096]]
        psR = [psA[0:64, 0:512], psA[0:64, 512:1024]]
        psW = [psA[:, 0:512], psA[:, 512:1024], psA[:, 1024:1536], psA[:, 1536:2048]]
        psT2 = [psA[:, ds(2048 + i * 512, 512)].bitcast(BF16)[:, 0:128] for i in range(2)]
        psB1 = psA[:, 3072:4096]
        psL = psA[:, 3072:3080]

        sm = sem_dict(ctx, nc, ["ld", "lw", "v", "t", "s", "g", "st"])
        blk = e(nc.Block())
        p = Prog(sm)

        def f(eng):
            for dst, src in [(onesf, onesf_i), (brrow, br), (borow, bo_i)]:
                eng.dma_start(out=dst[:], in_=src).then_inc(sm["ld"], 16)
        p.on("sync", f)
        ld_pro = p.bump("ld", 16 * 3)

        def f(eng):
            eng.dma_start(out=ident[:], in_=identp).then_inc(sm["lw"], 16)
            eng.dma_start(out=wr_sb[:].rearrange("p (c e) -> p c e", e=8),
                          in_=wr.rearrange("(c p) e -> p c e", p=128)).then_inc(sm["lw"], 16)
        p.on("gpsimd", f)
        lw_id = p.bump("lw", 32)

        vz = p.fenced("vector", "v", [
            lambda eng: eng.memset(eps_sb[:], 1e-5),
            lambda eng: eng.memset(ones1[:], 1.0),
        ])

        def f(eng, vz=vz, ld_pro=ld_pro):
            eng.wait_ge(sm["v"], vz)
            eng.wait_ge(sm["ld"], ld_pro)
            eng.matmul(psL, ones1[:], brrow[:], start=True, stop=True).then_inc(sm["t"], 1)
        p.on("tensor", f)
        tn = p.bump("t")
        vbr = p.fenced("vector", "v", [
            lambda eng: eng.tensor_copy(brrep[:], psL),
        ], pre_waits=[("t", tn)])

        def f(eng, vbr=vbr):
            eng.wait_ge(sm["v"], vbr)
            eng.matmul(psB1[:, 0:512], ones1[:], borow[:, 0:512], start=True, stop=True)
            eng.matmul(psB1[:, 512:1024], ones1[:], borow[:, 512:1024],
                       start=True, stop=True).then_inc(sm["t"], 1)
        p.on("tensor", f)
        tn = p.bump("t")
        vpro = p.fenced("vector", "v", [
            lambda eng: eng.tensor_copy(borep[:, 0:512], psB1[:, 0:512]),
            lambda eng: eng.tensor_copy(borep[:, 512:1024], psB1[:, 512:1024]),
        ], pre_waits=[("t", tn)])

        lastExp = [0, 0, 0]
        lastAV = [0, 0, 0]
        lastSTT = [0, 0, 0]
        lastOEvac = vpro
        lastREvac = vpro
        lastAttnSt = 0

        psPair = [psA[:, 0:1024], psA[:, 1024:2048], psA[:, 2048:3072]]

        for hp in range(8):
            def f(eng, hp=hp, w=p.val("t"), wg=p.val("g"), wv=p.val("v")):
                if hp > 0:
                    eng.wait_ge(sm["t"], w)
                    eng.wait_ge(sm["g"], wg)
                    eng.wait_ge(sm["v"], wv)
                eng.dma_start(out=qt[:], in_=qt8[ds(hp * 128, 128), :]).then_inc(sm["ld"], 16)
                eng.dma_start(out=kq[:], in_=krevT[ds(hp * 128, 128), :]).then_inc(sm["ld"], 16)
                eng.dma_start(out=shd[0][:],
                              in_=bass.AP(tensor=pbrevc.tensor, offset=(2 * hp) * 12416,
                                          ap=[[1, 128], [1, 12288]])).then_inc(sm["ld"], 16)
                eng.dma_start(out=shd[1][:],
                              in_=bass.AP(tensor=pbrevc.tensor, offset=(2 * hp + 1) * 12416,
                                          ap=[[1, 128], [1, 12288]])).then_inc(sm["ld"], 16)
                eng.dma_start(out=vt[0][:].rearrange("r (c x) -> r c x", x=65),
                              in_=vtok[ds(2 * hp, 1), :].rearrange("1 (c r x) -> r c x", r=128, x=65)
                              ).then_inc(sm["ld"], 16)
                eng.dma_start(out=vt[1][:].rearrange("r (c x) -> r c x", x=65),
                              in_=vtok[ds(2 * hp + 1, 1), :].rearrange("1 (c r x) -> r c x", r=128, x=65)
                              ).then_inc(sm["ld"], 16)
            p.on("sync", f)
            ld_hp = p.bump("ld", 16 * 6)

            for ib in range(NIB):
                ibp = ib % 2
                def f(eng, hp=hp, ib=ib, ibp=ibp, w=p.val("g")):
                    if hp + ib > 0:
                        eng.wait_ge(sm["g"], w)
                    for hh in range(2):
                        src = bass.AP(tensor=gate_i.tensor,
                                      offset=(2 * hp + hh) * T + ib * 512,
                                      ap=[[0, 128], [1, 512]])
                        eng.dma_start(out=grep[hh][ibp][:], in_=src).then_inc(sm["lw"], 16)
                p.on("gpsimd", f)
                lw_g = p.bump("lw", 32)

                u0s = [8064 - 128 * jc + ib * 512 for jc in range(NJC)]
                pend_av = []
                for jc in range(NJC):
                    par = jc % 3
                    p3 = jc % 3
                    u0 = u0s[jc]

                    def f(eng, hp=hp, ib=ib, jc=jc, par=par,
                          wA=lastExp[par], ld_hp=ld_hp, wR=lastREvac):
                        if jc < 3:
                            eng.wait_ge(sm["ld"], ld_hp)
                            eng.wait_ge(sm["lw"], lw_id)
                        if jc < 2:
                            eng.wait_ge(sm["v"], wR)
                        if wA:
                            eng.wait_ge(sm["s"], wA)
                        eng.matmul(psPair[par][:, 0:512], kq[0:64, ds(jc * 128, 128)],
                                   qt[0:64, ds(ib * 512, 512)], start=True, stop=True,
                                   tile_position=(0, 0)).then_inc(sm["t"], 1)
                        eng.matmul(psPair[par][:, 512:1024], kq[64:128, ds(jc * 128, 128)],
                                   qt[64:128, ds(ib * 512, 512)], start=True, stop=True,
                                   tile_position=(64, 0)).then_inc(sm["t"], 1)
                    p.on("tensor", f)
                    p.bump("t")
                    tqk = p.bump("t")

                    for hh in range(2):
                        def f(eng, hh=hh, par=par, u0=u0, ibp=ibp,
                              wfree=lastSTT[par], lw_g=lw_g, jc=jc, ld_hp=ld_hp):
                            if jc < 3:
                                eng.wait_ge(sm["lw"], lw_g)
                                eng.wait_ge(sm["ld"], ld_hp)
                            if hh == 0 and wfree:
                                eng.wait_ge(sm["v"], wfree)
                            eng.tensor_tensor(bt2[par][:, ds(hh * 512, 512)],
                                              shd[hh][:, ds(u0, 512)],
                                              grep[hh][ibp][:],
                                              ALU.mult).then_inc(sm["g"], 1)
                        p.on("gpsimd", f)
                        p.bump("g")
                    gbt = p.val("g")

                    def f(eng, par=par, tq=tqk, gb=gbt):
                        eng.wait_ge(sm["t"], tq)
                        eng.wait_ge(sm["g"], gb)
                        eng.scalar_tensor_tensor(psPair[par], bt2[par][:], 1.0,
                                                 psPair[par], ALU.mult,
                                                 ALU.add).then_inc(sm["v"], 1)
                    p.on("vector", f)
                    vstt = p.bump("v")
                    lastSTT[par] = vstt

                    def f(eng, par=par, p3=p3, vs=vstt, wpt=lastAV[p3]):
                        eng.wait_ge(sm["v"], vs)
                        if wpt:
                            eng.wait_ge(sm["t"], wpt)
                        eng.activation(pt3[p3][:], psPair[par], AF.Exp,
                                       bias=0.0, scale=1.0).then_inc(sm["s"], 1)
                    p.on("scalar", f)
                    sexp = p.bump("s")
                    lastExp[par] = sexp

                    def mkav(jc=jc, p3=p3, se=sexp, wO=lastOEvac):
                        def f(eng):
                            eng.wait_ge(sm["s"], se)
                            if jc == 0:
                                eng.wait_ge(sm["v"], wO)
                            eng.matmul(psO[0], vt[0][:, ds(jc * 65, 65)],
                                       pt3[p3][:, 0:512], start=(jc == 0),
                                       stop=(jc == NJC - 1)).then_inc(sm["t"], 1)
                            eng.matmul(psO[1], vt[1][:, ds(jc * 65, 65)],
                                       pt3[p3][:, 512:1024], start=(jc == 0),
                                       stop=(jc == NJC - 1)).then_inc(sm["t"], 1)
                        return f
                    pend_av.append(mkav())
                    if len(pend_av) > 2:
                        p.on("tensor", pend_av.pop(0))
                        p.bump("t")
                        lastAV[(jc - 2) % 3] = p.bump("t")
                for qi, f_ in enumerate(pend_av):
                    p.on("tensor", f_)
                    p.bump("t")
                    lastAV[(NJC - 2 + qi) % 3] = p.bump("t")
                tav = [p.val("t"), p.val("t")]

                vrec = p.fenced("vector", "v", [
                    lambda eng: eng.reciprocal(rsb[0][:], psO[0][64:65, :]),
                    lambda eng: eng.reciprocal(rsb[1][:], psO[1][64:65, :]),
                ], pre_waits=[("t", tav[1])])

                def f(eng, vrec=vrec, wR=lastREvac):
                    eng.wait_ge(sm["v"], vrec)
                    eng.matmul(psR[0], onesf[:], rsb[0][:], start=True, stop=True).then_inc(sm["t"], 1)
                    eng.matmul(psR[1], onesf[:], rsb[1][:], start=True, stop=True).then_inc(sm["t"], 1)
                p.on("tensor", f)
                trep = [p.bump("t"), p.bump("t")]

                vev = p.fenced("vector", "v", [
                    lambda eng: eng.tensor_copy(rrep[0][:], psR[0]),
                    lambda eng: eng.tensor_copy(rrep[1][:], psR[1]),
                    lambda eng: eng.tensor_tensor(attn_t[0:64, :], psO[0][0:64, :],
                                                  rrep[0][:], ALU.mult),
                    lambda eng: eng.tensor_tensor(attn_t[64:128, :], psO[1][0:64, :],
                                                  rrep[1][:], ALU.mult),
                ], pre_waits=[("t", trep[1]), ("st", lastAttnSt)])
                lastOEvac = vev
                lastREvac = vev

                def f(eng, vev=vev, hp=hp, ib=ib):
                    eng.wait_ge(sm["v"], vev)
                    eng.dma_start(out=attnfm[ds(hp * 128, 128), ds(ib * 512, 512)],
                                  in_=attn_t[:]).then_inc(sm["st"], 16)
                p.on("sync", f)
                lastAttnSt = p.bump("st", 16)

        st_flash = p.val("st")

        afull = kq[:, 0:4096]
        x1t4 = [e(nc.sbuf_tensor(f"x1t4_{i}", [128, 1024], F32)) for i in range(4)]

        wot_free = [0, 0]
        woev_free = [0, 0]
        afull_free = 0
        atok_free = [0, 0, 0, 0]
        x1t_free = [0, 0, 0, 0]
        x2t_free = [0, 0]
        st_x2 = 0
        for n in range(8):
            def f(eng, n=n, wld=afull_free, st_flash=st_flash):
                if n == 0:
                    eng.wait_ge(sm["st"], st_flash)
                if wld:
                    eng.wait_ge(sm["t"], wld)
                for k in range(8):
                    eng.dma_start(out=afull[:, ds(k * 512, 512)],
                                  in_=attnfm[ds(k * 128, 128), ds(n * 512, 512)]).then_inc(sm["ld"], 16)
                for tt in range(4):
                    pass
            p.on("sync", f)
            ld_af = p.bump("ld", 16 * 8)

            def f(eng, n=n, wv=[x1t_free[i] for i in range(4)]):
                for tt in range(4):
                    if wv[tt]:
                        eng.wait_ge(sm["v"], wv[tt])
                    eng.dma_start(out=x1t4[tt][:],
                                  in_=x1[ds(n * 512 + tt * 128, 128), :]).then_inc(sm["ld"], 16)
            p.on("sync", f)
            ld_x1 = p.bump("ld", 16 * 4)

            copies_start = p.val("v")
            for fo in range(8):
                wslot = fo % 2

                def f(eng, n=n, fo=fo, wslot=wslot, wv=wot_free[wslot]):
                    if wv:
                        eng.wait_ge(sm["t"], wv)
                    eng.dma_start(
                        out=wot[wslot][:].rearrange("p (k f) -> p k f", f=128),
                        in_=woT[:, ds(fo * 128, 128)].rearrange("(k p) f -> p k f", p=128),
                    ).then_inc(sm["lw"], 16)
                p.on("gpsimd", f)
                lww = p.bump("lw", 16)

                sfree = p.val("s") - 3
                def f(eng, n=n, fo=fo, wslot=wslot, lww=lww, ld_af=ld_af, sfree=sfree):
                    eng.wait_ge(sm["lw"], lww)
                    if fo == 0:
                        eng.wait_ge(sm["ld"], ld_af)
                    if fo >= 4 or n > 0:
                        eng.wait_ge(sm["s"], sfree)
                    for k in range(8):
                        inst = eng.matmul(psW[fo % 4], wot[wslot][:, ds(k * 128, 128)],
                                          afull[:, ds(k * 512, 512)],
                                          start=(k == 0), stop=(k == 7))
                    inst.then_inc(sm["t"], 1)
                p.on("tensor", f)
                tmm = p.bump("t")
                wot_free[wslot] = tmm
                if fo == 7:
                    afull_free = tmm

                def f(eng, tmm=tmm, fo=fo, wev=woev_free[fo % 2]):
                    eng.wait_ge(sm["t"], tmm)
                    if wev:
                        eng.wait_ge(sm["t"], wev)
                    eng.activation(wo_ev[fo % 2][:], psW[fo % 4], AF.Identity,
                                   bias=0.0, scale=1.0).then_inc(sm["s"], 1)
                p.on("scalar", f)
                sev = p.bump("s")

                vtr_base = p.val("v")
                for tt in range(4):
                    def f(eng, sev=sev, fo=fo, tt=tt, vtr_base=vtr_base):
                        if tt == 0:
                            eng.wait_ge(sm["s"], sev)
                            eng.wait_ge(sm["v"], vtr_base)
                        if tt >= 2:
                            eng.wait_ge(sm["v"], vtr_base + tt - 1)
                        eng.transpose(psT2[tt % 2][:], wo_ev[fo % 2][:, ds(tt * 128, 128)],
                                      ident[:]).then_inc(sm["t"], 1)
                    p.on("tensor", f)
                    ttr = p.bump("t")
                    if tt == 3:
                        woev_free[fo % 2] = ttr

                    def f(eng, ttr=ttr, fo=fo, tt=tt, wv=atok_free[tt]):
                        eng.wait_ge(sm["t"], ttr)
                        if fo == 0 and wv:
                            eng.wait_ge(sm["v"], wv)
                        eng.tensor_copy(atok[tt][:, ds(fo * 128, 128)],
                                        psT2[tt % 2][:]).then_inc(sm["v"], 1)
                    p.on("vector", f)
                    p.bump("v")
            v_copies = p.val("v")

            for tt in range(4):
                par = tt % 2
                vn = p.fenced("vector", "v", [
                    lambda eng, tt=tt, par=par: eng.tensor_tensor(
                        x2t[par][:], atok[tt][:], borep[:], ALU.add),
                    lambda eng, tt=tt, par=par: eng.tensor_add(
                        x2t[par][:], x2t[par][:], x1t4[tt][:]),
                ], pre_waits=[("v", v_copies), ("ld", ld_x1), ("st", x2t_free[par])])
                atok_free[tt] = vn
                x1t_free[tt] = vn

                def f(eng, vn=vn, n=n, tt=tt, par=par):
                    eng.wait_ge(sm["v"], vn)
                    eng.dma_start(out=x2_o[ds(n * 512 + tt * 128, 128), :],
                                  in_=x2t[par][:]).then_inc(sm["st"], 16)
                p.on("sync", f)
                x2t_free[par] = p.bump("st", 16)
        st_wo = p.val("st")

        em = LnRouterEmitter(p, sm, nc, ctx, "c", psT=psT2, psL=psL,
                             xt=[x1t4[0][:], x1t4[1][:]],
                             lnb=[atok[0][:], atok[1][:]],
                             fmb=[atok[2][:], atok[3][:]],
                             sq=afull[:, 0:1024])

        def f(eng, st_wo=st_wo):
            eng.wait_ge(sm["st"], st_wo)
        p.on("sync", f)
        def f(eng, st_wo=st_wo, lw_id=lw_id):
            eng.wait_ge(sm["st"], st_wo)
            eng.wait_ge(sm["lw"], lw_id)
        p.on("tensor", f)
        def f(eng, st_wo=st_wo):
            eng.wait_ge(sm["st"], st_wo)
        p.on("vector", f)
        def f(eng, st_wo=st_wo):
            eng.wait_ge(sm["st"], st_wo)
        p.on("scalar", f)
        for i in range(NT):
            em.tile(i, x2_o, ln3_o, sm2_o, wr_sb, brrep, eps_sb, ident)

        def f(eng, nst=p.val("st")):
            eng.wait_ge(sm["st"], nst)
        p.on("sync", f)

        p.replay(blk)
    return nc
def _shard_x(x):
    out = []
    for c in range(NCORES):
        b, h = c // 2, c % 2
        out.append(np.ascontiguousarray(x[b, h * T:(h + 1) * T, :]))
    return out


def _prep_B_host(inputs):
    hp = {}
    wup = np.asarray(inputs["L0_ffn_W_up"], np.float32)
    hp["wupP"] = np.ascontiguousarray(np.pad(wup, [(0, 0), (0, 0), (0, FPAD - 2730)]))
    bup = np.pad(np.asarray(inputs["L0_ffn_b_up"], np.float32), [(0, 0), (0, FPAD - 2730)])
    hp["bupP"] = np.ascontiguousarray(bup.reshape(4, 22, 128).transpose(2, 0, 1).reshape(128, 88))
    wdn = np.asarray(inputs["L0_ffn_W_down"], np.float32)
    hp["wdnP"] = np.ascontiguousarray(np.pad(wdn, [(0, 0), (0, FPAD - 2730), (0, 0)]))
    bdn = np.asarray(inputs["L0_ffn_b_down"], np.float32)
    hp["bdnP"] = np.ascontiguousarray(bdn.reshape(8, 8, 128).transpose(2, 0, 1).reshape(128, 64))
    for nm, key in [("wqT", "L0_attn_Wq"), ("wkT", "L0_attn_Wk"), ("wvT", "L0_attn_Wv")]:
        hp[nm] = np.ascontiguousarray(np.asarray(inputs[key], np.float32).T)
    for nm, key in [("bq_r", "L0_attn_bq"), ("bk_r", "L0_attn_bk"), ("bv_r", "L0_attn_bv")]:
        hp[nm] = np.ascontiguousarray(
            np.asarray(inputs[key], np.float32).reshape(8, 128).T)
    hp["gruW"] = np.ascontiguousarray(np.asarray(inputs["L0_attn_gru_W"], np.float32))
    hp["gruB"] = np.ascontiguousarray(np.asarray(inputs["L0_attn_gru_b"], np.float32).reshape(8, 1))
    hp["gruC"] = np.ascontiguousarray(np.asarray(inputs["L0_attn_gru_const"], np.float32).reshape(1, 16))
    hp["g48a_i"] = np.array([[1], [1], [1], [1], [0], [0], [0], [0]], np.float32)
    hp["g48b_i"] = np.array([[0], [0], [0], [0], [1], [1], [1], [1]], np.float32)
    M = np.zeros((128, 128), np.float32)
    for g in range(8):
        M[g * 16:(g + 1) * 16, g * 16:(g + 1) * 16] = 1.0
    hp["grpsum"] = M
    Ssel = np.zeros((128, 8), np.float32)
    for ee in range(8):
        Ssel[ee * 16, ee] = 1.0
    hp["thrsel"] = Ssel
    hp["eoff_i"] = (np.arange(8, dtype=np.float32) * C_CAP).reshape(8, 1)
    hp["bnd_i"] = ((np.arange(8, dtype=np.float32) + 1) * C_CAP).reshape(8, 1)
    hp["idf32p"] = np.eye(128, dtype=np.float32)
    return hp


def _sm_global(sm_cores):
    out = np.empty((B * S, E), np.float32)
    for c in range(NCORES):
        b, h = c // 2, c % 2
        out[b * S + h * T:(b * S + h * T) + T] = sm_cores[c]
    return out


def _tlog(tag, t0, res=None):
    import os, time, sys
    if os.environ.get("KTIME"):
        ns = getattr(res, "exec_time_ns", None) if res is not None else None
        print(f"[ktime] {tag}: {time.perf_counter() - t0:.2f}s exec_ns={ns}", file=sys.stderr)
    return time.perf_counter()


def kernel(**inputs):
    import time as _time
    _t0 = _time.perf_counter()
    x0 = np.asarray(inputs["L0_attn_hidden_states"], np.float32)
    consts = _host_consts()
    shards = _shard_x(x0)

    ncA = build_A()
    wr = np.ascontiguousarray(inputs["L0_ffn_Wr"], dtype=np.float32)
    br = np.ascontiguousarray(inputs["L0_ffn_br"], dtype=np.float32).reshape(1, E)
    mapsA = [dict(x=shards[c], wr=wr, br=br, identp=consts["ident"]) for c in range(NCORES)]
    resA = run_bass_kernel_spmd(ncA, mapsA, core_ids=list(range(NCORES)))
    _t0 = _tlog("A", _t0, resA)

    sm_cores = [np.asarray(resA.results[c]["sm_o"]) for c in range(NCORES)]
    ln_cores = [np.asarray(resA.results[c]["ln_o"]) for c in range(NCORES)]
    sm_all = _sm_global(sm_cores)
    sm_resh = np.ascontiguousarray(
        sm_all.reshape(16, 2048, 8).transpose(0, 2, 1).reshape(16, 8, 2048)
    )
    sm_resh = np.empty((128, 2048), np.float32)
    for pp in range(128):
        sm_resh[pp] = sm_all[(pp % 16) * 2048:(pp % 16 + 1) * 2048, pp // 16]

    hp = _prep_B_host(inputs)
    import os
    ncB = build_B(dbg=bool(os.environ.get('KDBG')))
    mapsB = []
    for c in range(NCORES):
        d = dict(hp)
        d["x0"] = shards[c]
        d["ln1"] = ln_cores[c]
        d["sm_resh"] = sm_resh
        d["smT_loc"] = np.ascontiguousarray(sm_cores[c].T)
        d["sm_loc"] = sm_cores[c]
        d["identp"] = consts["ident"]
        mapsB.append(d)
    resB = run_bass_kernel_spmd(ncB, mapsB, core_ids=list(range(NCORES)))
    _t0 = _tlog("B", _t0, resB)

    def as_u16(a):
        a = np.asarray(a)
        if a.dtype == np.uint16:
            return a
        return a.view(np.uint16) if a.dtype == bf16 else a.astype(bf16).view(np.uint16)

    kTs = [as_u16(resB.results[c]["kT_o"]) for c in range(NCORES)]
    vTs = [as_u16(resB.results[c]["vT_o"]) for c in range(NCORES)]
    qTs = [as_u16(resB.results[c]["qT_o"]) for c in range(NCORES)]
    gates = [np.asarray(resB.results[c]["gate_o"], np.float32) for c in range(NCORES)]
    x1s = [np.asarray(resB.results[c]["x1_o"], np.float32) for c in range(NCORES)]

    ones_bf16_u16 = np.full((NH, 64, 128, 1), 0x3F80, np.uint16)
    krevs, vtoks = {}, {}
    for bb in range(B):
        kfull = np.concatenate([kTs[2 * bb], kTs[2 * bb + 1]], axis=1)
        krevs[bb] = np.ascontiguousarray(
            kfull.reshape(D_MODEL, 64, 128)[:, :, ::-1].reshape(D_MODEL, S))
        vfull = np.concatenate([vTs[2 * bb], vTs[2 * bb + 1]], axis=1)
        vh = vfull.reshape(NH, 64, S).transpose(0, 2, 1)
        vrev = vh.reshape(NH, 64, 128, 64)[:, :, ::-1, :]
        vtoks[bb] = np.ascontiguousarray(
            np.concatenate([vrev, ones_bf16_u16], axis=3).reshape(NH, 64 * 128 * 65))

    btab = _bucket_table()
    re_in = np.asarray(inputs["L0_attn_rel_embed"], np.float32)
    pb = re_in[btab]
    pbrev = pb[::-1]
    woT = np.ascontiguousarray(np.asarray(inputs["L0_attn_Wo"], np.float32).T)
    bo_row = np.ascontiguousarray(np.asarray(inputs["L0_attn_bo"], np.float32).reshape(1, D_MODEL))
    onesf = np.ones((1, 64), np.float32)

    import os as _os
    ncC = build_C(dbg=bool(_os.environ.get('KDBG')))
    mapsC = []
    for c in range(NCORES):
        bb, hh = c // 2, c % 2
        q0 = hh * T
        pbc = np.zeros((NH, 12416), np.float32)
        avail = min(12416, 16383 - q0)
        pbc[:, 0:avail] = pbrev[q0:q0 + avail, :].T
        d = dict(
            x1=x1s[c], qt8=qTs[c], krevT=krevs[bb], vtok=vtoks[bb],
            gate_i=np.ascontiguousarray(gates[c]),
            pbrevc=np.ascontiguousarray(pbc), woT=woT, bo_i=bo_row,
            wr=wr, br=br, identp=consts["ident"], onesf_i=onesf,
        )
        mapsC.append(d)
    resC = run_bass_kernel_spmd(ncC, mapsC, core_ids=list(range(NCORES)))
    _t0 = _tlog("C", _t0, resC)

    sm2_cores = [np.asarray(resC.results[c]["sm2_o"], np.float32) for c in range(NCORES)]
    ln3_cores = [np.asarray(resC.results[c]["ln3_o"]) for c in range(NCORES)]
    x2_cores = [np.asarray(resC.results[c]["x2_o"], np.float32) for c in range(NCORES)]
    sm2_all = _sm_global(sm2_cores)
    sm2_resh = np.empty((128, 2048), np.float32)
    for pp in range(128):
        sm2_resh[pp] = sm2_all[(pp % 16) * 2048:(pp % 16 + 1) * 2048, pp // 16]
    mapsD = []
    for c in range(NCORES):
        d = dict(mapsB[c])
        d["x0"] = np.ascontiguousarray(x2_cores[c])
        d["ln1"] = np.ascontiguousarray(ln3_cores[c])
        d["sm_resh"] = sm2_resh
        d["smT_loc"] = np.ascontiguousarray(sm2_cores[c].T)
        d["sm_loc"] = sm2_cores[c]
        mapsD.append(d)
    ncD = build_B(mode="D")
    resD = run_bass_kernel_spmd(ncD, mapsD, core_ids=list(range(NCORES)))
    _t0 = _tlog("D", _t0, resD)

    out = np.empty((B, S, D_MODEL), np.float32)
    for c in range(NCORES):
        bb, hh = c // 2, c % 2
        out[bb, hh * T:(hh + 1) * T, :] = np.asarray(resD.results[c]["x1_o"], np.float32)
    if _os.environ.get("KDBG"):
        kernel._dbg = dict(resA=resA, resB=resB, resC=resC, resD=resD, mapsC=mapsC)
    return out
